# version 11
# baseline (speedup 1.0000x reference)
import contextlib
import os
import numpy as np
import ml_dtypes

import concourse.bass as bass
import concourse.mybir as mybir
from concourse.bass_utils import run_bass_kernel_spmd

F32 = mybir.dt.float32
BF16 = mybir.dt.bfloat16
I32 = mybir.dt.int32
AF = mybir.ActivationFunctionType
OP = mybir.AluOpType
AX = mybir.AxisListType

NCORES = 8
T = 4096
D = 1024
DC = D // 128
H = 16
HD = 64
W = 512
NCH = T // W
DFF = 4096
DI = 3072
OCF = DI // 128
_TRUNC_CAST = (os.environ.get("BASS_SIM_FP64") == "1"
               or os.environ.get("KERNEL_TRUNC_CAST") == "1")
KQ = 49152.0 if _TRUNC_CAST else 0.0
KQH = KQ + 0.5 if _TRUNC_CAST else 0.0
W8H = 8.5 if _TRUNC_CAST else 0.0
W8 = 8.0 if _TRUNC_CAST else 0.0
SGH = 0.5 if _TRUNC_CAST else 0.0
EPS = 1e-5
SCALE = 0.125

MX_XN = 0
MX_TMP = 40
MX_UP = 64
MX_GA = 256
MX_QN = 456
MX_KN = 464
MX_M = 480


class Sems:
    def __init__(self, nc, stack):
        self.nc, self.stack, self.h, self.n = nc, stack, {}, {}

    def get(self, name):
        if name not in self.h:
            self.h[name] = self.stack.enter_context(self.nc.semaphore(name))
            self.n[name] = 0
        return self.h[name]

    def inc(self, name, instr, amount=1):
        instr.then_inc(self.get(name), amount)
        self.n[name] += amount
        return instr

    def count(self, name):
        self.get(name)
        return self.n[name]

    def wait(self, eng, name, val=None):
        eng.wait_ge(self.get(name), self.n[name] if val is None else val)


def build_program():
    nc = bass.Bass("TRN2", target_bir_lowering=False)
    st = contextlib.ExitStack()
    S = Sems(nc, st)

    x_d = nc.dram_tensor("x", [T, D], F32, kind="ExternalInput").ap()
    wqkv_d = nc.dram_tensor("wqkv", [3 * D, D], F32, kind="ExternalInput").ap()
    wo_d = nc.dram_tensor("wo", [D, D], F32, kind="ExternalInput").ap()
    w1_d = nc.dram_tensor("w1", [DFF, D], F32, kind="ExternalInput").ap()
    b1_d = nc.dram_tensor("b1", [DFF], F32, kind="ExternalInput").ap()
    w2_d = nc.dram_tensor("w2", [D, DFF], F32, kind="ExternalInput").ap()
    b2_d = nc.dram_tensor("b2", [D], F32, kind="ExternalInput").ap()
    g1v_d = nc.dram_tensor("g1v", [D], F32, kind="ExternalInput").ap()
    b1v_d = nc.dram_tensor("b1v", [D], F32, kind="ExternalInput").ap()
    g2v_d = nc.dram_tensor("g2v", [D], F32, kind="ExternalInput").ap()
    b2v_d = nc.dram_tensor("b2v", [D], F32, kind="ExternalInput").ap()
    wg_d = nc.dram_tensor("wg", [DI, D], F32, kind="ExternalInput").ap()
    wu_d = nc.dram_tensor("wu", [DI, D], F32, kind="ExternalInput").ap()
    wd_d = nc.dram_tensor("wd", [D, DI], F32, kind="ExternalInput").ap()
    idf_d = nc.dram_tensor("idf", [128, 128], F32, kind="ExternalInput").ap()
    mU_d = nc.dram_tensor("mU", [128, 128], BF16, kind="ExternalInput").ap()
    mL_d = nc.dram_tensor("mL", [128, 128], BF16, kind="ExternalInput").ap()
    mq_d = nc.dram_tensor("mq", [128, 4 * 256], BF16, kind="ExternalInput").ap()
    out_d = nc.dram_tensor("out", [T, D], F32, kind="ExternalOutput").ap()

    wqkv_bf = nc.dram_tensor("wqkv_bf", [3 * D, D], BF16)
    wo_bf = nc.dram_tensor("wo_bf", [D, D], BF16)
    w1_bf = nc.dram_tensor("w1_bf", [DFF, D], BF16)
    w2_bf = nc.dram_tensor("w2_bf", [D, DFF], BF16)
    wg_dq = nc.dram_tensor("wg_dq", [DI, D], BF16)
    wu_dq = nc.dram_tensor("wu_dq", [DI, D], BF16)
    wd_dq = nc.dram_tensor("wd_dq", [D, DI], BF16)
    xn_h = nc.dram_tensor("xn_h", [DC, 128, T], F32)
    h_hbm = nc.dram_tensor("h_hbm", [DC, 128, T], BF16)
    up_h = nc.dram_tensor("up_h", [OCF, 128, T], F32)
    gr_h = nc.dram_tensor("gr_h", [OCF, 128, T], F32)
    t2_h = nc.dram_tensor("t2_h", [OCF, 128, T], F32)
    or_h = nc.dram_tensor("or_h", [OCF, 128, T], F32)
    x2_h = nc.dram_tensor("x2_h", [DC, 128, T], F32)
    acc_h = nc.dram_tensor("acc_h", [DC, 128, T], F32)
    cio = [(nc.dram_tensor("cin%d" % k, [1, 8], F32),
            nc.dram_tensor("cout%d" % k, [1, 8], F32)) for k in range(4)]

    idf = st.enter_context(nc.sbuf_tensor("idf_s", [128, 128], F32))
    mU = st.enter_context(nc.sbuf_tensor("mU_s", [128, 128], BF16))
    mL = st.enter_context(nc.sbuf_tensor("mL_s", [128, 128], BF16))
    mq = st.enter_context(nc.sbuf_tensor("mq_s", [128, 4 * 256], BF16))
    ones = st.enter_context(nc.sbuf_tensor("ones", [128, 128], F32))
    onepair = st.enter_context(nc.sbuf_tensor("onepair", [128, 2], F32))
    gb = st.enter_context(nc.sbuf_tensor("gb", [128, 48], F32))
    b1c = st.enter_context(nc.sbuf_tensor("b1c", [128, 32], F32))
    scl = st.enter_context(nc.sbuf_tensor("scl", [128, 16], F32))
    mxc = st.enter_context(nc.sbuf_tensor("mxc", [128, 512], F32))
    flat = st.enter_context(nc.sbuf_tensor("flat", [128, 2176], F32))
    nrm = st.enter_context(nc.sbuf_tensor("nrm", [128, 8], F32))

    G1C, B1C, G2C, B2C, MB2C = 0, 8, 16, 24, 32

    blk = st.enter_context(nc.Block())
    ENGS = ("sync", "gpsimd", "vector", "scalar", "tensor")

    def barrier(dma_sems=()):
        @blk.sync
        def _(sy):
            for sname in dma_sems:
                S.wait(sy, sname)
        for e in ENGS:
            @getattr(blk, e)
            def _(eng, e=e):
                S.inc("bar_" + e, eng.nop())
        for e in ENGS:
            @getattr(blk, e)
            def _(eng, e=e):
                for o in ENGS:
                    if o != e:
                        S.wait(eng, "bar_" + o)

    uniq = [0]

    def finalize_scalar_max(cols, prt, fb, wait_name):
        n = len(cols)
        c0 = cols[0]
        contig = all(cols[j] == c0 + j for j in range(n))
        assert contig
        uniq[0] += 1

        with nc.psum_tensor("pfz%d" % uniq[0], [128, 256], F32) as pfz:
            @blk.tensor
            def _(t):
                S.wait(t, wait_name)
                S.inc("fmt", t.transpose(pfz[0:n, 0:128], mxc[:, c0:c0 + n], idf[:]))

            @blk.vector
            def _(v):
                S.wait(v, "fmt")
                v.reduce_max(nrm[0:n, 7:8], pfz[0:n, 0:128], axis=AX.X)
                S.inc("fmq", v.drain())

            @blk.tensor
            def _(t):
                S.wait(t, "fmq")
                S.inc("fmt", t.transpose(pfz[0:1, 128:128 + n], nrm[0:n, 7:8], idf[0:n, 0:n]))

            @blk.vector
            def _(v):
                S.wait(v, "fmt")
                v.tensor_copy(flat[0:1, fb:fb + n], pfz[0:1, 128:128 + n])
                v.drain()
                if n < 8:
                    v.tensor_copy(flat[0:1, fb + n:fb + 8],
                                  flat[0:1, fb:fb + 1].unsqueeze(2).broadcast_to([1, 1, 8 - n]).squeeze(1))
                S.inc("fmv2", v.drain())

    def collective(k, fb):
        cin_k, cout_k = cio[k]

        @blk.gpsimd
        def _(g):
            S.wait(g, "fmv2")
            S.inc("ccw", g.dma_start(out=cin_k.ap()[0:1, 0:8], in_=flat[0:1, fb:fb + 8]), 16)
            S.wait(g, "ccw")
            S.inc("cc", g.collective_compute(
                "AllReduce", OP.max, replica_groups=[list(range(NCORES))],
                ins=[cin_k.ap().opt()], outs=[cout_k.ap().opt()]))
            S.wait(g, "cc")
            S.inc("ccr", g.dma_start(out=flat[0:1, fb + 8:fb + 16], in_=cout_k.ap()[0:1, 0:8]), 16)

    def derive_scales(pairs, fb):
        @blk.vector
        def _(v):
            S.wait(v, "ccr")
            for j, (c, q, _sc) in enumerate(pairs):
                v.tensor_scalar(flat[0:1, fb + 2 * j:fb + 2 * j + 1], flat[0:1, c:c + 1],
                                1.0 / q, 1e-12, OP.mult, OP.max)
            v.drain()
            for j in range(len(pairs)):
                v.reciprocal(flat[0:1, fb + 2 * j + 1:fb + 2 * j + 2],
                             flat[0:1, fb + 2 * j:fb + 2 * j + 1])
            S.inc("dsv", v.drain())

        uniq[0] += 1
        with nc.psum_tensor("pdsx%d" % uniq[0], [128, 16], F32) as pds:
            @blk.tensor
            def _(t):
                S.wait(t, "dsv")
                S.inc("dst", t.matmul(pds[:, 0:2 * len(pairs)], ones[0:1, 0:128],
                                      flat[0:1, fb:fb + 2 * len(pairs)], start=True, stop=True))

            @blk.vector
            def _(v):
                S.wait(v, "dst")
                for j, (_c, _q, c2) in enumerate(pairs):
                    v.tensor_copy(scl[:, c2:c2 + 2], pds[:, 2 * j:2 * j + 2])
                S.inc("dsv", v.drain())

    @blk.gpsimd
    def _(g):
        S.inc("cast1", g.dma_start(out=wqkv_bf.ap(), in_=wqkv_d), 16)
        S.inc("cast1", g.dma_start(out=wo_bf.ap(), in_=wo_d), 16)
        S.inc("cast2", g.dma_start(out=w1_bf.ap(), in_=w1_d), 16)
        S.inc("cast2", g.dma_start(out=w2_bf.ap(), in_=w2_d), 16)

    with nc.sbuf_tensor("vecst", [48, 128], F32) as vecst, \
         nc.sbuf_tensor("b1st", [32, 128], F32) as b1st, \
         nc.psum_tensor("pvec", [128, 48], F32) as pvec, \
         nc.psum_tensor("pb1", [128, 32], F32) as pb1:

        @blk.sync
        def _(sy):
            S.inc("cst", sy.dma_start(out=idf[:], in_=idf_d), 16)
            S.inc("cst", sy.dma_start(out=mU[:], in_=mU_d), 16)
            S.inc("cst", sy.dma_start(out=mL[:], in_=mL_d), 16)
            S.inc("cst", sy.dma_start(out=mq[:], in_=mq_d), 16)
            S.inc("cst", sy.dma_start(out=vecst[0:8, :], in_=g1v_d.rearrange("(a b) -> a b", b=128)), 16)
            S.inc("cst", sy.dma_start(out=vecst[8:16, :], in_=b1v_d.rearrange("(a b) -> a b", b=128)), 16)
            S.inc("cst", sy.dma_start(out=vecst[16:24, :], in_=g2v_d.rearrange("(a b) -> a b", b=128)), 16)
            S.inc("cst", sy.dma_start(out=vecst[24:32, :], in_=b2v_d.rearrange("(a b) -> a b", b=128)), 16)
            S.inc("cst", sy.dma_start(out=vecst[32:40, :], in_=b2_d.rearrange("(a b) -> a b", b=128)), 16)
            S.inc("cst", sy.dma_start(out=b1st[:, :], in_=b1_d.rearrange("(a b) -> a b", b=128)), 16)

        @blk.tensor
        def _(t):
            S.wait(t, "cst")
            t.transpose(pvec[:, 0:40], vecst[0:40, :], idf[0:40, 0:40])
            S.inc("fmt", t.transpose(pb1[:, 0:32], b1st[:, :], idf[0:32, 0:32]))

        @blk.vector
        def _(v):
            v.memset(ones[:], 1.0)
            v.memset(onepair[:, :], 0.0)
            v.drain()
            v.memset(onepair[0:64, 0:1], 1.0)
            v.memset(onepair[64:128, 1:2], 1.0)
            S.wait(v, "fmt")
            v.tensor_copy(gb[:, 0:40], pvec[:, 0:40])
            v.tensor_copy(b1c[:, 0:32], pb1[:, 0:32])
            S.inc("dsv", v.drain())

        barrier(("cst",))

    p1 = contextlib.ExitStack()
    hst = p1.enter_context(nc.sbuf_tensor("hst", [128, 2 * DC * W], BF16))
    xin = p1.enter_context(nc.sbuf_tensor("xin", [128, 2 * 4096], F32))
    p1st = p1.enter_context(nc.sbuf_tensor("p1st", [128, 64], F32))
    hpre = p1.enter_context(nc.sbuf_tensor("hpre", [128, 4096], F32))
    xnst = p1.enter_context(nc.sbuf_tensor("xnst", [128, 2 * DC * W], F32))
    ptr1 = p1.enter_context(nc.psum_tensor("ptr1", [128, 1024], F32))
    ptr2 = p1.enter_context(nc.psum_tensor("ptr2", [128, 1024], F32))

    for tt in range(NCH):
        buf = tt % 2
        xo = buf * 4096
        so = buf * DC * W
        ldsem = "ldA" if buf == 0 else "ldB"

        @blk.sync
        def _(sy, tt=tt, xo=xo, ldsem=ldsem):
            if tt >= 2:
                S.wait(sy, "p1v", S.count("p1v") - 1)
            src = x_d[tt * W:(tt + 1) * W, :].rearrange("(g p) d -> p g d", p=128)
            S.inc(ldsem, sy.dma_start(out=xin[:, xo:xo + 4096].rearrange("p (g d) -> p g d", d=1024), in_=src), 16)

        @blk.scalar
        def _(sc, tt=tt, xo=xo, ldsem=ldsem):
            S.wait(sc, ldsem)
            if tt >= 1:
                S.wait(sc, "p1t")
            for g in range(4):
                xg = xin[:, xo + g * 1024: xo + (g + 1) * 1024]
                sc.activation(hpre[:, g * 1024:(g + 1) * 1024], xg, AF.Square)
            S.inc("p1sa", sc.drain())

        @blk.vector
        def _(v, tt=tt, xo=xo, ldsem=ldsem):
            S.wait(v, ldsem)
            if tt == 0:
                S.wait(v, "dsv")
            for g in range(4):
                xg = xin[:, xo + g * 1024: xo + (g + 1) * 1024]
                v.reduce_sum(p1st[:, g:g + 1], xg, axis=AX.X)
            S.wait(v, "p1sa")
            for g in range(4):
                v.reduce_sum(p1st[:, 4 + g:5 + g], hpre[:, g * 1024:(g + 1) * 1024], axis=AX.X)
            v.drain()
            v.tensor_single_scalar(p1st[:, 8:12], p1st[:, 0:4], 1.0 / D, OP.mult)
            v.tensor_single_scalar(p1st[:, 12:16], p1st[:, 4:8], 1.0 / D, OP.mult)
            v.drain()
            v.tensor_tensor(p1st[:, 16:20], p1st[:, 8:12], p1st[:, 8:12], OP.mult)
            v.drain()
            v.tensor_tensor(p1st[:, 16:20], p1st[:, 12:16], p1st[:, 16:20], OP.subtract)
            v.drain()
            v.tensor_single_scalar(p1st[:, 16:20], p1st[:, 16:20], EPS, OP.add)
            S.inc("p1q", v.drain())

        @blk.scalar
        def _(sc):
            S.wait(sc, "p1q")
            S.inc("p1r", sc.activation(p1st[:, 20:24], p1st[:, 16:20], AF.Sqrt))

        @blk.vector
        def _(v, tt=tt, xo=xo):
            S.wait(v, "p1r")
            v.reciprocal(p1st[:, 24:28], p1st[:, 20:24])
            v.drain()
            for g in range(4):
                xg = xin[:, xo + g * 1024: xo + (g + 1) * 1024]
                v.tensor_scalar(xg, xg, p1st[:, 8 + g:9 + g], p1st[:, 24 + g:25 + g], OP.subtract, OP.mult)
            v.drain()
            for g in range(4):
                xg = xin[:, xo + g * 1024: xo + (g + 1) * 1024]
                v.reduce_max(mxc[:, MX_XN + tt * 4 + g:MX_XN + tt * 4 + g + 1], xg,
                             axis=AX.X, apply_absolute_value=True)
                v.reduce_sum(p1st[:, 32 + g:33 + g], xg, axis=AX.X)
            S.inc("p1b", v.drain())
            S.wait(v, "p1sa", S.count("p1sa") + 1)
            for g in range(4):
                v.reduce_sum(p1st[:, 36 + g:37 + g], hpre[:, g * 1024:(g + 1) * 1024], axis=AX.X)
            v.drain()
            v.tensor_single_scalar(p1st[:, 40:44], p1st[:, 32:36], 1.0 / D, OP.mult)
            v.tensor_single_scalar(p1st[:, 44:48], p1st[:, 36:40], 1.0 / D, OP.mult)
            v.drain()
            v.tensor_tensor(p1st[:, 48:52], p1st[:, 40:44], p1st[:, 40:44], OP.mult)
            v.drain()
            v.tensor_tensor(p1st[:, 48:52], p1st[:, 44:48], p1st[:, 48:52], OP.subtract)
            v.drain()
            v.tensor_single_scalar(p1st[:, 48:52], p1st[:, 48:52], EPS, OP.add)
            S.inc("p1q", v.drain())

        @blk.scalar
        def _(sc, xo=xo):
            S.wait(sc, "p1b")
            for g in range(4):
                xg = xin[:, xo + g * 1024: xo + (g + 1) * 1024]
                sc.activation(hpre[:, g * 1024:(g + 1) * 1024], xg, AF.Square)
            S.inc("p1sa", sc.drain())

        @blk.scalar
        def _(sc):
            S.wait(sc, "p1q")
            S.inc("p1r", sc.activation(p1st[:, 52:56], p1st[:, 48:52], AF.Sqrt))

        @blk.vector
        def _(v, tt=tt, xo=xo):
            S.wait(v, "p1r")
            v.reciprocal(p1st[:, 56:60], p1st[:, 52:56])
            v.drain()
            for g in range(4):
                xg = xin[:, xo + g * 1024: xo + (g + 1) * 1024]
                v.tensor_scalar(hpre[:, g * 1024:(g + 1) * 1024], xg,
                                p1st[:, 40 + g:41 + g], p1st[:, 56 + g:57 + g], OP.subtract, OP.mult)
            S.inc("p1b", v.drain())

        p1e_base = S.count("p1e")

        @blk.tensor
        def _(t, tt=tt, xo=xo, p1e_base=p1e_base):
            S.wait(t, "p1b")
            for i in range(32):
                g, dc = i // DC, i % DC
                pslot = i % 2
                if i >= 2:
                    t.wait_ge(S.get("p1e"), p1e_base + i - 1)
                t.transpose(ptr1[:, pslot * 512:pslot * 512 + 128],
                            xin[:, xo + g * 1024 + dc * 128: xo + g * 1024 + (dc + 1) * 128], idf[:])
                S.inc("p1t", t.transpose(ptr2[:, pslot * 512:pslot * 512 + 128],
                                         hpre[:, g * 1024 + dc * 128: g * 1024 + (dc + 1) * 128], idf[:]))

        p1t_base = S.count("p1t") - 32

        so2 = buf * DC * W

        @blk.vector
        def _(v, tt=tt, so=so, so2=so2, p1t_base=p1t_base):
            if tt >= 2:
                S.wait(v, "p1oA" if tt % 2 == 0 else "p1oB")
            for i in range(32):
                g, dc = i // DC, i % DC
                pslot = i % 2
                v.wait_ge(S.get("p1t"), p1t_base + i + 1)
                v.tensor_copy(xnst[:, so + dc * W + g * 128: so + dc * W + (g + 1) * 128],
                              ptr1[:, pslot * 512:pslot * 512 + 128])
                S.inc("p1e",
                      v.tensor_scalar(hst[:, so2 + dc * W + g * 128: so2 + dc * W + (g + 1) * 128],
                                      ptr2[:, pslot * 512:pslot * 512 + 128],
                                      gb[:, G1C + dc:G1C + dc + 1], gb[:, B1C + dc:B1C + dc + 1],
                                      OP.mult, OP.add))
            S.inc("p1v", v.drain())

        @blk.sync
        def _(sy, tt=tt, so=so, so2=so2):
            S.wait(sy, "p1v")
            dst = xn_h.ap()[:, :, tt * W:(tt + 1) * W].transpose([1, 0, 2])
            S.inc("p1oA" if tt % 2 == 0 else "p1oB",
                  sy.dma_start(out=dst,
                               in_=xnst[:, so:so + DC * W].rearrange("p (c w) -> p c w", w=W)), 16)
            dsth = h_hbm.ap()[:, :, tt * W:(tt + 1) * W].transpose([1, 0, 2])
            S.inc("p1oA" if tt % 2 == 0 else "p1oB",
                  sy.dma_start(out=dsth,
                               in_=hst[:, so2:so2 + DC * W].rearrange("p (c w) -> p c w", w=W)), 16)

    @blk.vector
    def _(v):
        S.wait(v, "p1e")
        v.reduce_max(mxc[:, MX_TMP:MX_TMP + 1], mxc[:, MX_XN:MX_XN + 4 * NCH], axis=AX.X)
        S.inc("fmq", v.drain())

    finalize_scalar_max([MX_TMP], 128, 0, "fmq")
    collective(0, 0)
    derive_scales([(8, 32767.0, 0)], 16)

    wst = p1.enter_context(nc.sbuf_tensor("wst", [128, 2 * DI], F32))
    wblk = p1.enter_context(nc.sbuf_tensor("wblk", [128, 3 * (DI // 32)], F32))
    wi32 = p1.enter_context(nc.sbuf_tensor("wi32", [128, DI], I32))
    wbf16 = p1.enter_context(nc.sbuf_tensor("wbf16", [128, 2 * DI], BF16))

    dqi = 0
    for mi, (src, dst, nrow, I) in enumerate(
            [(wg_d, wg_dq, DI // 128, 1024), (wu_d, wu_dq, DI // 128, 1024), (wd_d, wd_dq, D // 128, DI)]):
        nb = I // 32
        for r in range(nrow):
            buf = dqi % 2
            ldsem = "ldA" if buf == 0 else "ldB"
            stsem = "dqoA" if buf == 0 else "dqoB"

            @blk.sync
            def _(sy, src=src, r=r, buf=buf, I=I, dqi=dqi, ldsem=ldsem):
                if dqi >= 2:
                    S.wait(sy, "dqv", S.count("dqv") - 1)
                S.inc(ldsem, sy.dma_start(out=wst[:, buf * DI: buf * DI + I],
                                          in_=src[r * 128:(r + 1) * 128, :]), 16)

            @blk.vector
            def _(v, buf=buf, I=I, nb=nb, dqi=dqi, ldsem=ldsem, stsem=stsem):
                S.wait(v, ldsem)
                if dqi >= 2:
                    S.wait(v, stsem)
                w = wst[:, buf * DI: buf * DI + I]
                w3 = w.rearrange("p (b k) -> p b k", k=32)
                bm = wblk[:, 0:nb]
                sblk = wblk[:, nb:2 * nb]
                invs = wblk[:, 2 * nb:3 * nb]
                v.reduce_max(bm, w3, axis=AX.X, apply_absolute_value=True)
                v.drain()
                v.tensor_scalar(sblk, bm, 1.0 / 7.0, 1e-12, OP.mult, OP.max)
                v.drain()
                v.reciprocal(invs, sblk)
                v.drain()
                v.tensor_tensor(w3, w3, invs.unsqueeze(2).broadcast_to([128, nb, 32]), OP.mult)
                v.drain()
                v.tensor_single_scalar(wi32[:, 0:I], w, W8H, OP.add)
                v.drain()
                v.scalar_tensor_tensor(wbf16[:, buf * DI: buf * DI + I].rearrange("p (b k) -> p b k", k=32),
                                       wi32[:, 0:I].rearrange("p (b k) -> p b k", k=32), W8,
                                       sblk.unsqueeze(2).broadcast_to([128, nb, 32]),
                                       OP.subtract, OP.mult)
                S.inc("dqv", v.drain())

            @blk.sync
            def _(sy, dst=dst, r=r, buf=buf, I=I, stsem=stsem):
                S.wait(sy, "dqv")
                S.inc(stsem, sy.dma_start(out=dst.ap()[r * 128:(r + 1) * 128, :],
                                          in_=wbf16[:, buf * DI: buf * DI + I]), 16)
            dqi += 1

    p1.close()
    barrier(("p1oA", "p1oB", "dqoA", "dqoB", "ldA", "ldB", "cst"))

    p2 = contextlib.ExitStack()
    xq_sb = p2.enter_context(nc.sbuf_tensor("xq_sb", [128, DC * T], BF16))
    p2ld = p2.enter_context(nc.sbuf_tensor("p2ld", [128, 2 * W], F32))
    p2i = p2.enter_context(nc.sbuf_tensor("p2i", [128, W], I32))
    wtT = p2.enter_context(nc.sbuf_tensor("wtT", [128, DC * DI], BF16))
    p2ev = p2.enter_context(nc.sbuf_tensor("p2ev", [128, 4 * W], F32))
    p2ps = p2.enter_context(nc.psum_tensor("p2ps", [128, 4 * W], F32))

    for i in range(DC * NCH):
        dc, tt = i // NCH, i % NCH
        buf = i % 2
        ldsem = "ldA" if buf == 0 else "ldB"

        @blk.sync
        def _(sy, dc=dc, tt=tt, buf=buf, i=i, ldsem=ldsem):
            if i == 0:
                S.wait(sy, "p1oA")
                S.wait(sy, "p1oB")
            if i >= 2:
                S.wait(sy, "xqv", S.count("xqv") - 1)
            S.inc(ldsem, sy.dma_start(out=p2ld[:, buf * W:(buf + 1) * W],
                                      in_=xn_h.ap()[dc, :, tt * W:(tt + 1) * W]), 16)

        @blk.vector
        def _(v, dc=dc, tt=tt, buf=buf, i=i, ldsem=ldsem):
            S.wait(v, ldsem)
            if i == 0:
                S.wait(v, "dsv")
            xg = p2ld[:, buf * W:(buf + 1) * W]
            v.tensor_scalar(p2i[:, :], xg, scl[:, 1:2], KQH, OP.mult, OP.add)
            v.drain()
            v.tensor_scalar(xq_sb[:, dc * T + tt * W: dc * T + (tt + 1) * W], p2i[:, :],
                            KQ, scl[:, 0:1], OP.subtract, OP.mult)
            S.inc("xqv", v.drain())

    for mi, (wdq, dsth, mbase) in enumerate([(wu_dq, up_h, MX_UP), (wg_dq, gr_h, MX_GA)]):
        @blk.sync
        def _(sy, wdq=wdq, mi=mi):
            if mi == 0:
                S.wait(sy, "dqoA")
                S.wait(sy, "dqoB")
            else:
                S.wait(sy, "p2m")
            for dc in range(DC):
                S.inc("wTA", sy.dma_start(out=wtT[:, dc * DI:(dc + 1) * DI],
                                          in_=wdq.ap()[:, dc * 128:(dc + 1) * 128], transpose=True), 16)

        for j in range(OCF * NCH):
            oc, tt = j // NCH, j % NCH
            pslot = j % 4

            @blk.tensor
            def _(t, mi=mi, oc=oc, tt=tt, pslot=pslot, j=j):
                if j == 0:
                    S.wait(t, "wTA")
                    if mi == 0:
                        S.wait(t, "xqv")
                if S.count("p2s") >= 4:
                    S.wait(t, "p2s", S.count("p2s") - 3)
                ps = p2ps[:, pslot * W:(pslot + 1) * W]
                for dc in range(DC):
                    mm = t.matmul(ps, wtT[:, dc * DI + oc * 128: dc * DI + (oc + 1) * 128],
                                  xq_sb[:, dc * T + tt * W: dc * T + (tt + 1) * W],
                                  start=(dc == 0), stop=(dc == DC - 1))
                S.inc("p2m", mm)

            @blk.scalar
            def _(sc, pslot=pslot, j=j, mi=mi):
                S.wait(sc, "p2m")
                if S.count("p2e") >= 4:
                    S.wait(sc, "p2e", S.count("p2e") - 3)
                if S.count("p2s") >= 4:
                    S.wait(sc, "p2o%d" % pslot)
                S.inc("p2s", sc.copy(p2ev[:, pslot * W:(pslot + 1) * W], p2ps[:, pslot * W:(pslot + 1) * W]))

            @blk.vector
            def _(v, pslot=pslot, mbase=mbase, j=j):
                S.wait(v, "p2s")
                S.inc("p2e", v.reduce_max(mxc[:, mbase + j:mbase + j + 1],
                                          p2ev[:, pslot * W:(pslot + 1) * W],
                                          axis=AX.X, apply_absolute_value=True))

            @blk.sync
            def _(sy, oc=oc, tt=tt, pslot=pslot, dsth=dsth):
                S.wait(sy, "p2e")
                S.inc("p2o%d" % pslot,
                      sy.dma_start(out=dsth.ap()[oc, :, tt * W:(tt + 1) * W],
                                   in_=p2ev[:, pslot * W:(pslot + 1) * W]), 16)

    @blk.vector
    def _(v):
        S.wait(v, "p2e")
        v.reduce_max(mxc[:, MX_TMP:MX_TMP + 1], mxc[:, MX_UP:MX_UP + OCF * NCH], axis=AX.X)
        v.reduce_max(mxc[:, MX_TMP + 1:MX_TMP + 2], mxc[:, MX_GA:MX_GA + OCF * NCH], axis=AX.X)
        S.inc("fmq", v.drain())

    finalize_scalar_max([MX_TMP, MX_TMP + 1], 128, 32, "fmq")
    collective(1, 32)
    derive_scales([(40, 32767.0, 2), (41, 32767.0, 4)], 48)

    p2.close()
    barrier(("p2o0", "p2o1", "p2o2", "p2o3", "ldA", "ldB", "wTA"))

    f2 = contextlib.ExitStack()
    fld = f2.enter_context(nc.sbuf_tensor("fld", [128, 4 * W], F32))
    fi32 = f2.enter_context(nc.sbuf_tensor("fi32", [128, W], I32))
    fgate = f2.enter_context(nc.sbuf_tensor("fgate", [128, 4 * W], F32))
    ft2 = f2.enter_context(nc.sbuf_tensor("ft2", [128, 4 * W], F32))

    def emit_f2_load(j):
        if j >= OCF * NCH:
            return
        oc, tt = j // NCH, j % NCH
        buf = j % 4

        @blk.sync
        def _(sy):
            if j == 0:
                for s4 in range(4):
                    S.wait(sy, "p2o%d" % s4)
            if j >= 4:
                S.wait(sy, "f2v2", j - 3)
            S.inc("f2l%d" % buf, sy.dma_start(out=fld[:, buf * W:(buf + 1) * W],
                                              in_=gr_h.ap()[oc, :, tt * W:(tt + 1) * W]), 16)

    def emit_f2_tile(j):
        oc, tt = j // NCH, j % NCH
        buf = j % 4

        @blk.vector
        def _(v):
            S.wait(v, "f2l%d" % buf, 16 * (j // 4 + 1))
            if j == 0:
                S.wait(v, "dsv")
            if j >= 4:
                S.wait(v, "f2sg", j - 3)
                S.wait(v, "f2s%d" % buf)
            v.tensor_scalar(fi32[:, :], fld[:, buf * W:(buf + 1) * W], scl[:, 5:6], KQH,
                            OP.mult, OP.add)
            v.drain()
            v.tensor_scalar(fgate[:, buf * W:(buf + 1) * W], fi32[:, :], KQ, scl[:, 4:5],
                            OP.subtract, OP.mult)
            S.inc("f2v1", v.drain())

        @blk.scalar
        def _(sc):
            S.wait(sc, "f2v1")
            S.inc("f2sg", sc.activation(fld[:, buf * W:(buf + 1) * W],
                                        fgate[:, buf * W:(buf + 1) * W], AF.Exp, scale=-1.0))

        @blk.vector
        def _(v):
            S.wait(v, "f2sg")
            v.tensor_single_scalar(fld[:, buf * W:(buf + 1) * W], fld[:, buf * W:(buf + 1) * W],
                                   1.0, OP.add)
            v.drain()
            v.reciprocal(fld[:, buf * W:(buf + 1) * W], fld[:, buf * W:(buf + 1) * W])
            v.drain()
            v.tensor_scalar(fi32[:, :], fld[:, buf * W:(buf + 1) * W], 65536.0, SGH, OP.mult, OP.add)
            v.drain()
            v.tensor_scalar(fld[:, buf * W:(buf + 1) * W], fi32[:, :], 65535.0, 1.0 / 65536.0,
                            OP.min, OP.mult)
            v.drain()
            v.tensor_tensor(ft2[:, buf * W:(buf + 1) * W], fgate[:, buf * W:(buf + 1) * W],
                            fld[:, buf * W:(buf + 1) * W], OP.mult)
            v.drain()
            S.inc("f2v2", v.reduce_max(mxc[:, MX_UP + j:MX_UP + j + 1], ft2[:, buf * W:(buf + 1) * W],
                                       axis=AX.X, apply_absolute_value=True))

        @blk.sync
        def _(sy):
            S.wait(sy, "f2v2")
            S.inc("f2s%d" % buf, sy.dma_start(out=t2_h.ap()[oc, :, tt * W:(tt + 1) * W],
                                              in_=ft2[:, buf * W:(buf + 1) * W]), 16)

    def emit_f3_load(j):
        if j >= OCF * NCH:
            return
        oc, tt = j // NCH, j % NCH
        buf = j % 4

        @blk.sync
        def _(sy):
            if j == 0:
                for s4 in range(4):
                    S.wait(sy, "f2s%d" % s4)
            if j >= 4:
                S.wait(sy, "f3v", j - 3)
            S.inc("f3l%d" % buf, sy.dma_start(out=fld[:, buf * W:(buf + 1) * W],
                                              in_=t2_h.ap()[oc, :, tt * W:(tt + 1) * W]), 16)
            S.inc("f3l%d" % buf, sy.dma_start(out=fgate[:, buf * W:(buf + 1) * W],
                                              in_=up_h.ap()[oc, :, tt * W:(tt + 1) * W]), 16)

    def emit_f3_tile(j):
        oc, tt = j // NCH, j % NCH
        buf = j % 4

        @blk.vector
        def _(v):
            S.wait(v, "f3l%d" % buf, 32 * (j // 4 + 1))
            if j == 0:
                S.wait(v, "dsv")
            if j >= 4:
                S.wait(v, "f3s%d" % buf)
            v.tensor_scalar(fi32[:, :], fld[:, buf * W:(buf + 1) * W], scl[:, 7:8], KQH,
                            OP.mult, OP.add)
            v.drain()
            v.tensor_scalar(fld[:, buf * W:(buf + 1) * W], fi32[:, :], KQ, scl[:, 6:7],
                            OP.subtract, OP.mult)
            v.drain()
            v.tensor_scalar(fi32[:, :], fgate[:, buf * W:(buf + 1) * W], scl[:, 3:4], KQH,
                            OP.mult, OP.add)
            v.drain()
            v.tensor_scalar(fgate[:, buf * W:(buf + 1) * W], fi32[:, :], KQ, scl[:, 2:3],
                            OP.subtract, OP.mult)
            v.drain()
            v.tensor_tensor(ft2[:, buf * W:(buf + 1) * W], fld[:, buf * W:(buf + 1) * W],
                            fgate[:, buf * W:(buf + 1) * W], OP.mult)
            v.drain()
            S.inc("f3v", v.reduce_max(mxc[:, MX_GA + j:MX_GA + j + 1], ft2[:, buf * W:(buf + 1) * W],
                                      axis=AX.X, apply_absolute_value=True))

        @blk.sync
        def _(sy):
            S.wait(sy, "f3v")
            S.inc("f3s%d" % buf, sy.dma_start(out=or_h.ap()[oc, :, tt * W:(tt + 1) * W],
                                              in_=ft2[:, buf * W:(buf + 1) * W]), 16)

    p3 = contextlib.ExitStack()
    hstg = p3.enter_context(nc.sbuf_tensor("hstg", [128, 2 * DC * W], BF16))
    wqkvT = p3.enter_context(nc.sbuf_tensor("wqkvT", [128, DC * 3 * D], BF16))
    woT = p3.enter_context(nc.sbuf_tensor("woT", [128, DC * D], BF16))
    qf = p3.enter_context(nc.sbuf_tensor("qf", [128, DC * W], BF16))
    kf2 = p3.enter_context(nc.sbuf_tensor("kf2", [128, 2 * DC * W], BF16))
    vsb2 = p3.enter_context(nc.sbuf_tensor("vsb2", [128, 2 * 4 * H * 65], BF16))
    avn = p3.enter_context(nc.sbuf_tensor("avn", [128, DC * W], BF16))
    xnstg = p3.enter_context(nc.sbuf_tensor("xnstg", [128, 2 * W], F32))
    x2stg = p3.enter_context(nc.sbuf_tensor("x2stg", [128, 2 * W], F32))
    pTs = p3.enter_context(nc.sbuf_tensor("pTs", [128, 2 * 768], BF16))
    q2st = p3.enter_context(nc.sbuf_tensor("q2st", [128, W], F32))

    @blk.sync
    def _(sy):
        S.wait(sy, "cast1")
        for dc in range(DC):
            S.inc("wTA", sy.dma_start(out=wqkvT[:, dc * 3 * D:(dc + 1) * 3 * D],
                                      in_=wqkv_bf.ap()[:, dc * 128:(dc + 1) * 128], transpose=True), 16)
        for dc in range(DC):
            S.inc("wTA", sy.dma_start(out=woT[:, dc * D:(dc + 1) * D],
                                      in_=wo_bf.ap()[:, dc * 128:(dc + 1) * 128], transpose=True), 16)

    for c in range(NCH):
        cur, prv = c % 2, (c + 1) % 2
        VS = 4 * H * 65
        p3qm_base = S.count("p3qm")

        @blk.sync
        def _(sy, c=c, cur=cur, p3qm_base=p3qm_base):
            if c == 0:
                S.wait(sy, "p1oA")
                S.wait(sy, "p1oB")
            if c >= 2:
                S.wait(sy, "p3qm", p3qm_base - 24)
            S.inc("hlA" if cur == 0 else "hlB",
                  sy.dma_start(out=hstg[:, cur * DC * W: (cur + 1) * DC * W]
                               .rearrange("p (k w) -> p k w", w=W),
                               in_=h_hbm.ap()[:, :, c * W:(c + 1) * W].transpose([1, 0, 2])), 16)

        with nc.psum_tensor("pqk_%d" % c, [128, 2 * W], F32) as pqk:
            for wi, rowbase in ((0, 0), (1, D)):
                for oc in range(DC):
                    pslot = (wi * DC + oc) % 2

                    @blk.tensor
                    def _(t, c=c, cur=cur, wi=wi, rowbase=rowbase, oc=oc, pslot=pslot, pqk=pqk):
                        if c == 0 and wi == 0 and oc == 0:
                            S.wait(t, "wTA")
                        if wi == 0 and oc == 0:
                            S.wait(t, "hlA" if cur == 0 else "hlB")
                        if c > 0 and wi == 0 and oc == 0:
                            S.wait(t, "p3wv")
                        if S.count("p3qe") >= 2:
                            S.wait(t, "p3qe", S.count("p3qe") - 1)
                        ps = pqk[:, pslot * W:(pslot + 1) * W]
                        for dc in range(DC):
                            mm = t.matmul(ps, wqkvT[:, dc * 3 * D + rowbase + oc * 128:
                                                    dc * 3 * D + rowbase + (oc + 1) * 128],
                                          hstg[:, cur * DC * W + dc * W: cur * DC * W + (dc + 1) * W],
                                          start=(dc == 0), stop=(dc == DC - 1))
                        S.inc("p3qm", mm)

                    @blk.vector
                    def _(v, c=c, wi=wi, oc=oc, pslot=pslot, cur=cur, pqk=pqk):
                        S.wait(v, "p3qm")
                        if c > 0 and wi == 0 and oc == 0:
                            S.wait(v, "p3xm")
                        if wi == 0:
                            o = v.tensor_copy(qf[:, oc * W:(oc + 1) * W], pqk[:, pslot * W:(pslot + 1) * W])
                        else:
                            o = v.tensor_copy(kf2[:, cur * DC * W + oc * W: cur * DC * W + (oc + 1) * W],
                                              pqk[:, pslot * W:(pslot + 1) * W])
                        S.inc("p3qe", o)

        @blk.vector
        def _(v, c=c, cur=cur, VS=VS):
            S.wait(v, "p3av")
            v.memset(vsb2[:, cur * VS:(cur + 1) * VS].rearrange("p (s o) -> p s o", o=65)[:, :, 64:65], 1.0)
            S.inc("p3vo", v.drain())

        with contextlib.ExitStack() as vsc:
            pv8 = [vsc.enter_context(nc.psum_tensor("pv8_%d_%d" % (c, b), [128, 64], F32)) for b in range(8)]
            for g in range(4):
                for hg in range(2):
                    @blk.tensor
                    def _(t, c=c, g=g, hg=hg, pv8=pv8):
                        S.wait(t, "p3qe")
                        if S.count("p3ve") > 0:
                            S.wait(t, "p3ve")
                        for dc in range(DC):
                            hb = hstg[:, cur * DC * W + dc * W + g * 128: cur * DC * W + dc * W + (g + 1) * 128]
                            for hh in range(8):
                                h = hg * 8 + hh
                                mm = t.matmul(pv8[hh][:, 0:64], hb,
                                              wqkvT[:, dc * 3 * D + 2 * D + h * 64:
                                                    dc * 3 * D + 2 * D + (h + 1) * 64],
                                              start=(dc == 0), stop=(dc == DC - 1))
                        S.inc("p3qm", mm)

                    @blk.vector
                    def _(v, c=c, g=g, hg=hg, cur=cur, pv8=pv8, VS=VS):
                        S.wait(v, "p3qm")
                        S.wait(v, "p3vo")
                        for hh in range(8):
                            h = hg * 8 + hh
                            S.inc("p3ve", v.tensor_copy(
                                vsb2[:, cur * VS + (g * H + h) * 65: cur * VS + (g * H + h) * 65 + 64],
                                pv8[hh][:, 0:64]))

        with nc.psum_tensor("pnm_%d" % c, [128, W], F32) as pnm:
            for wi, mbase in ((0, MX_QN), (1, MX_KN + cur * 8)):
                for dc in range(DC):
                    @blk.scalar
                    def _(sc, wi=wi, dc=dc, cur=cur):
                        S.wait(sc, "p3qe")
                        S.wait(sc, "p3nt")
                        src = (qf[:, dc * W:(dc + 1) * W] if wi == 0
                               else kf2[:, cur * DC * W + dc * W: cur * DC * W + (dc + 1) * W])
                        S.inc("p3sq", sc.activation(q2st[:, :], src, AF.Square))

                    @blk.tensor
                    def _(t, wi=wi, dc=dc, pnm=pnm):
                        S.wait(t, "p3sq")
                        if wi == 0 and dc == 0:
                            S.wait(t, "p3ve")
                        S.wait(t, "p3nr")
                        S.inc("p3nt", t.matmul(pnm[0:2, 0:W], onepair[:, 0:2], q2st[:, :],
                                               start=True, stop=True))

                    @blk.vector
                    def _(v, wi=wi, dc=dc, mbase=mbase, pnm=pnm):
                        S.wait(v, "p3nt")
                        S.inc("p3nr", v.reduce_max(mxc[0:2, mbase + dc:mbase + dc + 1], pnm[0:2, 0:W],
                                                   axis=AX.X))

            @blk.vector
            def _(v, c=c, cur=cur):
                S.wait(v, "p3nr")
                v.reduce_max(nrm[0:2, 0:1], mxc[0:2, MX_QN:MX_QN + 8], axis=AX.X)
                if c > 0:
                    v.reduce_max(nrm[0:2, 1:2], mxc[0:2, MX_KN:MX_KN + 16], axis=AX.X)
                else:
                    v.reduce_max(nrm[0:2, 1:2], mxc[0:2, MX_KN + cur * 8:MX_KN + cur * 8 + 8], axis=AX.X)
                S.inc("p3na", v.drain())

            @blk.tensor
            def _(t, pnm=pnm):
                S.wait(t, "p3na")
                S.inc("p3nt", t.transpose(pnm[0:2, 128:130], nrm[0:2, 0:2], idf[0:2, 0:2]))

            @blk.vector
            def _(v, pnm=pnm):
                S.wait(v, "p3nt")
                v.reduce_max(nrm[0:2, 2:3], pnm[0:2, 128:130], axis=AX.X)
                S.inc("p3na", v.drain())

            @blk.tensor
            def _(t, pnm=pnm):
                S.wait(t, "p3na")
                S.inc("p3nt", t.transpose(pnm[0:1, 130:132], nrm[0:2, 2:3], idf[0:2, 0:2]))

            @blk.vector
            def _(v, pnm=pnm):
                S.wait(v, "p3nt")
                v.tensor_copy(nrm[0:1, 4:6], pnm[0:1, 130:132])
                S.inc("p3na", v.drain())

            @blk.scalar
            def _(sc):
                S.wait(sc, "p3na")
                S.inc("p3ns", sc.activation(nrm[0:1, 6:8], nrm[0:1, 4:6], AF.Sqrt))

            @blk.vector
            def _(v):
                S.wait(v, "p3ns")
                v.tensor_tensor(nrm[0:1, 3:4], nrm[0:1, 6:7], nrm[0:1, 7:8], OP.mult)
                v.drain()
                v.tensor_single_scalar(nrm[0:1, 3:4], nrm[0:1, 3:4], -SCALE, OP.mult)
                S.inc("p3na", v.drain())

            @blk.tensor
            def _(t, pnm=pnm):
                S.wait(t, "p3na")
                S.inc("p3nt", t.matmul(pnm[:, 132:133], ones[0:1, 0:128], nrm[0:1, 3:4],
                                       start=True, stop=True))

            @blk.vector
            def _(v, pnm=pnm):
                S.wait(v, "p3nt")
                S.wait(v, "p3xs")
                S.inc("p3nf", v.tensor_copy(mxc[:, MX_M:MX_M + 1], pnm[:, 132:133]))

        with nc.psum_tensor("pSa_%d" % c, [128, 768], F32) as pSa, \
             nc.psum_tensor("pSb_%d" % c, [128, 768], F32) as pSb, \
             nc.psum_tensor("pAVa_%d" % c, [65, W], F32) as pAVa, \
             nc.psum_tensor("pAVb_%d" % c, [65, W], F32) as pAVb, \
             nc.psum_tensor("pCB_%d" % c, [128, 2 * W], F32) as pCB:

            for dc in range(DC):
                for qi2 in range(2):
                    g0 = 4 * c + 2 * qi2
                    kts = [G for G in range(g0 - 4, g0 + 2) if G >= 0]
                    nkt = len(kts)
                    j0 = 6 - nkt
                    groups = [(gi, kts[gi:gi + 3]) for gi in range(0, nkt, 3)]

                    for gi, gkts in groups:
                        last = (gi + len(gkts) == nkt)

                        @blk.tensor
                        def _(t, c=c, dc=dc, qi2=qi2, gi=gi, gkts=gkts, cur=cur, prv=prv,
                              pSa=pSa, pSb=pSb):
                            S.wait(t, "p3nf")
                            S.wait(t, "p3xs")
                            if dc == 0 and qi2 == 0 and gi == 0:
                                S.wait(t, "p3ve")
                            for aa, ps in ((0, pSa), (64, pSb)):
                                for j, G in enumerate(gkts):
                                    buf = cur if G >= 4 * c else prv
                                    kcol = buf * DC * W + dc * W + (G % 4) * 128
                                    mm = t.matmul(ps[:, j * 256:(j + 1) * 256],
                                                  kf2[aa:aa + 64, kcol:kcol + 128],
                                                  qf[aa:aa + 64, dc * W + qi2 * 256: dc * W + qi2 * 256 + 256],
                                                  start=True, stop=True)
                            S.inc("p3xm", mm)

                        @blk.scalar
                        def _(sc, gkts=gkts, pSa=pSa, pSb=pSb):
                            S.wait(sc, "p3xm")
                            S.wait(sc, "p3av")
                            n = len(gkts)
                            for aa, ps in ((0, pSa), (1, pSb)):
                                S.inc("p3xs", sc.activation(pTs[:, aa * 768: aa * 768 + n * 256],
                                                            ps[:, 0:n * 256], AF.Exp,
                                                            bias=mxc[:, MX_M:MX_M + 1], scale=SCALE))

                        @blk.vector
                        def _(v, gi=gi, gkts=gkts, j0=j0):
                            S.wait(v, "p3xs")
                            for aa in range(2):
                                for j in range(len(gkts)):
                                    jr = j0 + gi + j
                                    if jr in (0, 1, 4, 5):
                                        mslot = {0: 0, 1: 1, 4: 2, 5: 3}[jr]
                                        v.tensor_tensor(pTs[:, aa * 768 + j * 256: aa * 768 + (j + 1) * 256],
                                                        pTs[:, aa * 768 + j * 256: aa * 768 + (j + 1) * 256],
                                                        mq[:, mslot * 256:(mslot + 1) * 256],
                                                        OP.mult)
                            S.inc("p3xv", v.drain())

                        @blk.tensor
                        def _(t, c=c, dc=dc, qi2=qi2, gi=gi, gkts=gkts, cur=cur, prv=prv,
                              nkt=nkt, last=last, pAVa=pAVa, pAVb=pAVb, VS=VS):
                            S.wait(t, "p3xv")
                            if qi2 == 0 and gi == 0:
                                S.wait(t, "p3xe")
                            for aa, pav in ((0, pAVa), (1, pAVb)):
                                h = 2 * dc + aa
                                for j, G in enumerate(gkts):
                                    buf = cur if G >= 4 * c else prv
                                    vcol = buf * VS + ((G % 4) * H + h) * 65
                                    mm = t.matmul(pav[0:65, qi2 * 256:(qi2 + 1) * 256],
                                                  vsb2[:, vcol:vcol + 65],
                                                  pTs[:, aa * 768 + j * 256: aa * 768 + (j + 1) * 256],
                                                  start=(gi == 0 and j == 0),
                                                  stop=(last and j == len(gkts) - 1))
                            S.inc("p3av", mm)

                for aa, pav in ((0, pAVa), (1, pAVb)):
                    @blk.vector
                    def _(v, dc=dc, aa=aa, pav=pav):
                        S.wait(v, "p3av")
                        v.tensor_copy(flat[64:65, 1024 + aa * W:1024 + aa * W + W], pav[64:65, 0:W])
                        v.drain()
                        v.reciprocal(flat[64:65, 1024 + aa * W:1024 + aa * W + W],
                                     flat[64:65, 1024 + aa * W:1024 + aa * W + W])
                        S.inc("p3xv", v.drain())

                    @blk.tensor
                    def _(t, aa=aa, pCB=pCB):
                        S.wait(t, "p3xv")
                        S.inc("p3lb", t.matmul(pCB[:, aa * W:(aa + 1) * W], ones[64:65, 0:128],
                                               flat[64:65, 1024 + aa * W:1024 + aa * W + W],
                                               start=True, stop=True))

                    @blk.vector
                    def _(v, dc=dc, aa=aa, pav=pav, pCB=pCB):
                        S.wait(v, "p3lb")
                        h = 2 * dc + aa
                        po = (h % 2) * 64
                        v.tensor_copy(flat[:, aa * W:(aa + 1) * W], pCB[:, aa * W:(aa + 1) * W])
                        v.drain()
                        cb = flat[po:po + 64, aa * W:(aa + 1) * W]
                        S.inc("p3xe", v.tensor_tensor(avn[po:po + 64, (h // 2) * W: (h // 2 + 1) * W],
                                                      pav[0:64, 0:W], cb, OP.mult))

        with nc.psum_tensor("pwo_%d" % c, [128, 2 * W], F32) as pwo:
            for oc in range(DC):
                pslot = oc % 2
                ldsem = "ldA" if pslot == 0 else "ldB"

                @blk.sync
                def _(sy, c=c, oc=oc, pslot=pslot, ldsem=ldsem):
                    if S.count("p3wv") >= 2:
                        S.wait(sy, "p3wv", S.count("p3wv") - 1)
                    S.inc(ldsem, sy.dma_start(out=xnstg[:, pslot * W:(pslot + 1) * W],
                                              in_=xn_h.ap()[oc, :, c * W:(c + 1) * W]), 16)

                @blk.tensor
                def _(t, c=c, oc=oc, pslot=pslot, pwo=pwo):
                    S.wait(t, "p3xe")
                    if S.count("p3wv") >= 2:
                        S.wait(t, "p3wv", S.count("p3wv") - 1)
                    ps = pwo[:, pslot * W:(pslot + 1) * W]
                    for dc in range(DC):
                        mm = t.matmul(ps, woT[:, dc * D + oc * 128: dc * D + (oc + 1) * 128],
                                      avn[:, dc * W:(dc + 1) * W],
                                      start=(dc == 0), stop=(dc == DC - 1))
                    S.inc("p3qm", mm)

                @blk.vector
                def _(v, pslot=pslot, ldsem=ldsem, pwo=pwo):
                    S.wait(v, "p3qm")
                    S.wait(v, ldsem)
                    if S.count("p3wv") >= 2:
                        S.wait(v, "p3wA" if pslot == 0 else "p3wB")
                    S.inc("p3wv", v.tensor_tensor(x2stg[:, pslot * W:(pslot + 1) * W],
                                                  pwo[:, pslot * W:(pslot + 1) * W],
                                                  xnstg[:, pslot * W:(pslot + 1) * W], OP.add))

                @blk.sync
                def _(sy, c=c, oc=oc, pslot=pslot):
                    S.wait(sy, "p3wv")
                    S.inc("p3wA" if pslot == 0 else "p3wB",
                          sy.dma_start(out=x2_h.ap()[oc, :, c * W:(c + 1) * W],
                                       in_=x2stg[:, pslot * W:(pslot + 1) * W]), 16)

        for jf in range(c * OCF, (c + 1) * OCF):
            if c == 0 and jf == 0:
                for jl in range(4):
                    emit_f2_load(jl)
            emit_f2_load(jf + 4)
            emit_f2_tile(jf)

    p3.close()

    @blk.vector
    def _(v):
        S.wait(v, "f2v2")
        v.reduce_max(mxc[:, MX_TMP:MX_TMP + 1], mxc[:, MX_UP:MX_UP + OCF * NCH], axis=AX.X)
        S.inc("fmq", v.drain())

    finalize_scalar_max([MX_TMP], 128, 64, "fmq")
    collective(2, 64)
    derive_scales([(72, 32767.0, 6)], 80)

    barrier(("p3wA", "p3wB", "f2s0", "f2s1", "f2s2", "f2s3", "f2l0", "f2l1", "f2l2", "f2l3", "ldA", "ldB", "hlA", "hlB", "wTA"))

    p4 = contextlib.ExitStack()
    w1T = p4.enter_context(nc.sbuf_tensor("w1T", [128, DC * DFF], BF16))
    w2T = p4.enter_context(nc.sbuf_tensor("w2T", [128, 32 * D], BF16))
    x2t = p4.enter_context(nc.sbuf_tensor("x2t", [128, 2 * DC * 256], F32))
    h2t = p4.enter_context(nc.sbuf_tensor("h2t", [128, DC * 256], BF16))
    g1buf = p4.enter_context(nc.sbuf_tensor("g1buf", [128, 32 * 256], BF16))
    sqstg = p4.enter_context(nc.sbuf_tensor("sqstg", [128, 2 * 256], F32))
    pmm = p4.enter_context(nc.psum_tensor("pmm", [128, 2 * W], F32))
    pstat = p4.enter_context(nc.psum_tensor("pstat", [2, 2 * W], F32))
    pab = p4.enter_context(nc.psum_tensor("pab", [128, 2 * W], F32))

    @blk.sync
    def _(sy):
        S.wait(sy, "cast1")
        S.wait(sy, "cast2")
        for dc in range(DC):
            S.inc("wTA", sy.dma_start(out=w1T[:, dc * DFF:(dc + 1) * DFF],
                                      in_=w1_bf.ap()[:, dc * 128:(dc + 1) * 128], transpose=True), 16)
        for kc in range(32):
            S.inc("wTA", sy.dma_start(out=w2T[:, kc * D:(kc + 1) * D],
                                      in_=w2_bf.ap()[:, kc * 128:(kc + 1) * 128], transpose=True), 16)

    WH = W // 2
    for tt in range(2 * NCH):
        xob = (tt % 2) * DC * 256
        @blk.sync
        def _(sy, tt=tt, xob=xob):
            if tt == 0:
                S.wait(sy, "p3wA")
                S.wait(sy, "p3wB")
            if tt >= 2:
                S.wait(sy, "p4av", S.count("p4av") - DC)
            for dc in range(DC):
                S.inc("ldA", sy.dma_start(out=x2t[:, xob + dc * WH:xob + (dc + 1) * WH],
                                          in_=x2_h.ap()[dc, :, tt * WH:(tt + 1) * WH]), 16)

        for dc in range(DC):
            @blk.scalar
            def _(sc, tt=tt, dc=dc, xob=xob):
                S.wait(sc, "ldA")
                if S.count("p4st") >= 2:
                    S.wait(sc, "p4st", S.count("p4st") - 1)
                S.inc("p4sq", sc.activation(sqstg[:, (dc % 2) * WH:(dc % 2 + 1) * WH],
                                            x2t[:, xob + dc * WH:xob + (dc + 1) * WH], AF.Square))

            @blk.tensor
            def _(t, tt=tt, dc=dc, xob=xob):
                S.wait(t, "p4sq")
                if tt == 0 and dc == 0:
                    S.wait(t, "wTA")
                if dc == 0:
                    S.wait(t, "ldA")
                    if tt > 0:
                        S.wait(t, "p4va")
                t.matmul(pstat[0:1, 0:WH], ones[:, 0:1], x2t[:, xob + dc * WH:xob + (dc + 1) * WH],
                         start=(dc == 0), stop=(dc == DC - 1))
                S.inc("p4st", t.matmul(pstat[0:1, W:W + WH], ones[:, 0:1],
                                       sqstg[:, (dc % 2) * WH:(dc % 2 + 1) * WH],
                                       start=(dc == 0), stop=(dc == DC - 1)))

        FB = 128

        @blk.vector
        def _(v, tt=tt):
            S.wait(v, "p4st")
            v.tensor_single_scalar(flat[0:1, FB:FB + WH], pstat[0:1, 0:WH], 1.0 / D, OP.mult)
            v.tensor_single_scalar(flat[0:1, FB + WH:FB + 2 * WH], pstat[0:1, W:W + WH], 1.0 / D, OP.mult)
            v.drain()
            v.tensor_tensor(flat[0:1, FB + 2 * WH:FB + 3 * WH], flat[0:1, FB:FB + WH],
                            flat[0:1, FB:FB + WH], OP.mult)
            v.drain()
            v.tensor_tensor(flat[0:1, FB + 2 * WH:FB + 3 * WH], flat[0:1, FB + WH:FB + 2 * WH],
                            flat[0:1, FB + 2 * WH:FB + 3 * WH], OP.subtract)
            v.drain()
            v.tensor_single_scalar(flat[0:1, FB + 2 * WH:FB + 3 * WH], flat[0:1, FB + 2 * WH:FB + 3 * WH],
                            EPS, OP.add)
            S.inc("p4va", v.drain())

        @blk.scalar
        def _(sc):
            S.wait(sc, "p4va")
            S.inc("p4sr", sc.activation(flat[0:1, FB + 3 * WH:FB + 4 * WH],
                                        flat[0:1, FB + 2 * WH:FB + 3 * WH], AF.Sqrt))

        @blk.vector
        def _(v):
            S.wait(v, "p4sr")
            v.reciprocal(flat[0:1, FB + 2 * WH:FB + 3 * WH], flat[0:1, FB + 3 * WH:FB + 4 * WH])
            v.drain()
            v.tensor_tensor(flat[0:1, FB:FB + WH], flat[0:1, FB:FB + WH],
                            flat[0:1, FB + 2 * WH:FB + 3 * WH], OP.mult)
            v.drain()
            v.tensor_single_scalar(flat[0:1, FB:FB + WH], flat[0:1, FB:FB + WH], -1.0, OP.mult)
            S.inc("p4va", v.drain())

        @blk.tensor
        def _(t, tt=tt):
            S.wait(t, "p4va")
            if tt > 0:
                S.wait(t, "p4h2")
            t.matmul(pab[:, 0:WH], ones[0:1, 0:128], flat[0:1, FB + 2 * WH:FB + 3 * WH], start=True, stop=True)
            S.inc("p4m1", t.matmul(pab[:, W:W + WH], ones[0:1, 0:128], flat[0:1, FB:FB + WH],
                                   start=True, stop=True))

        @blk.vector
        def _(v, tt=tt):
            S.wait(v, "p4m1")
            if tt > 0:
                S.wait(v, "p4ge")
            for dc in range(DC):
                v.tensor_tensor(sqstg[:, 0:WH], x2t[:, xob + dc * WH:xob + (dc + 1) * WH], pab[:, 0:WH], OP.mult)
                v.drain()
                v.tensor_tensor(sqstg[:, 0:WH], sqstg[:, 0:WH], pab[:, W:W + WH], OP.add)
                v.drain()
                v.tensor_scalar(h2t[:, dc * WH:(dc + 1) * WH], sqstg[:, 0:WH],
                                gb[:, G2C + dc:G2C + dc + 1], gb[:, B2C + dc:B2C + dc + 1],
                                OP.mult, OP.add)
                v.drain()
            S.inc("p4h2", v.drain())

        for o1 in range(32):
            pslot = o1 % 2

            @blk.tensor
            def _(t, tt=tt, o1=o1, pslot=pslot):
                S.wait(t, "p4h2")
                if S.count("p4ge") >= 2:
                    S.wait(t, "p4ge", S.count("p4ge") - 1)
                for dc in range(DC):
                    mm = t.matmul(pmm[:, pslot * W:pslot * W + WH],
                                  w1T[:, dc * DFF + o1 * 128: dc * DFF + (o1 + 1) * 128],
                                  h2t[:, dc * WH:(dc + 1) * WH], start=(dc == 0), stop=(dc == DC - 1))
                S.inc("p4m1", mm)

            @blk.scalar
            def _(sc, tt=tt, o1=o1, pslot=pslot):
                S.wait(sc, "p4m1")
                if tt > 0 and o1 == 0:
                    S.wait(sc, "p4m1")
                S.inc("p4ge", sc.activation(g1buf[:, o1 * WH:(o1 + 1) * WH],
                                            pmm[:, pslot * W:pslot * W + WH],
                                            AF.Gelu, bias=b1c[:, o1:o1 + 1], scale=1.0))

        for o2 in range(DC):
            pslot = o2 % 2

            @blk.tensor
            def _(t, tt=tt, o2=o2, pslot=pslot):
                S.wait(t, "p4ge")
                if S.count("p4av") >= 2:
                    S.wait(t, "p4av", S.count("p4av") - 1)
                for kc in range(32):
                    mm = t.matmul(pmm[:, pslot * W:pslot * W + WH],
                                  w2T[:, kc * D + o2 * 128: kc * D + (o2 + 1) * 128],
                                  g1buf[:, kc * WH:(kc + 1) * WH], start=(kc == 0), stop=(kc == 31))
                S.inc("p4m1", mm)

            @blk.vector
            def _(v, o2=o2, pslot=pslot, xob=xob):
                S.wait(v, "p4m1")
                if S.count("p4av") >= 2:
                    S.wait(v, "p3wA" if pslot == 0 else "p3wB")
                S.inc("p4av", v.scalar_tensor_tensor(flat[:, 1152 + pslot * WH:1152 + (pslot + 1) * WH],
                                                     pmm[:, pslot * W:pslot * W + WH],
                                                     gb[:, MB2C + o2:MB2C + o2 + 1],
                                                     x2t[:, xob + o2 * WH:xob + (o2 + 1) * WH], OP.add, OP.add))

            @blk.sync
            def _(sy, tt=tt, o2=o2, pslot=pslot):
                S.wait(sy, "p4av")
                S.inc("p3wA" if pslot == 0 else "p3wB",
                      sy.dma_start(out=acc_h.ap()[o2, :, tt * WH:(tt + 1) * WH],
                                   in_=flat[:, 1152 + pslot * WH:1152 + (pslot + 1) * WH]), 16)

        for jf in range(tt * (OCF * NCH) // (2 * NCH), (tt + 1) * (OCF * NCH) // (2 * NCH)):
            if tt == 0 and jf == 0:
                for jl in range(4):
                    emit_f3_load(jl)
            emit_f3_load(jf + 4)
            emit_f3_tile(jf)

    p4.close()

    @blk.vector
    def _(v):
        S.wait(v, "f3v")
        v.reduce_max(mxc[:, MX_TMP:MX_TMP + 1], mxc[:, MX_GA:MX_GA + OCF * NCH], axis=AX.X)
        S.inc("fmq", v.drain())

    finalize_scalar_max([MX_TMP], 128, 96, "fmq")
    collective(3, 96)
    derive_scales([(104, 32767.0, 8)], 112)

    f2.close()
    barrier(("p3wA", "p3wB", "f3s0", "f3s1", "f3s2", "f3s3", "f3l0", "f3l1", "f3l2", "f3l3", "ldA", "wTA"))

    f4 = contextlib.ExitStack()
    wdT = f4.enter_context(nc.sbuf_tensor("wdT", [128, OCF * D], BF16))
    obuf = f4.enter_context(nc.sbuf_tensor("obuf", [128, 2 * OCF * W], BF16))
    old = f4.enter_context(nc.sbuf_tensor("old", [128, 2 * W], F32))
    oi32 = f4.enter_context(nc.sbuf_tensor("oi32", [128, W], I32))
    accld = f4.enter_context(nc.sbuf_tensor("accld", [128, 2 * W], F32))
    sumf = f4.enter_context(nc.sbuf_tensor("sumf", [128, DC * W], F32))
    xfin = f4.enter_context(nc.sbuf_tensor("xfin", [128, 2 * 1024], F32))
    outst = f4.enter_context(nc.sbuf_tensor("outst", [128, 2 * 1024], F32))
    pdn = f4.enter_context(nc.psum_tensor("pdn", [128, 2 * W], F32))
    pfin = f4.enter_context(nc.psum_tensor("pfin", [128, 2 * 1024], F32))

    @blk.sync
    def _(sy):
        S.wait(sy, "dqoA")
        S.wait(sy, "dqoB")
        for kc in range(OCF):
            S.inc("wTA", sy.dma_start(out=wdT[:, kc * D:(kc + 1) * D],
                                      in_=wd_dq.ap()[:, kc * 128:(kc + 1) * 128], transpose=True), 16)

    for tt in range(NCH):
        ob = (tt % 2) * OCF * W
        for kc in range(OCF):
            buf = kc % 2
            ldsem = "ldA" if buf == 0 else "ldB"

            @blk.sync
            def _(sy, tt=tt, kc=kc, buf=buf, ldsem=ldsem):
                if tt * OCF + kc >= 2:
                    S.wait(sy, "f4v", S.count("f4v") - 1)
                S.inc(ldsem, sy.dma_start(out=old[:, buf * W:(buf + 1) * W],
                                          in_=or_h.ap()[kc, :, tt * W:(tt + 1) * W]), 16)

            @blk.vector
            def _(v, tt=tt, kc=kc, buf=buf, ldsem=ldsem, ob=ob):
                S.wait(v, ldsem)
                if tt == 0 and kc == 0:
                    S.wait(v, "dsv")
                if tt >= 2 and kc == 0:
                    S.wait(v, "f4mm", S.count("f4mm") - 40)
                v.tensor_scalar(oi32[:, :], old[:, buf * W:(buf + 1) * W], scl[:, 9:10], KQH,
                                OP.mult, OP.add)
                v.drain()
                v.tensor_scalar(obuf[:, ob + kc * W:ob + (kc + 1) * W], oi32[:, :], KQ, scl[:, 8:9],
                                OP.subtract, OP.mult)
                S.inc("f4v", v.drain())

        for d2 in range(DC):
            pslot = d2 % 2
            lasem = "f4la" if pslot == 0 else "f4lb"

            @blk.sync
            def _(sy, tt=tt, d2=d2, pslot=pslot, lasem=lasem):
                if S.count("f4av") >= 2:
                    S.wait(sy, "f4av", S.count("f4av") - 1)
                S.inc(lasem, sy.dma_start(out=accld[:, pslot * W:(pslot + 1) * W],
                                          in_=acc_h.ap()[d2, :, tt * W:(tt + 1) * W]), 16)

            @blk.tensor
            def _(t, tt=tt, d2=d2, pslot=pslot, ob=ob):
                S.wait(t, "f4v")
                if tt == 0 and d2 == 0:
                    S.wait(t, "wTA")
                if S.count("f4av") >= 2:
                    S.wait(t, "f4av", S.count("f4av") - 1)
                for kc in range(OCF):
                    mm = t.matmul(pdn[:, pslot * W:(pslot + 1) * W],
                                  wdT[:, kc * D + d2 * 128: kc * D + (d2 + 1) * 128],
                                  obuf[:, ob + kc * W:ob + (kc + 1) * W], start=(kc == 0), stop=(kc == OCF - 1))
                S.inc("f4mm", mm)

            @blk.vector
            def _(v, d2=d2, pslot=pslot, lasem=lasem):
                S.wait(v, "f4mm")
                S.wait(v, lasem)
                S.wait(v, "f4mm")
                S.inc("f4av", v.tensor_tensor(sumf[:, d2 * W:(d2 + 1) * W],
                                              pdn[:, pslot * W:(pslot + 1) * W],
                                              accld[:, pslot * W:(pslot + 1) * W], OP.add))

        for g in range(4):
            gbuf = g % 2
            lxsem = "f4lx" if gbuf == 0 else "f4ly"
            stsem = "stA" if gbuf == 0 else "stB"

            @blk.sync
            def _(sy, tt=tt, g=g, gbuf=gbuf, lxsem=lxsem):
                if S.count("f4fv") >= 2:
                    S.wait(sy, "f4fv", S.count("f4fv") - 1)
                src = x_d[tt * W + g * 128: tt * W + (g + 1) * 128, :]
                S.inc(lxsem, sy.dma_start(out=xfin[:, gbuf * 1024:(gbuf + 1) * 1024], in_=src), 16)

            @blk.tensor
            def _(t, tt=tt, g=g, gbuf=gbuf):
                S.wait(t, "f4av")
                if S.count("f4fv") >= 2:
                    S.wait(t, "f4fv", S.count("f4fv") - 1)
                for d2 in range(DC):
                    S.inc("f4mm", t.transpose(pfin[:, gbuf * 1024 + d2 * 128: gbuf * 1024 + (d2 + 1) * 128],
                                              sumf[:, d2 * W + g * 128: d2 * W + (g + 1) * 128], idf[:]))

            @blk.vector
            def _(v, g=g, gbuf=gbuf, lxsem=lxsem, stsem=stsem):
                S.wait(v, "f4mm")
                S.wait(v, lxsem)
                if S.count("f4fv") >= 2:
                    S.wait(v, stsem)
                S.inc("f4fv", v.tensor_tensor(outst[:, gbuf * 1024:(gbuf + 1) * 1024],
                                              pfin[:, gbuf * 1024:(gbuf + 1) * 1024],
                                              xfin[:, gbuf * 1024:(gbuf + 1) * 1024], OP.add))

            @blk.sync
            def _(sy, tt=tt, g=g, gbuf=gbuf, stsem=stsem):
                S.wait(sy, "f4fv")
                S.inc(stsem, sy.dma_start(out=out_d[tt * W + g * 128: tt * W + (g + 1) * 128, :],
                                          in_=outst[:, gbuf * 1024:(gbuf + 1) * 1024]), 16)

    @blk.sync
    def _(sy):
        S.wait(sy, "stA")
        S.wait(sy, "stB")

    f4.close()
    return nc, st


_PROG_CACHE = {}


def _get_program():
    if "nc" not in _PROG_CACHE:
        nc, st = build_program()
        _PROG_CACHE["nc"] = nc
        _PROG_CACHE["st"] = st
    return _PROG_CACHE["nc"]


def make_in_maps(inputs):
    x_full = np.ascontiguousarray(np.asarray(inputs["L0_attn_x"], dtype=np.float32))
    rk = np.arange(128)
    mUh = (rk[:, None] >= rk[None, :] + 1).astype(ml_dtypes.bfloat16)
    mLh = (rk[:, None] <= rk[None, :]).astype(ml_dtypes.bfloat16)
    rq = np.arange(256)
    m0 = (rk[:, None] >= rq[None, :] + 1)
    m1 = (rk[:, None] >= rq[None, :] - 127)
    m4 = (rk[:, None] <= rq[None, :])
    m5 = (rk[:, None] <= rq[None, :] - 128)
    mqh = np.concatenate([m0, m1, m4, m5], axis=1).astype(ml_dtypes.bfloat16)
    idn = np.eye(128, dtype=np.float32)
    common = {
        "wqkv": np.ascontiguousarray(np.asarray(inputs["L0_attn_w_qkv"], dtype=np.float32)),
        "wo": np.ascontiguousarray(np.asarray(inputs["L0_attn_w_o"], dtype=np.float32)),
        "w1": np.ascontiguousarray(np.asarray(inputs["L0_attn_mlp_w1"], dtype=np.float32)),
        "b1": np.ascontiguousarray(np.asarray(inputs["L0_attn_mlp_b1"], dtype=np.float32)),
        "w2": np.ascontiguousarray(np.asarray(inputs["L0_attn_mlp_w2"], dtype=np.float32)),
        "b2": np.ascontiguousarray(np.asarray(inputs["L0_attn_mlp_b2"], dtype=np.float32)),
        "g1v": np.ascontiguousarray(np.asarray(inputs["L0_attn_ln1_g"], dtype=np.float32)),
        "b1v": np.ascontiguousarray(np.asarray(inputs["L0_attn_ln1_b"], dtype=np.float32)),
        "g2v": np.ascontiguousarray(np.asarray(inputs["L0_attn_ln2_g"], dtype=np.float32)),
        "b2v": np.ascontiguousarray(np.asarray(inputs["L0_attn_ln2_b"], dtype=np.float32)),
        "wg": np.ascontiguousarray(np.asarray(inputs["L0_ffn_w_gate"], dtype=np.float32)),
        "wu": np.ascontiguousarray(np.asarray(inputs["L0_ffn_w_up"], dtype=np.float32)),
        "wd": np.ascontiguousarray(np.asarray(inputs["L0_ffn_w_down"], dtype=np.float32)),
        "idf": idn,
        "mU": mUh,
        "mL": mLh,
        "mq": mqh,
    }
    return [dict(common, x=np.ascontiguousarray(x_full[c])) for c in range(NCORES)]


def kernel(**inputs):
    in_maps = make_in_maps(inputs)
    nc = _get_program()
    res = run_bass_kernel_spmd(nc, in_maps, core_ids=list(range(NCORES)))
    return np.stack([np.asarray(r["out"], dtype=np.float32) for r in res.results], axis=0)


# version 12
# speedup vs baseline: 1.0644x; 1.0644x over previous
import contextlib
import os
import numpy as np
import ml_dtypes

import concourse.bass as bass
import concourse.mybir as mybir
from concourse.bass_utils import run_bass_kernel_spmd

F32 = mybir.dt.float32
BF16 = mybir.dt.bfloat16
I32 = mybir.dt.int32
AF = mybir.ActivationFunctionType
OP = mybir.AluOpType
AX = mybir.AxisListType

NCORES = 8
T = 4096
D = 1024
DC = D // 128
H = 16
HD = 64
W = 512
NCH = T // W
DFF = 4096
DI = 3072
OCF = DI // 128
_TRUNC_CAST = (os.environ.get("BASS_SIM_FP64") == "1"
               or os.environ.get("KERNEL_TRUNC_CAST") == "1")
KQ = 49152.0 if _TRUNC_CAST else 0.0
KQH = KQ + 0.5 if _TRUNC_CAST else 0.0
W8H = 8.5 if _TRUNC_CAST else 0.0
W8 = 8.0 if _TRUNC_CAST else 0.0
SGH = 0.5 if _TRUNC_CAST else 0.0
EPS = 1e-5
SCALE = 0.125

MX_XN = 0
MX_TMP = 40
MX_UP = 64
MX_GA = 256
MX_QN = 456
MX_KN = 464
MX_M = 480


class Sems:
    def __init__(self, nc, stack):
        self.nc, self.stack, self.h, self.n = nc, stack, {}, {}

    def get(self, name):
        if name not in self.h:
            self.h[name] = self.stack.enter_context(self.nc.semaphore(name))
            self.n[name] = 0
        return self.h[name]

    def inc(self, name, instr, amount=1):
        instr.then_inc(self.get(name), amount)
        self.n[name] += amount
        return instr

    def count(self, name):
        self.get(name)
        return self.n[name]

    def wait(self, eng, name, val=None):
        eng.wait_ge(self.get(name), self.n[name] if val is None else val)


def build_program():
    nc = bass.Bass("TRN2", target_bir_lowering=False)
    st = contextlib.ExitStack()
    S = Sems(nc, st)

    x_d = nc.dram_tensor("x", [T, D], F32, kind="ExternalInput").ap()
    wqkv_d = nc.dram_tensor("wqkv", [3 * D, D], F32, kind="ExternalInput").ap()
    wo_d = nc.dram_tensor("wo", [D, D], F32, kind="ExternalInput").ap()
    w1_d = nc.dram_tensor("w1", [DFF, D], F32, kind="ExternalInput").ap()
    b1_d = nc.dram_tensor("b1", [DFF], F32, kind="ExternalInput").ap()
    w2_d = nc.dram_tensor("w2", [D, DFF], F32, kind="ExternalInput").ap()
    b2_d = nc.dram_tensor("b2", [D], F32, kind="ExternalInput").ap()
    g1v_d = nc.dram_tensor("g1v", [D], F32, kind="ExternalInput").ap()
    b1v_d = nc.dram_tensor("b1v", [D], F32, kind="ExternalInput").ap()
    g2v_d = nc.dram_tensor("g2v", [D], F32, kind="ExternalInput").ap()
    b2v_d = nc.dram_tensor("b2v", [D], F32, kind="ExternalInput").ap()
    wg_d = nc.dram_tensor("wg", [DI, D], F32, kind="ExternalInput").ap()
    wu_d = nc.dram_tensor("wu", [DI, D], F32, kind="ExternalInput").ap()
    wd_d = nc.dram_tensor("wd", [D, DI], F32, kind="ExternalInput").ap()
    idf_d = nc.dram_tensor("idf", [128, 128], F32, kind="ExternalInput").ap()
    mU_d = nc.dram_tensor("mU", [128, 128], BF16, kind="ExternalInput").ap()
    mL_d = nc.dram_tensor("mL", [128, 128], BF16, kind="ExternalInput").ap()
    mq_d = nc.dram_tensor("mq", [128, 4 * 256], BF16, kind="ExternalInput").ap()
    out_d = nc.dram_tensor("out", [T, D], F32, kind="ExternalOutput").ap()

    wqkv_bf = nc.dram_tensor("wqkv_bf", [3 * D, D], BF16)
    wo_bf = nc.dram_tensor("wo_bf", [D, D], BF16)
    w1_bf = nc.dram_tensor("w1_bf", [DFF, D], BF16)
    w2_bf = nc.dram_tensor("w2_bf", [D, DFF], BF16)
    wg_dq = nc.dram_tensor("wg_dq", [DI, D], BF16)
    wu_dq = nc.dram_tensor("wu_dq", [DI, D], BF16)
    wd_dq = nc.dram_tensor("wd_dq", [D, DI], BF16)
    xn_h = nc.dram_tensor("xn_h", [DC, 128, T], F32)
    h_hbm = nc.dram_tensor("h_hbm", [DC, 128, T], BF16)
    up_h = nc.dram_tensor("up_h", [OCF, 128, T], F32)
    gr_h = nc.dram_tensor("gr_h", [OCF, 128, T], F32)
    t2_h = nc.dram_tensor("t2_h", [OCF, 128, T], F32)
    or_h = nc.dram_tensor("or_h", [OCF, 128, T], F32)
    x2_h = nc.dram_tensor("x2_h", [DC, 128, T], F32)
    acc_h = nc.dram_tensor("acc_h", [DC, 128, T], F32)
    cio = [(nc.dram_tensor("cin%d" % k, [1, 8], F32),
            nc.dram_tensor("cout%d" % k, [1, 8], F32)) for k in range(4)]

    idf = st.enter_context(nc.sbuf_tensor("idf_s", [128, 128], F32))
    mU = st.enter_context(nc.sbuf_tensor("mU_s", [128, 128], BF16))
    mL = st.enter_context(nc.sbuf_tensor("mL_s", [128, 128], BF16))
    mq = st.enter_context(nc.sbuf_tensor("mq_s", [128, 4 * 256], BF16))
    ones = st.enter_context(nc.sbuf_tensor("ones", [128, 128], F32))
    onepair = st.enter_context(nc.sbuf_tensor("onepair", [128, 2], F32))
    gb = st.enter_context(nc.sbuf_tensor("gb", [128, 48], F32))
    b1c = st.enter_context(nc.sbuf_tensor("b1c", [128, 32], F32))
    scl = st.enter_context(nc.sbuf_tensor("scl", [128, 16], F32))
    mxc = st.enter_context(nc.sbuf_tensor("mxc", [128, 512], F32))
    flat = st.enter_context(nc.sbuf_tensor("flat", [128, 2176], F32))
    nrm = st.enter_context(nc.sbuf_tensor("nrm", [128, 8], F32))

    G1C, B1C, G2C, B2C, MB2C = 0, 8, 16, 24, 32

    blk = st.enter_context(nc.Block())
    ENGS = ("sync", "gpsimd", "vector", "scalar", "tensor")

    def barrier(dma_sems=()):
        @blk.sync
        def _(sy):
            for sname in dma_sems:
                S.wait(sy, sname)
        for e in ENGS:
            @getattr(blk, e)
            def _(eng, e=e):
                S.inc("bar_" + e, eng.nop())
        for e in ENGS:
            @getattr(blk, e)
            def _(eng, e=e):
                for o in ENGS:
                    if o != e:
                        S.wait(eng, "bar_" + o)

    uniq = [0]

    def finalize_scalar_max(cols, prt, fb, wait_name):
        n = len(cols)
        c0 = cols[0]
        contig = all(cols[j] == c0 + j for j in range(n))
        assert contig
        uniq[0] += 1

        with nc.psum_tensor("pfz%d" % uniq[0], [128, 256], F32) as pfz:
            @blk.tensor
            def _(t):
                S.wait(t, wait_name)
                S.inc("fmt", t.transpose(pfz[0:n, 0:128], mxc[:, c0:c0 + n], idf[:]))

            @blk.vector
            def _(v):
                S.wait(v, "fmt")
                v.reduce_max(nrm[0:n, 7:8], pfz[0:n, 0:128], axis=AX.X)
                S.inc("fmq", v.drain())

            @blk.tensor
            def _(t):
                S.wait(t, "fmq")
                S.inc("fmt", t.transpose(pfz[0:1, 128:128 + n], nrm[0:n, 7:8], idf[0:n, 0:n]))

            @blk.vector
            def _(v):
                S.wait(v, "fmt")
                v.tensor_copy(flat[0:1, fb:fb + n], pfz[0:1, 128:128 + n])
                v.drain()
                if n < 8:
                    v.tensor_copy(flat[0:1, fb + n:fb + 8],
                                  flat[0:1, fb:fb + 1].unsqueeze(2).broadcast_to([1, 1, 8 - n]).squeeze(1))
                S.inc("fmv2", v.drain())

    def collective(k, fb):
        cin_k, cout_k = cio[k]

        @blk.gpsimd
        def _(g):
            S.wait(g, "fmv2")
            S.inc("ccw", g.dma_start(out=cin_k.ap()[0:1, 0:8], in_=flat[0:1, fb:fb + 8]), 16)
            S.wait(g, "ccw")
            S.inc("cc", g.collective_compute(
                "AllReduce", OP.max, replica_groups=[list(range(NCORES))],
                ins=[cin_k.ap().opt()], outs=[cout_k.ap().opt()]))
            S.wait(g, "cc")
            S.inc("ccr", g.dma_start(out=flat[0:1, fb + 8:fb + 16], in_=cout_k.ap()[0:1, 0:8]), 16)

    def derive_scales(pairs, fb):
        @blk.vector
        def _(v):
            S.wait(v, "ccr")
            for j, (c, q, _sc) in enumerate(pairs):
                v.tensor_scalar(flat[0:1, fb + 2 * j:fb + 2 * j + 1], flat[0:1, c:c + 1],
                                1.0 / q, 1e-12, OP.mult, OP.max)
            v.drain()
            for j in range(len(pairs)):
                v.reciprocal(flat[0:1, fb + 2 * j + 1:fb + 2 * j + 2],
                             flat[0:1, fb + 2 * j:fb + 2 * j + 1])
            S.inc("dsv", v.drain())

        uniq[0] += 1
        with nc.psum_tensor("pdsx%d" % uniq[0], [128, 16], F32) as pds:
            @blk.tensor
            def _(t):
                S.wait(t, "dsv")
                S.inc("dst", t.matmul(pds[:, 0:2 * len(pairs)], ones[0:1, 0:128],
                                      flat[0:1, fb:fb + 2 * len(pairs)], start=True, stop=True))

            @blk.vector
            def _(v):
                S.wait(v, "dst")
                for j, (_c, _q, c2) in enumerate(pairs):
                    v.tensor_copy(scl[:, c2:c2 + 2], pds[:, 2 * j:2 * j + 2])
                S.inc("dsv", v.drain())

    @blk.gpsimd
    def _(g):
        S.inc("cast1", g.dma_start(out=wqkv_bf.ap(), in_=wqkv_d), 16)
        S.inc("cast1", g.dma_start(out=wo_bf.ap(), in_=wo_d), 16)
        S.inc("cast2", g.dma_start(out=w1_bf.ap(), in_=w1_d), 16)
        S.inc("cast2", g.dma_start(out=w2_bf.ap(), in_=w2_d), 16)

    with nc.sbuf_tensor("vecst", [48, 128], F32) as vecst, \
         nc.sbuf_tensor("b1st", [32, 128], F32) as b1st, \
         nc.psum_tensor("pvec", [128, 48], F32) as pvec, \
         nc.psum_tensor("pb1", [128, 32], F32) as pb1:

        @blk.sync
        def _(sy):
            S.inc("cst", sy.dma_start(out=idf[:], in_=idf_d), 16)
            S.inc("cst", sy.dma_start(out=mU[:], in_=mU_d), 16)
            S.inc("cst", sy.dma_start(out=mL[:], in_=mL_d), 16)
            S.inc("cst", sy.dma_start(out=mq[:], in_=mq_d), 16)
            S.inc("cst", sy.dma_start(out=vecst[0:8, :], in_=g1v_d.rearrange("(a b) -> a b", b=128)), 16)
            S.inc("cst", sy.dma_start(out=vecst[8:16, :], in_=b1v_d.rearrange("(a b) -> a b", b=128)), 16)
            S.inc("cst", sy.dma_start(out=vecst[16:24, :], in_=g2v_d.rearrange("(a b) -> a b", b=128)), 16)
            S.inc("cst", sy.dma_start(out=vecst[24:32, :], in_=b2v_d.rearrange("(a b) -> a b", b=128)), 16)
            S.inc("cst", sy.dma_start(out=vecst[32:40, :], in_=b2_d.rearrange("(a b) -> a b", b=128)), 16)
            S.inc("cst", sy.dma_start(out=b1st[:, :], in_=b1_d.rearrange("(a b) -> a b", b=128)), 16)

        @blk.tensor
        def _(t):
            S.wait(t, "cst")
            t.transpose(pvec[:, 0:40], vecst[0:40, :], idf[0:40, 0:40])
            S.inc("fmt", t.transpose(pb1[:, 0:32], b1st[:, :], idf[0:32, 0:32]))

        @blk.vector
        def _(v):
            v.memset(ones[:], 1.0)
            v.memset(onepair[:, :], 0.0)
            v.drain()
            v.memset(onepair[0:64, 0:1], 1.0)
            v.memset(onepair[64:128, 1:2], 1.0)
            S.wait(v, "fmt")
            v.tensor_copy(gb[:, 0:40], pvec[:, 0:40])
            v.tensor_copy(b1c[:, 0:32], pb1[:, 0:32])
            S.inc("dsv", v.drain())

        barrier(("cst",))

    p1 = contextlib.ExitStack()
    hst = p1.enter_context(nc.sbuf_tensor("hst", [128, 2 * DC * W], BF16))
    xin = p1.enter_context(nc.sbuf_tensor("xin", [128, 2 * 4096], F32))
    p1st = p1.enter_context(nc.sbuf_tensor("p1st", [128, 64], F32))
    hpre = p1.enter_context(nc.sbuf_tensor("hpre", [128, 4096], F32))
    xnst = p1.enter_context(nc.sbuf_tensor("xnst", [128, 2 * DC * W], F32))
    ptr1 = p1.enter_context(nc.psum_tensor("ptr1", [128, 1024], F32))
    ptr2 = p1.enter_context(nc.psum_tensor("ptr2", [128, 1024], F32))

    for tt in range(NCH):
        buf = tt % 2
        xo = buf * 4096
        so = buf * DC * W
        ldsem = "ldA" if buf == 0 else "ldB"

        @blk.sync
        def _(sy, tt=tt, xo=xo, ldsem=ldsem):
            if tt >= 2:
                S.wait(sy, "p1v", S.count("p1v") - 1)
            src = x_d[tt * W:(tt + 1) * W, :].rearrange("(g p) d -> p g d", p=128)
            S.inc(ldsem, sy.dma_start(out=xin[:, xo:xo + 4096].rearrange("p (g d) -> p g d", d=1024), in_=src), 16)

        @blk.scalar
        def _(sc, tt=tt, xo=xo, ldsem=ldsem):
            S.wait(sc, ldsem)
            if tt >= 1:
                S.wait(sc, "p1t")
            for g in range(4):
                xg = xin[:, xo + g * 1024: xo + (g + 1) * 1024]
                sc.activation(hpre[:, g * 1024:(g + 1) * 1024], xg, AF.Square)
            S.inc("p1sa", sc.drain())

        @blk.vector
        def _(v, tt=tt, xo=xo, ldsem=ldsem):
            S.wait(v, ldsem)
            if tt == 0:
                S.wait(v, "dsv")
            for g in range(4):
                xg = xin[:, xo + g * 1024: xo + (g + 1) * 1024]
                v.reduce_sum(p1st[:, g:g + 1], xg, axis=AX.X)
            S.wait(v, "p1sa")
            for g in range(4):
                v.reduce_sum(p1st[:, 4 + g:5 + g], hpre[:, g * 1024:(g + 1) * 1024], axis=AX.X)
            v.drain()
            v.tensor_single_scalar(p1st[:, 8:12], p1st[:, 0:4], 1.0 / D, OP.mult)
            v.tensor_single_scalar(p1st[:, 12:16], p1st[:, 4:8], 1.0 / D, OP.mult)
            v.drain()
            v.tensor_tensor(p1st[:, 16:20], p1st[:, 8:12], p1st[:, 8:12], OP.mult)
            v.drain()
            v.tensor_tensor(p1st[:, 16:20], p1st[:, 12:16], p1st[:, 16:20], OP.subtract)
            v.drain()
            v.tensor_single_scalar(p1st[:, 16:20], p1st[:, 16:20], EPS, OP.add)
            S.inc("p1q", v.drain())

        @blk.scalar
        def _(sc):
            S.wait(sc, "p1q")
            S.inc("p1r", sc.activation(p1st[:, 20:24], p1st[:, 16:20], AF.Sqrt))

        @blk.vector
        def _(v, tt=tt, xo=xo):
            S.wait(v, "p1r")
            v.reciprocal(p1st[:, 24:28], p1st[:, 20:24])
            v.drain()
            for g in range(4):
                xg = xin[:, xo + g * 1024: xo + (g + 1) * 1024]
                v.tensor_scalar(xg, xg, p1st[:, 8 + g:9 + g], p1st[:, 24 + g:25 + g], OP.subtract, OP.mult)
            v.drain()
            for g in range(4):
                xg = xin[:, xo + g * 1024: xo + (g + 1) * 1024]
                v.reduce_max(mxc[:, MX_XN + tt * 4 + g:MX_XN + tt * 4 + g + 1], xg,
                             axis=AX.X, apply_absolute_value=True)
                v.reduce_sum(p1st[:, 32 + g:33 + g], xg, axis=AX.X)
            S.inc("p1b", v.drain())
            S.wait(v, "p1sa", S.count("p1sa") + 1)
            for g in range(4):
                v.reduce_sum(p1st[:, 36 + g:37 + g], hpre[:, g * 1024:(g + 1) * 1024], axis=AX.X)
            v.drain()
            v.tensor_single_scalar(p1st[:, 40:44], p1st[:, 32:36], 1.0 / D, OP.mult)
            v.tensor_single_scalar(p1st[:, 44:48], p1st[:, 36:40], 1.0 / D, OP.mult)
            v.drain()
            v.tensor_tensor(p1st[:, 48:52], p1st[:, 40:44], p1st[:, 40:44], OP.mult)
            v.drain()
            v.tensor_tensor(p1st[:, 48:52], p1st[:, 44:48], p1st[:, 48:52], OP.subtract)
            v.drain()
            v.tensor_single_scalar(p1st[:, 48:52], p1st[:, 48:52], EPS, OP.add)
            S.inc("p1q", v.drain())

        @blk.scalar
        def _(sc, xo=xo):
            S.wait(sc, "p1b")
            for g in range(4):
                xg = xin[:, xo + g * 1024: xo + (g + 1) * 1024]
                sc.activation(hpre[:, g * 1024:(g + 1) * 1024], xg, AF.Square)
            S.inc("p1sa", sc.drain())

        @blk.scalar
        def _(sc):
            S.wait(sc, "p1q")
            S.inc("p1r", sc.activation(p1st[:, 52:56], p1st[:, 48:52], AF.Sqrt))

        @blk.vector
        def _(v, tt=tt, xo=xo):
            S.wait(v, "p1r")
            v.reciprocal(p1st[:, 56:60], p1st[:, 52:56])
            v.drain()
            for g in range(4):
                xg = xin[:, xo + g * 1024: xo + (g + 1) * 1024]
                v.tensor_scalar(hpre[:, g * 1024:(g + 1) * 1024], xg,
                                p1st[:, 40 + g:41 + g], p1st[:, 56 + g:57 + g], OP.subtract, OP.mult)
            S.inc("p1b", v.drain())

        p1e_base = S.count("p1e")

        @blk.tensor
        def _(t, tt=tt, xo=xo, p1e_base=p1e_base):
            S.wait(t, "p1b")
            for i in range(32):
                g, dc = i // DC, i % DC
                pslot = i % 2
                if i >= 2:
                    t.wait_ge(S.get("p1e"), p1e_base + i - 1)
                t.transpose(ptr1[:, pslot * 512:pslot * 512 + 128],
                            xin[:, xo + g * 1024 + dc * 128: xo + g * 1024 + (dc + 1) * 128], idf[:])
                S.inc("p1t", t.transpose(ptr2[:, pslot * 512:pslot * 512 + 128],
                                         hpre[:, g * 1024 + dc * 128: g * 1024 + (dc + 1) * 128], idf[:]))

        p1t_base = S.count("p1t") - 32

        so2 = buf * DC * W

        @blk.vector
        def _(v, tt=tt, so=so, so2=so2, p1t_base=p1t_base):
            if tt >= 2:
                S.wait(v, "p1oA" if tt % 2 == 0 else "p1oB")
            for i in range(32):
                g, dc = i // DC, i % DC
                pslot = i % 2
                v.wait_ge(S.get("p1t"), p1t_base + i + 1)
                v.tensor_copy(xnst[:, so + dc * W + g * 128: so + dc * W + (g + 1) * 128],
                              ptr1[:, pslot * 512:pslot * 512 + 128])
                S.inc("p1e",
                      v.tensor_scalar(hst[:, so2 + dc * W + g * 128: so2 + dc * W + (g + 1) * 128],
                                      ptr2[:, pslot * 512:pslot * 512 + 128],
                                      gb[:, G1C + dc:G1C + dc + 1], gb[:, B1C + dc:B1C + dc + 1],
                                      OP.mult, OP.add))
            S.inc("p1v", v.drain())

        @blk.sync
        def _(sy, tt=tt, so=so, so2=so2):
            S.wait(sy, "p1v")
            dst = xn_h.ap()[:, :, tt * W:(tt + 1) * W].transpose([1, 0, 2])
            S.inc("p1oA" if tt % 2 == 0 else "p1oB",
                  sy.dma_start(out=dst,
                               in_=xnst[:, so:so + DC * W].rearrange("p (c w) -> p c w", w=W)), 16)
            dsth = h_hbm.ap()[:, :, tt * W:(tt + 1) * W].transpose([1, 0, 2])
            S.inc("p1oA" if tt % 2 == 0 else "p1oB",
                  sy.dma_start(out=dsth,
                               in_=hst[:, so2:so2 + DC * W].rearrange("p (c w) -> p c w", w=W)), 16)

    @blk.vector
    def _(v):
        S.wait(v, "p1e")
        v.reduce_max(mxc[:, MX_TMP:MX_TMP + 1], mxc[:, MX_XN:MX_XN + 4 * NCH], axis=AX.X)
        S.inc("fmq", v.drain())

    finalize_scalar_max([MX_TMP], 128, 0, "fmq")
    collective(0, 0)
    derive_scales([(8, 32767.0, 0)], 16)

    wst = p1.enter_context(nc.sbuf_tensor("wst", [128, 2 * DI], F32))
    wblk = p1.enter_context(nc.sbuf_tensor("wblk", [128, 3 * (DI // 32)], F32))
    wi32 = p1.enter_context(nc.sbuf_tensor("wi32", [128, DI], I32))
    wbf16 = p1.enter_context(nc.sbuf_tensor("wbf16", [128, 2 * DI], BF16))

    dqi = 0
    for mi, (src, dst, nrow, I) in enumerate(
            [(wg_d, wg_dq, DI // 128, 1024), (wu_d, wu_dq, DI // 128, 1024), (wd_d, wd_dq, D // 128, DI)]):
        nb = I // 32
        for r in range(nrow):
            buf = dqi % 2
            ldsem = "ldA" if buf == 0 else "ldB"
            stsem = "dqoA" if buf == 0 else "dqoB"

            @blk.sync
            def _(sy, src=src, r=r, buf=buf, I=I, dqi=dqi, ldsem=ldsem):
                if dqi >= 2:
                    S.wait(sy, "dqv", S.count("dqv") - 1)
                S.inc(ldsem, sy.dma_start(out=wst[:, buf * DI: buf * DI + I],
                                          in_=src[r * 128:(r + 1) * 128, :]), 16)

            @blk.vector
            def _(v, buf=buf, I=I, nb=nb, dqi=dqi, ldsem=ldsem, stsem=stsem):
                S.wait(v, ldsem)
                if dqi >= 2:
                    S.wait(v, stsem)
                w = wst[:, buf * DI: buf * DI + I]
                w3 = w.rearrange("p (b k) -> p b k", k=32)
                bm = wblk[:, 0:nb]
                sblk = wblk[:, nb:2 * nb]
                invs = wblk[:, 2 * nb:3 * nb]
                v.reduce_max(bm, w3, axis=AX.X, apply_absolute_value=True)
                v.drain()
                v.tensor_scalar(sblk, bm, 1.0 / 7.0, 1e-12, OP.mult, OP.max)
                v.drain()
                v.reciprocal(invs, sblk)
                v.drain()
                v.tensor_tensor(w3, w3, invs.unsqueeze(2).broadcast_to([128, nb, 32]), OP.mult)
                v.drain()
                v.tensor_single_scalar(wi32[:, 0:I], w, W8H, OP.add)
                v.drain()
                v.scalar_tensor_tensor(wbf16[:, buf * DI: buf * DI + I].rearrange("p (b k) -> p b k", k=32),
                                       wi32[:, 0:I].rearrange("p (b k) -> p b k", k=32), W8,
                                       sblk.unsqueeze(2).broadcast_to([128, nb, 32]),
                                       OP.subtract, OP.mult)
                S.inc("dqv", v.drain())

            @blk.sync
            def _(sy, dst=dst, r=r, buf=buf, I=I, stsem=stsem):
                S.wait(sy, "dqv")
                S.inc(stsem, sy.dma_start(out=dst.ap()[r * 128:(r + 1) * 128, :],
                                          in_=wbf16[:, buf * DI: buf * DI + I]), 16)
            dqi += 1

    p1.close()
    barrier(("p1oA", "p1oB", "dqoA", "dqoB", "ldA", "ldB", "cst"))

    p2 = contextlib.ExitStack()
    xq_sb = p2.enter_context(nc.sbuf_tensor("xq_sb", [128, DC * T], BF16))
    p2ld = p2.enter_context(nc.sbuf_tensor("p2ld", [128, 2 * W], F32))
    p2i = p2.enter_context(nc.sbuf_tensor("p2i", [128, W], I32))
    wtT = p2.enter_context(nc.sbuf_tensor("wtT", [128, DC * DI], BF16))
    p2ev = p2.enter_context(nc.sbuf_tensor("p2ev", [128, 4 * W], F32))
    p2ps = p2.enter_context(nc.psum_tensor("p2ps", [128, 4 * W], F32))

    for i in range(DC * NCH):
        dc, tt = i // NCH, i % NCH
        buf = i % 2
        ldsem = "ldA" if buf == 0 else "ldB"

        @blk.sync
        def _(sy, dc=dc, tt=tt, buf=buf, i=i, ldsem=ldsem):
            if i == 0:
                S.wait(sy, "p1oA")
                S.wait(sy, "p1oB")
            if i >= 2:
                S.wait(sy, "xqv", S.count("xqv") - 1)
            S.inc(ldsem, sy.dma_start(out=p2ld[:, buf * W:(buf + 1) * W],
                                      in_=xn_h.ap()[dc, :, tt * W:(tt + 1) * W]), 16)

        @blk.vector
        def _(v, dc=dc, tt=tt, buf=buf, i=i, ldsem=ldsem):
            S.wait(v, ldsem)
            if i == 0:
                S.wait(v, "dsv")
            xg = p2ld[:, buf * W:(buf + 1) * W]
            v.tensor_scalar(p2i[:, :], xg, scl[:, 1:2], KQH, OP.mult, OP.add)
            v.drain()
            v.tensor_scalar(xq_sb[:, dc * T + tt * W: dc * T + (tt + 1) * W], p2i[:, :],
                            KQ, scl[:, 0:1], OP.subtract, OP.mult)
            S.inc("xqv", v.drain())

    for mi, (wdq, dsth, mbase) in enumerate([(wu_dq, up_h, MX_UP), (wg_dq, gr_h, MX_GA)]):
        @blk.sync
        def _(sy, wdq=wdq, mi=mi):
            if mi == 0:
                S.wait(sy, "dqoA")
                S.wait(sy, "dqoB")
            else:
                S.wait(sy, "p2m")
            for dc in range(DC):
                S.inc("wTA", sy.dma_start(out=wtT[:, dc * DI:(dc + 1) * DI],
                                          in_=wdq.ap()[:, dc * 128:(dc + 1) * 128], transpose=True), 16)

        for j in range(OCF * NCH):
            oc, tt = j // NCH, j % NCH
            pslot = j % 4

            @blk.tensor
            def _(t, mi=mi, oc=oc, tt=tt, pslot=pslot, j=j):
                if j == 0:
                    S.wait(t, "wTA")
                    if mi == 0:
                        S.wait(t, "xqv")
                if S.count("p2s") >= 4:
                    S.wait(t, "p2s", S.count("p2s") - 3)
                ps = p2ps[:, pslot * W:(pslot + 1) * W]
                for dc in range(DC):
                    mm = t.matmul(ps, wtT[:, dc * DI + oc * 128: dc * DI + (oc + 1) * 128],
                                  xq_sb[:, dc * T + tt * W: dc * T + (tt + 1) * W],
                                  start=(dc == 0), stop=(dc == DC - 1))
                S.inc("p2m", mm)

            @blk.scalar
            def _(sc, pslot=pslot, j=j, mi=mi):
                S.wait(sc, "p2m")
                if S.count("p2e") >= 4:
                    S.wait(sc, "p2e", S.count("p2e") - 3)
                if S.count("p2s") >= 4:
                    S.wait(sc, "p2o%d" % pslot)
                S.inc("p2s", sc.copy(p2ev[:, pslot * W:(pslot + 1) * W], p2ps[:, pslot * W:(pslot + 1) * W]))

            @blk.vector
            def _(v, pslot=pslot, mbase=mbase, j=j):
                S.wait(v, "p2s")
                S.inc("p2e", v.reduce_max(mxc[:, mbase + j:mbase + j + 1],
                                          p2ev[:, pslot * W:(pslot + 1) * W],
                                          axis=AX.X, apply_absolute_value=True))

            @blk.sync
            def _(sy, oc=oc, tt=tt, pslot=pslot, dsth=dsth):
                S.wait(sy, "p2e")
                S.inc("p2o%d" % pslot,
                      sy.dma_start(out=dsth.ap()[oc, :, tt * W:(tt + 1) * W],
                                   in_=p2ev[:, pslot * W:(pslot + 1) * W]), 16)

    @blk.vector
    def _(v):
        S.wait(v, "p2e")
        v.reduce_max(mxc[:, MX_TMP:MX_TMP + 1], mxc[:, MX_UP:MX_UP + OCF * NCH], axis=AX.X)
        v.reduce_max(mxc[:, MX_TMP + 1:MX_TMP + 2], mxc[:, MX_GA:MX_GA + OCF * NCH], axis=AX.X)
        S.inc("fmq", v.drain())

    finalize_scalar_max([MX_TMP, MX_TMP + 1], 128, 32, "fmq")
    collective(1, 32)
    derive_scales([(40, 32767.0, 2), (41, 32767.0, 4)], 48)

    p2.close()
    barrier(("p2o0", "p2o1", "p2o2", "p2o3", "ldA", "ldB", "wTA"))

    f2 = contextlib.ExitStack()
    fld = f2.enter_context(nc.sbuf_tensor("fld", [128, 4 * W], F32))
    fi32 = f2.enter_context(nc.sbuf_tensor("fi32", [128, W], I32))
    fgate = f2.enter_context(nc.sbuf_tensor("fgate", [128, 4 * W], F32))
    ft2 = f2.enter_context(nc.sbuf_tensor("ft2", [128, 4 * W], F32))

    def emit_f2_load(j):
        if j >= OCF * NCH:
            return
        oc, tt = j // NCH, j % NCH
        buf = j % 4

        @blk.sync
        def _(sy):
            if j == 0:
                for s4 in range(4):
                    S.wait(sy, "p2o%d" % s4)
            if j >= 4:
                S.wait(sy, "f2v2", j - 3)
            S.inc("f2l%d" % buf, sy.dma_start(out=fld[:, buf * W:(buf + 1) * W],
                                              in_=gr_h.ap()[oc, :, tt * W:(tt + 1) * W]), 16)

    def emit_f2_tile(j):
        oc, tt = j // NCH, j % NCH
        buf = j % 4

        @blk.vector
        def _(v):
            S.wait(v, "f2l%d" % buf, 16 * (j // 4 + 1))
            if j == 0:
                S.wait(v, "dsv")
            if j >= 4:
                S.wait(v, "f2sg", j - 3)
                S.wait(v, "f2s%d" % buf)
            v.tensor_scalar(fi32[:, :], fld[:, buf * W:(buf + 1) * W], scl[:, 5:6], KQH,
                            OP.mult, OP.add)
            v.drain()
            v.tensor_scalar(fgate[:, buf * W:(buf + 1) * W], fi32[:, :], KQ, scl[:, 4:5],
                            OP.subtract, OP.mult)
            S.inc("f2v1", v.drain())

        @blk.scalar
        def _(sc):
            S.wait(sc, "f2v1")
            S.inc("f2sg", sc.activation(fld[:, buf * W:(buf + 1) * W],
                                        fgate[:, buf * W:(buf + 1) * W], AF.Tanh, scale=0.5))

        @blk.vector
        def _(v):
            S.wait(v, "f2sg")
            v.tensor_scalar(fi32[:, :], fld[:, buf * W:(buf + 1) * W], 32768.0, 32768.0 + SGH,
                            OP.mult, OP.add)
            v.drain()
            v.tensor_scalar(fld[:, buf * W:(buf + 1) * W], fi32[:, :], 65535.0, 1.0 / 65536.0,
                            OP.min, OP.mult)
            v.drain()
            v.tensor_tensor(ft2[:, buf * W:(buf + 1) * W], fgate[:, buf * W:(buf + 1) * W],
                            fld[:, buf * W:(buf + 1) * W], OP.mult)
            v.drain()
            S.inc("f2v2", v.reduce_max(mxc[:, MX_UP + j:MX_UP + j + 1], ft2[:, buf * W:(buf + 1) * W],
                                       axis=AX.X, apply_absolute_value=True))

        @blk.sync
        def _(sy):
            S.wait(sy, "f2v2")
            S.inc("f2s%d" % buf, sy.dma_start(out=t2_h.ap()[oc, :, tt * W:(tt + 1) * W],
                                              in_=ft2[:, buf * W:(buf + 1) * W]), 16)

    def emit_f3_load(j):
        if j >= OCF * NCH:
            return
        oc, tt = j // NCH, j % NCH
        buf = j % 4

        @blk.sync
        def _(sy):
            if j == 0:
                for s4 in range(4):
                    S.wait(sy, "f2s%d" % s4)
            if j >= 4:
                S.wait(sy, "f3v", j - 3)
            S.inc("f3l%d" % buf, sy.dma_start(out=fld[:, buf * W:(buf + 1) * W],
                                              in_=t2_h.ap()[oc, :, tt * W:(tt + 1) * W]), 16)
            S.inc("f3l%d" % buf, sy.dma_start(out=fgate[:, buf * W:(buf + 1) * W],
                                              in_=up_h.ap()[oc, :, tt * W:(tt + 1) * W]), 16)

    def emit_f3_tile(j):
        oc, tt = j // NCH, j % NCH
        buf = j % 4

        @blk.vector
        def _(v):
            S.wait(v, "f3l%d" % buf, 32 * (j // 4 + 1))
            if j == 0:
                S.wait(v, "dsv")
            if j >= 4:
                S.wait(v, "f3s%d" % buf)
            v.tensor_scalar(fi32[:, :], fld[:, buf * W:(buf + 1) * W], scl[:, 7:8], KQH,
                            OP.mult, OP.add)
            v.drain()
            v.tensor_scalar(fld[:, buf * W:(buf + 1) * W], fi32[:, :], KQ, scl[:, 6:7],
                            OP.subtract, OP.mult)
            v.drain()
            v.tensor_scalar(fi32[:, :], fgate[:, buf * W:(buf + 1) * W], scl[:, 3:4], KQH,
                            OP.mult, OP.add)
            v.drain()
            v.tensor_scalar(fgate[:, buf * W:(buf + 1) * W], fi32[:, :], KQ, scl[:, 2:3],
                            OP.subtract, OP.mult)
            v.drain()
            v.tensor_tensor(ft2[:, buf * W:(buf + 1) * W], fld[:, buf * W:(buf + 1) * W],
                            fgate[:, buf * W:(buf + 1) * W], OP.mult)
            v.drain()
            S.inc("f3v", v.reduce_max(mxc[:, MX_GA + j:MX_GA + j + 1], ft2[:, buf * W:(buf + 1) * W],
                                      axis=AX.X, apply_absolute_value=True))

        @blk.sync
        def _(sy):
            S.wait(sy, "f3v")
            S.inc("f3s%d" % buf, sy.dma_start(out=or_h.ap()[oc, :, tt * W:(tt + 1) * W],
                                              in_=ft2[:, buf * W:(buf + 1) * W]), 16)

    p3 = contextlib.ExitStack()
    hstg = p3.enter_context(nc.sbuf_tensor("hstg", [128, 2 * DC * W], BF16))
    wqkvT = p3.enter_context(nc.sbuf_tensor("wqkvT", [128, DC * 3 * D], BF16))
    woT = p3.enter_context(nc.sbuf_tensor("woT", [128, DC * D], BF16))
    qf = p3.enter_context(nc.sbuf_tensor("qf", [128, DC * W], BF16))
    kf2 = p3.enter_context(nc.sbuf_tensor("kf2", [128, 2 * DC * W], BF16))
    vsb2 = p3.enter_context(nc.sbuf_tensor("vsb2", [128, 2 * 4 * H * 65], BF16))
    avn = p3.enter_context(nc.sbuf_tensor("avn", [128, DC * W], BF16))
    xnstg = p3.enter_context(nc.sbuf_tensor("xnstg", [128, 2 * W], F32))
    x2stg = p3.enter_context(nc.sbuf_tensor("x2stg", [128, 2 * W], F32))
    pTs = p3.enter_context(nc.sbuf_tensor("pTs", [128, 2 * 768], BF16))
    q2st = p3.enter_context(nc.sbuf_tensor("q2st", [128, W], F32))

    @blk.sync
    def _(sy):
        S.wait(sy, "cast1")
        for dc in range(DC):
            S.inc("wTA", sy.dma_start(out=wqkvT[:, dc * 3 * D:(dc + 1) * 3 * D],
                                      in_=wqkv_bf.ap()[:, dc * 128:(dc + 1) * 128], transpose=True), 16)
        for dc in range(DC):
            S.inc("wTA", sy.dma_start(out=woT[:, dc * D:(dc + 1) * D],
                                      in_=wo_bf.ap()[:, dc * 128:(dc + 1) * 128], transpose=True), 16)

    for c in range(NCH):
        cur, prv = c % 2, (c + 1) % 2
        VS = 4 * H * 65
        p3qm_base = S.count("p3qm")

        @blk.sync
        def _(sy, c=c, cur=cur, p3qm_base=p3qm_base):
            if c == 0:
                S.wait(sy, "p1oA")
                S.wait(sy, "p1oB")
            if c >= 2:
                S.wait(sy, "p3qm", p3qm_base - 24)
            S.inc("hlA" if cur == 0 else "hlB",
                  sy.dma_start(out=hstg[:, cur * DC * W: (cur + 1) * DC * W]
                               .rearrange("p (k w) -> p k w", w=W),
                               in_=h_hbm.ap()[:, :, c * W:(c + 1) * W].transpose([1, 0, 2])), 16)

        with nc.psum_tensor("pqk_%d" % c, [128, 2 * W], F32) as pqk:
            for wi, rowbase in ((0, 0), (1, D)):
                for oc in range(DC):
                    pslot = (wi * DC + oc) % 2

                    @blk.tensor
                    def _(t, c=c, cur=cur, wi=wi, rowbase=rowbase, oc=oc, pslot=pslot, pqk=pqk):
                        if c == 0 and wi == 0 and oc == 0:
                            S.wait(t, "wTA")
                        if wi == 0 and oc == 0:
                            S.wait(t, "hlA" if cur == 0 else "hlB")
                        if c > 0 and wi == 0 and oc == 0:
                            S.wait(t, "p3wv")
                        if S.count("p3qe") >= 2:
                            S.wait(t, "p3qe", S.count("p3qe") - 1)
                        ps = pqk[:, pslot * W:(pslot + 1) * W]
                        for dc in range(DC):
                            mm = t.matmul(ps, wqkvT[:, dc * 3 * D + rowbase + oc * 128:
                                                    dc * 3 * D + rowbase + (oc + 1) * 128],
                                          hstg[:, cur * DC * W + dc * W: cur * DC * W + (dc + 1) * W],
                                          start=(dc == 0), stop=(dc == DC - 1))
                        S.inc("p3qm", mm)

                    @blk.vector
                    def _(v, c=c, wi=wi, oc=oc, pslot=pslot, cur=cur, pqk=pqk):
                        S.wait(v, "p3qm")
                        if c > 0 and wi == 0 and oc == 0:
                            S.wait(v, "p3xm")
                        if wi == 0:
                            o = v.tensor_copy(qf[:, oc * W:(oc + 1) * W], pqk[:, pslot * W:(pslot + 1) * W])
                        else:
                            o = v.tensor_copy(kf2[:, cur * DC * W + oc * W: cur * DC * W + (oc + 1) * W],
                                              pqk[:, pslot * W:(pslot + 1) * W])
                        S.inc("p3qe", o)

        @blk.vector
        def _(v, c=c, cur=cur, VS=VS):
            S.wait(v, "p3av")
            v.memset(vsb2[:, cur * VS:(cur + 1) * VS].rearrange("p (s o) -> p s o", o=65)[:, :, 64:65], 1.0)
            S.inc("p3vo", v.drain())

        with contextlib.ExitStack() as vsc:
            pv8 = [vsc.enter_context(nc.psum_tensor("pv8_%d_%d" % (c, b), [128, 64], F32)) for b in range(8)]
            for g in range(4):
                for hg in range(2):
                    @blk.tensor
                    def _(t, c=c, g=g, hg=hg, pv8=pv8):
                        S.wait(t, "p3qe")
                        if S.count("p3ve") > 0:
                            S.wait(t, "p3ve")
                        for dc in range(DC):
                            hb = hstg[:, cur * DC * W + dc * W + g * 128: cur * DC * W + dc * W + (g + 1) * 128]
                            for hh in range(8):
                                h = hg * 8 + hh
                                mm = t.matmul(pv8[hh][:, 0:64], hb,
                                              wqkvT[:, dc * 3 * D + 2 * D + h * 64:
                                                    dc * 3 * D + 2 * D + (h + 1) * 64],
                                              start=(dc == 0), stop=(dc == DC - 1))
                        S.inc("p3qm", mm)

                    @blk.vector
                    def _(v, c=c, g=g, hg=hg, cur=cur, pv8=pv8, VS=VS):
                        S.wait(v, "p3qm")
                        S.wait(v, "p3vo")
                        for hh in range(8):
                            h = hg * 8 + hh
                            S.inc("p3ve", v.tensor_copy(
                                vsb2[:, cur * VS + (g * H + h) * 65: cur * VS + (g * H + h) * 65 + 64],
                                pv8[hh][:, 0:64]))

        with nc.psum_tensor("pnm_%d" % c, [128, W], F32) as pnm:
            for wi, mbase in ((0, MX_QN), (1, MX_KN + cur * 8)):
                for dc in range(DC):
                    @blk.scalar
                    def _(sc, wi=wi, dc=dc, cur=cur):
                        S.wait(sc, "p3qe")
                        S.wait(sc, "p3nt")
                        src = (qf[:, dc * W:(dc + 1) * W] if wi == 0
                               else kf2[:, cur * DC * W + dc * W: cur * DC * W + (dc + 1) * W])
                        S.inc("p3sq", sc.activation(q2st[:, :], src, AF.Square))

                    @blk.tensor
                    def _(t, wi=wi, dc=dc, pnm=pnm):
                        S.wait(t, "p3sq")
                        if wi == 0 and dc == 0:
                            S.wait(t, "p3ve")
                        S.wait(t, "p3nr")
                        S.inc("p3nt", t.matmul(pnm[0:2, 0:W], onepair[:, 0:2], q2st[:, :],
                                               start=True, stop=True))

                    @blk.vector
                    def _(v, wi=wi, dc=dc, mbase=mbase, pnm=pnm):
                        S.wait(v, "p3nt")
                        S.inc("p3nr", v.reduce_max(mxc[0:2, mbase + dc:mbase + dc + 1], pnm[0:2, 0:W],
                                                   axis=AX.X))

            @blk.vector
            def _(v, c=c, cur=cur):
                S.wait(v, "p3nr")
                v.reduce_max(nrm[0:2, 0:1], mxc[0:2, MX_QN:MX_QN + 8], axis=AX.X)
                if c > 0:
                    v.reduce_max(nrm[0:2, 1:2], mxc[0:2, MX_KN:MX_KN + 16], axis=AX.X)
                else:
                    v.reduce_max(nrm[0:2, 1:2], mxc[0:2, MX_KN + cur * 8:MX_KN + cur * 8 + 8], axis=AX.X)
                S.inc("p3na", v.drain())

            @blk.tensor
            def _(t, pnm=pnm):
                S.wait(t, "p3na")
                S.inc("p3nt", t.transpose(pnm[0:2, 128:130], nrm[0:2, 0:2], idf[0:2, 0:2]))

            @blk.vector
            def _(v, pnm=pnm):
                S.wait(v, "p3nt")
                v.reduce_max(nrm[0:2, 2:3], pnm[0:2, 128:130], axis=AX.X)
                S.inc("p3na", v.drain())

            @blk.tensor
            def _(t, pnm=pnm):
                S.wait(t, "p3na")
                S.inc("p3nt", t.transpose(pnm[0:1, 130:132], nrm[0:2, 2:3], idf[0:2, 0:2]))

            @blk.vector
            def _(v, pnm=pnm):
                S.wait(v, "p3nt")
                v.reduce_sum(nrm[0:1, 3:4], pnm[0:1, 130:132], axis=AX.X)
                v.drain()
                v.tensor_single_scalar(nrm[0:1, 3:4], nrm[0:1, 3:4], -0.5 * SCALE, OP.mult)
                S.inc("p3na", v.drain())

            @blk.tensor
            def _(t, pnm=pnm):
                S.wait(t, "p3na")
                S.inc("p3nt", t.matmul(pnm[:, 132:133], ones[0:1, 0:128], nrm[0:1, 3:4],
                                       start=True, stop=True))

            @blk.vector
            def _(v, pnm=pnm):
                S.wait(v, "p3nt")
                S.wait(v, "p3xs")
                S.inc("p3nf", v.tensor_copy(mxc[:, MX_M:MX_M + 1], pnm[:, 132:133]))

        with nc.psum_tensor("pSa_%d" % c, [128, 768], F32) as pSa, \
             nc.psum_tensor("pSb_%d" % c, [128, 768], F32) as pSb, \
             nc.psum_tensor("pAVa_%d" % c, [65, W], F32) as pAVa, \
             nc.psum_tensor("pAVb_%d" % c, [65, W], F32) as pAVb, \
             nc.psum_tensor("pCB_%d" % c, [128, 2 * W], F32) as pCB:

            for dc in range(DC):
                for qi2 in range(2):
                    g0 = 4 * c + 2 * qi2
                    kts = [G for G in range(g0 - 4, g0 + 2) if G >= 0]
                    nkt = len(kts)
                    j0 = 6 - nkt
                    groups = [(gi, kts[gi:gi + 3]) for gi in range(0, nkt, 3)]

                    for gi, gkts in groups:
                        last = (gi + len(gkts) == nkt)

                        @blk.tensor
                        def _(t, c=c, dc=dc, qi2=qi2, gi=gi, gkts=gkts, cur=cur, prv=prv,
                              pSa=pSa, pSb=pSb):
                            S.wait(t, "p3nf")
                            S.wait(t, "p3xs")
                            if dc == 0 and qi2 == 0 and gi == 0:
                                S.wait(t, "p3ve")
                            for aa, ps in ((0, pSa), (64, pSb)):
                                for j, G in enumerate(gkts):
                                    buf = cur if G >= 4 * c else prv
                                    kcol = buf * DC * W + dc * W + (G % 4) * 128
                                    mm = t.matmul(ps[:, j * 256:(j + 1) * 256],
                                                  kf2[aa:aa + 64, kcol:kcol + 128],
                                                  qf[aa:aa + 64, dc * W + qi2 * 256: dc * W + qi2 * 256 + 256],
                                                  start=True, stop=True)
                            S.inc("p3xm", mm)

                        @blk.scalar
                        def _(sc, gkts=gkts, pSa=pSa, pSb=pSb):
                            S.wait(sc, "p3xm")
                            S.wait(sc, "p3av")
                            n = len(gkts)
                            for aa, ps in ((0, pSa), (1, pSb)):
                                S.inc("p3xs", sc.activation(pTs[:, aa * 768: aa * 768 + n * 256],
                                                            ps[:, 0:n * 256], AF.Exp,
                                                            bias=mxc[:, MX_M:MX_M + 1], scale=SCALE))

                        @blk.vector
                        def _(v, gi=gi, gkts=gkts, j0=j0):
                            S.wait(v, "p3xs")
                            for aa in range(2):
                                for j in range(len(gkts)):
                                    jr = j0 + gi + j
                                    if jr in (0, 1, 4, 5):
                                        mslot = {0: 0, 1: 1, 4: 2, 5: 3}[jr]
                                        v.tensor_tensor(pTs[:, aa * 768 + j * 256: aa * 768 + (j + 1) * 256],
                                                        pTs[:, aa * 768 + j * 256: aa * 768 + (j + 1) * 256],
                                                        mq[:, mslot * 256:(mslot + 1) * 256],
                                                        OP.mult)
                            S.inc("p3xv", v.drain())

                        @blk.tensor
                        def _(t, c=c, dc=dc, qi2=qi2, gi=gi, gkts=gkts, cur=cur, prv=prv,
                              nkt=nkt, last=last, pAVa=pAVa, pAVb=pAVb, VS=VS):
                            S.wait(t, "p3xv")
                            if qi2 == 0 and gi == 0:
                                S.wait(t, "p3xe")
                            for aa, pav in ((0, pAVa), (1, pAVb)):
                                h = 2 * dc + aa
                                for j, G in enumerate(gkts):
                                    buf = cur if G >= 4 * c else prv
                                    vcol = buf * VS + ((G % 4) * H + h) * 65
                                    mm = t.matmul(pav[0:65, qi2 * 256:(qi2 + 1) * 256],
                                                  vsb2[:, vcol:vcol + 65],
                                                  pTs[:, aa * 768 + j * 256: aa * 768 + (j + 1) * 256],
                                                  start=(gi == 0 and j == 0),
                                                  stop=(last and j == len(gkts) - 1))
                            S.inc("p3av", mm)

                for aa, pav in ((0, pAVa), (1, pAVb)):
                    @blk.vector
                    def _(v, dc=dc, aa=aa, pav=pav):
                        S.wait(v, "p3av")
                        v.tensor_copy(flat[64:65, 1024 + aa * W:1024 + aa * W + W], pav[64:65, 0:W])
                        v.drain()
                        v.reciprocal(flat[64:65, 1024 + aa * W:1024 + aa * W + W],
                                     flat[64:65, 1024 + aa * W:1024 + aa * W + W])
                        S.inc("p3xv", v.drain())

                    @blk.tensor
                    def _(t, aa=aa, pCB=pCB):
                        S.wait(t, "p3xv")
                        S.inc("p3lb", t.matmul(pCB[:, aa * W:(aa + 1) * W], ones[64:65, 0:128],
                                               flat[64:65, 1024 + aa * W:1024 + aa * W + W],
                                               start=True, stop=True))

                    @blk.vector
                    def _(v, dc=dc, aa=aa, pav=pav, pCB=pCB):
                        S.wait(v, "p3lb")
                        h = 2 * dc + aa
                        po = (h % 2) * 64
                        v.tensor_copy(flat[:, aa * W:(aa + 1) * W], pCB[:, aa * W:(aa + 1) * W])
                        v.drain()
                        cb = flat[po:po + 64, aa * W:(aa + 1) * W]
                        S.inc("p3xe", v.tensor_tensor(avn[po:po + 64, (h // 2) * W: (h // 2 + 1) * W],
                                                      pav[0:64, 0:W], cb, OP.mult))

        with nc.psum_tensor("pwo_%d" % c, [128, 2 * W], F32) as pwo:
            for oc in range(DC):
                pslot = oc % 2
                ldsem = "ldA" if pslot == 0 else "ldB"

                @blk.sync
                def _(sy, c=c, oc=oc, pslot=pslot, ldsem=ldsem):
                    if S.count("p3wv") >= 2:
                        S.wait(sy, "p3wv", S.count("p3wv") - 1)
                    S.inc(ldsem, sy.dma_start(out=xnstg[:, pslot * W:(pslot + 1) * W],
                                              in_=xn_h.ap()[oc, :, c * W:(c + 1) * W]), 16)

                @blk.tensor
                def _(t, c=c, oc=oc, pslot=pslot, pwo=pwo):
                    S.wait(t, "p3xe")
                    if S.count("p3wv") >= 2:
                        S.wait(t, "p3wv", S.count("p3wv") - 1)
                    ps = pwo[:, pslot * W:(pslot + 1) * W]
                    for dc in range(DC):
                        mm = t.matmul(ps, woT[:, dc * D + oc * 128: dc * D + (oc + 1) * 128],
                                      avn[:, dc * W:(dc + 1) * W],
                                      start=(dc == 0), stop=(dc == DC - 1))
                    S.inc("p3qm", mm)

                @blk.vector
                def _(v, pslot=pslot, ldsem=ldsem, pwo=pwo):
                    S.wait(v, "p3qm")
                    S.wait(v, ldsem)
                    if S.count("p3wv") >= 2:
                        S.wait(v, "p3wA" if pslot == 0 else "p3wB")
                    S.inc("p3wv", v.tensor_tensor(x2stg[:, pslot * W:(pslot + 1) * W],
                                                  pwo[:, pslot * W:(pslot + 1) * W],
                                                  xnstg[:, pslot * W:(pslot + 1) * W], OP.add))

                @blk.sync
                def _(sy, c=c, oc=oc, pslot=pslot):
                    S.wait(sy, "p3wv")
                    S.inc("p3wA" if pslot == 0 else "p3wB",
                          sy.dma_start(out=x2_h.ap()[oc, :, c * W:(c + 1) * W],
                                       in_=x2stg[:, pslot * W:(pslot + 1) * W]), 16)

        for jf in range(c * OCF, (c + 1) * OCF):
            if c == 0 and jf == 0:
                for jl in range(4):
                    emit_f2_load(jl)
            emit_f2_load(jf + 4)
            emit_f2_tile(jf)

    p3.close()

    @blk.vector
    def _(v):
        S.wait(v, "f2v2")
        v.reduce_max(mxc[:, MX_TMP:MX_TMP + 1], mxc[:, MX_UP:MX_UP + OCF * NCH], axis=AX.X)
        S.inc("fmq", v.drain())

    finalize_scalar_max([MX_TMP], 128, 64, "fmq")
    collective(2, 64)
    derive_scales([(72, 32767.0, 6)], 80)

    barrier(("p3wA", "p3wB", "f2s0", "f2s1", "f2s2", "f2s3", "f2l0", "f2l1", "f2l2", "f2l3", "ldA", "ldB", "hlA", "hlB", "wTA"))

    p4 = contextlib.ExitStack()
    w1T = p4.enter_context(nc.sbuf_tensor("w1T", [128, DC * DFF], BF16))
    w2T = p4.enter_context(nc.sbuf_tensor("w2T", [128, 32 * D], BF16))
    x2t = p4.enter_context(nc.sbuf_tensor("x2t", [128, 2 * DC * 256], F32))
    h2t = p4.enter_context(nc.sbuf_tensor("h2t", [128, DC * 256], BF16))
    g1buf = p4.enter_context(nc.sbuf_tensor("g1buf", [128, 32 * 256], BF16))
    sqstg = p4.enter_context(nc.sbuf_tensor("sqstg", [128, 2 * 256], F32))
    pmm = p4.enter_context(nc.psum_tensor("pmm", [128, 2 * W], F32))
    pstat = p4.enter_context(nc.psum_tensor("pstat", [2, 2 * W], F32))
    pab = p4.enter_context(nc.psum_tensor("pab", [128, 2 * W], F32))

    @blk.sync
    def _(sy):
        S.wait(sy, "cast1")
        S.wait(sy, "cast2")
        for dc in range(DC):
            S.inc("wTA", sy.dma_start(out=w1T[:, dc * DFF:(dc + 1) * DFF],
                                      in_=w1_bf.ap()[:, dc * 128:(dc + 1) * 128], transpose=True), 16)
        for kc in range(32):
            S.inc("wTA", sy.dma_start(out=w2T[:, kc * D:(kc + 1) * D],
                                      in_=w2_bf.ap()[:, kc * 128:(kc + 1) * 128], transpose=True), 16)

    WH = W // 2
    for tt in range(2 * NCH):
        xob = (tt % 2) * DC * 256
        @blk.sync
        def _(sy, tt=tt, xob=xob):
            if tt == 0:
                S.wait(sy, "p3wA")
                S.wait(sy, "p3wB")
            if tt >= 2:
                S.wait(sy, "p4av", S.count("p4av") - DC)
            for dc in range(DC):
                S.inc("ldA", sy.dma_start(out=x2t[:, xob + dc * WH:xob + (dc + 1) * WH],
                                          in_=x2_h.ap()[dc, :, tt * WH:(tt + 1) * WH]), 16)

        for dc in range(DC):
            @blk.scalar
            def _(sc, tt=tt, dc=dc, xob=xob):
                S.wait(sc, "ldA")
                if S.count("p4st") >= 2:
                    S.wait(sc, "p4st", S.count("p4st") - 1)
                S.inc("p4sq", sc.activation(sqstg[:, (dc % 2) * WH:(dc % 2 + 1) * WH],
                                            x2t[:, xob + dc * WH:xob + (dc + 1) * WH], AF.Square))

            @blk.tensor
            def _(t, tt=tt, dc=dc, xob=xob):
                S.wait(t, "p4sq")
                if tt == 0 and dc == 0:
                    S.wait(t, "wTA")
                if dc == 0:
                    S.wait(t, "ldA")
                    if tt > 0:
                        S.wait(t, "p4va")
                t.matmul(pstat[0:1, 0:WH], ones[:, 0:1], x2t[:, xob + dc * WH:xob + (dc + 1) * WH],
                         start=(dc == 0), stop=(dc == DC - 1))
                S.inc("p4st", t.matmul(pstat[0:1, W:W + WH], ones[:, 0:1],
                                       sqstg[:, (dc % 2) * WH:(dc % 2 + 1) * WH],
                                       start=(dc == 0), stop=(dc == DC - 1)))

        FB = 128

        @blk.vector
        def _(v, tt=tt):
            S.wait(v, "p4st")
            v.tensor_single_scalar(flat[0:1, FB:FB + WH], pstat[0:1, 0:WH], 1.0 / D, OP.mult)
            v.tensor_single_scalar(flat[0:1, FB + WH:FB + 2 * WH], pstat[0:1, W:W + WH], 1.0 / D, OP.mult)
            v.drain()
            v.tensor_tensor(flat[0:1, FB + 2 * WH:FB + 3 * WH], flat[0:1, FB:FB + WH],
                            flat[0:1, FB:FB + WH], OP.mult)
            v.drain()
            v.tensor_tensor(flat[0:1, FB + 2 * WH:FB + 3 * WH], flat[0:1, FB + WH:FB + 2 * WH],
                            flat[0:1, FB + 2 * WH:FB + 3 * WH], OP.subtract)
            v.drain()
            v.tensor_single_scalar(flat[0:1, FB + 2 * WH:FB + 3 * WH], flat[0:1, FB + 2 * WH:FB + 3 * WH],
                            EPS, OP.add)
            S.inc("p4va", v.drain())

        @blk.scalar
        def _(sc):
            S.wait(sc, "p4va")
            S.inc("p4sr", sc.activation(flat[0:1, FB + 3 * WH:FB + 4 * WH],
                                        flat[0:1, FB + 2 * WH:FB + 3 * WH], AF.Sqrt))

        @blk.vector
        def _(v):
            S.wait(v, "p4sr")
            v.reciprocal(flat[0:1, FB + 2 * WH:FB + 3 * WH], flat[0:1, FB + 3 * WH:FB + 4 * WH])
            v.drain()
            v.tensor_tensor(flat[0:1, FB:FB + WH], flat[0:1, FB:FB + WH],
                            flat[0:1, FB + 2 * WH:FB + 3 * WH], OP.mult)
            v.drain()
            v.tensor_single_scalar(flat[0:1, FB:FB + WH], flat[0:1, FB:FB + WH], -1.0, OP.mult)
            S.inc("p4va", v.drain())

        @blk.tensor
        def _(t, tt=tt):
            S.wait(t, "p4va")
            if tt > 0:
                S.wait(t, "p4h2")
            t.matmul(pab[:, 0:WH], ones[0:1, 0:128], flat[0:1, FB + 2 * WH:FB + 3 * WH], start=True, stop=True)
            S.inc("p4m1", t.matmul(pab[:, W:W + WH], ones[0:1, 0:128], flat[0:1, FB:FB + WH],
                                   start=True, stop=True))

        @blk.vector
        def _(v, tt=tt):
            S.wait(v, "p4m1")
            if tt > 0:
                S.wait(v, "p4ge")
            for dc in range(DC):
                v.tensor_tensor(sqstg[:, 0:WH], x2t[:, xob + dc * WH:xob + (dc + 1) * WH], pab[:, 0:WH], OP.mult)
                v.drain()
                v.tensor_tensor(sqstg[:, 0:WH], sqstg[:, 0:WH], pab[:, W:W + WH], OP.add)
                v.drain()
                v.tensor_scalar(h2t[:, dc * WH:(dc + 1) * WH], sqstg[:, 0:WH],
                                gb[:, G2C + dc:G2C + dc + 1], gb[:, B2C + dc:B2C + dc + 1],
                                OP.mult, OP.add)
                v.drain()
            S.inc("p4h2", v.drain())

        for o1 in range(32):
            pslot = o1 % 2

            @blk.tensor
            def _(t, tt=tt, o1=o1, pslot=pslot):
                S.wait(t, "p4h2")
                if S.count("p4ge") >= 2:
                    S.wait(t, "p4ge", S.count("p4ge") - 1)
                for dc in range(DC):
                    mm = t.matmul(pmm[:, pslot * W:pslot * W + WH],
                                  w1T[:, dc * DFF + o1 * 128: dc * DFF + (o1 + 1) * 128],
                                  h2t[:, dc * WH:(dc + 1) * WH], start=(dc == 0), stop=(dc == DC - 1))
                S.inc("p4m1", mm)

            @blk.scalar
            def _(sc, tt=tt, o1=o1, pslot=pslot):
                S.wait(sc, "p4m1")
                if tt > 0 and o1 == 0:
                    S.wait(sc, "p4m1")
                S.inc("p4ge", sc.activation(g1buf[:, o1 * WH:(o1 + 1) * WH],
                                            pmm[:, pslot * W:pslot * W + WH],
                                            AF.Gelu, bias=b1c[:, o1:o1 + 1], scale=1.0))

        for o2 in range(DC):
            pslot = o2 % 2

            @blk.tensor
            def _(t, tt=tt, o2=o2, pslot=pslot):
                S.wait(t, "p4ge")
                if S.count("p4av") >= 2:
                    S.wait(t, "p4av", S.count("p4av") - 1)
                for kc in range(32):
                    mm = t.matmul(pmm[:, pslot * W:pslot * W + WH],
                                  w2T[:, kc * D + o2 * 128: kc * D + (o2 + 1) * 128],
                                  g1buf[:, kc * WH:(kc + 1) * WH], start=(kc == 0), stop=(kc == 31))
                S.inc("p4m1", mm)

            @blk.vector
            def _(v, o2=o2, pslot=pslot, xob=xob):
                S.wait(v, "p4m1")
                if S.count("p4av") >= 2:
                    S.wait(v, "p3wA" if pslot == 0 else "p3wB")
                S.inc("p4av", v.scalar_tensor_tensor(flat[:, 1152 + pslot * WH:1152 + (pslot + 1) * WH],
                                                     pmm[:, pslot * W:pslot * W + WH],
                                                     gb[:, MB2C + o2:MB2C + o2 + 1],
                                                     x2t[:, xob + o2 * WH:xob + (o2 + 1) * WH], OP.add, OP.add))

            @blk.sync
            def _(sy, tt=tt, o2=o2, pslot=pslot):
                S.wait(sy, "p4av")
                S.inc("p3wA" if pslot == 0 else "p3wB",
                      sy.dma_start(out=acc_h.ap()[o2, :, tt * WH:(tt + 1) * WH],
                                   in_=flat[:, 1152 + pslot * WH:1152 + (pslot + 1) * WH]), 16)

        for jf in range(tt * (OCF * NCH) // (2 * NCH), (tt + 1) * (OCF * NCH) // (2 * NCH)):
            if tt == 0 and jf == 0:
                for jl in range(4):
                    emit_f3_load(jl)
            emit_f3_load(jf + 4)
            emit_f3_tile(jf)

    p4.close()

    @blk.vector
    def _(v):
        S.wait(v, "f3v")
        v.reduce_max(mxc[:, MX_TMP:MX_TMP + 1], mxc[:, MX_GA:MX_GA + OCF * NCH], axis=AX.X)
        S.inc("fmq", v.drain())

    finalize_scalar_max([MX_TMP], 128, 96, "fmq")
    collective(3, 96)
    derive_scales([(104, 32767.0, 8)], 112)

    f2.close()
    barrier(("p3wA", "p3wB", "f3s0", "f3s1", "f3s2", "f3s3", "f3l0", "f3l1", "f3l2", "f3l3", "ldA", "wTA"))

    f4 = contextlib.ExitStack()
    wdT = f4.enter_context(nc.sbuf_tensor("wdT", [128, OCF * D], BF16))
    obuf = f4.enter_context(nc.sbuf_tensor("obuf", [128, 2 * OCF * W], BF16))
    old = f4.enter_context(nc.sbuf_tensor("old", [128, 2 * W], F32))
    oi32 = f4.enter_context(nc.sbuf_tensor("oi32", [128, W], I32))
    accld = f4.enter_context(nc.sbuf_tensor("accld", [128, 2 * W], F32))
    sumf = f4.enter_context(nc.sbuf_tensor("sumf", [128, DC * W], F32))
    xfin = f4.enter_context(nc.sbuf_tensor("xfin", [128, 2 * 1024], F32))
    outst = f4.enter_context(nc.sbuf_tensor("outst", [128, 2 * 1024], F32))
    pdn = f4.enter_context(nc.psum_tensor("pdn", [128, 2 * W], F32))
    pfin = f4.enter_context(nc.psum_tensor("pfin", [128, 2 * 1024], F32))

    @blk.sync
    def _(sy):
        S.wait(sy, "dqoA")
        S.wait(sy, "dqoB")
        for kc in range(OCF):
            S.inc("wTA", sy.dma_start(out=wdT[:, kc * D:(kc + 1) * D],
                                      in_=wd_dq.ap()[:, kc * 128:(kc + 1) * 128], transpose=True), 16)

    for tt in range(NCH):
        ob = (tt % 2) * OCF * W
        for kc in range(OCF):
            buf = kc % 2
            ldsem = "ldA" if buf == 0 else "ldB"

            @blk.sync
            def _(sy, tt=tt, kc=kc, buf=buf, ldsem=ldsem):
                if tt * OCF + kc >= 2:
                    S.wait(sy, "f4v", S.count("f4v") - 1)
                S.inc(ldsem, sy.dma_start(out=old[:, buf * W:(buf + 1) * W],
                                          in_=or_h.ap()[kc, :, tt * W:(tt + 1) * W]), 16)

            @blk.vector
            def _(v, tt=tt, kc=kc, buf=buf, ldsem=ldsem, ob=ob):
                S.wait(v, ldsem)
                if tt == 0 and kc == 0:
                    S.wait(v, "dsv")
                if tt >= 2 and kc == 0:
                    S.wait(v, "f4mm", S.count("f4mm") - 40)
                v.tensor_scalar(oi32[:, :], old[:, buf * W:(buf + 1) * W], scl[:, 9:10], KQH,
                                OP.mult, OP.add)
                v.drain()
                v.tensor_scalar(obuf[:, ob + kc * W:ob + (kc + 1) * W], oi32[:, :], KQ, scl[:, 8:9],
                                OP.subtract, OP.mult)
                S.inc("f4v", v.drain())

        for d2 in range(DC):
            pslot = d2 % 2
            lasem = "f4la" if pslot == 0 else "f4lb"

            @blk.sync
            def _(sy, tt=tt, d2=d2, pslot=pslot, lasem=lasem):
                if S.count("f4av") >= 2:
                    S.wait(sy, "f4av", S.count("f4av") - 1)
                S.inc(lasem, sy.dma_start(out=accld[:, pslot * W:(pslot + 1) * W],
                                          in_=acc_h.ap()[d2, :, tt * W:(tt + 1) * W]), 16)

            @blk.tensor
            def _(t, tt=tt, d2=d2, pslot=pslot, ob=ob):
                S.wait(t, "f4v")
                if tt == 0 and d2 == 0:
                    S.wait(t, "wTA")
                if S.count("f4av") >= 2:
                    S.wait(t, "f4av", S.count("f4av") - 1)
                for kc in range(OCF):
                    mm = t.matmul(pdn[:, pslot * W:(pslot + 1) * W],
                                  wdT[:, kc * D + d2 * 128: kc * D + (d2 + 1) * 128],
                                  obuf[:, ob + kc * W:ob + (kc + 1) * W], start=(kc == 0), stop=(kc == OCF - 1))
                S.inc("f4mm", mm)

            @blk.vector
            def _(v, d2=d2, pslot=pslot, lasem=lasem):
                S.wait(v, "f4mm")
                S.wait(v, lasem)
                S.wait(v, "f4mm")
                S.inc("f4av", v.tensor_tensor(sumf[:, d2 * W:(d2 + 1) * W],
                                              pdn[:, pslot * W:(pslot + 1) * W],
                                              accld[:, pslot * W:(pslot + 1) * W], OP.add))

        for g in range(4):
            gbuf = g % 2
            lxsem = "f4lx" if gbuf == 0 else "f4ly"
            stsem = "stA" if gbuf == 0 else "stB"

            @blk.sync
            def _(sy, tt=tt, g=g, gbuf=gbuf, lxsem=lxsem):
                if S.count("f4fv") >= 2:
                    S.wait(sy, "f4fv", S.count("f4fv") - 1)
                src = x_d[tt * W + g * 128: tt * W + (g + 1) * 128, :]
                S.inc(lxsem, sy.dma_start(out=xfin[:, gbuf * 1024:(gbuf + 1) * 1024], in_=src), 16)

            @blk.tensor
            def _(t, tt=tt, g=g, gbuf=gbuf):
                S.wait(t, "f4av")
                if S.count("f4fv") >= 2:
                    S.wait(t, "f4fv", S.count("f4fv") - 1)
                for d2 in range(DC):
                    S.inc("f4mm", t.transpose(pfin[:, gbuf * 1024 + d2 * 128: gbuf * 1024 + (d2 + 1) * 128],
                                              sumf[:, d2 * W + g * 128: d2 * W + (g + 1) * 128], idf[:]))

            @blk.vector
            def _(v, g=g, gbuf=gbuf, lxsem=lxsem, stsem=stsem):
                S.wait(v, "f4mm")
                S.wait(v, lxsem)
                if S.count("f4fv") >= 2:
                    S.wait(v, stsem)
                S.inc("f4fv", v.tensor_tensor(outst[:, gbuf * 1024:(gbuf + 1) * 1024],
                                              pfin[:, gbuf * 1024:(gbuf + 1) * 1024],
                                              xfin[:, gbuf * 1024:(gbuf + 1) * 1024], OP.add))

            @blk.sync
            def _(sy, tt=tt, g=g, gbuf=gbuf, stsem=stsem):
                S.wait(sy, "f4fv")
                S.inc(stsem, sy.dma_start(out=out_d[tt * W + g * 128: tt * W + (g + 1) * 128, :],
                                          in_=outst[:, gbuf * 1024:(gbuf + 1) * 1024]), 16)

    @blk.sync
    def _(sy):
        S.wait(sy, "stA")
        S.wait(sy, "stB")

    f4.close()
    return nc, st


_PROG_CACHE = {}


def _get_program():
    if "nc" not in _PROG_CACHE:
        nc, st = build_program()
        _PROG_CACHE["nc"] = nc
        _PROG_CACHE["st"] = st
    return _PROG_CACHE["nc"]


def make_in_maps(inputs):
    x_full = np.ascontiguousarray(np.asarray(inputs["L0_attn_x"], dtype=np.float32))
    rk = np.arange(128)
    mUh = (rk[:, None] >= rk[None, :] + 1).astype(ml_dtypes.bfloat16)
    mLh = (rk[:, None] <= rk[None, :]).astype(ml_dtypes.bfloat16)
    rq = np.arange(256)
    m0 = (rk[:, None] >= rq[None, :] + 1)
    m1 = (rk[:, None] >= rq[None, :] - 127)
    m4 = (rk[:, None] <= rq[None, :])
    m5 = (rk[:, None] <= rq[None, :] - 128)
    mqh = np.concatenate([m0, m1, m4, m5], axis=1).astype(ml_dtypes.bfloat16)
    idn = np.eye(128, dtype=np.float32)
    common = {
        "wqkv": np.ascontiguousarray(np.asarray(inputs["L0_attn_w_qkv"], dtype=np.float32)),
        "wo": np.ascontiguousarray(np.asarray(inputs["L0_attn_w_o"], dtype=np.float32)),
        "w1": np.ascontiguousarray(np.asarray(inputs["L0_attn_mlp_w1"], dtype=np.float32)),
        "b1": np.ascontiguousarray(np.asarray(inputs["L0_attn_mlp_b1"], dtype=np.float32)),
        "w2": np.ascontiguousarray(np.asarray(inputs["L0_attn_mlp_w2"], dtype=np.float32)),
        "b2": np.ascontiguousarray(np.asarray(inputs["L0_attn_mlp_b2"], dtype=np.float32)),
        "g1v": np.ascontiguousarray(np.asarray(inputs["L0_attn_ln1_g"], dtype=np.float32)),
        "b1v": np.ascontiguousarray(np.asarray(inputs["L0_attn_ln1_b"], dtype=np.float32)),
        "g2v": np.ascontiguousarray(np.asarray(inputs["L0_attn_ln2_g"], dtype=np.float32)),
        "b2v": np.ascontiguousarray(np.asarray(inputs["L0_attn_ln2_b"], dtype=np.float32)),
        "wg": np.ascontiguousarray(np.asarray(inputs["L0_ffn_w_gate"], dtype=np.float32)),
        "wu": np.ascontiguousarray(np.asarray(inputs["L0_ffn_w_up"], dtype=np.float32)),
        "wd": np.ascontiguousarray(np.asarray(inputs["L0_ffn_w_down"], dtype=np.float32)),
        "idf": idn,
        "mU": mUh,
        "mL": mLh,
        "mq": mqh,
    }
    return [dict(common, x=np.ascontiguousarray(x_full[c])) for c in range(NCORES)]


def kernel(**inputs):
    in_maps = make_in_maps(inputs)
    nc = _get_program()
    res = run_bass_kernel_spmd(nc, in_maps, core_ids=list(range(NCORES)))
    return np.stack([np.asarray(r["out"], dtype=np.float32) for r in res.results], axis=0)


# version 13
# speedup vs baseline: 1.0677x; 1.0031x over previous
import contextlib
import os
import numpy as np
import ml_dtypes

import concourse.bass as bass
import concourse.bass_utils as _bass_utils
import concourse.mybir as mybir
from concourse.bass_utils import run_bass_kernel_spmd

if not getattr(_bass_utils, "_dyn_act_table_patched", False):
    _orig_get_walrus_args = _bass_utils.get_walrus_args

    def _get_walrus_args_dyn(*a, **k):
        return _orig_get_walrus_args(*a, **k) + ["--enable-dynamic-act-table"]

    _bass_utils.get_walrus_args = _get_walrus_args_dyn
    _bass_utils._dyn_act_table_patched = True

F32 = mybir.dt.float32
BF16 = mybir.dt.bfloat16
I32 = mybir.dt.int32
AF = mybir.ActivationFunctionType
OP = mybir.AluOpType
AX = mybir.AxisListType

NCORES = 8
T = 4096
D = 1024
DC = D // 128
H = 16
HD = 64
W = 512
NCH = T // W
DFF = 4096
DI = 3072
OCF = DI // 128
_TRUNC_CAST = (os.environ.get("BASS_SIM_FP64") == "1"
               or os.environ.get("KERNEL_TRUNC_CAST") == "1")
KQ = 49152.0 if _TRUNC_CAST else 0.0
KQH = KQ + 0.5 if _TRUNC_CAST else 0.0
W8H = 8.5 if _TRUNC_CAST else 0.0
W8 = 8.0 if _TRUNC_CAST else 0.0
SGH = 0.5 if _TRUNC_CAST else 0.0
EPS = 1e-5
SCALE = 0.125

MX_XN = 0
MX_TMP = 40
MX_UP = 64
MX_GA = 256
MX_QN = 456
MX_KN = 464
MX_M = 480


class Sems:
    def __init__(self, nc, stack):
        self.nc, self.stack, self.h, self.n = nc, stack, {}, {}

    def get(self, name):
        if name not in self.h:
            self.h[name] = self.stack.enter_context(self.nc.semaphore(name))
            self.n[name] = 0
        return self.h[name]

    def inc(self, name, instr, amount=1):
        instr.then_inc(self.get(name), amount)
        self.n[name] += amount
        return instr

    def count(self, name):
        self.get(name)
        return self.n[name]

    def wait(self, eng, name, val=None):
        eng.wait_ge(self.get(name), self.n[name] if val is None else val)


def build_program():
    nc = bass.Bass("TRN2", target_bir_lowering=False)
    st = contextlib.ExitStack()
    S = Sems(nc, st)

    x_d = nc.dram_tensor("x", [T, D], F32, kind="ExternalInput").ap()
    wqkv_d = nc.dram_tensor("wqkv", [3 * D, D], F32, kind="ExternalInput").ap()
    wo_d = nc.dram_tensor("wo", [D, D], F32, kind="ExternalInput").ap()
    w1_d = nc.dram_tensor("w1", [DFF, D], F32, kind="ExternalInput").ap()
    b1_d = nc.dram_tensor("b1", [DFF], F32, kind="ExternalInput").ap()
    w2_d = nc.dram_tensor("w2", [D, DFF], F32, kind="ExternalInput").ap()
    b2_d = nc.dram_tensor("b2", [D], F32, kind="ExternalInput").ap()
    g1v_d = nc.dram_tensor("g1v", [D], F32, kind="ExternalInput").ap()
    b1v_d = nc.dram_tensor("b1v", [D], F32, kind="ExternalInput").ap()
    g2v_d = nc.dram_tensor("g2v", [D], F32, kind="ExternalInput").ap()
    b2v_d = nc.dram_tensor("b2v", [D], F32, kind="ExternalInput").ap()
    wg_d = nc.dram_tensor("wg", [DI, D], F32, kind="ExternalInput").ap()
    wu_d = nc.dram_tensor("wu", [DI, D], F32, kind="ExternalInput").ap()
    wd_d = nc.dram_tensor("wd", [D, DI], F32, kind="ExternalInput").ap()
    idf_d = nc.dram_tensor("idf", [128, 128], F32, kind="ExternalInput").ap()
    mU_d = nc.dram_tensor("mU", [128, 128], BF16, kind="ExternalInput").ap()
    mL_d = nc.dram_tensor("mL", [128, 128], BF16, kind="ExternalInput").ap()
    mq_d = nc.dram_tensor("mq", [128, 4 * 256], BF16, kind="ExternalInput").ap()
    out_d = nc.dram_tensor("out", [T, D], F32, kind="ExternalOutput").ap()

    wqkv_bf = nc.dram_tensor("wqkv_bf", [3 * D, D], BF16)
    wo_bf = nc.dram_tensor("wo_bf", [D, D], BF16)
    w1_bf = nc.dram_tensor("w1_bf", [DFF, D], BF16)
    w2_bf = nc.dram_tensor("w2_bf", [D, DFF], BF16)
    wg_dq = nc.dram_tensor("wg_dq", [DI, D], BF16)
    wu_dq = nc.dram_tensor("wu_dq", [DI, D], BF16)
    wd_dq = nc.dram_tensor("wd_dq", [D, DI], BF16)
    xn_h = nc.dram_tensor("xn_h", [DC, 128, T], F32)
    h_hbm = nc.dram_tensor("h_hbm", [DC, 128, T], BF16)
    up_h = nc.dram_tensor("up_h", [OCF, 128, T], F32)
    gr_h = nc.dram_tensor("gr_h", [OCF, 128, T], F32)
    t2_h = nc.dram_tensor("t2_h", [OCF, 128, T], F32)
    or_h = nc.dram_tensor("or_h", [OCF, 128, T], F32)
    x2_h = nc.dram_tensor("x2_h", [DC, 128, T], F32)
    acc_h = nc.dram_tensor("acc_h", [DC, 128, T], F32)
    cio = [(nc.dram_tensor("cin%d" % k, [1, 8], F32),
            nc.dram_tensor("cout%d" % k, [1, 8], F32)) for k in range(4)]

    idf = st.enter_context(nc.sbuf_tensor("idf_s", [128, 128], F32))
    mU = st.enter_context(nc.sbuf_tensor("mU_s", [128, 128], BF16))
    mL = st.enter_context(nc.sbuf_tensor("mL_s", [128, 128], BF16))
    mq = st.enter_context(nc.sbuf_tensor("mq_s", [128, 4 * 256], BF16))
    ones = st.enter_context(nc.sbuf_tensor("ones", [128, 128], F32))
    onepair = st.enter_context(nc.sbuf_tensor("onepair", [128, 2], F32))
    gb = st.enter_context(nc.sbuf_tensor("gb", [128, 48], F32))
    b1c = st.enter_context(nc.sbuf_tensor("b1c", [128, 32], F32))
    scl = st.enter_context(nc.sbuf_tensor("scl", [128, 16], F32))
    mxc = st.enter_context(nc.sbuf_tensor("mxc", [128, 512], F32))
    flat = st.enter_context(nc.sbuf_tensor("flat", [128, 2176], F32))
    nrm = st.enter_context(nc.sbuf_tensor("nrm", [128, 8], F32))

    G1C, B1C, G2C, B2C, MB2C = 0, 8, 16, 24, 32

    blk = st.enter_context(nc.Block())
    ENGS = ("sync", "gpsimd", "vector", "scalar", "tensor")

    def barrier(dma_sems=()):
        @blk.sync
        def _(sy):
            for sname in dma_sems:
                S.wait(sy, sname)
        for e in ENGS:
            @getattr(blk, e)
            def _(eng, e=e):
                S.inc("bar_" + e, eng.nop())
        for e in ENGS:
            @getattr(blk, e)
            def _(eng, e=e):
                for o in ENGS:
                    if o != e:
                        S.wait(eng, "bar_" + o)

    uniq = [0]

    def finalize_scalar_max(cols, prt, fb, wait_name):
        n = len(cols)
        c0 = cols[0]
        contig = all(cols[j] == c0 + j for j in range(n))
        assert contig
        uniq[0] += 1

        with nc.psum_tensor("pfz%d" % uniq[0], [128, 256], F32) as pfz:
            @blk.tensor
            def _(t):
                S.wait(t, wait_name)
                S.inc("fmt", t.transpose(pfz[0:n, 0:128], mxc[:, c0:c0 + n], idf[:]))

            @blk.vector
            def _(v):
                S.wait(v, "fmt")
                v.reduce_max(nrm[0:n, 7:8], pfz[0:n, 0:128], axis=AX.X)
                S.inc("fmq", v.drain())

            @blk.tensor
            def _(t):
                S.wait(t, "fmq")
                S.inc("fmt", t.transpose(pfz[0:1, 128:128 + n], nrm[0:n, 7:8], idf[0:n, 0:n]))

            @blk.vector
            def _(v):
                S.wait(v, "fmt")
                v.tensor_copy(flat[0:1, fb:fb + n], pfz[0:1, 128:128 + n])
                v.drain()
                if n < 8:
                    v.tensor_copy(flat[0:1, fb + n:fb + 8],
                                  flat[0:1, fb:fb + 1].unsqueeze(2).broadcast_to([1, 1, 8 - n]).squeeze(1))
                S.inc("fmv2", v.drain())

    def collective(k, fb):
        cin_k, cout_k = cio[k]

        @blk.gpsimd
        def _(g):
            S.wait(g, "fmv2")
            S.inc("ccw", g.dma_start(out=cin_k.ap()[0:1, 0:8], in_=flat[0:1, fb:fb + 8]), 16)
            S.wait(g, "ccw")
            S.inc("cc", g.collective_compute(
                "AllReduce", OP.max, replica_groups=[list(range(NCORES))],
                ins=[cin_k.ap().opt()], outs=[cout_k.ap().opt()]))
            S.wait(g, "cc")
            S.inc("ccr", g.dma_start(out=flat[0:1, fb + 8:fb + 16], in_=cout_k.ap()[0:1, 0:8]), 16)

    def derive_scales(pairs, fb):
        @blk.vector
        def _(v):
            S.wait(v, "ccr")
            for j, (c, q, _sc) in enumerate(pairs):
                v.tensor_scalar(flat[0:1, fb + 2 * j:fb + 2 * j + 1], flat[0:1, c:c + 1],
                                1.0 / q, 1e-12, OP.mult, OP.max)
            v.drain()
            for j in range(len(pairs)):
                v.reciprocal(flat[0:1, fb + 2 * j + 1:fb + 2 * j + 2],
                             flat[0:1, fb + 2 * j:fb + 2 * j + 1])
            S.inc("dsv", v.drain())

        uniq[0] += 1
        with nc.psum_tensor("pdsx%d" % uniq[0], [128, 16], F32) as pds:
            @blk.tensor
            def _(t):
                S.wait(t, "dsv")
                S.inc("dst", t.matmul(pds[:, 0:2 * len(pairs)], ones[0:1, 0:128],
                                      flat[0:1, fb:fb + 2 * len(pairs)], start=True, stop=True))

            @blk.vector
            def _(v):
                S.wait(v, "dst")
                for j, (_c, _q, c2) in enumerate(pairs):
                    v.tensor_copy(scl[:, c2:c2 + 2], pds[:, 2 * j:2 * j + 2])
                S.inc("dsv", v.drain())

    @blk.gpsimd
    def _(g):
        S.inc("cast1", g.dma_start(out=wqkv_bf.ap(), in_=wqkv_d), 16)
        S.inc("cast1", g.dma_start(out=wo_bf.ap(), in_=wo_d), 16)
        S.inc("cast2", g.dma_start(out=w1_bf.ap(), in_=w1_d), 16)
        S.inc("cast2", g.dma_start(out=w2_bf.ap(), in_=w2_d), 16)

    with nc.sbuf_tensor("vecst", [48, 128], F32) as vecst, \
         nc.sbuf_tensor("b1st", [32, 128], F32) as b1st, \
         nc.psum_tensor("pvec", [128, 48], F32) as pvec, \
         nc.psum_tensor("pb1", [128, 32], F32) as pb1:

        @blk.sync
        def _(sy):
            S.inc("cst", sy.dma_start(out=idf[:], in_=idf_d), 16)
            S.inc("cst", sy.dma_start(out=mU[:], in_=mU_d), 16)
            S.inc("cst", sy.dma_start(out=mL[:], in_=mL_d), 16)
            S.inc("cst", sy.dma_start(out=mq[:], in_=mq_d), 16)
            S.inc("cst", sy.dma_start(out=vecst[0:8, :], in_=g1v_d.rearrange("(a b) -> a b", b=128)), 16)
            S.inc("cst", sy.dma_start(out=vecst[8:16, :], in_=b1v_d.rearrange("(a b) -> a b", b=128)), 16)
            S.inc("cst", sy.dma_start(out=vecst[16:24, :], in_=g2v_d.rearrange("(a b) -> a b", b=128)), 16)
            S.inc("cst", sy.dma_start(out=vecst[24:32, :], in_=b2v_d.rearrange("(a b) -> a b", b=128)), 16)
            S.inc("cst", sy.dma_start(out=vecst[32:40, :], in_=b2_d.rearrange("(a b) -> a b", b=128)), 16)
            S.inc("cst", sy.dma_start(out=b1st[:, :], in_=b1_d.rearrange("(a b) -> a b", b=128)), 16)

        @blk.tensor
        def _(t):
            S.wait(t, "cst")
            t.transpose(pvec[:, 0:40], vecst[0:40, :], idf[0:40, 0:40])
            S.inc("fmt", t.transpose(pb1[:, 0:32], b1st[:, :], idf[0:32, 0:32]))

        @blk.vector
        def _(v):
            v.memset(ones[:], 1.0)
            v.memset(onepair[:, :], 0.0)
            v.drain()
            v.memset(onepair[0:64, 0:1], 1.0)
            v.memset(onepair[64:128, 1:2], 1.0)
            S.wait(v, "fmt")
            v.tensor_copy(gb[:, 0:40], pvec[:, 0:40])
            v.tensor_copy(b1c[:, 0:32], pb1[:, 0:32])
            S.inc("dsv", v.drain())

        barrier(("cst",))

    p1 = contextlib.ExitStack()
    hst = p1.enter_context(nc.sbuf_tensor("hst", [128, 2 * DC * W], BF16))
    xin = p1.enter_context(nc.sbuf_tensor("xin", [128, 2 * 4096], F32))
    p1st = p1.enter_context(nc.sbuf_tensor("p1st", [128, 64], F32))
    hpre = p1.enter_context(nc.sbuf_tensor("hpre", [128, 4096], F32))
    xnst = p1.enter_context(nc.sbuf_tensor("xnst", [128, 2 * DC * W], F32))
    ptr1 = p1.enter_context(nc.psum_tensor("ptr1", [128, 1024], F32))
    ptr2 = p1.enter_context(nc.psum_tensor("ptr2", [128, 1024], F32))

    for tt in range(NCH):
        buf = tt % 2
        xo = buf * 4096
        so = buf * DC * W
        ldsem = "ldA" if buf == 0 else "ldB"

        @blk.sync
        def _(sy, tt=tt, xo=xo, ldsem=ldsem):
            if tt >= 2:
                S.wait(sy, "p1v", S.count("p1v") - 1)
            src = x_d[tt * W:(tt + 1) * W, :].rearrange("(g p) d -> p g d", p=128)
            S.inc(ldsem, sy.dma_start(out=xin[:, xo:xo + 4096].rearrange("p (g d) -> p g d", d=1024), in_=src), 16)

        @blk.scalar
        def _(sc, tt=tt, xo=xo, ldsem=ldsem):
            S.wait(sc, ldsem)
            if tt >= 1:
                S.wait(sc, "p1t")
            for g in range(4):
                xg = xin[:, xo + g * 1024: xo + (g + 1) * 1024]
                sc.activation(hpre[:, g * 1024:(g + 1) * 1024], xg, AF.Square)
            S.inc("p1sa", sc.drain())

        @blk.vector
        def _(v, tt=tt, xo=xo, ldsem=ldsem):
            S.wait(v, ldsem)
            if tt == 0:
                S.wait(v, "dsv")
            for g in range(4):
                xg = xin[:, xo + g * 1024: xo + (g + 1) * 1024]
                v.reduce_sum(p1st[:, g:g + 1], xg, axis=AX.X)
            S.wait(v, "p1sa")
            for g in range(4):
                v.reduce_sum(p1st[:, 4 + g:5 + g], hpre[:, g * 1024:(g + 1) * 1024], axis=AX.X)
            v.drain()
            v.tensor_single_scalar(p1st[:, 8:12], p1st[:, 0:4], 1.0 / D, OP.mult)
            v.tensor_single_scalar(p1st[:, 12:16], p1st[:, 4:8], 1.0 / D, OP.mult)
            v.drain()
            v.tensor_tensor(p1st[:, 16:20], p1st[:, 8:12], p1st[:, 8:12], OP.mult)
            v.drain()
            v.tensor_tensor(p1st[:, 16:20], p1st[:, 12:16], p1st[:, 16:20], OP.subtract)
            v.drain()
            v.tensor_single_scalar(p1st[:, 16:20], p1st[:, 16:20], EPS, OP.add)
            S.inc("p1q", v.drain())

        @blk.scalar
        def _(sc):
            S.wait(sc, "p1q")
            S.inc("p1r", sc.activation(p1st[:, 20:24], p1st[:, 16:20], AF.Sqrt))

        @blk.vector
        def _(v, tt=tt, xo=xo):
            S.wait(v, "p1r")
            v.reciprocal(p1st[:, 24:28], p1st[:, 20:24])
            v.drain()
            for g in range(4):
                xg = xin[:, xo + g * 1024: xo + (g + 1) * 1024]
                v.tensor_scalar(xg, xg, p1st[:, 8 + g:9 + g], p1st[:, 24 + g:25 + g], OP.subtract, OP.mult)
            v.drain()
            for g in range(4):
                xg = xin[:, xo + g * 1024: xo + (g + 1) * 1024]
                v.reduce_max(mxc[:, MX_XN + tt * 4 + g:MX_XN + tt * 4 + g + 1], xg,
                             axis=AX.X, apply_absolute_value=True)
                v.reduce_sum(p1st[:, 32 + g:33 + g], xg, axis=AX.X)
            S.inc("p1b", v.drain())
            S.wait(v, "p1sa", S.count("p1sa") + 1)
            for g in range(4):
                v.reduce_sum(p1st[:, 36 + g:37 + g], hpre[:, g * 1024:(g + 1) * 1024], axis=AX.X)
            v.drain()
            v.tensor_single_scalar(p1st[:, 40:44], p1st[:, 32:36], 1.0 / D, OP.mult)
            v.tensor_single_scalar(p1st[:, 44:48], p1st[:, 36:40], 1.0 / D, OP.mult)
            v.drain()
            v.tensor_tensor(p1st[:, 48:52], p1st[:, 40:44], p1st[:, 40:44], OP.mult)
            v.drain()
            v.tensor_tensor(p1st[:, 48:52], p1st[:, 44:48], p1st[:, 48:52], OP.subtract)
            v.drain()
            v.tensor_single_scalar(p1st[:, 48:52], p1st[:, 48:52], EPS, OP.add)
            S.inc("p1q", v.drain())

        @blk.scalar
        def _(sc, xo=xo):
            S.wait(sc, "p1b")
            for g in range(4):
                xg = xin[:, xo + g * 1024: xo + (g + 1) * 1024]
                sc.activation(hpre[:, g * 1024:(g + 1) * 1024], xg, AF.Square)
            S.inc("p1sa", sc.drain())

        @blk.scalar
        def _(sc):
            S.wait(sc, "p1q")
            S.inc("p1r", sc.activation(p1st[:, 52:56], p1st[:, 48:52], AF.Sqrt))

        @blk.vector
        def _(v, tt=tt, xo=xo):
            S.wait(v, "p1r")
            v.reciprocal(p1st[:, 56:60], p1st[:, 52:56])
            v.drain()
            for g in range(4):
                xg = xin[:, xo + g * 1024: xo + (g + 1) * 1024]
                v.tensor_scalar(hpre[:, g * 1024:(g + 1) * 1024], xg,
                                p1st[:, 40 + g:41 + g], p1st[:, 56 + g:57 + g], OP.subtract, OP.mult)
            S.inc("p1b", v.drain())

        p1e_base = S.count("p1e")

        @blk.tensor
        def _(t, tt=tt, xo=xo, p1e_base=p1e_base):
            S.wait(t, "p1b")
            for i in range(32):
                g, dc = i // DC, i % DC
                pslot = i % 2
                if i >= 2:
                    t.wait_ge(S.get("p1e"), p1e_base + i - 1)
                t.transpose(ptr1[:, pslot * 512:pslot * 512 + 128],
                            xin[:, xo + g * 1024 + dc * 128: xo + g * 1024 + (dc + 1) * 128], idf[:])
                S.inc("p1t", t.transpose(ptr2[:, pslot * 512:pslot * 512 + 128],
                                         hpre[:, g * 1024 + dc * 128: g * 1024 + (dc + 1) * 128], idf[:]))

        p1t_base = S.count("p1t") - 32

        so2 = buf * DC * W

        @blk.vector
        def _(v, tt=tt, so=so, so2=so2, p1t_base=p1t_base):
            if tt >= 2:
                S.wait(v, "p1oA" if tt % 2 == 0 else "p1oB")
            for i in range(32):
                g, dc = i // DC, i % DC
                pslot = i % 2
                v.wait_ge(S.get("p1t"), p1t_base + i + 1)
                v.tensor_copy(xnst[:, so + dc * W + g * 128: so + dc * W + (g + 1) * 128],
                              ptr1[:, pslot * 512:pslot * 512 + 128])
                S.inc("p1e",
                      v.tensor_scalar(hst[:, so2 + dc * W + g * 128: so2 + dc * W + (g + 1) * 128],
                                      ptr2[:, pslot * 512:pslot * 512 + 128],
                                      gb[:, G1C + dc:G1C + dc + 1], gb[:, B1C + dc:B1C + dc + 1],
                                      OP.mult, OP.add))
            S.inc("p1v", v.drain())

        @blk.sync
        def _(sy, tt=tt, so=so, so2=so2):
            S.wait(sy, "p1v")
            dst = xn_h.ap()[:, :, tt * W:(tt + 1) * W].transpose([1, 0, 2])
            S.inc("p1oA" if tt % 2 == 0 else "p1oB",
                  sy.dma_start(out=dst,
                               in_=xnst[:, so:so + DC * W].rearrange("p (c w) -> p c w", w=W)), 16)
            dsth = h_hbm.ap()[:, :, tt * W:(tt + 1) * W].transpose([1, 0, 2])
            S.inc("p1oA" if tt % 2 == 0 else "p1oB",
                  sy.dma_start(out=dsth,
                               in_=hst[:, so2:so2 + DC * W].rearrange("p (c w) -> p c w", w=W)), 16)

    @blk.vector
    def _(v):
        S.wait(v, "p1e")
        v.reduce_max(mxc[:, MX_TMP:MX_TMP + 1], mxc[:, MX_XN:MX_XN + 4 * NCH], axis=AX.X)
        S.inc("fmq", v.drain())

    finalize_scalar_max([MX_TMP], 128, 0, "fmq")
    collective(0, 0)
    derive_scales([(8, 32767.0, 0)], 16)

    wst = p1.enter_context(nc.sbuf_tensor("wst", [128, 2 * DI], F32))
    wblk = p1.enter_context(nc.sbuf_tensor("wblk", [128, 3 * (DI // 32)], F32))
    wi32 = p1.enter_context(nc.sbuf_tensor("wi32", [128, DI], I32))
    wbf16 = p1.enter_context(nc.sbuf_tensor("wbf16", [128, 2 * DI], BF16))

    dqi = 0
    for mi, (src, dst, nrow, I) in enumerate(
            [(wg_d, wg_dq, DI // 128, 1024), (wu_d, wu_dq, DI // 128, 1024), (wd_d, wd_dq, D // 128, DI)]):
        nb = I // 32
        for r in range(nrow):
            buf = dqi % 2
            ldsem = "ldA" if buf == 0 else "ldB"
            stsem = "dqoA" if buf == 0 else "dqoB"

            @blk.sync
            def _(sy, src=src, r=r, buf=buf, I=I, dqi=dqi, ldsem=ldsem):
                if dqi >= 2:
                    S.wait(sy, "dqv", S.count("dqv") - 1)
                S.inc(ldsem, sy.dma_start(out=wst[:, buf * DI: buf * DI + I],
                                          in_=src[r * 128:(r + 1) * 128, :]), 16)

            @blk.vector
            def _(v, buf=buf, I=I, nb=nb, dqi=dqi, ldsem=ldsem, stsem=stsem):
                S.wait(v, ldsem)
                if dqi >= 2:
                    S.wait(v, stsem)
                w = wst[:, buf * DI: buf * DI + I]
                w3 = w.rearrange("p (b k) -> p b k", k=32)
                bm = wblk[:, 0:nb]
                sblk = wblk[:, nb:2 * nb]
                invs = wblk[:, 2 * nb:3 * nb]
                v.reduce_max(bm, w3, axis=AX.X, apply_absolute_value=True)
                v.drain()
                v.tensor_scalar(sblk, bm, 1.0 / 7.0, 1e-12, OP.mult, OP.max)
                v.drain()
                v.reciprocal(invs, sblk)
                v.drain()
                v.tensor_tensor(w3, w3, invs.unsqueeze(2).broadcast_to([128, nb, 32]), OP.mult)
                v.drain()
                v.tensor_single_scalar(wi32[:, 0:I], w, W8H, OP.add)
                v.drain()
                v.scalar_tensor_tensor(wbf16[:, buf * DI: buf * DI + I].rearrange("p (b k) -> p b k", k=32),
                                       wi32[:, 0:I].rearrange("p (b k) -> p b k", k=32), W8,
                                       sblk.unsqueeze(2).broadcast_to([128, nb, 32]),
                                       OP.subtract, OP.mult)
                S.inc("dqv", v.drain())

            @blk.sync
            def _(sy, dst=dst, r=r, buf=buf, I=I, stsem=stsem):
                S.wait(sy, "dqv")
                S.inc(stsem, sy.dma_start(out=dst.ap()[r * 128:(r + 1) * 128, :],
                                          in_=wbf16[:, buf * DI: buf * DI + I]), 16)
            dqi += 1

    p1.close()
    barrier(("p1oA", "p1oB", "dqoA", "dqoB", "ldA", "ldB", "cst"))

    p2 = contextlib.ExitStack()
    xq_sb = p2.enter_context(nc.sbuf_tensor("xq_sb", [128, DC * T], BF16))
    p2ld = p2.enter_context(nc.sbuf_tensor("p2ld", [128, 2 * W], F32))
    p2i = p2.enter_context(nc.sbuf_tensor("p2i", [128, W], I32))
    wtT = p2.enter_context(nc.sbuf_tensor("wtT", [128, DC * DI], BF16))
    p2ev = p2.enter_context(nc.sbuf_tensor("p2ev", [128, 4 * W], F32))
    p2ps = p2.enter_context(nc.psum_tensor("p2ps", [128, 4 * W], F32))

    for i in range(DC * NCH):
        dc, tt = i // NCH, i % NCH
        buf = i % 2
        ldsem = "ldA" if buf == 0 else "ldB"

        @blk.sync
        def _(sy, dc=dc, tt=tt, buf=buf, i=i, ldsem=ldsem):
            if i == 0:
                S.wait(sy, "p1oA")
                S.wait(sy, "p1oB")
            if i >= 2:
                S.wait(sy, "xqv", S.count("xqv") - 1)
            S.inc(ldsem, sy.dma_start(out=p2ld[:, buf * W:(buf + 1) * W],
                                      in_=xn_h.ap()[dc, :, tt * W:(tt + 1) * W]), 16)

        @blk.vector
        def _(v, dc=dc, tt=tt, buf=buf, i=i, ldsem=ldsem):
            S.wait(v, ldsem)
            if i == 0:
                S.wait(v, "dsv")
            xg = p2ld[:, buf * W:(buf + 1) * W]
            v.tensor_scalar(p2i[:, :], xg, scl[:, 1:2], KQH, OP.mult, OP.add)
            v.drain()
            v.tensor_scalar(xq_sb[:, dc * T + tt * W: dc * T + (tt + 1) * W], p2i[:, :],
                            KQ, scl[:, 0:1], OP.subtract, OP.mult)
            S.inc("xqv", v.drain())

    for mi, (wdq, dsth, mbase) in enumerate([(wu_dq, up_h, MX_UP), (wg_dq, gr_h, MX_GA)]):
        @blk.sync
        def _(sy, wdq=wdq, mi=mi):
            if mi == 0:
                S.wait(sy, "dqoA")
                S.wait(sy, "dqoB")
            else:
                S.wait(sy, "p2m")
            for dc in range(DC):
                S.inc("wTA", sy.dma_start(out=wtT[:, dc * DI:(dc + 1) * DI],
                                          in_=wdq.ap()[:, dc * 128:(dc + 1) * 128], transpose=True), 16)

        for j in range(OCF * NCH):
            oc, tt = j // NCH, j % NCH
            pslot = j % 4

            @blk.tensor
            def _(t, mi=mi, oc=oc, tt=tt, pslot=pslot, j=j):
                if j == 0:
                    S.wait(t, "wTA")
                    if mi == 0:
                        S.wait(t, "xqv")
                if S.count("p2s") >= 4:
                    S.wait(t, "p2s", S.count("p2s") - 3)
                ps = p2ps[:, pslot * W:(pslot + 1) * W]
                for dc in range(DC):
                    mm = t.matmul(ps, wtT[:, dc * DI + oc * 128: dc * DI + (oc + 1) * 128],
                                  xq_sb[:, dc * T + tt * W: dc * T + (tt + 1) * W],
                                  start=(dc == 0), stop=(dc == DC - 1))
                S.inc("p2m", mm)

            @blk.scalar
            def _(sc, pslot=pslot, j=j, mi=mi):
                S.wait(sc, "p2m")
                if S.count("p2e") >= 4:
                    S.wait(sc, "p2e", S.count("p2e") - 3)
                if S.count("p2s") >= 4:
                    S.wait(sc, "p2o%d" % pslot)
                S.inc("p2s", sc.copy(p2ev[:, pslot * W:(pslot + 1) * W], p2ps[:, pslot * W:(pslot + 1) * W]))

            @blk.vector
            def _(v, pslot=pslot, mbase=mbase, j=j):
                S.wait(v, "p2s")
                S.inc("p2e", v.reduce_max(mxc[:, mbase + j:mbase + j + 1],
                                          p2ev[:, pslot * W:(pslot + 1) * W],
                                          axis=AX.X, apply_absolute_value=True))

            @blk.sync
            def _(sy, oc=oc, tt=tt, pslot=pslot, dsth=dsth):
                S.wait(sy, "p2e")
                S.inc("p2o%d" % pslot,
                      sy.dma_start(out=dsth.ap()[oc, :, tt * W:(tt + 1) * W],
                                   in_=p2ev[:, pslot * W:(pslot + 1) * W]), 16)

    @blk.vector
    def _(v):
        S.wait(v, "p2e")
        v.reduce_max(mxc[:, MX_TMP:MX_TMP + 1], mxc[:, MX_UP:MX_UP + OCF * NCH], axis=AX.X)
        v.reduce_max(mxc[:, MX_TMP + 1:MX_TMP + 2], mxc[:, MX_GA:MX_GA + OCF * NCH], axis=AX.X)
        S.inc("fmq", v.drain())

    finalize_scalar_max([MX_TMP, MX_TMP + 1], 128, 32, "fmq")
    collective(1, 32)
    derive_scales([(40, 32767.0, 2), (41, 32767.0, 4)], 48)

    p2.close()
    barrier(("p2o0", "p2o1", "p2o2", "p2o3", "ldA", "ldB", "wTA"))

    f2 = contextlib.ExitStack()
    fld = f2.enter_context(nc.sbuf_tensor("fld", [128, 4 * W], F32))
    fi32 = f2.enter_context(nc.sbuf_tensor("fi32", [128, W], I32))
    fgate = f2.enter_context(nc.sbuf_tensor("fgate", [128, 4 * W], F32))
    ft2 = f2.enter_context(nc.sbuf_tensor("ft2", [128, 4 * W], F32))

    def emit_f2_load(j):
        if j >= OCF * NCH:
            return
        oc, tt = j // NCH, j % NCH
        buf = j % 4

        @blk.sync
        def _(sy):
            if j == 0:
                for s4 in range(4):
                    S.wait(sy, "p2o%d" % s4)
            if j >= 4:
                S.wait(sy, "f2v2", j - 3)
            S.inc("f2l%d" % buf, sy.dma_start(out=fld[:, buf * W:(buf + 1) * W],
                                              in_=gr_h.ap()[oc, :, tt * W:(tt + 1) * W]), 16)

    def emit_f2_tile(j):
        oc, tt = j // NCH, j % NCH
        buf = j % 4

        @blk.vector
        def _(v):
            S.wait(v, "f2l%d" % buf, 16 * (j // 4 + 1))
            if j == 0:
                S.wait(v, "dsv")
            if j >= 4:
                S.wait(v, "f2sg", j - 3)
                S.wait(v, "f2s%d" % buf)
            v.tensor_scalar(fi32[:, :], fld[:, buf * W:(buf + 1) * W], scl[:, 5:6], KQH,
                            OP.mult, OP.add)
            v.drain()
            v.tensor_scalar(fgate[:, buf * W:(buf + 1) * W], fi32[:, :], KQ, scl[:, 4:5],
                            OP.subtract, OP.mult)
            S.inc("f2v1", v.drain())

        @blk.scalar
        def _(sc):
            S.wait(sc, "f2v1")
            S.inc("f2sg", sc.activation(fld[:, buf * W:(buf + 1) * W],
                                        fgate[:, buf * W:(buf + 1) * W], AF.Tanh, scale=0.5))

        @blk.vector
        def _(v):
            S.wait(v, "f2sg")
            v.tensor_scalar(fi32[:, :], fld[:, buf * W:(buf + 1) * W], 32768.0, 32768.0 + SGH,
                            OP.mult, OP.add)
            v.drain()
            v.tensor_scalar(fld[:, buf * W:(buf + 1) * W], fi32[:, :], 65535.0, 1.0 / 65536.0,
                            OP.min, OP.mult)
            v.drain()
            v.tensor_tensor(ft2[:, buf * W:(buf + 1) * W], fgate[:, buf * W:(buf + 1) * W],
                            fld[:, buf * W:(buf + 1) * W], OP.mult)
            v.drain()
            S.inc("f2v2", v.reduce_max(mxc[:, MX_UP + j:MX_UP + j + 1], ft2[:, buf * W:(buf + 1) * W],
                                       axis=AX.X, apply_absolute_value=True))

        @blk.sync
        def _(sy):
            S.wait(sy, "f2v2")
            S.inc("f2s%d" % buf, sy.dma_start(out=t2_h.ap()[oc, :, tt * W:(tt + 1) * W],
                                              in_=ft2[:, buf * W:(buf + 1) * W]), 16)

    def emit_f3_load(j):
        if j >= OCF * NCH:
            return
        oc, tt = j // NCH, j % NCH
        buf = j % 4

        @blk.sync
        def _(sy):
            if j == 0:
                for s4 in range(4):
                    S.wait(sy, "f2s%d" % s4)
            if j >= 4:
                S.wait(sy, "f3v", j - 3)
            S.inc("f3l%d" % buf, sy.dma_start(out=fld[:, buf * W:(buf + 1) * W],
                                              in_=t2_h.ap()[oc, :, tt * W:(tt + 1) * W]), 16)
            S.inc("f3l%d" % buf, sy.dma_start(out=fgate[:, buf * W:(buf + 1) * W],
                                              in_=up_h.ap()[oc, :, tt * W:(tt + 1) * W]), 16)

    def emit_f3_tile(j):
        oc, tt = j // NCH, j % NCH
        buf = j % 4

        @blk.vector
        def _(v):
            S.wait(v, "f3l%d" % buf, 32 * (j // 4 + 1))
            if j == 0:
                S.wait(v, "dsv")
            if j >= 4:
                S.wait(v, "f3s%d" % buf)
            v.tensor_scalar(fi32[:, :], fld[:, buf * W:(buf + 1) * W], scl[:, 7:8], KQH,
                            OP.mult, OP.add)
            v.drain()
            v.tensor_scalar(fld[:, buf * W:(buf + 1) * W], fi32[:, :], KQ, scl[:, 6:7],
                            OP.subtract, OP.mult)
            v.drain()
            v.tensor_scalar(fi32[:, :], fgate[:, buf * W:(buf + 1) * W], scl[:, 3:4], KQH,
                            OP.mult, OP.add)
            v.drain()
            v.tensor_scalar(fgate[:, buf * W:(buf + 1) * W], fi32[:, :], KQ, scl[:, 2:3],
                            OP.subtract, OP.mult)
            v.drain()
            v.tensor_tensor(ft2[:, buf * W:(buf + 1) * W], fld[:, buf * W:(buf + 1) * W],
                            fgate[:, buf * W:(buf + 1) * W], OP.mult)
            v.drain()
            S.inc("f3v", v.reduce_max(mxc[:, MX_GA + j:MX_GA + j + 1], ft2[:, buf * W:(buf + 1) * W],
                                      axis=AX.X, apply_absolute_value=True))

        @blk.sync
        def _(sy):
            S.wait(sy, "f3v")
            S.inc("f3s%d" % buf, sy.dma_start(out=or_h.ap()[oc, :, tt * W:(tt + 1) * W],
                                              in_=ft2[:, buf * W:(buf + 1) * W]), 16)

    p3 = contextlib.ExitStack()
    hstg = p3.enter_context(nc.sbuf_tensor("hstg", [128, 2 * DC * W], BF16))
    wqkvT = p3.enter_context(nc.sbuf_tensor("wqkvT", [128, DC * 3 * D], BF16))
    woT = p3.enter_context(nc.sbuf_tensor("woT", [128, DC * D], BF16))
    qf = p3.enter_context(nc.sbuf_tensor("qf", [128, DC * W], BF16))
    kf2 = p3.enter_context(nc.sbuf_tensor("kf2", [128, 2 * DC * W], BF16))
    vsb2 = p3.enter_context(nc.sbuf_tensor("vsb2", [128, 2 * 4 * H * 65], BF16))
    avn = p3.enter_context(nc.sbuf_tensor("avn", [128, DC * W], BF16))
    xnstg = p3.enter_context(nc.sbuf_tensor("xnstg", [128, 2 * W], F32))
    x2stg = p3.enter_context(nc.sbuf_tensor("x2stg", [128, 2 * W], F32))
    pTs = p3.enter_context(nc.sbuf_tensor("pTs", [128, 2 * 768], BF16))
    q2st = p3.enter_context(nc.sbuf_tensor("q2st", [128, W], F32))

    @blk.sync
    def _(sy):
        S.wait(sy, "cast1")
        for dc in range(DC):
            S.inc("wTA", sy.dma_start(out=wqkvT[:, dc * 3 * D:(dc + 1) * 3 * D],
                                      in_=wqkv_bf.ap()[:, dc * 128:(dc + 1) * 128], transpose=True), 16)
        for dc in range(DC):
            S.inc("wTA", sy.dma_start(out=woT[:, dc * D:(dc + 1) * D],
                                      in_=wo_bf.ap()[:, dc * 128:(dc + 1) * 128], transpose=True), 16)

    for c in range(NCH):
        cur, prv = c % 2, (c + 1) % 2
        VS = 4 * H * 65
        p3qm_base = S.count("p3qm")

        @blk.sync
        def _(sy, c=c, cur=cur, p3qm_base=p3qm_base):
            if c == 0:
                S.wait(sy, "p1oA")
                S.wait(sy, "p1oB")
            if c >= 2:
                S.wait(sy, "p3qm", p3qm_base - 24)
            S.inc("hlA" if cur == 0 else "hlB",
                  sy.dma_start(out=hstg[:, cur * DC * W: (cur + 1) * DC * W]
                               .rearrange("p (k w) -> p k w", w=W),
                               in_=h_hbm.ap()[:, :, c * W:(c + 1) * W].transpose([1, 0, 2])), 16)

        with nc.psum_tensor("pqk_%d" % c, [128, 2 * W], F32) as pqk:
            for wi, rowbase in ((0, 0), (1, D)):
                for oc in range(DC):
                    pslot = (wi * DC + oc) % 2

                    @blk.tensor
                    def _(t, c=c, cur=cur, wi=wi, rowbase=rowbase, oc=oc, pslot=pslot, pqk=pqk):
                        if c == 0 and wi == 0 and oc == 0:
                            S.wait(t, "wTA")
                        if wi == 0 and oc == 0:
                            S.wait(t, "hlA" if cur == 0 else "hlB")
                        if c > 0 and wi == 0 and oc == 0:
                            S.wait(t, "p3wv")
                        if S.count("p3qe") >= 2:
                            S.wait(t, "p3qe", S.count("p3qe") - 1)
                        ps = pqk[:, pslot * W:(pslot + 1) * W]
                        for dc in range(DC):
                            mm = t.matmul(ps, wqkvT[:, dc * 3 * D + rowbase + oc * 128:
                                                    dc * 3 * D + rowbase + (oc + 1) * 128],
                                          hstg[:, cur * DC * W + dc * W: cur * DC * W + (dc + 1) * W],
                                          start=(dc == 0), stop=(dc == DC - 1))
                        S.inc("p3qm", mm)

                    @blk.vector
                    def _(v, c=c, wi=wi, oc=oc, pslot=pslot, cur=cur, pqk=pqk):
                        S.wait(v, "p3qm")
                        if c > 0 and wi == 0 and oc == 0:
                            S.wait(v, "p3xm")
                        if wi == 0:
                            o = v.tensor_copy(qf[:, oc * W:(oc + 1) * W], pqk[:, pslot * W:(pslot + 1) * W])
                        else:
                            o = v.tensor_copy(kf2[:, cur * DC * W + oc * W: cur * DC * W + (oc + 1) * W],
                                              pqk[:, pslot * W:(pslot + 1) * W])
                        S.inc("p3qe", o)

        @blk.vector
        def _(v, c=c, cur=cur, VS=VS):
            S.wait(v, "p3av")
            v.memset(vsb2[:, cur * VS:(cur + 1) * VS].rearrange("p (s o) -> p s o", o=65)[:, :, 64:65], 1.0)
            S.inc("p3vo", v.drain())

        with contextlib.ExitStack() as vsc:
            pv8 = [vsc.enter_context(nc.psum_tensor("pv8_%d_%d" % (c, b), [128, 64], F32)) for b in range(8)]
            for g in range(4):
                for hg in range(2):
                    @blk.tensor
                    def _(t, c=c, g=g, hg=hg, pv8=pv8):
                        S.wait(t, "p3qe")
                        if S.count("p3ve") > 0:
                            S.wait(t, "p3ve")
                        for dc in range(DC):
                            hb = hstg[:, cur * DC * W + dc * W + g * 128: cur * DC * W + dc * W + (g + 1) * 128]
                            for hh in range(8):
                                h = hg * 8 + hh
                                mm = t.matmul(pv8[hh][:, 0:64], hb,
                                              wqkvT[:, dc * 3 * D + 2 * D + h * 64:
                                                    dc * 3 * D + 2 * D + (h + 1) * 64],
                                              start=(dc == 0), stop=(dc == DC - 1))
                        S.inc("p3qm", mm)

                    @blk.vector
                    def _(v, c=c, g=g, hg=hg, cur=cur, pv8=pv8, VS=VS):
                        S.wait(v, "p3qm")
                        S.wait(v, "p3vo")
                        for hh in range(8):
                            h = hg * 8 + hh
                            S.inc("p3ve", v.tensor_copy(
                                vsb2[:, cur * VS + (g * H + h) * 65: cur * VS + (g * H + h) * 65 + 64],
                                pv8[hh][:, 0:64]))

        with nc.psum_tensor("pnm_%d" % c, [128, W], F32) as pnm:
            for wi, mbase in ((0, MX_QN), (1, MX_KN + cur * 8)):
                for dc in range(DC):
                    @blk.scalar
                    def _(sc, wi=wi, dc=dc, cur=cur):
                        S.wait(sc, "p3qe")
                        S.wait(sc, "p3nt")
                        src = (qf[:, dc * W:(dc + 1) * W] if wi == 0
                               else kf2[:, cur * DC * W + dc * W: cur * DC * W + (dc + 1) * W])
                        S.inc("p3sq", sc.activation(q2st[:, :], src, AF.Square))

                    @blk.tensor
                    def _(t, wi=wi, dc=dc, pnm=pnm):
                        S.wait(t, "p3sq")
                        if wi == 0 and dc == 0:
                            S.wait(t, "p3ve")
                        S.wait(t, "p3nr")
                        S.inc("p3nt", t.matmul(pnm[0:2, 0:W], onepair[:, 0:2], q2st[:, :],
                                               start=True, stop=True))

                    @blk.vector
                    def _(v, wi=wi, dc=dc, mbase=mbase, pnm=pnm):
                        S.wait(v, "p3nt")
                        S.inc("p3nr", v.reduce_max(mxc[0:2, mbase + dc:mbase + dc + 1], pnm[0:2, 0:W],
                                                   axis=AX.X))

            @blk.vector
            def _(v, c=c, cur=cur):
                S.wait(v, "p3nr")
                v.reduce_max(nrm[0:2, 0:1], mxc[0:2, MX_QN:MX_QN + 8], axis=AX.X)
                if c > 0:
                    v.reduce_max(nrm[0:2, 1:2], mxc[0:2, MX_KN:MX_KN + 16], axis=AX.X)
                else:
                    v.reduce_max(nrm[0:2, 1:2], mxc[0:2, MX_KN + cur * 8:MX_KN + cur * 8 + 8], axis=AX.X)
                S.inc("p3na", v.drain())

            @blk.tensor
            def _(t, pnm=pnm):
                S.wait(t, "p3na")
                S.inc("p3nt", t.transpose(pnm[0:2, 128:130], nrm[0:2, 0:2], idf[0:2, 0:2]))

            @blk.vector
            def _(v, pnm=pnm):
                S.wait(v, "p3nt")
                v.reduce_max(nrm[0:2, 2:3], pnm[0:2, 128:130], axis=AX.X)
                S.inc("p3na", v.drain())

            @blk.tensor
            def _(t, pnm=pnm):
                S.wait(t, "p3na")
                S.inc("p3nt", t.transpose(pnm[0:1, 130:132], nrm[0:2, 2:3], idf[0:2, 0:2]))

            @blk.vector
            def _(v, pnm=pnm):
                S.wait(v, "p3nt")
                v.reduce_sum(nrm[0:1, 3:4], pnm[0:1, 130:132], axis=AX.X)
                v.drain()
                v.tensor_single_scalar(nrm[0:1, 3:4], nrm[0:1, 3:4], -0.5 * SCALE, OP.mult)
                S.inc("p3na", v.drain())

            @blk.tensor
            def _(t, pnm=pnm):
                S.wait(t, "p3na")
                S.inc("p3nt", t.matmul(pnm[:, 132:133], ones[0:1, 0:128], nrm[0:1, 3:4],
                                       start=True, stop=True))

            @blk.vector
            def _(v, pnm=pnm):
                S.wait(v, "p3nt")
                S.wait(v, "p3xs")
                S.inc("p3nf", v.tensor_copy(mxc[:, MX_M:MX_M + 1], pnm[:, 132:133]))

        with nc.psum_tensor("pSa_%d" % c, [128, 768], F32) as pSa, \
             nc.psum_tensor("pSb_%d" % c, [128, 768], F32) as pSb, \
             nc.psum_tensor("pAVa_%d" % c, [65, W], F32) as pAVa, \
             nc.psum_tensor("pAVb_%d" % c, [65, W], F32) as pAVb, \
             nc.psum_tensor("pCB_%d" % c, [128, 2 * W], F32) as pCB:

            for dc in range(DC):
                for qi2 in range(2):
                    g0 = 4 * c + 2 * qi2
                    kts = [G for G in range(g0 - 4, g0 + 2) if G >= 0]
                    nkt = len(kts)
                    j0 = 6 - nkt
                    groups = [(gi, kts[gi:gi + 3]) for gi in range(0, nkt, 3)]

                    for gi, gkts in groups:
                        last = (gi + len(gkts) == nkt)

                        @blk.tensor
                        def _(t, c=c, dc=dc, qi2=qi2, gi=gi, gkts=gkts, cur=cur, prv=prv,
                              pSa=pSa, pSb=pSb):
                            S.wait(t, "p3nf")
                            S.wait(t, "p3xs")
                            if dc == 0 and qi2 == 0 and gi == 0:
                                S.wait(t, "p3ve")
                            for aa, ps in ((0, pSa), (64, pSb)):
                                for j, G in enumerate(gkts):
                                    buf = cur if G >= 4 * c else prv
                                    kcol = buf * DC * W + dc * W + (G % 4) * 128
                                    mm = t.matmul(ps[:, j * 256:(j + 1) * 256],
                                                  kf2[aa:aa + 64, kcol:kcol + 128],
                                                  qf[aa:aa + 64, dc * W + qi2 * 256: dc * W + qi2 * 256 + 256],
                                                  start=True, stop=True)
                            S.inc("p3xm", mm)

                        @blk.scalar
                        def _(sc, gkts=gkts, pSa=pSa, pSb=pSb):
                            S.wait(sc, "p3xm")
                            S.wait(sc, "p3av")
                            n = len(gkts)
                            for aa, ps in ((0, pSa), (1, pSb)):
                                S.inc("p3xs", sc.activation(pTs[:, aa * 768: aa * 768 + n * 256],
                                                            ps[:, 0:n * 256], AF.Exp,
                                                            bias=mxc[:, MX_M:MX_M + 1], scale=SCALE))

                        @blk.vector
                        def _(v, gi=gi, gkts=gkts, j0=j0):
                            S.wait(v, "p3xs")
                            for aa in range(2):
                                for j in range(len(gkts)):
                                    jr = j0 + gi + j
                                    if jr in (0, 1, 4, 5):
                                        mslot = {0: 0, 1: 1, 4: 2, 5: 3}[jr]
                                        v.tensor_tensor(pTs[:, aa * 768 + j * 256: aa * 768 + (j + 1) * 256],
                                                        pTs[:, aa * 768 + j * 256: aa * 768 + (j + 1) * 256],
                                                        mq[:, mslot * 256:(mslot + 1) * 256],
                                                        OP.mult)
                            S.inc("p3xv", v.drain())

                        @blk.tensor
                        def _(t, c=c, dc=dc, qi2=qi2, gi=gi, gkts=gkts, cur=cur, prv=prv,
                              nkt=nkt, last=last, pAVa=pAVa, pAVb=pAVb, VS=VS):
                            S.wait(t, "p3xv")
                            if qi2 == 0 and gi == 0:
                                S.wait(t, "p3xe")
                            for aa, pav in ((0, pAVa), (1, pAVb)):
                                h = 2 * dc + aa
                                for j, G in enumerate(gkts):
                                    buf = cur if G >= 4 * c else prv
                                    vcol = buf * VS + ((G % 4) * H + h) * 65
                                    mm = t.matmul(pav[0:65, qi2 * 256:(qi2 + 1) * 256],
                                                  vsb2[:, vcol:vcol + 65],
                                                  pTs[:, aa * 768 + j * 256: aa * 768 + (j + 1) * 256],
                                                  start=(gi == 0 and j == 0),
                                                  stop=(last and j == len(gkts) - 1))
                            S.inc("p3av", mm)

                for aa, pav in ((0, pAVa), (1, pAVb)):
                    @blk.vector
                    def _(v, dc=dc, aa=aa, pav=pav):
                        S.wait(v, "p3av")
                        v.tensor_copy(flat[64:65, 1024 + aa * W:1024 + aa * W + W], pav[64:65, 0:W])
                        v.drain()
                        v.reciprocal(flat[64:65, 1024 + aa * W:1024 + aa * W + W],
                                     flat[64:65, 1024 + aa * W:1024 + aa * W + W])
                        S.inc("p3xv", v.drain())

                    @blk.tensor
                    def _(t, aa=aa, pCB=pCB):
                        S.wait(t, "p3xv")
                        S.inc("p3lb", t.matmul(pCB[:, aa * W:(aa + 1) * W], ones[64:65, 0:128],
                                               flat[64:65, 1024 + aa * W:1024 + aa * W + W],
                                               start=True, stop=True))

                    @blk.vector
                    def _(v, dc=dc, aa=aa, pav=pav, pCB=pCB):
                        S.wait(v, "p3lb")
                        h = 2 * dc + aa
                        po = (h % 2) * 64
                        v.tensor_copy(flat[:, aa * W:(aa + 1) * W], pCB[:, aa * W:(aa + 1) * W])
                        v.drain()
                        cb = flat[po:po + 64, aa * W:(aa + 1) * W]
                        S.inc("p3xe", v.tensor_tensor(avn[po:po + 64, (h // 2) * W: (h // 2 + 1) * W],
                                                      pav[0:64, 0:W], cb, OP.mult))

        with nc.psum_tensor("pwo_%d" % c, [128, 2 * W], F32) as pwo:
            for oc in range(DC):
                pslot = oc % 2
                ldsem = "ldA" if pslot == 0 else "ldB"

                @blk.sync
                def _(sy, c=c, oc=oc, pslot=pslot, ldsem=ldsem):
                    if S.count("p3wv") >= 2:
                        S.wait(sy, "p3wv", S.count("p3wv") - 1)
                    S.inc(ldsem, sy.dma_start(out=xnstg[:, pslot * W:(pslot + 1) * W],
                                              in_=xn_h.ap()[oc, :, c * W:(c + 1) * W]), 16)

                @blk.tensor
                def _(t, c=c, oc=oc, pslot=pslot, pwo=pwo):
                    S.wait(t, "p3xe")
                    if S.count("p3wv") >= 2:
                        S.wait(t, "p3wv", S.count("p3wv") - 1)
                    ps = pwo[:, pslot * W:(pslot + 1) * W]
                    for dc in range(DC):
                        mm = t.matmul(ps, woT[:, dc * D + oc * 128: dc * D + (oc + 1) * 128],
                                      avn[:, dc * W:(dc + 1) * W],
                                      start=(dc == 0), stop=(dc == DC - 1))
                    S.inc("p3qm", mm)

                @blk.vector
                def _(v, pslot=pslot, ldsem=ldsem, pwo=pwo):
                    S.wait(v, "p3qm")
                    S.wait(v, ldsem)
                    if S.count("p3wv") >= 2:
                        S.wait(v, "p3wA" if pslot == 0 else "p3wB")
                    S.inc("p3wv", v.tensor_tensor(x2stg[:, pslot * W:(pslot + 1) * W],
                                                  pwo[:, pslot * W:(pslot + 1) * W],
                                                  xnstg[:, pslot * W:(pslot + 1) * W], OP.add))

                @blk.sync
                def _(sy, c=c, oc=oc, pslot=pslot):
                    S.wait(sy, "p3wv")
                    S.inc("p3wA" if pslot == 0 else "p3wB",
                          sy.dma_start(out=x2_h.ap()[oc, :, c * W:(c + 1) * W],
                                       in_=x2stg[:, pslot * W:(pslot + 1) * W]), 16)

        for jf in range(c * OCF, (c + 1) * OCF):
            if c == 0 and jf == 0:
                for jl in range(4):
                    emit_f2_load(jl)
            emit_f2_load(jf + 4)
            emit_f2_tile(jf)

    p3.close()

    @blk.vector
    def _(v):
        S.wait(v, "f2v2")
        v.reduce_max(mxc[:, MX_TMP:MX_TMP + 1], mxc[:, MX_UP:MX_UP + OCF * NCH], axis=AX.X)
        S.inc("fmq", v.drain())

    finalize_scalar_max([MX_TMP], 128, 64, "fmq")
    collective(2, 64)
    derive_scales([(72, 32767.0, 6)], 80)

    barrier(("p3wA", "p3wB", "f2s0", "f2s1", "f2s2", "f2s3", "f2l0", "f2l1", "f2l2", "f2l3", "ldA", "ldB", "hlA", "hlB", "wTA"))

    p4 = contextlib.ExitStack()
    w1T = p4.enter_context(nc.sbuf_tensor("w1T", [128, DC * DFF], BF16))
    w2T = p4.enter_context(nc.sbuf_tensor("w2T", [128, 32 * D], BF16))
    x2t = p4.enter_context(nc.sbuf_tensor("x2t", [128, 2 * DC * 256], F32))
    h2t = p4.enter_context(nc.sbuf_tensor("h2t", [128, DC * 256], BF16))
    g1buf = p4.enter_context(nc.sbuf_tensor("g1buf", [128, 32 * 256], BF16))
    sqstg = p4.enter_context(nc.sbuf_tensor("sqstg", [128, 2 * 256], F32))
    pmm = p4.enter_context(nc.psum_tensor("pmm", [128, 2 * W], F32))
    pstat = p4.enter_context(nc.psum_tensor("pstat", [2, 2 * W], F32))
    pab = p4.enter_context(nc.psum_tensor("pab", [128, 2 * W], F32))

    @blk.sync
    def _(sy):
        S.wait(sy, "cast1")
        S.wait(sy, "cast2")
        for dc in range(DC):
            S.inc("wTA", sy.dma_start(out=w1T[:, dc * DFF:(dc + 1) * DFF],
                                      in_=w1_bf.ap()[:, dc * 128:(dc + 1) * 128], transpose=True), 16)
        for kc in range(32):
            S.inc("wTA", sy.dma_start(out=w2T[:, kc * D:(kc + 1) * D],
                                      in_=w2_bf.ap()[:, kc * 128:(kc + 1) * 128], transpose=True), 16)

    WH = W // 2
    for tt in range(2 * NCH):
        xob = (tt % 2) * DC * 256
        @blk.sync
        def _(sy, tt=tt, xob=xob):
            if tt == 0:
                S.wait(sy, "p3wA")
                S.wait(sy, "p3wB")
            if tt >= 2:
                S.wait(sy, "p4av", S.count("p4av") - DC)
            for dc in range(DC):
                S.inc("ldA", sy.dma_start(out=x2t[:, xob + dc * WH:xob + (dc + 1) * WH],
                                          in_=x2_h.ap()[dc, :, tt * WH:(tt + 1) * WH]), 16)

        for dc in range(DC):
            @blk.scalar
            def _(sc, tt=tt, dc=dc, xob=xob):
                S.wait(sc, "ldA")
                if S.count("p4st") >= 2:
                    S.wait(sc, "p4st", S.count("p4st") - 1)
                S.inc("p4sq", sc.activation(sqstg[:, (dc % 2) * WH:(dc % 2 + 1) * WH],
                                            x2t[:, xob + dc * WH:xob + (dc + 1) * WH], AF.Square))

            @blk.tensor
            def _(t, tt=tt, dc=dc, xob=xob):
                S.wait(t, "p4sq")
                if tt == 0 and dc == 0:
                    S.wait(t, "wTA")
                if dc == 0:
                    S.wait(t, "ldA")
                    if tt > 0:
                        S.wait(t, "p4va")
                t.matmul(pstat[0:1, 0:WH], ones[:, 0:1], x2t[:, xob + dc * WH:xob + (dc + 1) * WH],
                         start=(dc == 0), stop=(dc == DC - 1))
                S.inc("p4st", t.matmul(pstat[0:1, W:W + WH], ones[:, 0:1],
                                       sqstg[:, (dc % 2) * WH:(dc % 2 + 1) * WH],
                                       start=(dc == 0), stop=(dc == DC - 1)))

        FB = 128

        @blk.vector
        def _(v, tt=tt):
            S.wait(v, "p4st")
            v.tensor_single_scalar(flat[0:1, FB:FB + WH], pstat[0:1, 0:WH], 1.0 / D, OP.mult)
            v.tensor_single_scalar(flat[0:1, FB + WH:FB + 2 * WH], pstat[0:1, W:W + WH], 1.0 / D, OP.mult)
            v.drain()
            v.tensor_tensor(flat[0:1, FB + 2 * WH:FB + 3 * WH], flat[0:1, FB:FB + WH],
                            flat[0:1, FB:FB + WH], OP.mult)
            v.drain()
            v.tensor_tensor(flat[0:1, FB + 2 * WH:FB + 3 * WH], flat[0:1, FB + WH:FB + 2 * WH],
                            flat[0:1, FB + 2 * WH:FB + 3 * WH], OP.subtract)
            v.drain()
            v.tensor_single_scalar(flat[0:1, FB + 2 * WH:FB + 3 * WH], flat[0:1, FB + 2 * WH:FB + 3 * WH],
                            EPS, OP.add)
            S.inc("p4va", v.drain())

        @blk.scalar
        def _(sc):
            S.wait(sc, "p4va")
            S.inc("p4sr", sc.activation(flat[0:1, FB + 3 * WH:FB + 4 * WH],
                                        flat[0:1, FB + 2 * WH:FB + 3 * WH], AF.Sqrt))

        @blk.vector
        def _(v):
            S.wait(v, "p4sr")
            v.reciprocal(flat[0:1, FB + 2 * WH:FB + 3 * WH], flat[0:1, FB + 3 * WH:FB + 4 * WH])
            v.drain()
            v.tensor_tensor(flat[0:1, FB:FB + WH], flat[0:1, FB:FB + WH],
                            flat[0:1, FB + 2 * WH:FB + 3 * WH], OP.mult)
            v.drain()
            v.tensor_single_scalar(flat[0:1, FB:FB + WH], flat[0:1, FB:FB + WH], -1.0, OP.mult)
            S.inc("p4va", v.drain())

        @blk.tensor
        def _(t, tt=tt):
            S.wait(t, "p4va")
            if tt > 0:
                S.wait(t, "p4h2")
            t.matmul(pab[:, 0:WH], ones[0:1, 0:128], flat[0:1, FB + 2 * WH:FB + 3 * WH], start=True, stop=True)
            S.inc("p4m1", t.matmul(pab[:, W:W + WH], ones[0:1, 0:128], flat[0:1, FB:FB + WH],
                                   start=True, stop=True))

        @blk.vector
        def _(v, tt=tt):
            S.wait(v, "p4m1")
            if tt > 0:
                S.wait(v, "p4ge")
            for dc in range(DC):
                v.tensor_tensor(sqstg[:, 0:WH], x2t[:, xob + dc * WH:xob + (dc + 1) * WH], pab[:, 0:WH], OP.mult)
                v.drain()
                v.tensor_tensor(sqstg[:, 0:WH], sqstg[:, 0:WH], pab[:, W:W + WH], OP.add)
                v.drain()
                v.tensor_scalar(h2t[:, dc * WH:(dc + 1) * WH], sqstg[:, 0:WH],
                                gb[:, G2C + dc:G2C + dc + 1], gb[:, B2C + dc:B2C + dc + 1],
                                OP.mult, OP.add)
                v.drain()
            S.inc("p4h2", v.drain())

        for o1 in range(32):
            pslot = o1 % 2

            @blk.tensor
            def _(t, tt=tt, o1=o1, pslot=pslot):
                S.wait(t, "p4h2")
                if S.count("p4ge") >= 2:
                    S.wait(t, "p4ge", S.count("p4ge") - 1)
                for dc in range(DC):
                    mm = t.matmul(pmm[:, pslot * W:pslot * W + WH],
                                  w1T[:, dc * DFF + o1 * 128: dc * DFF + (o1 + 1) * 128],
                                  h2t[:, dc * WH:(dc + 1) * WH], start=(dc == 0), stop=(dc == DC - 1))
                S.inc("p4m1", mm)

            @blk.scalar
            def _(sc, tt=tt, o1=o1, pslot=pslot):
                S.wait(sc, "p4m1")
                if tt > 0 and o1 == 0:
                    S.wait(sc, "p4m1")
                S.inc("p4ge", sc.activation(g1buf[:, o1 * WH:(o1 + 1) * WH],
                                            pmm[:, pslot * W:pslot * W + WH],
                                            AF.Gelu, bias=b1c[:, o1:o1 + 1], scale=1.0))

        for o2 in range(DC):
            pslot = o2 % 2

            @blk.tensor
            def _(t, tt=tt, o2=o2, pslot=pslot):
                S.wait(t, "p4ge")
                if S.count("p4av") >= 2:
                    S.wait(t, "p4av", S.count("p4av") - 1)
                for kc in range(32):
                    mm = t.matmul(pmm[:, pslot * W:pslot * W + WH],
                                  w2T[:, kc * D + o2 * 128: kc * D + (o2 + 1) * 128],
                                  g1buf[:, kc * WH:(kc + 1) * WH], start=(kc == 0), stop=(kc == 31))
                S.inc("p4m1", mm)

            @blk.vector
            def _(v, o2=o2, pslot=pslot, xob=xob):
                S.wait(v, "p4m1")
                if S.count("p4av") >= 2:
                    S.wait(v, "p3wA" if pslot == 0 else "p3wB")
                S.inc("p4av", v.scalar_tensor_tensor(flat[:, 1152 + pslot * WH:1152 + (pslot + 1) * WH],
                                                     pmm[:, pslot * W:pslot * W + WH],
                                                     gb[:, MB2C + o2:MB2C + o2 + 1],
                                                     x2t[:, xob + o2 * WH:xob + (o2 + 1) * WH], OP.add, OP.add))

            @blk.sync
            def _(sy, tt=tt, o2=o2, pslot=pslot):
                S.wait(sy, "p4av")
                S.inc("p3wA" if pslot == 0 else "p3wB",
                      sy.dma_start(out=acc_h.ap()[o2, :, tt * WH:(tt + 1) * WH],
                                   in_=flat[:, 1152 + pslot * WH:1152 + (pslot + 1) * WH]), 16)

        for jf in range(tt * (OCF * NCH) // (2 * NCH), (tt + 1) * (OCF * NCH) // (2 * NCH)):
            if tt == 0 and jf == 0:
                for jl in range(4):
                    emit_f3_load(jl)
            emit_f3_load(jf + 4)
            emit_f3_tile(jf)

    p4.close()

    @blk.vector
    def _(v):
        S.wait(v, "f3v")
        v.reduce_max(mxc[:, MX_TMP:MX_TMP + 1], mxc[:, MX_GA:MX_GA + OCF * NCH], axis=AX.X)
        S.inc("fmq", v.drain())

    finalize_scalar_max([MX_TMP], 128, 96, "fmq")
    collective(3, 96)
    derive_scales([(104, 32767.0, 8)], 112)

    f2.close()
    barrier(("p3wA", "p3wB", "f3s0", "f3s1", "f3s2", "f3s3", "f3l0", "f3l1", "f3l2", "f3l3", "ldA", "wTA"))

    f4 = contextlib.ExitStack()
    wdT = f4.enter_context(nc.sbuf_tensor("wdT", [128, OCF * D], BF16))
    obuf = f4.enter_context(nc.sbuf_tensor("obuf", [128, 2 * OCF * W], BF16))
    old = f4.enter_context(nc.sbuf_tensor("old", [128, 2 * W], F32))
    oi32 = f4.enter_context(nc.sbuf_tensor("oi32", [128, W], I32))
    accld = f4.enter_context(nc.sbuf_tensor("accld", [128, 2 * W], F32))
    sumf = f4.enter_context(nc.sbuf_tensor("sumf", [128, DC * W], F32))
    xfin = f4.enter_context(nc.sbuf_tensor("xfin", [128, 2 * 1024], F32))
    outst = f4.enter_context(nc.sbuf_tensor("outst", [128, 2 * 1024], F32))
    pdn = f4.enter_context(nc.psum_tensor("pdn", [128, 2 * W], F32))
    pfin = f4.enter_context(nc.psum_tensor("pfin", [128, 2 * 1024], F32))

    @blk.sync
    def _(sy):
        S.wait(sy, "dqoA")
        S.wait(sy, "dqoB")
        for kc in range(OCF):
            S.inc("wTA", sy.dma_start(out=wdT[:, kc * D:(kc + 1) * D],
                                      in_=wd_dq.ap()[:, kc * 128:(kc + 1) * 128], transpose=True), 16)

    for tt in range(NCH):
        ob = (tt % 2) * OCF * W
        for kc in range(OCF):
            buf = kc % 2
            ldsem = "ldA" if buf == 0 else "ldB"

            @blk.sync
            def _(sy, tt=tt, kc=kc, buf=buf, ldsem=ldsem):
                if tt * OCF + kc >= 2:
                    S.wait(sy, "f4v", S.count("f4v") - 1)
                S.inc(ldsem, sy.dma_start(out=old[:, buf * W:(buf + 1) * W],
                                          in_=or_h.ap()[kc, :, tt * W:(tt + 1) * W]), 16)

            @blk.vector
            def _(v, tt=tt, kc=kc, buf=buf, ldsem=ldsem, ob=ob):
                S.wait(v, ldsem)
                if tt == 0 and kc == 0:
                    S.wait(v, "dsv")
                if tt >= 2 and kc == 0:
                    S.wait(v, "f4mm", S.count("f4mm") - 40)
                v.tensor_scalar(oi32[:, :], old[:, buf * W:(buf + 1) * W], scl[:, 9:10], KQH,
                                OP.mult, OP.add)
                v.drain()
                v.tensor_scalar(obuf[:, ob + kc * W:ob + (kc + 1) * W], oi32[:, :], KQ, scl[:, 8:9],
                                OP.subtract, OP.mult)
                S.inc("f4v", v.drain())

        for d2 in range(DC):
            pslot = d2 % 2
            lasem = "f4la" if pslot == 0 else "f4lb"

            @blk.sync
            def _(sy, tt=tt, d2=d2, pslot=pslot, lasem=lasem):
                if S.count("f4av") >= 2:
                    S.wait(sy, "f4av", S.count("f4av") - 1)
                S.inc(lasem, sy.dma_start(out=accld[:, pslot * W:(pslot + 1) * W],
                                          in_=acc_h.ap()[d2, :, tt * W:(tt + 1) * W]), 16)

            @blk.tensor
            def _(t, tt=tt, d2=d2, pslot=pslot, ob=ob):
                S.wait(t, "f4v")
                if tt == 0 and d2 == 0:
                    S.wait(t, "wTA")
                if S.count("f4av") >= 2:
                    S.wait(t, "f4av", S.count("f4av") - 1)
                for kc in range(OCF):
                    mm = t.matmul(pdn[:, pslot * W:(pslot + 1) * W],
                                  wdT[:, kc * D + d2 * 128: kc * D + (d2 + 1) * 128],
                                  obuf[:, ob + kc * W:ob + (kc + 1) * W], start=(kc == 0), stop=(kc == OCF - 1))
                S.inc("f4mm", mm)

            @blk.vector
            def _(v, d2=d2, pslot=pslot, lasem=lasem):
                S.wait(v, "f4mm")
                S.wait(v, lasem)
                S.wait(v, "f4mm")
                S.inc("f4av", v.tensor_tensor(sumf[:, d2 * W:(d2 + 1) * W],
                                              pdn[:, pslot * W:(pslot + 1) * W],
                                              accld[:, pslot * W:(pslot + 1) * W], OP.add))

        for g in range(4):
            gbuf = g % 2
            lxsem = "f4lx" if gbuf == 0 else "f4ly"
            stsem = "stA" if gbuf == 0 else "stB"

            @blk.sync
            def _(sy, tt=tt, g=g, gbuf=gbuf, lxsem=lxsem):
                if S.count("f4fv") >= 2:
                    S.wait(sy, "f4fv", S.count("f4fv") - 1)
                src = x_d[tt * W + g * 128: tt * W + (g + 1) * 128, :]
                S.inc(lxsem, sy.dma_start(out=xfin[:, gbuf * 1024:(gbuf + 1) * 1024], in_=src), 16)

            @blk.tensor
            def _(t, tt=tt, g=g, gbuf=gbuf):
                S.wait(t, "f4av")
                if S.count("f4fv") >= 2:
                    S.wait(t, "f4fv", S.count("f4fv") - 1)
                for d2 in range(DC):
                    S.inc("f4mm", t.transpose(pfin[:, gbuf * 1024 + d2 * 128: gbuf * 1024 + (d2 + 1) * 128],
                                              sumf[:, d2 * W + g * 128: d2 * W + (g + 1) * 128], idf[:]))

            @blk.vector
            def _(v, g=g, gbuf=gbuf, lxsem=lxsem, stsem=stsem):
                S.wait(v, "f4mm")
                S.wait(v, lxsem)
                if S.count("f4fv") >= 2:
                    S.wait(v, stsem)
                S.inc("f4fv", v.tensor_tensor(outst[:, gbuf * 1024:(gbuf + 1) * 1024],
                                              pfin[:, gbuf * 1024:(gbuf + 1) * 1024],
                                              xfin[:, gbuf * 1024:(gbuf + 1) * 1024], OP.add))

            @blk.sync
            def _(sy, tt=tt, g=g, gbuf=gbuf, stsem=stsem):
                S.wait(sy, "f4fv")
                S.inc(stsem, sy.dma_start(out=out_d[tt * W + g * 128: tt * W + (g + 1) * 128, :],
                                          in_=outst[:, gbuf * 1024:(gbuf + 1) * 1024]), 16)

    @blk.sync
    def _(sy):
        S.wait(sy, "stA")
        S.wait(sy, "stB")

    f4.close()
    return nc, st


_PROG_CACHE = {}


def _get_program():
    if "nc" not in _PROG_CACHE:
        nc, st = build_program()
        _PROG_CACHE["nc"] = nc
        _PROG_CACHE["st"] = st
    return _PROG_CACHE["nc"]


def make_in_maps(inputs):
    x_full = np.ascontiguousarray(np.asarray(inputs["L0_attn_x"], dtype=np.float32))
    rk = np.arange(128)
    mUh = (rk[:, None] >= rk[None, :] + 1).astype(ml_dtypes.bfloat16)
    mLh = (rk[:, None] <= rk[None, :]).astype(ml_dtypes.bfloat16)
    rq = np.arange(256)
    m0 = (rk[:, None] >= rq[None, :] + 1)
    m1 = (rk[:, None] >= rq[None, :] - 127)
    m4 = (rk[:, None] <= rq[None, :])
    m5 = (rk[:, None] <= rq[None, :] - 128)
    mqh = np.concatenate([m0, m1, m4, m5], axis=1).astype(ml_dtypes.bfloat16)
    idn = np.eye(128, dtype=np.float32)
    common = {
        "wqkv": np.ascontiguousarray(np.asarray(inputs["L0_attn_w_qkv"], dtype=np.float32)),
        "wo": np.ascontiguousarray(np.asarray(inputs["L0_attn_w_o"], dtype=np.float32)),
        "w1": np.ascontiguousarray(np.asarray(inputs["L0_attn_mlp_w1"], dtype=np.float32)),
        "b1": np.ascontiguousarray(np.asarray(inputs["L0_attn_mlp_b1"], dtype=np.float32)),
        "w2": np.ascontiguousarray(np.asarray(inputs["L0_attn_mlp_w2"], dtype=np.float32)),
        "b2": np.ascontiguousarray(np.asarray(inputs["L0_attn_mlp_b2"], dtype=np.float32)),
        "g1v": np.ascontiguousarray(np.asarray(inputs["L0_attn_ln1_g"], dtype=np.float32)),
        "b1v": np.ascontiguousarray(np.asarray(inputs["L0_attn_ln1_b"], dtype=np.float32)),
        "g2v": np.ascontiguousarray(np.asarray(inputs["L0_attn_ln2_g"], dtype=np.float32)),
        "b2v": np.ascontiguousarray(np.asarray(inputs["L0_attn_ln2_b"], dtype=np.float32)),
        "wg": np.ascontiguousarray(np.asarray(inputs["L0_ffn_w_gate"], dtype=np.float32)),
        "wu": np.ascontiguousarray(np.asarray(inputs["L0_ffn_w_up"], dtype=np.float32)),
        "wd": np.ascontiguousarray(np.asarray(inputs["L0_ffn_w_down"], dtype=np.float32)),
        "idf": idn,
        "mU": mUh,
        "mL": mLh,
        "mq": mqh,
    }
    return [dict(common, x=np.ascontiguousarray(x_full[c])) for c in range(NCORES)]


def kernel(**inputs):
    in_maps = make_in_maps(inputs)
    nc = _get_program()
    res = run_bass_kernel_spmd(nc, in_maps, core_ids=list(range(NCORES)))
    return np.stack([np.asarray(r["out"], dtype=np.float32) for r in res.results], axis=0)


# version 14
# speedup vs baseline: 1.1624x; 1.0887x over previous
import contextlib
import os
import numpy as np
import ml_dtypes

import concourse.bass as bass
import concourse.bass_utils as _bass_utils
import concourse.mybir as mybir
from concourse.bass_utils import run_bass_kernel_spmd

if not getattr(_bass_utils, "_dyn_act_table_patched", False):
    _orig_get_walrus_args = _bass_utils.get_walrus_args

    def _get_walrus_args_dyn(*a, **k):
        return _orig_get_walrus_args(*a, **k) + ["--enable-dynamic-act-table"]

    _bass_utils.get_walrus_args = _get_walrus_args_dyn
    _bass_utils._dyn_act_table_patched = True

F32 = mybir.dt.float32
BF16 = mybir.dt.bfloat16
I32 = mybir.dt.int32
AF = mybir.ActivationFunctionType
OP = mybir.AluOpType
AX = mybir.AxisListType

NCORES = 8
T = 4096
D = 1024
DC = D // 128
H = 16
HD = 64
W = 512
NCH = T // W
DFF = 4096
DI = 3072
OCF = DI // 128
_TRUNC_CAST = (os.environ.get("BASS_SIM_FP64") == "1"
               or os.environ.get("KERNEL_TRUNC_CAST") == "1")
KQ = 49152.0 if _TRUNC_CAST else 0.0
KQH = KQ + 0.5 if _TRUNC_CAST else 0.0
W8H = 8.5 if _TRUNC_CAST else 0.0
W8 = 8.0 if _TRUNC_CAST else 0.0
SGH = 0.5 if _TRUNC_CAST else 0.0
EPS = 1e-5
SCALE = 0.125

MX_XN = 0
MX_TMP = 40
MX_UP = 64
MX_GA = 256
MX_QN = 456
MX_KN = 464
MX_M = 480


class Sems:
    def __init__(self, nc, stack):
        self.nc, self.stack, self.h, self.n = nc, stack, {}, {}

    def get(self, name):
        if name not in self.h:
            self.h[name] = self.stack.enter_context(self.nc.semaphore(name))
            self.n[name] = 0
        return self.h[name]

    def inc(self, name, instr, amount=1):
        instr.then_inc(self.get(name), amount)
        self.n[name] += amount
        return instr

    def count(self, name):
        self.get(name)
        return self.n[name]

    def wait(self, eng, name, val=None):
        eng.wait_ge(self.get(name), self.n[name] if val is None else val)


def build_program():
    nc = bass.Bass("TRN2", target_bir_lowering=False)
    st = contextlib.ExitStack()
    S = Sems(nc, st)

    x_d = nc.dram_tensor("x", [T, D], F32, kind="ExternalInput").ap()
    wqkv_d = nc.dram_tensor("wqkv", [3 * D, D], F32, kind="ExternalInput").ap()
    wo_d = nc.dram_tensor("wo", [D, D], F32, kind="ExternalInput").ap()
    w1_d = nc.dram_tensor("w1", [DFF, D], F32, kind="ExternalInput").ap()
    b1_d = nc.dram_tensor("b1", [DFF], F32, kind="ExternalInput").ap()
    w2_d = nc.dram_tensor("w2", [D, DFF], F32, kind="ExternalInput").ap()
    b2_d = nc.dram_tensor("b2", [D], F32, kind="ExternalInput").ap()
    g1v_d = nc.dram_tensor("g1v", [D], F32, kind="ExternalInput").ap()
    b1v_d = nc.dram_tensor("b1v", [D], F32, kind="ExternalInput").ap()
    g2v_d = nc.dram_tensor("g2v", [D], F32, kind="ExternalInput").ap()
    b2v_d = nc.dram_tensor("b2v", [D], F32, kind="ExternalInput").ap()
    wg_d = nc.dram_tensor("wg", [DI, D], F32, kind="ExternalInput").ap()
    wu_d = nc.dram_tensor("wu", [DI, D], F32, kind="ExternalInput").ap()
    wd_d = nc.dram_tensor("wd", [D, DI], F32, kind="ExternalInput").ap()
    idf_d = nc.dram_tensor("idf", [128, 128], F32, kind="ExternalInput").ap()
    mU_d = nc.dram_tensor("mU", [128, 128], BF16, kind="ExternalInput").ap()
    mL_d = nc.dram_tensor("mL", [128, 128], BF16, kind="ExternalInput").ap()
    mq_d = nc.dram_tensor("mq", [128, 4 * 256], BF16, kind="ExternalInput").ap()
    out_d = nc.dram_tensor("out", [T, D], F32, kind="ExternalOutput").ap()

    wqkv_bf = nc.dram_tensor("wqkv_bf", [3 * D, D], BF16)
    wo_bf = nc.dram_tensor("wo_bf", [D, D], BF16)
    w1_bf = nc.dram_tensor("w1_bf", [DFF, D], BF16)
    w2_bf = nc.dram_tensor("w2_bf", [D, DFF], BF16)
    wg_dq = nc.dram_tensor("wg_dq", [DI, D], BF16)
    wu_dq = nc.dram_tensor("wu_dq", [DI, D], BF16)
    wd_dq = nc.dram_tensor("wd_dq", [D, DI], BF16)
    xn_h = nc.dram_tensor("xn_h", [DC, 128, T], F32)
    h_hbm = nc.dram_tensor("h_hbm", [DC, 128, T], BF16)
    up_h = nc.dram_tensor("up_h", [OCF, 128, T], F32)
    gr_h = nc.dram_tensor("gr_h", [OCF, 128, T], F32)
    t2_h = nc.dram_tensor("t2_h", [OCF, 128, T], F32)
    or_h = nc.dram_tensor("or_h", [OCF, 128, T], F32)
    x2_h = nc.dram_tensor("x2_h", [DC, 128, T], F32)
    acc_h = nc.dram_tensor("acc_h", [DC, 128, T], F32)
    cio = [(nc.dram_tensor("cin%d" % k, [1, 8], F32),
            nc.dram_tensor("cout%d" % k, [1, 8], F32)) for k in range(4)]

    idf = st.enter_context(nc.sbuf_tensor("idf_s", [128, 128], F32))
    mU = st.enter_context(nc.sbuf_tensor("mU_s", [128, 128], BF16))
    mL = st.enter_context(nc.sbuf_tensor("mL_s", [128, 128], BF16))
    mq = st.enter_context(nc.sbuf_tensor("mq_s", [128, 4 * 256], BF16))
    ones = st.enter_context(nc.sbuf_tensor("ones", [128, 128], F32))
    onepair = st.enter_context(nc.sbuf_tensor("onepair", [128, 2], F32))
    gb = st.enter_context(nc.sbuf_tensor("gb", [128, 48], F32))
    b1c = st.enter_context(nc.sbuf_tensor("b1c", [128, 32], F32))
    scl = st.enter_context(nc.sbuf_tensor("scl", [128, 16], F32))
    mxc = st.enter_context(nc.sbuf_tensor("mxc", [128, 512], F32))
    flat = st.enter_context(nc.sbuf_tensor("flat", [128, 2176], F32))
    nrm = st.enter_context(nc.sbuf_tensor("nrm", [128, 8], F32))

    G1C, B1C, G2C, B2C, MB2C = 0, 8, 16, 24, 32

    blk = st.enter_context(nc.Block())
    ENGS = ("sync", "gpsimd", "vector", "scalar", "tensor")

    def barrier(dma_sems=()):
        @blk.sync
        def _(sy):
            for sname in dma_sems:
                S.wait(sy, sname)
        for e in ENGS:
            @getattr(blk, e)
            def _(eng, e=e):
                S.inc("bar_" + e, eng.nop())
        for e in ENGS:
            @getattr(blk, e)
            def _(eng, e=e):
                for o in ENGS:
                    if o != e:
                        S.wait(eng, "bar_" + o)

    uniq = [0]

    def finalize_scalar_max(cols, prt, fb, wait_name):
        n = len(cols)
        c0 = cols[0]
        contig = all(cols[j] == c0 + j for j in range(n))
        assert contig
        uniq[0] += 1

        with nc.psum_tensor("pfz%d" % uniq[0], [128, 256], F32) as pfz:
            @blk.tensor
            def _(t):
                S.wait(t, wait_name)
                S.inc("fmt", t.transpose(pfz[0:n, 0:128], mxc[:, c0:c0 + n], idf[:]))

            @blk.vector
            def _(v):
                S.wait(v, "fmt")
                v.reduce_max(nrm[0:n, 7:8], pfz[0:n, 0:128], axis=AX.X)
                S.inc("fmq", v.drain())

            @blk.tensor
            def _(t):
                S.wait(t, "fmq")
                S.inc("fmt", t.transpose(pfz[0:1, 128:128 + n], nrm[0:n, 7:8], idf[0:n, 0:n]))

            @blk.vector
            def _(v):
                S.wait(v, "fmt")
                v.tensor_copy(flat[0:1, fb:fb + n], pfz[0:1, 128:128 + n])
                v.drain()
                if n < 8:
                    v.tensor_copy(flat[0:1, fb + n:fb + 8],
                                  flat[0:1, fb:fb + 1].unsqueeze(2).broadcast_to([1, 1, 8 - n]).squeeze(1))
                S.inc("fmv2", v.drain())

    def collective(k, fb):
        cin_k, cout_k = cio[k]

        @blk.gpsimd
        def _(g):
            S.wait(g, "fmv2")
            S.inc("ccw", g.dma_start(out=cin_k.ap()[0:1, 0:8], in_=flat[0:1, fb:fb + 8]), 16)
            S.wait(g, "ccw")
            S.inc("cc", g.collective_compute(
                "AllReduce", OP.max, replica_groups=[list(range(NCORES))],
                ins=[cin_k.ap().opt()], outs=[cout_k.ap().opt()]))
            S.wait(g, "cc")
            S.inc("ccr", g.dma_start(out=flat[0:1, fb + 8:fb + 16], in_=cout_k.ap()[0:1, 0:8]), 16)

    def derive_scales(pairs, fb):
        @blk.vector
        def _(v):
            S.wait(v, "ccr")
            for j, (c, q, _sc) in enumerate(pairs):
                v.tensor_scalar(flat[0:1, fb + 2 * j:fb + 2 * j + 1], flat[0:1, c:c + 1],
                                1.0 / q, 1e-12, OP.mult, OP.max)
            v.drain()
            for j in range(len(pairs)):
                v.reciprocal(flat[0:1, fb + 2 * j + 1:fb + 2 * j + 2],
                             flat[0:1, fb + 2 * j:fb + 2 * j + 1])
            S.inc("dsv", v.drain())

        uniq[0] += 1
        with nc.psum_tensor("pdsx%d" % uniq[0], [128, 16], F32) as pds:
            @blk.tensor
            def _(t):
                S.wait(t, "dsv")
                S.inc("dst", t.matmul(pds[:, 0:2 * len(pairs)], ones[0:1, 0:128],
                                      flat[0:1, fb:fb + 2 * len(pairs)], start=True, stop=True))

            @blk.vector
            def _(v):
                S.wait(v, "dst")
                for j, (_c, _q, c2) in enumerate(pairs):
                    v.tensor_copy(scl[:, c2:c2 + 2], pds[:, 2 * j:2 * j + 2])
                S.inc("dsv", v.drain())

    @blk.gpsimd
    def _(g):
        S.inc("cast1", g.dma_start(out=wqkv_bf.ap(), in_=wqkv_d), 16)
        S.inc("cast1", g.dma_start(out=wo_bf.ap(), in_=wo_d), 16)
        S.inc("cast2", g.dma_start(out=w1_bf.ap(), in_=w1_d), 16)
        S.inc("cast2", g.dma_start(out=w2_bf.ap(), in_=w2_d), 16)

    with nc.sbuf_tensor("vecst", [48, 128], F32) as vecst, \
         nc.sbuf_tensor("b1st", [32, 128], F32) as b1st, \
         nc.psum_tensor("pvec", [128, 48], F32) as pvec, \
         nc.psum_tensor("pb1", [128, 32], F32) as pb1:

        @blk.sync
        def _(sy):
            S.inc("cst", sy.dma_start(out=idf[:], in_=idf_d), 16)
            S.inc("cst", sy.dma_start(out=mU[:], in_=mU_d), 16)
            S.inc("cst", sy.dma_start(out=mL[:], in_=mL_d), 16)
            S.inc("cst", sy.dma_start(out=mq[:], in_=mq_d), 16)
            S.inc("cst", sy.dma_start(out=vecst[0:8, :], in_=g1v_d.rearrange("(a b) -> a b", b=128)), 16)
            S.inc("cst", sy.dma_start(out=vecst[8:16, :], in_=b1v_d.rearrange("(a b) -> a b", b=128)), 16)
            S.inc("cst", sy.dma_start(out=vecst[16:24, :], in_=g2v_d.rearrange("(a b) -> a b", b=128)), 16)
            S.inc("cst", sy.dma_start(out=vecst[24:32, :], in_=b2v_d.rearrange("(a b) -> a b", b=128)), 16)
            S.inc("cst", sy.dma_start(out=vecst[32:40, :], in_=b2_d.rearrange("(a b) -> a b", b=128)), 16)
            S.inc("cst", sy.dma_start(out=b1st[:, :], in_=b1_d.rearrange("(a b) -> a b", b=128)), 16)

        @blk.tensor
        def _(t):
            S.wait(t, "cst")
            t.transpose(pvec[:, 0:40], vecst[0:40, :], idf[0:40, 0:40])
            S.inc("fmt", t.transpose(pb1[:, 0:32], b1st[:, :], idf[0:32, 0:32]))

        @blk.vector
        def _(v):
            v.memset(ones[:], 1.0)
            v.memset(onepair[:, :], 0.0)
            v.drain()
            v.memset(onepair[0:64, 0:1], 1.0)
            v.memset(onepair[64:128, 1:2], 1.0)
            S.wait(v, "fmt")
            v.tensor_copy(gb[:, 0:40], pvec[:, 0:40])
            v.tensor_copy(b1c[:, 0:32], pb1[:, 0:32])
            S.inc("dsv", v.drain())

        barrier(("cst",))

    p1 = contextlib.ExitStack()
    hst = p1.enter_context(nc.sbuf_tensor("hst", [128, 2 * DC * W], BF16))
    xin = p1.enter_context(nc.sbuf_tensor("xin", [128, 2 * 4096], F32))
    p1st = p1.enter_context(nc.sbuf_tensor("p1st", [128, 64], F32))
    hpre = p1.enter_context(nc.sbuf_tensor("hpre", [128, 4096], F32))
    xnst = p1.enter_context(nc.sbuf_tensor("xnst", [128, 2 * DC * W], F32))
    ptr1 = p1.enter_context(nc.psum_tensor("ptr1", [128, 1024], F32))
    ptr2 = p1.enter_context(nc.psum_tensor("ptr2", [128, 1024], F32))

    for tt in range(NCH):
        buf = tt % 2
        xo = buf * 4096
        so = buf * DC * W
        ldsem = "ldA" if buf == 0 else "ldB"

        @blk.sync
        def _(sy, tt=tt, xo=xo, ldsem=ldsem):
            if tt >= 2:
                S.wait(sy, "p1v", S.count("p1v") - 1)
            src = x_d[tt * W:(tt + 1) * W, :].rearrange("(g p) d -> p g d", p=128)
            S.inc(ldsem, sy.dma_start(out=xin[:, xo:xo + 4096].rearrange("p (g d) -> p g d", d=1024), in_=src), 16)

        @blk.scalar
        def _(sc, tt=tt, xo=xo, ldsem=ldsem):
            S.wait(sc, ldsem)
            if tt >= 1:
                S.wait(sc, "p1t")
            for g in range(4):
                xg = xin[:, xo + g * 1024: xo + (g + 1) * 1024]
                sc.activation(hpre[:, g * 1024:(g + 1) * 1024], xg, AF.Square)
            S.inc("p1sa", sc.drain())

        @blk.vector
        def _(v, tt=tt, xo=xo, ldsem=ldsem):
            S.wait(v, ldsem)
            if tt == 0:
                S.wait(v, "dsv")
            for g in range(4):
                xg = xin[:, xo + g * 1024: xo + (g + 1) * 1024]
                v.reduce_sum(p1st[:, g:g + 1], xg, axis=AX.X)
            S.wait(v, "p1sa")
            for g in range(4):
                v.reduce_sum(p1st[:, 4 + g:5 + g], hpre[:, g * 1024:(g + 1) * 1024], axis=AX.X)
            v.drain()
            v.tensor_single_scalar(p1st[:, 8:12], p1st[:, 0:4], 1.0 / D, OP.mult)
            v.tensor_single_scalar(p1st[:, 12:16], p1st[:, 4:8], 1.0 / D, OP.mult)
            v.drain()
            v.tensor_tensor(p1st[:, 16:20], p1st[:, 8:12], p1st[:, 8:12], OP.mult)
            v.drain()
            v.tensor_tensor(p1st[:, 16:20], p1st[:, 12:16], p1st[:, 16:20], OP.subtract)
            v.drain()
            v.tensor_single_scalar(p1st[:, 16:20], p1st[:, 16:20], EPS, OP.add)
            S.inc("p1q", v.drain())

        @blk.scalar
        def _(sc):
            S.wait(sc, "p1q")
            S.inc("p1r", sc.activation(p1st[:, 20:24], p1st[:, 16:20], AF.Sqrt))

        @blk.vector
        def _(v, tt=tt, xo=xo):
            S.wait(v, "p1r")
            v.reciprocal(p1st[:, 24:28], p1st[:, 20:24])
            v.drain()
            for g in range(4):
                xg = xin[:, xo + g * 1024: xo + (g + 1) * 1024]
                v.tensor_scalar(xg, xg, p1st[:, 8 + g:9 + g], p1st[:, 24 + g:25 + g], OP.subtract, OP.mult)
            v.drain()
            for g in range(4):
                xg = xin[:, xo + g * 1024: xo + (g + 1) * 1024]
                v.reduce_max(mxc[:, MX_XN + tt * 4 + g:MX_XN + tt * 4 + g + 1], xg,
                             axis=AX.X, apply_absolute_value=True)
                v.reduce_sum(p1st[:, 32 + g:33 + g], xg, axis=AX.X)
            S.inc("p1b", v.drain())
            S.wait(v, "p1sa", S.count("p1sa") + 1)
            for g in range(4):
                v.reduce_sum(p1st[:, 36 + g:37 + g], hpre[:, g * 1024:(g + 1) * 1024], axis=AX.X)
            v.drain()
            v.tensor_single_scalar(p1st[:, 40:44], p1st[:, 32:36], 1.0 / D, OP.mult)
            v.tensor_single_scalar(p1st[:, 44:48], p1st[:, 36:40], 1.0 / D, OP.mult)
            v.drain()
            v.tensor_tensor(p1st[:, 48:52], p1st[:, 40:44], p1st[:, 40:44], OP.mult)
            v.drain()
            v.tensor_tensor(p1st[:, 48:52], p1st[:, 44:48], p1st[:, 48:52], OP.subtract)
            v.drain()
            v.tensor_single_scalar(p1st[:, 48:52], p1st[:, 48:52], EPS, OP.add)
            S.inc("p1q", v.drain())

        @blk.scalar
        def _(sc, xo=xo):
            S.wait(sc, "p1b")
            for g in range(4):
                xg = xin[:, xo + g * 1024: xo + (g + 1) * 1024]
                sc.activation(hpre[:, g * 1024:(g + 1) * 1024], xg, AF.Square)
            S.inc("p1sa", sc.drain())

        @blk.scalar
        def _(sc):
            S.wait(sc, "p1q")
            S.inc("p1r", sc.activation(p1st[:, 52:56], p1st[:, 48:52], AF.Sqrt))

        @blk.vector
        def _(v, tt=tt, xo=xo):
            S.wait(v, "p1r")
            v.reciprocal(p1st[:, 56:60], p1st[:, 52:56])
            v.drain()
            for g in range(4):
                xg = xin[:, xo + g * 1024: xo + (g + 1) * 1024]
                v.tensor_scalar(hpre[:, g * 1024:(g + 1) * 1024], xg,
                                p1st[:, 40 + g:41 + g], p1st[:, 56 + g:57 + g], OP.subtract, OP.mult)
            S.inc("p1b", v.drain())

        p1e_base = S.count("p1e")

        @blk.tensor
        def _(t, tt=tt, xo=xo, p1e_base=p1e_base):
            S.wait(t, "p1b")
            for i in range(32):
                g, dc = i // DC, i % DC
                pslot = i % 2
                if i >= 2:
                    t.wait_ge(S.get("p1e"), p1e_base + i - 1)
                t.transpose(ptr1[:, pslot * 512:pslot * 512 + 128],
                            xin[:, xo + g * 1024 + dc * 128: xo + g * 1024 + (dc + 1) * 128], idf[:])
                S.inc("p1t", t.transpose(ptr2[:, pslot * 512:pslot * 512 + 128],
                                         hpre[:, g * 1024 + dc * 128: g * 1024 + (dc + 1) * 128], idf[:]))

        p1t_base = S.count("p1t") - 32

        so2 = buf * DC * W

        @blk.vector
        def _(v, tt=tt, so=so, so2=so2, p1t_base=p1t_base):
            if tt >= 2:
                S.wait(v, "p1oA" if tt % 2 == 0 else "p1oB")
            for i in range(32):
                g, dc = i // DC, i % DC
                pslot = i % 2
                v.wait_ge(S.get("p1t"), p1t_base + i + 1)
                v.tensor_copy(xnst[:, so + dc * W + g * 128: so + dc * W + (g + 1) * 128],
                              ptr1[:, pslot * 512:pslot * 512 + 128])
                S.inc("p1e",
                      v.tensor_scalar(hst[:, so2 + dc * W + g * 128: so2 + dc * W + (g + 1) * 128],
                                      ptr2[:, pslot * 512:pslot * 512 + 128],
                                      gb[:, G1C + dc:G1C + dc + 1], gb[:, B1C + dc:B1C + dc + 1],
                                      OP.mult, OP.add))
            S.inc("p1v", v.drain())

        @blk.sync
        def _(sy, tt=tt, so=so, so2=so2):
            S.wait(sy, "p1v")
            dst = xn_h.ap()[:, :, tt * W:(tt + 1) * W].transpose([1, 0, 2])
            S.inc("p1oA" if tt % 2 == 0 else "p1oB",
                  sy.dma_start(out=dst,
                               in_=xnst[:, so:so + DC * W].rearrange("p (c w) -> p c w", w=W)), 16)
            dsth = h_hbm.ap()[:, :, tt * W:(tt + 1) * W].transpose([1, 0, 2])
            S.inc("p1oA" if tt % 2 == 0 else "p1oB",
                  sy.dma_start(out=dsth,
                               in_=hst[:, so2:so2 + DC * W].rearrange("p (c w) -> p c w", w=W)), 16)

    @blk.vector
    def _(v):
        S.wait(v, "p1e")
        v.reduce_max(mxc[:, MX_TMP:MX_TMP + 1], mxc[:, MX_XN:MX_XN + 4 * NCH], axis=AX.X)
        S.inc("fmq", v.drain())

    finalize_scalar_max([MX_TMP], 128, 0, "fmq")
    collective(0, 0)
    derive_scales([(8, 32767.0, 0)], 16)

    wst = p1.enter_context(nc.sbuf_tensor("wst", [128, 2 * DI], F32))
    wblk = p1.enter_context(nc.sbuf_tensor("wblk", [128, 3 * (DI // 32)], F32))
    wi32 = p1.enter_context(nc.sbuf_tensor("wi32", [128, DI], I32))
    wbf16 = p1.enter_context(nc.sbuf_tensor("wbf16", [128, 2 * DI], BF16))

    dqi = 0
    for mi, (src, dst, nrow, I) in enumerate(
            [(wg_d, wg_dq, DI // 128, 1024), (wu_d, wu_dq, DI // 128, 1024), (wd_d, wd_dq, D // 128, DI)]):
        nb = I // 32
        for r in range(nrow):
            buf = dqi % 2
            ldsem = "ldA" if buf == 0 else "ldB"
            stsem = "dqoA" if buf == 0 else "dqoB"

            @blk.sync
            def _(sy, src=src, r=r, buf=buf, I=I, dqi=dqi, ldsem=ldsem):
                if dqi >= 2:
                    S.wait(sy, "dqv", S.count("dqv") - 1)
                S.inc(ldsem, sy.dma_start(out=wst[:, buf * DI: buf * DI + I],
                                          in_=src[r * 128:(r + 1) * 128, :]), 16)

            @blk.vector
            def _(v, buf=buf, I=I, nb=nb, dqi=dqi, ldsem=ldsem, stsem=stsem):
                S.wait(v, ldsem)
                if dqi >= 2:
                    S.wait(v, stsem)
                w = wst[:, buf * DI: buf * DI + I]
                w3 = w.rearrange("p (b k) -> p b k", k=32)
                bm = wblk[:, 0:nb]
                sblk = wblk[:, nb:2 * nb]
                invs = wblk[:, 2 * nb:3 * nb]
                v.reduce_max(bm, w3, axis=AX.X, apply_absolute_value=True)
                v.drain()
                v.tensor_scalar(sblk, bm, 1.0 / 7.0, 1e-12, OP.mult, OP.max)
                v.drain()
                v.reciprocal(invs, sblk)
                v.drain()
                v.tensor_tensor(w3, w3, invs.unsqueeze(2).broadcast_to([128, nb, 32]), OP.mult)
                v.drain()
                v.tensor_single_scalar(wi32[:, 0:I], w, W8H, OP.add)
                v.drain()
                v.scalar_tensor_tensor(wbf16[:, buf * DI: buf * DI + I].rearrange("p (b k) -> p b k", k=32),
                                       wi32[:, 0:I].rearrange("p (b k) -> p b k", k=32), W8,
                                       sblk.unsqueeze(2).broadcast_to([128, nb, 32]),
                                       OP.subtract, OP.mult)
                S.inc("dqv", v.drain())

            @blk.sync
            def _(sy, dst=dst, r=r, buf=buf, I=I, stsem=stsem):
                S.wait(sy, "dqv")
                S.inc(stsem, sy.dma_start(out=dst.ap()[r * 128:(r + 1) * 128, :],
                                          in_=wbf16[:, buf * DI: buf * DI + I]), 16)
            dqi += 1

    p1.close()
    barrier(("p1oA", "p1oB", "dqoA", "dqoB", "ldA", "ldB", "cst"))

    p2 = contextlib.ExitStack()
    xq_sb = p2.enter_context(nc.sbuf_tensor("xq_sb", [128, DC * T], BF16))
    p2ld = p2.enter_context(nc.sbuf_tensor("p2ld", [128, 2 * W], F32))
    p2i = p2.enter_context(nc.sbuf_tensor("p2i", [128, W], I32))
    wtT = p2.enter_context(nc.sbuf_tensor("wtT", [128, DC * DI], BF16))
    p2ev = p2.enter_context(nc.sbuf_tensor("p2ev", [128, 4 * W], F32))
    p2ps = p2.enter_context(nc.psum_tensor("p2ps", [128, 4 * W], F32))

    for i in range(DC * NCH):
        dc, tt = i // NCH, i % NCH
        buf = i % 2
        ldsem = "ldA" if buf == 0 else "ldB"

        @blk.sync
        def _(sy, dc=dc, tt=tt, buf=buf, i=i, ldsem=ldsem):
            if i == 0:
                S.wait(sy, "p1oA")
                S.wait(sy, "p1oB")
            if i >= 2:
                S.wait(sy, "xqv", S.count("xqv") - 1)
            S.inc(ldsem, sy.dma_start(out=p2ld[:, buf * W:(buf + 1) * W],
                                      in_=xn_h.ap()[dc, :, tt * W:(tt + 1) * W]), 16)

        @blk.vector
        def _(v, dc=dc, tt=tt, buf=buf, i=i, ldsem=ldsem):
            S.wait(v, ldsem)
            if i == 0:
                S.wait(v, "dsv")
            xg = p2ld[:, buf * W:(buf + 1) * W]
            v.tensor_scalar(p2i[:, :], xg, scl[:, 1:2], KQH, OP.mult, OP.add)
            v.drain()
            v.tensor_scalar(xq_sb[:, dc * T + tt * W: dc * T + (tt + 1) * W], p2i[:, :],
                            KQ, scl[:, 0:1], OP.subtract, OP.mult)
            S.inc("xqv", v.drain())

    for mi, (wdq, dsth, mbase) in enumerate([(wu_dq, up_h, MX_UP), (wg_dq, gr_h, MX_GA)]):
        @blk.sync
        def _(sy, wdq=wdq, mi=mi):
            if mi == 0:
                S.wait(sy, "dqoA")
                S.wait(sy, "dqoB")
            else:
                S.wait(sy, "p2m")
            for dc in range(DC):
                S.inc("wTA", sy.dma_start(out=wtT[:, dc * DI:(dc + 1) * DI],
                                          in_=wdq.ap()[:, dc * 128:(dc + 1) * 128], transpose=True), 16)

        NT = OCF * NCH
        m0 = S.count("p2m")
        s0_ = S.count("p2s")
        e0 = S.count("p2e")
        o0 = [S.count("p2o%d" % s4) for s4 in range(4)]
        for j in range(NT):
            oc, tt = j // NCH, j % NCH
            pslot = j % 4

            @blk.tensor
            def _(t, mi=mi, oc=oc, tt=tt, pslot=pslot, j=j, s0_=s0_):
                if j == 0:
                    S.wait(t, "wTA")
                    if mi == 0:
                        S.wait(t, "xqv")
                if mi > 0 or j >= 4:
                    t.wait_ge(S.get("p2s"), s0_ + j - 3)
                ps = p2ps[:, pslot * W:(pslot + 1) * W]
                for dc in range(DC):
                    mm = t.matmul(ps, wtT[:, dc * DI + oc * 128: dc * DI + (oc + 1) * 128],
                                  xq_sb[:, dc * T + tt * W: dc * T + (tt + 1) * W],
                                  start=(dc == 0), stop=(dc == DC - 1))
                S.inc("p2m", mm)

            if j % 8 == 7 or j == NT - 1:
                jlo = (j // 8) * 8

                @blk.scalar
                def _(sc, mi=mi, jlo=jlo, jhi=j, m0=m0, e0=e0, o0=o0):
                    for jj in range(jlo, jhi + 1):
                        ps4 = jj % 4
                        sc.wait_ge(S.get("p2m"), m0 + jj + 1)
                        if mi > 0 or jj >= 4:
                            sc.wait_ge(S.get("p2e"), e0 + jj - 3)
                            sc.wait_ge(S.get("p2o%d" % ps4), o0[ps4] + 16 * ((jj - 4) // 4 + 1))
                        S.inc("p2s", sc.copy(p2ev[:, ps4 * W:(ps4 + 1) * W],
                                             p2ps[:, ps4 * W:(ps4 + 1) * W]))

            @blk.vector
            def _(v, pslot=pslot, mbase=mbase, j=j, s0_=s0_):
                v.wait_ge(S.get("p2s"), s0_ + j + 1)
                S.inc("p2e", v.reduce_max(mxc[:, mbase + j:mbase + j + 1],
                                          p2ev[:, pslot * W:(pslot + 1) * W],
                                          axis=AX.X, apply_absolute_value=True))

            @blk.sync
            def _(sy, oc=oc, tt=tt, pslot=pslot, dsth=dsth, j=j, e0=e0):
                sy.wait_ge(S.get("p2e"), e0 + j + 1)
                S.inc("p2o%d" % pslot,
                      sy.dma_start(out=dsth.ap()[oc, :, tt * W:(tt + 1) * W],
                                   in_=p2ev[:, pslot * W:(pslot + 1) * W]), 16)

    @blk.vector
    def _(v):
        S.wait(v, "p2e")
        v.reduce_max(mxc[:, MX_TMP:MX_TMP + 1], mxc[:, MX_UP:MX_UP + OCF * NCH], axis=AX.X)
        v.reduce_max(mxc[:, MX_TMP + 1:MX_TMP + 2], mxc[:, MX_GA:MX_GA + OCF * NCH], axis=AX.X)
        S.inc("fmq", v.drain())

    finalize_scalar_max([MX_TMP, MX_TMP + 1], 128, 32, "fmq")
    collective(1, 32)
    derive_scales([(40, 32767.0, 2), (41, 32767.0, 4)], 48)

    p2.close()
    barrier(("p2o0", "p2o1", "p2o2", "p2o3", "ldA", "ldB", "wTA"))

    f2 = contextlib.ExitStack()
    fld = f2.enter_context(nc.sbuf_tensor("fld", [128, 4 * W], F32))
    fi32 = f2.enter_context(nc.sbuf_tensor("fi32", [128, W], I32))
    fgate = f2.enter_context(nc.sbuf_tensor("fgate", [128, 4 * W], F32))
    ft2 = f2.enter_context(nc.sbuf_tensor("ft2", [128, 4 * W], F32))

    def emit_f2_load(j):
        if j >= OCF * NCH:
            return
        oc, tt = j // NCH, j % NCH
        buf = j % 4

        @blk.sync
        def _(sy):
            if j == 0:
                for s4 in range(4):
                    S.wait(sy, "p2o%d" % s4)
            if j >= 4:
                S.wait(sy, "f2v2", j - 3)
            S.inc("f2l%d" % buf, sy.dma_start(out=fld[:, buf * W:(buf + 1) * W],
                                              in_=gr_h.ap()[oc, :, tt * W:(tt + 1) * W]), 16)

    def emit_f2_tile(j):
        oc, tt = j // NCH, j % NCH
        buf = j % 4

        @blk.vector
        def _(v):
            S.wait(v, "f2l%d" % buf, 16 * (j // 4 + 1))
            if j == 0:
                S.wait(v, "dsv")
            if j >= 4:
                S.wait(v, "f2sg", j - 3)
                S.wait(v, "f2s%d" % buf)
            v.tensor_scalar(fi32[:, :], fld[:, buf * W:(buf + 1) * W], scl[:, 5:6], KQH,
                            OP.mult, OP.add)
            v.drain()
            v.tensor_scalar(fgate[:, buf * W:(buf + 1) * W], fi32[:, :], KQ, scl[:, 4:5],
                            OP.subtract, OP.mult)
            S.inc("f2v1", v.drain())

        if buf == 3 or j == OCF * NCH - 1:
            @blk.scalar
            def _(sc):
                for jj in range(j - buf, j + 1):
                    bb = jj % 4
                    sc.wait_ge(S.get("f2v1"), jj + 1)
                    S.inc("f2sg", sc.activation(fld[:, bb * W:(bb + 1) * W],
                                                fgate[:, bb * W:(bb + 1) * W], AF.Tanh, scale=0.5))

        @blk.vector
        def _(v):
            S.wait(v, "f2sg", j + 1)
            v.tensor_scalar(fi32[:, :], fld[:, buf * W:(buf + 1) * W], 32768.0, 32768.0 + SGH,
                            OP.mult, OP.add)
            v.drain()
            v.tensor_scalar(fld[:, buf * W:(buf + 1) * W], fi32[:, :], 65535.0, 1.0 / 65536.0,
                            OP.min, OP.mult)
            v.drain()
            v.tensor_tensor(ft2[:, buf * W:(buf + 1) * W], fgate[:, buf * W:(buf + 1) * W],
                            fld[:, buf * W:(buf + 1) * W], OP.mult)
            v.drain()
            S.inc("f2v2", v.reduce_max(mxc[:, MX_UP + j:MX_UP + j + 1], ft2[:, buf * W:(buf + 1) * W],
                                       axis=AX.X, apply_absolute_value=True))

        @blk.sync
        def _(sy):
            S.wait(sy, "f2v2")
            S.inc("f2s%d" % buf, sy.dma_start(out=t2_h.ap()[oc, :, tt * W:(tt + 1) * W],
                                              in_=ft2[:, buf * W:(buf + 1) * W]), 16)

    def emit_f3_load(j):
        if j >= OCF * NCH:
            return
        oc, tt = j // NCH, j % NCH
        buf = j % 4

        @blk.sync
        def _(sy):
            if j == 0:
                for s4 in range(4):
                    S.wait(sy, "f2s%d" % s4)
            if j >= 4:
                S.wait(sy, "f3v", j - 3)
            S.inc("f3l%d" % buf, sy.dma_start(out=fld[:, buf * W:(buf + 1) * W],
                                              in_=t2_h.ap()[oc, :, tt * W:(tt + 1) * W]), 16)
            S.inc("f3l%d" % buf, sy.dma_start(out=fgate[:, buf * W:(buf + 1) * W],
                                              in_=up_h.ap()[oc, :, tt * W:(tt + 1) * W]), 16)

    def emit_f3_tile(j):
        oc, tt = j // NCH, j % NCH
        buf = j % 4

        @blk.vector
        def _(v):
            S.wait(v, "f3l%d" % buf, 32 * (j // 4 + 1))
            if j == 0:
                S.wait(v, "dsv")
            if j >= 4:
                S.wait(v, "f3s%d" % buf)
            v.tensor_scalar(fi32[:, :], fld[:, buf * W:(buf + 1) * W], scl[:, 7:8], KQH,
                            OP.mult, OP.add)
            v.drain()
            v.tensor_scalar(fld[:, buf * W:(buf + 1) * W], fi32[:, :], KQ, scl[:, 6:7],
                            OP.subtract, OP.mult)
            v.drain()
            v.tensor_scalar(fi32[:, :], fgate[:, buf * W:(buf + 1) * W], scl[:, 3:4], KQH,
                            OP.mult, OP.add)
            v.drain()
            v.tensor_scalar(fgate[:, buf * W:(buf + 1) * W], fi32[:, :], KQ, scl[:, 2:3],
                            OP.subtract, OP.mult)
            v.drain()
            v.tensor_tensor(ft2[:, buf * W:(buf + 1) * W], fld[:, buf * W:(buf + 1) * W],
                            fgate[:, buf * W:(buf + 1) * W], OP.mult)
            v.drain()
            S.inc("f3v", v.reduce_max(mxc[:, MX_GA + j:MX_GA + j + 1], ft2[:, buf * W:(buf + 1) * W],
                                      axis=AX.X, apply_absolute_value=True))

        @blk.sync
        def _(sy):
            S.wait(sy, "f3v")
            S.inc("f3s%d" % buf, sy.dma_start(out=or_h.ap()[oc, :, tt * W:(tt + 1) * W],
                                              in_=ft2[:, buf * W:(buf + 1) * W]), 16)

    p3 = contextlib.ExitStack()
    hstg = p3.enter_context(nc.sbuf_tensor("hstg", [128, 2 * DC * W], BF16))
    wqkvT = p3.enter_context(nc.sbuf_tensor("wqkvT", [128, DC * 3 * D], BF16))
    woT = p3.enter_context(nc.sbuf_tensor("woT", [128, DC * D], BF16))
    qf = p3.enter_context(nc.sbuf_tensor("qf", [128, DC * W], BF16))
    kf2 = p3.enter_context(nc.sbuf_tensor("kf2", [128, 2 * DC * W], BF16))
    vsb2 = p3.enter_context(nc.sbuf_tensor("vsb2", [128, 2 * 4 * H * 65], BF16))
    avn = p3.enter_context(nc.sbuf_tensor("avn", [128, DC * W], BF16))
    xnstg = p3.enter_context(nc.sbuf_tensor("xnstg", [128, 2 * W], F32))
    x2stg = p3.enter_context(nc.sbuf_tensor("x2stg", [128, 2 * W], F32))
    pTs = p3.enter_context(nc.sbuf_tensor("pTs", [128, 2 * 768], BF16))
    q2st = p3.enter_context(nc.sbuf_tensor("q2st", [128, 2 * W], F32))

    @blk.sync
    def _(sy):
        S.wait(sy, "cast1")
        for dc in range(DC):
            S.inc("wTA", sy.dma_start(out=wqkvT[:, dc * 3 * D:(dc + 1) * 3 * D],
                                      in_=wqkv_bf.ap()[:, dc * 128:(dc + 1) * 128], transpose=True), 16)
        for dc in range(DC):
            S.inc("wTA", sy.dma_start(out=woT[:, dc * D:(dc + 1) * D],
                                      in_=wo_bf.ap()[:, dc * 128:(dc + 1) * 128], transpose=True), 16)

    for c in range(NCH):
        cur, prv = c % 2, (c + 1) % 2
        VS = 4 * H * 65
        p3qm_base = S.count("p3qm")

        @blk.sync
        def _(sy, c=c, cur=cur, p3qm_base=p3qm_base):
            if c == 0:
                S.wait(sy, "p1oA")
                S.wait(sy, "p1oB")
            if c >= 2:
                S.wait(sy, "p3qm", p3qm_base - 24)
            S.inc("hlA" if cur == 0 else "hlB",
                  sy.dma_start(out=hstg[:, cur * DC * W: (cur + 1) * DC * W]
                               .rearrange("p (k w) -> p k w", w=W),
                               in_=h_hbm.ap()[:, :, c * W:(c + 1) * W].transpose([1, 0, 2])), 16)

        with nc.psum_tensor("pqk_%d" % c, [128, 2 * W], F32) as pqk:
            for wi, rowbase in ((0, 0), (1, D)):
                for oc in range(DC):
                    pslot = (wi * DC + oc) % 2

                    @blk.tensor
                    def _(t, c=c, cur=cur, wi=wi, rowbase=rowbase, oc=oc, pslot=pslot, pqk=pqk):
                        if c == 0 and wi == 0 and oc == 0:
                            S.wait(t, "wTA")
                        if wi == 0 and oc == 0:
                            S.wait(t, "hlA" if cur == 0 else "hlB")
                        if c > 0 and wi == 0 and oc == 0:
                            S.wait(t, "p3wv")
                        if S.count("p3qe") >= 2:
                            S.wait(t, "p3qe", S.count("p3qe") - 1)
                        ps = pqk[:, pslot * W:(pslot + 1) * W]
                        for dc in range(DC):
                            mm = t.matmul(ps, wqkvT[:, dc * 3 * D + rowbase + oc * 128:
                                                    dc * 3 * D + rowbase + (oc + 1) * 128],
                                          hstg[:, cur * DC * W + dc * W: cur * DC * W + (dc + 1) * W],
                                          start=(dc == 0), stop=(dc == DC - 1))
                        S.inc("p3qm", mm)

                    @blk.vector
                    def _(v, c=c, wi=wi, oc=oc, pslot=pslot, cur=cur, pqk=pqk):
                        S.wait(v, "p3qm")
                        if c > 0 and wi == 0 and oc == 0:
                            S.wait(v, "p3xm")
                        if wi == 0:
                            o = v.tensor_copy(qf[:, oc * W:(oc + 1) * W], pqk[:, pslot * W:(pslot + 1) * W])
                        else:
                            o = v.tensor_copy(kf2[:, cur * DC * W + oc * W: cur * DC * W + (oc + 1) * W],
                                              pqk[:, pslot * W:(pslot + 1) * W])
                        S.inc("p3qe", o)

        @blk.vector
        def _(v, c=c, cur=cur, VS=VS):
            S.wait(v, "p3av")
            v.memset(vsb2[:, cur * VS:(cur + 1) * VS].rearrange("p (s o) -> p s o", o=65)[:, :, 64:65], 1.0)
            S.inc("p3vo", v.drain())

        with contextlib.ExitStack() as vsc:
            pv8 = [vsc.enter_context(nc.psum_tensor("pv8_%d_%d" % (c, b), [128, 64], F32)) for b in range(8)]
            for g in range(4):
                for hg in range(2):
                    @blk.tensor
                    def _(t, c=c, g=g, hg=hg, pv8=pv8):
                        S.wait(t, "p3qe")
                        if S.count("p3ve") > 0:
                            S.wait(t, "p3ve")
                        for dc in range(DC):
                            hb = hstg[:, cur * DC * W + dc * W + g * 128: cur * DC * W + dc * W + (g + 1) * 128]
                            for hh in range(8):
                                h = hg * 8 + hh
                                mm = t.matmul(pv8[hh][:, 0:64], hb,
                                              wqkvT[:, dc * 3 * D + 2 * D + h * 64:
                                                    dc * 3 * D + 2 * D + (h + 1) * 64],
                                              start=(dc == 0), stop=(dc == DC - 1))
                        S.inc("p3qm", mm)

                    @blk.vector
                    def _(v, c=c, g=g, hg=hg, cur=cur, pv8=pv8, VS=VS):
                        S.wait(v, "p3qm")
                        S.wait(v, "p3vo")
                        for hh in range(8):
                            h = hg * 8 + hh
                            S.inc("p3ve", v.tensor_copy(
                                vsb2[:, cur * VS + (g * H + h) * 65: cur * VS + (g * H + h) * 65 + 64],
                                pv8[hh][:, 0:64]))

        with nc.psum_tensor("pnm_%d" % c, [128, W], F32) as pnm:
            its = [(wi, dc, mbase) for wi, mbase in ((0, MX_QN), (1, MX_KN + cur * 8))
                   for dc in range(DC)]
            nt_base = S.count("p3nt")
            sq_base = S.count("p3sq")

            @blk.scalar
            def _(sc, its=its, cur=cur, nt_base=nt_base):
                S.wait(sc, "p3qe")
                for i, (wi, dc, mbase) in enumerate(its):
                    if i >= 2:
                        sc.wait_ge(S.get("p3nt"), nt_base + i - 1)
                    src = (qf[:, dc * W:(dc + 1) * W] if wi == 0
                           else kf2[:, cur * DC * W + dc * W: cur * DC * W + (dc + 1) * W])
                    S.inc("p3sq", sc.activation(q2st[:, (i % 2) * W:(i % 2 + 1) * W], src, AF.Square))

            for i, (wi, dc, mbase) in enumerate(its):
                @blk.tensor
                def _(t, i=i, wi=wi, dc=dc, pnm=pnm, sq_base=sq_base):
                    t.wait_ge(S.get("p3sq"), sq_base + i + 1)
                    if wi == 0 and dc == 0:
                        S.wait(t, "p3ve")
                    S.wait(t, "p3nr")
                    S.inc("p3nt", t.matmul(pnm[0:2, 0:W], onepair[:, 0:2],
                                           q2st[:, (i % 2) * W:(i % 2 + 1) * W],
                                           start=True, stop=True))

                @blk.vector
                def _(v, i=i, mbase=mbase, dc=dc, pnm=pnm):
                    S.wait(v, "p3nt")
                    S.inc("p3nr", v.reduce_max(mxc[0:2, mbase + dc:mbase + dc + 1], pnm[0:2, 0:W],
                                               axis=AX.X))

            @blk.vector
            def _(v, c=c, cur=cur):
                S.wait(v, "p3nr")
                v.reduce_max(nrm[0:2, 0:1], mxc[0:2, MX_QN:MX_QN + 8], axis=AX.X)
                if c > 0:
                    v.reduce_max(nrm[0:2, 1:2], mxc[0:2, MX_KN:MX_KN + 16], axis=AX.X)
                else:
                    v.reduce_max(nrm[0:2, 1:2], mxc[0:2, MX_KN + cur * 8:MX_KN + cur * 8 + 8], axis=AX.X)
                S.inc("p3na", v.drain())

            @blk.tensor
            def _(t, pnm=pnm):
                S.wait(t, "p3na")
                S.inc("p3nt", t.transpose(pnm[0:2, 128:130], nrm[0:2, 0:2], idf[0:2, 0:2]))

            @blk.vector
            def _(v, pnm=pnm):
                S.wait(v, "p3nt")
                v.reduce_max(nrm[0:2, 2:3], pnm[0:2, 128:130], axis=AX.X)
                S.inc("p3na", v.drain())

            @blk.tensor
            def _(t, pnm=pnm):
                S.wait(t, "p3na")
                S.inc("p3nt", t.transpose(pnm[0:1, 130:132], nrm[0:2, 2:3], idf[0:2, 0:2]))

            @blk.vector
            def _(v, pnm=pnm):
                S.wait(v, "p3nt")
                v.reduce_sum(nrm[0:1, 3:4], pnm[0:1, 130:132], axis=AX.X)
                v.drain()
                v.tensor_single_scalar(nrm[0:1, 3:4], nrm[0:1, 3:4], -0.5 * SCALE, OP.mult)
                S.inc("p3na", v.drain())

            @blk.tensor
            def _(t, pnm=pnm):
                S.wait(t, "p3na")
                S.inc("p3nt", t.matmul(pnm[:, 132:133], ones[0:1, 0:128], nrm[0:1, 3:4],
                                       start=True, stop=True))

            @blk.vector
            def _(v, pnm=pnm):
                S.wait(v, "p3nt")
                S.wait(v, "p3xs")
                S.inc("p3nf", v.tensor_copy(mxc[:, MX_M:MX_M + 1], pnm[:, 132:133]))

        with nc.psum_tensor("pSa_%d" % c, [128, 768], F32) as pSa, \
             nc.psum_tensor("pSb_%d" % c, [128, 768], F32) as pSb, \
             nc.psum_tensor("pAVa_%d" % c, [65, W], F32) as pAVa, \
             nc.psum_tensor("pAVb_%d" % c, [65, W], F32) as pAVb, \
             nc.psum_tensor("pCB_%d" % c, [128, 2 * W], F32) as pCB:

            for dc in range(DC):
                xs_base = S.count("p3xs")
                xm_base = S.count("p3xm")
                av_base = S.count("p3av")
                eblk = []
                k = 0
                for qi2 in range(2):
                    g0 = 4 * c + 2 * qi2
                    kts = [G for G in range(g0 - 4, g0 + 2) if G >= 0]
                    nkt = len(kts)
                    j0 = 6 - nkt
                    groups = [(gi, kts[gi:gi + 3]) for gi in range(0, nkt, 3)]

                    for gi, gkts in groups:
                        last = (gi + len(gkts) == nkt)

                        @blk.tensor
                        def _(t, c=c, dc=dc, qi2=qi2, gi=gi, gkts=gkts, cur=cur, prv=prv,
                              pSa=pSa, pSb=pSb, k=k, xs_base=xs_base):
                            S.wait(t, "p3nf")
                            if k == 0:
                                S.wait(t, "p3xs")
                            else:
                                t.wait_ge(S.get("p3xs"), xs_base + 2 * k)
                            if dc == 0 and qi2 == 0 and gi == 0:
                                S.wait(t, "p3ve")
                            for aa, ps in ((0, pSa), (64, pSb)):
                                for j, G in enumerate(gkts):
                                    buf = cur if G >= 4 * c else prv
                                    kcol = buf * DC * W + dc * W + (G % 4) * 128
                                    mm = t.matmul(ps[:, j * 256:(j + 1) * 256],
                                                  kf2[aa:aa + 64, kcol:kcol + 128],
                                                  qf[aa:aa + 64, dc * W + qi2 * 256: dc * W + qi2 * 256 + 256],
                                                  start=True, stop=True)
                            S.inc("p3xm", mm)

                        @blk.vector
                        def _(v, gi=gi, gkts=gkts, j0=j0, k=k, xs_base=xs_base):
                            v.wait_ge(S.get("p3xs"), xs_base + 2 * (k + 1))
                            for aa in range(2):
                                for j in range(len(gkts)):
                                    jr = j0 + gi + j
                                    if jr in (0, 1, 4, 5):
                                        mslot = {0: 0, 1: 1, 4: 2, 5: 3}[jr]
                                        v.tensor_tensor(pTs[:, aa * 768 + j * 256: aa * 768 + (j + 1) * 256],
                                                        pTs[:, aa * 768 + j * 256: aa * 768 + (j + 1) * 256],
                                                        mq[:, mslot * 256:(mslot + 1) * 256],
                                                        OP.mult)
                            S.inc("p3xv", v.drain())

                        @blk.tensor
                        def _(t, c=c, dc=dc, qi2=qi2, gi=gi, gkts=gkts, cur=cur, prv=prv,
                              nkt=nkt, last=last, pAVa=pAVa, pAVb=pAVb, VS=VS):
                            S.wait(t, "p3xv")
                            if qi2 == 0 and gi == 0:
                                S.wait(t, "p3xe")
                            for aa, pav in ((0, pAVa), (1, pAVb)):
                                h = 2 * dc + aa
                                for j, G in enumerate(gkts):
                                    buf = cur if G >= 4 * c else prv
                                    vcol = buf * VS + ((G % 4) * H + h) * 65
                                    mm = t.matmul(pav[0:65, qi2 * 256:(qi2 + 1) * 256],
                                                  vsb2[:, vcol:vcol + 65],
                                                  pTs[:, aa * 768 + j * 256: aa * 768 + (j + 1) * 256],
                                                  start=(gi == 0 and j == 0),
                                                  stop=(last and j == len(gkts) - 1))
                            S.inc("p3av", mm)

                        eblk.append((k, gkts))
                        k += 1

                @blk.scalar
                def _(sc, eblk=eblk, xm_base=xm_base, av_base=av_base, pSa=pSa, pSb=pSb):
                    for k, gkts in eblk:
                        sc.wait_ge(S.get("p3xm"), xm_base + k + 1)
                        sc.wait_ge(S.get("p3av"), av_base + k)
                        n = len(gkts)
                        for aa, ps in ((0, pSa), (1, pSb)):
                            S.inc("p3xs", sc.activation(pTs[:, aa * 768: aa * 768 + n * 256],
                                                        ps[:, 0:n * 256], AF.Exp,
                                                        bias=mxc[:, MX_M:MX_M + 1], scale=SCALE))

                for aa, pav in ((0, pAVa), (1, pAVb)):
                    @blk.vector
                    def _(v, dc=dc, aa=aa, pav=pav):
                        S.wait(v, "p3av")
                        v.tensor_copy(flat[64:65, 1024 + aa * W:1024 + aa * W + W], pav[64:65, 0:W])
                        v.drain()
                        v.reciprocal(flat[64:65, 1024 + aa * W:1024 + aa * W + W],
                                     flat[64:65, 1024 + aa * W:1024 + aa * W + W])
                        S.inc("p3xv", v.drain())

                    @blk.tensor
                    def _(t, aa=aa, pCB=pCB):
                        S.wait(t, "p3xv")
                        S.inc("p3lb", t.matmul(pCB[:, aa * W:(aa + 1) * W], ones[64:65, 0:128],
                                               flat[64:65, 1024 + aa * W:1024 + aa * W + W],
                                               start=True, stop=True))

                    @blk.vector
                    def _(v, dc=dc, aa=aa, pav=pav, pCB=pCB):
                        S.wait(v, "p3lb")
                        h = 2 * dc + aa
                        po = (h % 2) * 64
                        v.tensor_copy(flat[:, aa * W:(aa + 1) * W], pCB[:, aa * W:(aa + 1) * W])
                        v.drain()
                        cb = flat[po:po + 64, aa * W:(aa + 1) * W]
                        S.inc("p3xe", v.tensor_tensor(avn[po:po + 64, (h // 2) * W: (h // 2 + 1) * W],
                                                      pav[0:64, 0:W], cb, OP.mult))

        with nc.psum_tensor("pwo_%d" % c, [128, 2 * W], F32) as pwo:
            for oc in range(DC):
                pslot = oc % 2
                ldsem = "ldA" if pslot == 0 else "ldB"

                @blk.sync
                def _(sy, c=c, oc=oc, pslot=pslot, ldsem=ldsem):
                    if S.count("p3wv") >= 2:
                        S.wait(sy, "p3wv", S.count("p3wv") - 1)
                    S.inc(ldsem, sy.dma_start(out=xnstg[:, pslot * W:(pslot + 1) * W],
                                              in_=xn_h.ap()[oc, :, c * W:(c + 1) * W]), 16)

                @blk.tensor
                def _(t, c=c, oc=oc, pslot=pslot, pwo=pwo):
                    S.wait(t, "p3xe")
                    if S.count("p3wv") >= 2:
                        S.wait(t, "p3wv", S.count("p3wv") - 1)
                    ps = pwo[:, pslot * W:(pslot + 1) * W]
                    for dc in range(DC):
                        mm = t.matmul(ps, woT[:, dc * D + oc * 128: dc * D + (oc + 1) * 128],
                                      avn[:, dc * W:(dc + 1) * W],
                                      start=(dc == 0), stop=(dc == DC - 1))
                    S.inc("p3qm", mm)

                @blk.vector
                def _(v, pslot=pslot, ldsem=ldsem, pwo=pwo):
                    S.wait(v, "p3qm")
                    S.wait(v, ldsem)
                    if S.count("p3wv") >= 2:
                        S.wait(v, "p3wA" if pslot == 0 else "p3wB")
                    S.inc("p3wv", v.tensor_tensor(x2stg[:, pslot * W:(pslot + 1) * W],
                                                  pwo[:, pslot * W:(pslot + 1) * W],
                                                  xnstg[:, pslot * W:(pslot + 1) * W], OP.add))

                @blk.sync
                def _(sy, c=c, oc=oc, pslot=pslot):
                    S.wait(sy, "p3wv")
                    S.inc("p3wA" if pslot == 0 else "p3wB",
                          sy.dma_start(out=x2_h.ap()[oc, :, c * W:(c + 1) * W],
                                       in_=x2stg[:, pslot * W:(pslot + 1) * W]), 16)

        for jf in range(c * OCF, (c + 1) * OCF):
            if c == 0 and jf == 0:
                for jl in range(4):
                    emit_f2_load(jl)
            emit_f2_load(jf + 4)
            emit_f2_tile(jf)

    p3.close()

    @blk.vector
    def _(v):
        S.wait(v, "f2v2")
        v.reduce_max(mxc[:, MX_TMP:MX_TMP + 1], mxc[:, MX_UP:MX_UP + OCF * NCH], axis=AX.X)
        S.inc("fmq", v.drain())

    finalize_scalar_max([MX_TMP], 128, 64, "fmq")
    collective(2, 64)
    derive_scales([(72, 32767.0, 6)], 80)

    barrier(("p3wA", "p3wB", "f2s0", "f2s1", "f2s2", "f2s3", "f2l0", "f2l1", "f2l2", "f2l3", "ldA", "ldB", "hlA", "hlB", "wTA"))

    p4 = contextlib.ExitStack()
    w1T = p4.enter_context(nc.sbuf_tensor("w1T", [128, DC * DFF], BF16))
    w2T = p4.enter_context(nc.sbuf_tensor("w2T", [128, 32 * D], BF16))
    x2t = p4.enter_context(nc.sbuf_tensor("x2t", [128, 2 * DC * 256], F32))
    h2t = p4.enter_context(nc.sbuf_tensor("h2t", [128, DC * 256], BF16))
    g1buf = p4.enter_context(nc.sbuf_tensor("g1buf", [128, 32 * 256], BF16))
    sqstg = p4.enter_context(nc.sbuf_tensor("sqstg", [128, 2 * 256], F32))
    pmm = p4.enter_context(nc.psum_tensor("pmm", [128, 2 * W], F32))
    pstat = p4.enter_context(nc.psum_tensor("pstat", [2, 2 * W], F32))
    pab = p4.enter_context(nc.psum_tensor("pab", [128, 2 * W], F32))

    @blk.sync
    def _(sy):
        S.wait(sy, "cast1")
        S.wait(sy, "cast2")
        for dc in range(DC):
            S.inc("wTA", sy.dma_start(out=w1T[:, dc * DFF:(dc + 1) * DFF],
                                      in_=w1_bf.ap()[:, dc * 128:(dc + 1) * 128], transpose=True), 16)
        for kc in range(32):
            S.inc("wTA", sy.dma_start(out=w2T[:, kc * D:(kc + 1) * D],
                                      in_=w2_bf.ap()[:, kc * 128:(kc + 1) * 128], transpose=True), 16)

    WH = W // 2
    for tt in range(2 * NCH):
        xob = (tt % 2) * DC * 256
        @blk.sync
        def _(sy, tt=tt, xob=xob):
            if tt == 0:
                S.wait(sy, "p3wA")
                S.wait(sy, "p3wB")
            if tt >= 2:
                S.wait(sy, "p4av", S.count("p4av") - DC)
            for dc in range(DC):
                S.inc("ldA", sy.dma_start(out=x2t[:, xob + dc * WH:xob + (dc + 1) * WH],
                                          in_=x2_h.ap()[dc, :, tt * WH:(tt + 1) * WH]), 16)

        st_base = S.count("p4st")
        sq4_base = S.count("p4sq")

        @blk.scalar
        def _(sc, tt=tt, xob=xob, st_base=st_base):
            S.wait(sc, "ldA")
            for dc in range(DC):
                if st_base + dc >= 2:
                    sc.wait_ge(S.get("p4st"), st_base + dc - 1)
                S.inc("p4sq", sc.activation(sqstg[:, (dc % 2) * WH:(dc % 2 + 1) * WH],
                                            x2t[:, xob + dc * WH:xob + (dc + 1) * WH], AF.Square))

        for dc in range(DC):
            @blk.tensor
            def _(t, tt=tt, dc=dc, xob=xob, sq4_base=sq4_base):
                t.wait_ge(S.get("p4sq"), sq4_base + dc + 1)
                if tt == 0 and dc == 0:
                    S.wait(t, "wTA")
                if dc == 0:
                    S.wait(t, "ldA")
                    if tt > 0:
                        S.wait(t, "p4va")
                t.matmul(pstat[0:1, 0:WH], ones[:, 0:1], x2t[:, xob + dc * WH:xob + (dc + 1) * WH],
                         start=(dc == 0), stop=(dc == DC - 1))
                S.inc("p4st", t.matmul(pstat[0:1, W:W + WH], ones[:, 0:1],
                                       sqstg[:, (dc % 2) * WH:(dc % 2 + 1) * WH],
                                       start=(dc == 0), stop=(dc == DC - 1)))

        FB = 128

        @blk.vector
        def _(v, tt=tt):
            S.wait(v, "p4st")
            v.tensor_single_scalar(flat[0:1, FB:FB + WH], pstat[0:1, 0:WH], 1.0 / D, OP.mult)
            v.tensor_single_scalar(flat[0:1, FB + WH:FB + 2 * WH], pstat[0:1, W:W + WH], 1.0 / D, OP.mult)
            v.drain()
            v.tensor_tensor(flat[0:1, FB + 2 * WH:FB + 3 * WH], flat[0:1, FB:FB + WH],
                            flat[0:1, FB:FB + WH], OP.mult)
            v.drain()
            v.tensor_tensor(flat[0:1, FB + 2 * WH:FB + 3 * WH], flat[0:1, FB + WH:FB + 2 * WH],
                            flat[0:1, FB + 2 * WH:FB + 3 * WH], OP.subtract)
            v.drain()
            v.tensor_single_scalar(flat[0:1, FB + 2 * WH:FB + 3 * WH], flat[0:1, FB + 2 * WH:FB + 3 * WH],
                            EPS, OP.add)
            S.inc("p4va", v.drain())

        @blk.scalar
        def _(sc):
            S.wait(sc, "p4va")
            S.inc("p4sr", sc.activation(flat[0:1, FB + 3 * WH:FB + 4 * WH],
                                        flat[0:1, FB + 2 * WH:FB + 3 * WH], AF.Sqrt))

        @blk.vector
        def _(v):
            S.wait(v, "p4sr")
            v.reciprocal(flat[0:1, FB + 2 * WH:FB + 3 * WH], flat[0:1, FB + 3 * WH:FB + 4 * WH])
            v.drain()
            v.tensor_tensor(flat[0:1, FB:FB + WH], flat[0:1, FB:FB + WH],
                            flat[0:1, FB + 2 * WH:FB + 3 * WH], OP.mult)
            v.drain()
            v.tensor_single_scalar(flat[0:1, FB:FB + WH], flat[0:1, FB:FB + WH], -1.0, OP.mult)
            S.inc("p4va", v.drain())

        @blk.tensor
        def _(t, tt=tt):
            S.wait(t, "p4va")
            if tt > 0:
                S.wait(t, "p4h2")
            t.matmul(pab[:, 0:WH], ones[0:1, 0:128], flat[0:1, FB + 2 * WH:FB + 3 * WH], start=True, stop=True)
            S.inc("p4m1", t.matmul(pab[:, W:W + WH], ones[0:1, 0:128], flat[0:1, FB:FB + WH],
                                   start=True, stop=True))

        @blk.vector
        def _(v, tt=tt):
            S.wait(v, "p4m1")
            if tt > 0:
                S.wait(v, "p4ge")
            for dc in range(DC):
                v.tensor_tensor(sqstg[:, 0:WH], x2t[:, xob + dc * WH:xob + (dc + 1) * WH], pab[:, 0:WH], OP.mult)
                v.drain()
                v.tensor_tensor(sqstg[:, 0:WH], sqstg[:, 0:WH], pab[:, W:W + WH], OP.add)
                v.drain()
                v.tensor_scalar(h2t[:, dc * WH:(dc + 1) * WH], sqstg[:, 0:WH],
                                gb[:, G2C + dc:G2C + dc + 1], gb[:, B2C + dc:B2C + dc + 1],
                                OP.mult, OP.add)
                v.drain()
            S.inc("p4h2", v.drain())

        p4m1_base = S.count("p4m1")
        p4ge_base = S.count("p4ge")
        for o1 in range(32):
            pslot = o1 % 2

            @blk.tensor
            def _(t, tt=tt, o1=o1, pslot=pslot, p4ge_base=p4ge_base):
                S.wait(t, "p4h2")
                if o1 >= 2:
                    t.wait_ge(S.get("p4ge"), p4ge_base + o1 - 1)
                for dc in range(DC):
                    mm = t.matmul(pmm[:, pslot * W:pslot * W + WH],
                                  w1T[:, dc * DFF + o1 * 128: dc * DFF + (o1 + 1) * 128],
                                  h2t[:, dc * WH:(dc + 1) * WH], start=(dc == 0), stop=(dc == DC - 1))
                S.inc("p4m1", mm)

        @blk.scalar
        def _(sc, tt=tt, p4m1_base=p4m1_base):
            if tt > 0:
                S.wait(sc, "p4m1", p4m1_base)
            for o1 in range(32):
                pslot = o1 % 2
                sc.wait_ge(S.get("p4m1"), p4m1_base + o1 + 1)
                S.inc("p4ge", sc.activation(g1buf[:, o1 * WH:(o1 + 1) * WH],
                                            pmm[:, pslot * W:pslot * W + WH],
                                            AF.Gelu, bias=b1c[:, o1:o1 + 1], scale=1.0))

        for o2 in range(DC):
            pslot = o2 % 2

            @blk.tensor
            def _(t, tt=tt, o2=o2, pslot=pslot):
                S.wait(t, "p4ge")
                if S.count("p4av") >= 2:
                    S.wait(t, "p4av", S.count("p4av") - 1)
                for kc in range(32):
                    mm = t.matmul(pmm[:, pslot * W:pslot * W + WH],
                                  w2T[:, kc * D + o2 * 128: kc * D + (o2 + 1) * 128],
                                  g1buf[:, kc * WH:(kc + 1) * WH], start=(kc == 0), stop=(kc == 31))
                S.inc("p4m1", mm)

            @blk.vector
            def _(v, o2=o2, pslot=pslot, xob=xob):
                S.wait(v, "p4m1")
                if S.count("p4av") >= 2:
                    S.wait(v, "p3wA" if pslot == 0 else "p3wB")
                S.inc("p4av", v.scalar_tensor_tensor(flat[:, 1152 + pslot * WH:1152 + (pslot + 1) * WH],
                                                     pmm[:, pslot * W:pslot * W + WH],
                                                     gb[:, MB2C + o2:MB2C + o2 + 1],
                                                     x2t[:, xob + o2 * WH:xob + (o2 + 1) * WH], OP.add, OP.add))

            @blk.sync
            def _(sy, tt=tt, o2=o2, pslot=pslot):
                S.wait(sy, "p4av")
                S.inc("p3wA" if pslot == 0 else "p3wB",
                      sy.dma_start(out=acc_h.ap()[o2, :, tt * WH:(tt + 1) * WH],
                                   in_=flat[:, 1152 + pslot * WH:1152 + (pslot + 1) * WH]), 16)

        for jf in range(tt * (OCF * NCH) // (2 * NCH), (tt + 1) * (OCF * NCH) // (2 * NCH)):
            if tt == 0 and jf == 0:
                for jl in range(4):
                    emit_f3_load(jl)
            emit_f3_load(jf + 4)
            emit_f3_tile(jf)

    p4.close()

    @blk.vector
    def _(v):
        S.wait(v, "f3v")
        v.reduce_max(mxc[:, MX_TMP:MX_TMP + 1], mxc[:, MX_GA:MX_GA + OCF * NCH], axis=AX.X)
        S.inc("fmq", v.drain())

    finalize_scalar_max([MX_TMP], 128, 96, "fmq")
    collective(3, 96)
    derive_scales([(104, 32767.0, 8)], 112)

    f2.close()
    barrier(("p3wA", "p3wB", "f3s0", "f3s1", "f3s2", "f3s3", "f3l0", "f3l1", "f3l2", "f3l3", "ldA", "wTA"))

    f4 = contextlib.ExitStack()
    wdT = f4.enter_context(nc.sbuf_tensor("wdT", [128, OCF * D], BF16))
    obuf = f4.enter_context(nc.sbuf_tensor("obuf", [128, 2 * OCF * W], BF16))
    old = f4.enter_context(nc.sbuf_tensor("old", [128, 2 * W], F32))
    oi32 = f4.enter_context(nc.sbuf_tensor("oi32", [128, W], I32))
    accld = f4.enter_context(nc.sbuf_tensor("accld", [128, 2 * W], F32))
    sumf = f4.enter_context(nc.sbuf_tensor("sumf", [128, DC * W], F32))
    xfin = f4.enter_context(nc.sbuf_tensor("xfin", [128, 2 * 1024], F32))
    outst = f4.enter_context(nc.sbuf_tensor("outst", [128, 2 * 1024], F32))
    pdn = f4.enter_context(nc.psum_tensor("pdn", [128, 2 * W], F32))
    pfin = f4.enter_context(nc.psum_tensor("pfin", [128, 2 * 1024], F32))

    @blk.sync
    def _(sy):
        S.wait(sy, "dqoA")
        S.wait(sy, "dqoB")
        for kc in range(OCF):
            S.inc("wTA", sy.dma_start(out=wdT[:, kc * D:(kc + 1) * D],
                                      in_=wd_dq.ap()[:, kc * 128:(kc + 1) * 128], transpose=True), 16)

    for tt in range(NCH):
        ob = (tt % 2) * OCF * W
        for kc in range(OCF):
            buf = kc % 2
            ldsem = "ldA" if buf == 0 else "ldB"

            @blk.sync
            def _(sy, tt=tt, kc=kc, buf=buf, ldsem=ldsem):
                if tt * OCF + kc >= 2:
                    S.wait(sy, "f4v", S.count("f4v") - 1)
                S.inc(ldsem, sy.dma_start(out=old[:, buf * W:(buf + 1) * W],
                                          in_=or_h.ap()[kc, :, tt * W:(tt + 1) * W]), 16)

            @blk.vector
            def _(v, tt=tt, kc=kc, buf=buf, ldsem=ldsem, ob=ob):
                S.wait(v, ldsem)
                if tt == 0 and kc == 0:
                    S.wait(v, "dsv")
                if tt >= 2 and kc == 0:
                    S.wait(v, "f4mm", S.count("f4mm") - 40)
                v.tensor_scalar(oi32[:, :], old[:, buf * W:(buf + 1) * W], scl[:, 9:10], KQH,
                                OP.mult, OP.add)
                v.drain()
                v.tensor_scalar(obuf[:, ob + kc * W:ob + (kc + 1) * W], oi32[:, :], KQ, scl[:, 8:9],
                                OP.subtract, OP.mult)
                S.inc("f4v", v.drain())

        for d2 in range(DC):
            pslot = d2 % 2
            lasem = "f4la" if pslot == 0 else "f4lb"

            @blk.sync
            def _(sy, tt=tt, d2=d2, pslot=pslot, lasem=lasem):
                if S.count("f4av") >= 2:
                    S.wait(sy, "f4av", S.count("f4av") - 1)
                S.inc(lasem, sy.dma_start(out=accld[:, pslot * W:(pslot + 1) * W],
                                          in_=acc_h.ap()[d2, :, tt * W:(tt + 1) * W]), 16)

            @blk.tensor
            def _(t, tt=tt, d2=d2, pslot=pslot, ob=ob):
                S.wait(t, "f4v")
                if tt == 0 and d2 == 0:
                    S.wait(t, "wTA")
                if S.count("f4av") >= 2:
                    S.wait(t, "f4av", S.count("f4av") - 1)
                for kc in range(OCF):
                    mm = t.matmul(pdn[:, pslot * W:(pslot + 1) * W],
                                  wdT[:, kc * D + d2 * 128: kc * D + (d2 + 1) * 128],
                                  obuf[:, ob + kc * W:ob + (kc + 1) * W], start=(kc == 0), stop=(kc == OCF - 1))
                S.inc("f4mm", mm)

            @blk.vector
            def _(v, d2=d2, pslot=pslot, lasem=lasem):
                S.wait(v, "f4mm")
                S.wait(v, lasem)
                S.wait(v, "f4mm")
                S.inc("f4av", v.tensor_tensor(sumf[:, d2 * W:(d2 + 1) * W],
                                              pdn[:, pslot * W:(pslot + 1) * W],
                                              accld[:, pslot * W:(pslot + 1) * W], OP.add))

        for g in range(4):
            gbuf = g % 2
            lxsem = "f4lx" if gbuf == 0 else "f4ly"
            stsem = "stA" if gbuf == 0 else "stB"

            @blk.sync
            def _(sy, tt=tt, g=g, gbuf=gbuf, lxsem=lxsem):
                if S.count("f4fv") >= 2:
                    S.wait(sy, "f4fv", S.count("f4fv") - 1)
                src = x_d[tt * W + g * 128: tt * W + (g + 1) * 128, :]
                S.inc(lxsem, sy.dma_start(out=xfin[:, gbuf * 1024:(gbuf + 1) * 1024], in_=src), 16)

            @blk.tensor
            def _(t, tt=tt, g=g, gbuf=gbuf):
                S.wait(t, "f4av")
                if S.count("f4fv") >= 2:
                    S.wait(t, "f4fv", S.count("f4fv") - 1)
                for d2 in range(DC):
                    S.inc("f4mm", t.transpose(pfin[:, gbuf * 1024 + d2 * 128: gbuf * 1024 + (d2 + 1) * 128],
                                              sumf[:, d2 * W + g * 128: d2 * W + (g + 1) * 128], idf[:]))

            @blk.vector
            def _(v, g=g, gbuf=gbuf, lxsem=lxsem, stsem=stsem):
                S.wait(v, "f4mm")
                S.wait(v, lxsem)
                if S.count("f4fv") >= 2:
                    S.wait(v, stsem)
                S.inc("f4fv", v.tensor_tensor(outst[:, gbuf * 1024:(gbuf + 1) * 1024],
                                              pfin[:, gbuf * 1024:(gbuf + 1) * 1024],
                                              xfin[:, gbuf * 1024:(gbuf + 1) * 1024], OP.add))

            @blk.sync
            def _(sy, tt=tt, g=g, gbuf=gbuf, stsem=stsem):
                S.wait(sy, "f4fv")
                S.inc(stsem, sy.dma_start(out=out_d[tt * W + g * 128: tt * W + (g + 1) * 128, :],
                                          in_=outst[:, gbuf * 1024:(gbuf + 1) * 1024]), 16)

    @blk.sync
    def _(sy):
        S.wait(sy, "stA")
        S.wait(sy, "stB")

    f4.close()
    return nc, st


_PROG_CACHE = {}


def _get_program():
    if "nc" not in _PROG_CACHE:
        nc, st = build_program()
        _PROG_CACHE["nc"] = nc
        _PROG_CACHE["st"] = st
    return _PROG_CACHE["nc"]


def make_in_maps(inputs):
    x_full = np.ascontiguousarray(np.asarray(inputs["L0_attn_x"], dtype=np.float32))
    rk = np.arange(128)
    mUh = (rk[:, None] >= rk[None, :] + 1).astype(ml_dtypes.bfloat16)
    mLh = (rk[:, None] <= rk[None, :]).astype(ml_dtypes.bfloat16)
    rq = np.arange(256)
    m0 = (rk[:, None] >= rq[None, :] + 1)
    m1 = (rk[:, None] >= rq[None, :] - 127)
    m4 = (rk[:, None] <= rq[None, :])
    m5 = (rk[:, None] <= rq[None, :] - 128)
    mqh = np.concatenate([m0, m1, m4, m5], axis=1).astype(ml_dtypes.bfloat16)
    idn = np.eye(128, dtype=np.float32)
    common = {
        "wqkv": np.ascontiguousarray(np.asarray(inputs["L0_attn_w_qkv"], dtype=np.float32)),
        "wo": np.ascontiguousarray(np.asarray(inputs["L0_attn_w_o"], dtype=np.float32)),
        "w1": np.ascontiguousarray(np.asarray(inputs["L0_attn_mlp_w1"], dtype=np.float32)),
        "b1": np.ascontiguousarray(np.asarray(inputs["L0_attn_mlp_b1"], dtype=np.float32)),
        "w2": np.ascontiguousarray(np.asarray(inputs["L0_attn_mlp_w2"], dtype=np.float32)),
        "b2": np.ascontiguousarray(np.asarray(inputs["L0_attn_mlp_b2"], dtype=np.float32)),
        "g1v": np.ascontiguousarray(np.asarray(inputs["L0_attn_ln1_g"], dtype=np.float32)),
        "b1v": np.ascontiguousarray(np.asarray(inputs["L0_attn_ln1_b"], dtype=np.float32)),
        "g2v": np.ascontiguousarray(np.asarray(inputs["L0_attn_ln2_g"], dtype=np.float32)),
        "b2v": np.ascontiguousarray(np.asarray(inputs["L0_attn_ln2_b"], dtype=np.float32)),
        "wg": np.ascontiguousarray(np.asarray(inputs["L0_ffn_w_gate"], dtype=np.float32)),
        "wu": np.ascontiguousarray(np.asarray(inputs["L0_ffn_w_up"], dtype=np.float32)),
        "wd": np.ascontiguousarray(np.asarray(inputs["L0_ffn_w_down"], dtype=np.float32)),
        "idf": idn,
        "mU": mUh,
        "mL": mLh,
        "mq": mqh,
    }
    return [dict(common, x=np.ascontiguousarray(x_full[c])) for c in range(NCORES)]


def kernel(**inputs):
    in_maps = make_in_maps(inputs)
    nc = _get_program()
    res = run_bass_kernel_spmd(nc, in_maps, core_ids=list(range(NCORES)))
    return np.stack([np.asarray(r["out"], dtype=np.float32) for r in res.results], axis=0)


# version 15
# speedup vs baseline: 1.2077x; 1.0389x over previous
import contextlib
import os
import numpy as np
import ml_dtypes

import concourse.bass as bass
import concourse.bass_utils as _bass_utils
import concourse.mybir as mybir
from concourse.bass_utils import run_bass_kernel_spmd

if not getattr(_bass_utils, "_dyn_act_table_patched", False):
    _orig_get_walrus_args = _bass_utils.get_walrus_args

    def _get_walrus_args_dyn(*a, **k):
        return _orig_get_walrus_args(*a, **k) + ["--enable-dynamic-act-table"]

    _bass_utils.get_walrus_args = _get_walrus_args_dyn
    _bass_utils._dyn_act_table_patched = True

F32 = mybir.dt.float32
BF16 = mybir.dt.bfloat16
I32 = mybir.dt.int32
AF = mybir.ActivationFunctionType
OP = mybir.AluOpType
AX = mybir.AxisListType

NCORES = 8
T = 4096
D = 1024
DC = D // 128
H = 16
HD = 64
W = 512
NCH = T // W
DFF = 4096
DI = 3072
OCF = DI // 128
_TRUNC_CAST = (os.environ.get("BASS_SIM_FP64") == "1"
               or os.environ.get("KERNEL_TRUNC_CAST") == "1")
KQ = 49152.0 if _TRUNC_CAST else 0.0
KQH = KQ + 0.5 if _TRUNC_CAST else 0.0
W8H = 8.5 if _TRUNC_CAST else 0.0
W8 = 8.0 if _TRUNC_CAST else 0.0
SGH = 0.5 if _TRUNC_CAST else 0.0
EPS = 1e-5
SCALE = 0.125

MX_XN = 0
MX_TMP = 40
MX_UP = 64
MX_GA = 256
MX_QN = 456
MX_KN = 464
MX_M = 480


class Sems:
    def __init__(self, nc, stack):
        self.nc, self.stack, self.h, self.n = nc, stack, {}, {}

    def get(self, name):
        if name not in self.h:
            self.h[name] = self.stack.enter_context(self.nc.semaphore(name))
            self.n[name] = 0
        return self.h[name]

    def inc(self, name, instr, amount=1):
        instr.then_inc(self.get(name), amount)
        self.n[name] += amount
        return instr

    def count(self, name):
        self.get(name)
        return self.n[name]

    def wait(self, eng, name, val=None):
        eng.wait_ge(self.get(name), self.n[name] if val is None else val)


def build_program():
    nc = bass.Bass("TRN2", target_bir_lowering=False)
    st = contextlib.ExitStack()
    S = Sems(nc, st)

    x_d = nc.dram_tensor("x", [T, D], F32, kind="ExternalInput").ap()
    wqkv_d = nc.dram_tensor("wqkv", [3 * D, D], F32, kind="ExternalInput").ap()
    wo_d = nc.dram_tensor("wo", [D, D], F32, kind="ExternalInput").ap()
    w1_d = nc.dram_tensor("w1", [DFF, D], F32, kind="ExternalInput").ap()
    b1_d = nc.dram_tensor("b1", [DFF], F32, kind="ExternalInput").ap()
    w2_d = nc.dram_tensor("w2", [D, DFF], F32, kind="ExternalInput").ap()
    b2_d = nc.dram_tensor("b2", [D], F32, kind="ExternalInput").ap()
    g1v_d = nc.dram_tensor("g1v", [D], F32, kind="ExternalInput").ap()
    b1v_d = nc.dram_tensor("b1v", [D], F32, kind="ExternalInput").ap()
    g2v_d = nc.dram_tensor("g2v", [D], F32, kind="ExternalInput").ap()
    b2v_d = nc.dram_tensor("b2v", [D], F32, kind="ExternalInput").ap()
    wg_d = nc.dram_tensor("wg", [DI, D], F32, kind="ExternalInput").ap()
    wu_d = nc.dram_tensor("wu", [DI, D], F32, kind="ExternalInput").ap()
    wd_d = nc.dram_tensor("wd", [D, DI], F32, kind="ExternalInput").ap()
    idf_d = nc.dram_tensor("idf", [128, 128], F32, kind="ExternalInput").ap()
    mU_d = nc.dram_tensor("mU", [128, 128], BF16, kind="ExternalInput").ap()
    mL_d = nc.dram_tensor("mL", [128, 128], BF16, kind="ExternalInput").ap()
    mq_d = nc.dram_tensor("mq", [128, 4 * 256], BF16, kind="ExternalInput").ap()
    out_d = nc.dram_tensor("out", [T, D], F32, kind="ExternalOutput").ap()

    wqkv_bf = nc.dram_tensor("wqkv_bf", [3 * D, D], BF16)
    wo_bf = nc.dram_tensor("wo_bf", [D, D], BF16)
    w1_bf = nc.dram_tensor("w1_bf", [DFF, D], BF16)
    w2_bf = nc.dram_tensor("w2_bf", [D, DFF], BF16)
    wg_dq = nc.dram_tensor("wg_dq", [DI, D], BF16)
    wu_dq = nc.dram_tensor("wu_dq", [DI, D], BF16)
    wd_dq = nc.dram_tensor("wd_dq", [D, DI], BF16)
    xn_h = nc.dram_tensor("xn_h", [DC, 128, T], F32)
    h_hbm = nc.dram_tensor("h_hbm", [DC, 128, T], BF16)
    up_h = nc.dram_tensor("up_h", [OCF, 128, T], F32)
    gr_h = nc.dram_tensor("gr_h", [OCF, 128, T], F32)
    t2_h = nc.dram_tensor("t2_h", [OCF, 128, T], F32)
    or_h = nc.dram_tensor("or_h", [OCF, 128, T], F32)
    x2_h = nc.dram_tensor("x2_h", [DC, 128, T], F32)
    acc_h = nc.dram_tensor("acc_h", [DC, 128, T], F32)
    cio = [(nc.dram_tensor("cin%d" % k, [1, 8], F32),
            nc.dram_tensor("cout%d" % k, [1, 8], F32)) for k in range(4)]

    idf = st.enter_context(nc.sbuf_tensor("idf_s", [128, 128], F32))
    mU = st.enter_context(nc.sbuf_tensor("mU_s", [128, 128], BF16))
    mL = st.enter_context(nc.sbuf_tensor("mL_s", [128, 128], BF16))
    mq = st.enter_context(nc.sbuf_tensor("mq_s", [128, 4 * 256], BF16))
    ones = st.enter_context(nc.sbuf_tensor("ones", [128, 128], F32))
    onepair = st.enter_context(nc.sbuf_tensor("onepair", [128, 2], F32))
    gb = st.enter_context(nc.sbuf_tensor("gb", [128, 48], F32))
    b1c = st.enter_context(nc.sbuf_tensor("b1c", [128, 32], F32))
    scl = st.enter_context(nc.sbuf_tensor("scl", [128, 16], F32))
    mxc = st.enter_context(nc.sbuf_tensor("mxc", [128, 512], F32))
    flat = st.enter_context(nc.sbuf_tensor("flat", [128, 2176], F32))
    nrm = st.enter_context(nc.sbuf_tensor("nrm", [128, 8], F32))

    G1C, B1C, G2C, B2C, MB2C = 0, 8, 16, 24, 32

    blk = st.enter_context(nc.Block())
    ENGS = ("sync", "gpsimd", "vector", "scalar", "tensor")

    def barrier(dma_sems=()):
        @blk.sync
        def _(sy):
            for sname in dma_sems:
                S.wait(sy, sname)
        for e in ENGS:
            @getattr(blk, e)
            def _(eng, e=e):
                S.inc("bar_" + e, eng.nop())
        for e in ENGS:
            @getattr(blk, e)
            def _(eng, e=e):
                for o in ENGS:
                    if o != e:
                        S.wait(eng, "bar_" + o)

    uniq = [0]

    def finalize_scalar_max(cols, prt, fb, wait_name):
        n = len(cols)
        c0 = cols[0]
        contig = all(cols[j] == c0 + j for j in range(n))
        assert contig
        uniq[0] += 1

        with nc.psum_tensor("pfz%d" % uniq[0], [128, 256], F32) as pfz:
            @blk.tensor
            def _(t):
                S.wait(t, wait_name)
                S.inc("fmt", t.transpose(pfz[0:n, 0:128], mxc[:, c0:c0 + n], idf[:]))

            @blk.vector
            def _(v):
                S.wait(v, "fmt")
                v.reduce_max(nrm[0:n, 7:8], pfz[0:n, 0:128], axis=AX.X)
                S.inc("fmq", v.drain())

            @blk.tensor
            def _(t):
                S.wait(t, "fmq")
                S.inc("fmt", t.transpose(pfz[0:1, 128:128 + n], nrm[0:n, 7:8], idf[0:n, 0:n]))

            @blk.vector
            def _(v):
                S.wait(v, "fmt")
                v.tensor_copy(flat[0:1, fb:fb + n], pfz[0:1, 128:128 + n])
                v.drain()
                if n < 8:
                    v.tensor_copy(flat[0:1, fb + n:fb + 8],
                                  flat[0:1, fb:fb + 1].unsqueeze(2).broadcast_to([1, 1, 8 - n]).squeeze(1))
                S.inc("fmv2", v.drain())

    def collective(k, fb):
        cin_k, cout_k = cio[k]

        @blk.gpsimd
        def _(g):
            S.wait(g, "fmv2")
            S.inc("ccw", g.dma_start(out=cin_k.ap()[0:1, 0:8], in_=flat[0:1, fb:fb + 8]), 16)
            S.wait(g, "ccw")
            S.inc("cc", g.collective_compute(
                "AllReduce", OP.max, replica_groups=[list(range(NCORES))],
                ins=[cin_k.ap().opt()], outs=[cout_k.ap().opt()]))
            S.wait(g, "cc")
            S.inc("ccr", g.dma_start(out=flat[0:1, fb + 8:fb + 16], in_=cout_k.ap()[0:1, 0:8]), 16)

    def derive_scales(pairs, fb):
        @blk.vector
        def _(v):
            S.wait(v, "ccr")
            for j, (c, q, _sc) in enumerate(pairs):
                v.tensor_scalar(flat[0:1, fb + 2 * j:fb + 2 * j + 1], flat[0:1, c:c + 1],
                                1.0 / q, 1e-12, OP.mult, OP.max)
            v.drain()
            for j in range(len(pairs)):
                v.reciprocal(flat[0:1, fb + 2 * j + 1:fb + 2 * j + 2],
                             flat[0:1, fb + 2 * j:fb + 2 * j + 1])
            S.inc("dsv", v.drain())

        uniq[0] += 1
        with nc.psum_tensor("pdsx%d" % uniq[0], [128, 16], F32) as pds:
            @blk.tensor
            def _(t):
                S.wait(t, "dsv")
                S.inc("dst", t.matmul(pds[:, 0:2 * len(pairs)], ones[0:1, 0:128],
                                      flat[0:1, fb:fb + 2 * len(pairs)], start=True, stop=True))

            @blk.vector
            def _(v):
                S.wait(v, "dst")
                for j, (_c, _q, c2) in enumerate(pairs):
                    v.tensor_copy(scl[:, c2:c2 + 2], pds[:, 2 * j:2 * j + 2])
                S.inc("dsv", v.drain())

    @blk.gpsimd
    def _(g):
        S.inc("cast1", g.dma_start(out=wqkv_bf.ap(), in_=wqkv_d), 16)
        S.inc("cast1", g.dma_start(out=wo_bf.ap(), in_=wo_d), 16)
        S.inc("cast2", g.dma_start(out=w1_bf.ap(), in_=w1_d), 16)
        S.inc("cast2", g.dma_start(out=w2_bf.ap(), in_=w2_d), 16)

    with nc.sbuf_tensor("vecst", [48, 128], F32) as vecst, \
         nc.sbuf_tensor("b1st", [32, 128], F32) as b1st, \
         nc.psum_tensor("pvec", [128, 48], F32) as pvec, \
         nc.psum_tensor("pb1", [128, 32], F32) as pb1:

        @blk.sync
        def _(sy):
            S.inc("cst", sy.dma_start(out=idf[:], in_=idf_d), 16)
            S.inc("cst", sy.dma_start(out=mU[:], in_=mU_d), 16)
            S.inc("cst", sy.dma_start(out=mL[:], in_=mL_d), 16)
            S.inc("cst", sy.dma_start(out=mq[:], in_=mq_d), 16)
            S.inc("cst", sy.dma_start(out=vecst[0:8, :], in_=g1v_d.rearrange("(a b) -> a b", b=128)), 16)
            S.inc("cst", sy.dma_start(out=vecst[8:16, :], in_=b1v_d.rearrange("(a b) -> a b", b=128)), 16)
            S.inc("cst", sy.dma_start(out=vecst[16:24, :], in_=g2v_d.rearrange("(a b) -> a b", b=128)), 16)
            S.inc("cst", sy.dma_start(out=vecst[24:32, :], in_=b2v_d.rearrange("(a b) -> a b", b=128)), 16)
            S.inc("cst", sy.dma_start(out=vecst[32:40, :], in_=b2_d.rearrange("(a b) -> a b", b=128)), 16)
            S.inc("cst", sy.dma_start(out=b1st[:, :], in_=b1_d.rearrange("(a b) -> a b", b=128)), 16)

        @blk.tensor
        def _(t):
            S.wait(t, "cst")
            t.transpose(pvec[:, 0:40], vecst[0:40, :], idf[0:40, 0:40])
            S.inc("fmt", t.transpose(pb1[:, 0:32], b1st[:, :], idf[0:32, 0:32]))

        @blk.vector
        def _(v):
            v.memset(ones[:], 1.0)
            v.memset(onepair[:, :], 0.0)
            v.drain()
            v.memset(onepair[0:64, 0:1], 1.0)
            v.memset(onepair[64:128, 1:2], 1.0)
            S.wait(v, "fmt")
            v.tensor_copy(gb[:, 0:40], pvec[:, 0:40])
            v.tensor_copy(b1c[:, 0:32], pb1[:, 0:32])
            S.inc("dsv", v.drain())

        barrier(("cst",))

    p1 = contextlib.ExitStack()
    hst = p1.enter_context(nc.sbuf_tensor("hst", [128, 2 * DC * W], BF16))
    xin = p1.enter_context(nc.sbuf_tensor("xin", [128, 2 * 4096], F32))
    p1st = p1.enter_context(nc.sbuf_tensor("p1st", [128, 64], F32))
    hpre = p1.enter_context(nc.sbuf_tensor("hpre", [128, 4096], F32))
    xnst = p1.enter_context(nc.sbuf_tensor("xnst", [128, 2 * DC * W], F32))
    ptr1 = p1.enter_context(nc.psum_tensor("ptr1", [128, 1024], F32))
    ptr2 = p1.enter_context(nc.psum_tensor("ptr2", [128, 1024], F32))

    for tt in range(NCH):
        buf = tt % 2
        xo = buf * 4096
        so = buf * DC * W
        ldsem = "ldA" if buf == 0 else "ldB"

        @blk.sync
        def _(sy, tt=tt, xo=xo, ldsem=ldsem):
            if tt >= 2:
                S.wait(sy, "p1v", S.count("p1v") - 1)
            src = x_d[tt * W:(tt + 1) * W, :].rearrange("(g p) d -> p g d", p=128)
            S.inc(ldsem, sy.dma_start(out=xin[:, xo:xo + 4096].rearrange("p (g d) -> p g d", d=1024), in_=src), 16)

        @blk.scalar
        def _(sc, tt=tt, xo=xo, ldsem=ldsem):
            S.wait(sc, ldsem)
            if tt >= 1:
                S.wait(sc, "p1t")
            for g in range(4):
                xg = xin[:, xo + g * 1024: xo + (g + 1) * 1024]
                sc.activation(hpre[:, g * 1024:(g + 1) * 1024], xg, AF.Square)
            S.inc("p1sa", sc.drain())

        @blk.vector
        def _(v, tt=tt, xo=xo, ldsem=ldsem):
            S.wait(v, ldsem)
            if tt == 0:
                S.wait(v, "dsv")
            for g in range(4):
                xg = xin[:, xo + g * 1024: xo + (g + 1) * 1024]
                v.reduce_sum(p1st[:, g:g + 1], xg, axis=AX.X)
            S.wait(v, "p1sa")
            for g in range(4):
                v.reduce_sum(p1st[:, 4 + g:5 + g], hpre[:, g * 1024:(g + 1) * 1024], axis=AX.X)
            v.drain()
            v.tensor_single_scalar(p1st[:, 8:12], p1st[:, 0:4], 1.0 / D, OP.mult)
            v.tensor_single_scalar(p1st[:, 12:16], p1st[:, 4:8], 1.0 / D, OP.mult)
            v.drain()
            v.tensor_tensor(p1st[:, 16:20], p1st[:, 8:12], p1st[:, 8:12], OP.mult)
            v.drain()
            v.tensor_tensor(p1st[:, 16:20], p1st[:, 12:16], p1st[:, 16:20], OP.subtract)
            v.drain()
            v.tensor_single_scalar(p1st[:, 16:20], p1st[:, 16:20], EPS, OP.add)
            S.inc("p1q", v.drain())

        @blk.scalar
        def _(sc):
            S.wait(sc, "p1q")
            S.inc("p1r", sc.activation(p1st[:, 20:24], p1st[:, 16:20], AF.Sqrt))

        @blk.vector
        def _(v, tt=tt, xo=xo):
            S.wait(v, "p1r")
            v.reciprocal(p1st[:, 24:28], p1st[:, 20:24])
            v.drain()
            for g in range(4):
                xg = xin[:, xo + g * 1024: xo + (g + 1) * 1024]
                v.tensor_scalar(xg, xg, p1st[:, 8 + g:9 + g], p1st[:, 24 + g:25 + g], OP.subtract, OP.mult)
            v.drain()
            for g in range(4):
                xg = xin[:, xo + g * 1024: xo + (g + 1) * 1024]
                v.reduce_max(mxc[:, MX_XN + tt * 4 + g:MX_XN + tt * 4 + g + 1], xg,
                             axis=AX.X, apply_absolute_value=True)
                v.reduce_sum(p1st[:, 32 + g:33 + g], xg, axis=AX.X)
            S.inc("p1b", v.drain())
            S.wait(v, "p1sa", S.count("p1sa") + 1)
            for g in range(4):
                v.reduce_sum(p1st[:, 36 + g:37 + g], hpre[:, g * 1024:(g + 1) * 1024], axis=AX.X)
            v.drain()
            v.tensor_single_scalar(p1st[:, 40:44], p1st[:, 32:36], 1.0 / D, OP.mult)
            v.tensor_single_scalar(p1st[:, 44:48], p1st[:, 36:40], 1.0 / D, OP.mult)
            v.drain()
            v.tensor_tensor(p1st[:, 48:52], p1st[:, 40:44], p1st[:, 40:44], OP.mult)
            v.drain()
            v.tensor_tensor(p1st[:, 48:52], p1st[:, 44:48], p1st[:, 48:52], OP.subtract)
            v.drain()
            v.tensor_single_scalar(p1st[:, 48:52], p1st[:, 48:52], EPS, OP.add)
            S.inc("p1q", v.drain())

        @blk.scalar
        def _(sc, xo=xo):
            S.wait(sc, "p1b")
            for g in range(4):
                xg = xin[:, xo + g * 1024: xo + (g + 1) * 1024]
                sc.activation(hpre[:, g * 1024:(g + 1) * 1024], xg, AF.Square)
            S.inc("p1sa", sc.drain())

        @blk.scalar
        def _(sc):
            S.wait(sc, "p1q")
            S.inc("p1r", sc.activation(p1st[:, 52:56], p1st[:, 48:52], AF.Sqrt))

        @blk.vector
        def _(v, tt=tt, xo=xo):
            S.wait(v, "p1r")
            v.reciprocal(p1st[:, 56:60], p1st[:, 52:56])
            v.drain()
            for g in range(4):
                xg = xin[:, xo + g * 1024: xo + (g + 1) * 1024]
                v.tensor_scalar(hpre[:, g * 1024:(g + 1) * 1024], xg,
                                p1st[:, 40 + g:41 + g], p1st[:, 56 + g:57 + g], OP.subtract, OP.mult)
            S.inc("p1b", v.drain())

        p1e_base = S.count("p1e")

        @blk.tensor
        def _(t, tt=tt, xo=xo, p1e_base=p1e_base):
            S.wait(t, "p1b")
            for i in range(32):
                g, dc = i // DC, i % DC
                pslot = i % 2
                if i >= 2:
                    t.wait_ge(S.get("p1e"), p1e_base + i - 1)
                t.transpose(ptr1[:, pslot * 512:pslot * 512 + 128],
                            xin[:, xo + g * 1024 + dc * 128: xo + g * 1024 + (dc + 1) * 128], idf[:])
                S.inc("p1t", t.transpose(ptr2[:, pslot * 512:pslot * 512 + 128],
                                         hpre[:, g * 1024 + dc * 128: g * 1024 + (dc + 1) * 128], idf[:]))

        p1t_base = S.count("p1t") - 32

        so2 = buf * DC * W

        @blk.vector
        def _(v, tt=tt, so=so, so2=so2, p1t_base=p1t_base):
            if tt >= 2:
                S.wait(v, "p1oA" if tt % 2 == 0 else "p1oB")
            for i in range(32):
                g, dc = i // DC, i % DC
                pslot = i % 2
                v.wait_ge(S.get("p1t"), p1t_base + i + 1)
                v.tensor_copy(xnst[:, so + dc * W + g * 128: so + dc * W + (g + 1) * 128],
                              ptr1[:, pslot * 512:pslot * 512 + 128])
                S.inc("p1e",
                      v.tensor_scalar(hst[:, so2 + dc * W + g * 128: so2 + dc * W + (g + 1) * 128],
                                      ptr2[:, pslot * 512:pslot * 512 + 128],
                                      gb[:, G1C + dc:G1C + dc + 1], gb[:, B1C + dc:B1C + dc + 1],
                                      OP.mult, OP.add))
            S.inc("p1v", v.drain())

        @blk.sync
        def _(sy, tt=tt, so=so, so2=so2):
            S.wait(sy, "p1v")
            dst = xn_h.ap()[:, :, tt * W:(tt + 1) * W].transpose([1, 0, 2])
            S.inc("p1oA" if tt % 2 == 0 else "p1oB",
                  sy.dma_start(out=dst,
                               in_=xnst[:, so:so + DC * W].rearrange("p (c w) -> p c w", w=W)), 16)
            dsth = h_hbm.ap()[:, :, tt * W:(tt + 1) * W].transpose([1, 0, 2])
            S.inc("p1oA" if tt % 2 == 0 else "p1oB",
                  sy.dma_start(out=dsth,
                               in_=hst[:, so2:so2 + DC * W].rearrange("p (c w) -> p c w", w=W)), 16)

    @blk.vector
    def _(v):
        S.wait(v, "p1e")
        v.reduce_max(mxc[:, MX_TMP:MX_TMP + 1], mxc[:, MX_XN:MX_XN + 4 * NCH], axis=AX.X)
        S.inc("fmq", v.drain())

    finalize_scalar_max([MX_TMP], 128, 0, "fmq")
    collective(0, 0)
    derive_scales([(8, 32767.0, 0)], 16)

    wst = p1.enter_context(nc.sbuf_tensor("wst", [128, 2 * DI], F32))
    wblk = p1.enter_context(nc.sbuf_tensor("wblk", [128, 3 * (DI // 32)], F32))
    wi32 = p1.enter_context(nc.sbuf_tensor("wi32", [128, DI], I32))
    wbf16 = p1.enter_context(nc.sbuf_tensor("wbf16", [128, 2 * DI], BF16))

    dqi = 0
    for mi, (src, dst, nrow, I) in enumerate(
            [(wg_d, wg_dq, DI // 128, 1024), (wu_d, wu_dq, DI // 128, 1024), (wd_d, wd_dq, D // 128, DI)]):
        nb = I // 32
        for r in range(nrow):
            buf = dqi % 2
            ldsem = "ldA" if buf == 0 else "ldB"
            stsem = "dqoA" if buf == 0 else "dqoB"

            @blk.sync
            def _(sy, src=src, r=r, buf=buf, I=I, dqi=dqi, ldsem=ldsem):
                if dqi >= 2:
                    S.wait(sy, "dqv", S.count("dqv") - 1)
                S.inc(ldsem, sy.dma_start(out=wst[:, buf * DI: buf * DI + I],
                                          in_=src[r * 128:(r + 1) * 128, :]), 16)

            @blk.vector
            def _(v, buf=buf, I=I, nb=nb, dqi=dqi, ldsem=ldsem, stsem=stsem):
                S.wait(v, ldsem)
                if dqi >= 2:
                    S.wait(v, stsem)
                w = wst[:, buf * DI: buf * DI + I]
                w3 = w.rearrange("p (b k) -> p b k", k=32)
                bm = wblk[:, 0:nb]
                sblk = wblk[:, nb:2 * nb]
                invs = wblk[:, 2 * nb:3 * nb]
                v.reduce_max(bm, w3, axis=AX.X, apply_absolute_value=True)
                v.drain()
                v.tensor_scalar(sblk, bm, 1.0 / 7.0, 1e-12, OP.mult, OP.max)
                v.drain()
                v.reciprocal(invs, sblk)
                v.drain()
                v.tensor_tensor(w3, w3, invs.unsqueeze(2).broadcast_to([128, nb, 32]), OP.mult)
                v.drain()
                v.tensor_single_scalar(wi32[:, 0:I], w, W8H, OP.add)
                v.drain()
                v.scalar_tensor_tensor(wbf16[:, buf * DI: buf * DI + I].rearrange("p (b k) -> p b k", k=32),
                                       wi32[:, 0:I].rearrange("p (b k) -> p b k", k=32), W8,
                                       sblk.unsqueeze(2).broadcast_to([128, nb, 32]),
                                       OP.subtract, OP.mult)
                S.inc("dqv", v.drain())

            @blk.sync
            def _(sy, dst=dst, r=r, buf=buf, I=I, stsem=stsem):
                S.wait(sy, "dqv")
                S.inc(stsem, sy.dma_start(out=dst.ap()[r * 128:(r + 1) * 128, :],
                                          in_=wbf16[:, buf * DI: buf * DI + I]), 16)
            dqi += 1

    p1.close()
    barrier(("p1oA", "p1oB", "dqoA", "dqoB", "ldA", "ldB", "cst"))

    p2 = contextlib.ExitStack()
    xq_sb = p2.enter_context(nc.sbuf_tensor("xq_sb", [128, DC * T], BF16))
    p2ld = p2.enter_context(nc.sbuf_tensor("p2ld", [128, 2 * W], F32))
    p2i = p2.enter_context(nc.sbuf_tensor("p2i", [128, W], I32))
    wtT = p2.enter_context(nc.sbuf_tensor("wtT", [128, DC * DI], BF16))
    p2ev = p2.enter_context(nc.sbuf_tensor("p2ev", [128, 4 * W], F32))
    p2ps = p2.enter_context(nc.psum_tensor("p2ps", [128, 4 * W], F32))

    for i in range(DC * NCH):
        dc, tt = i // NCH, i % NCH
        buf = i % 2
        ldsem = "ldA" if buf == 0 else "ldB"

        @blk.sync
        def _(sy, dc=dc, tt=tt, buf=buf, i=i, ldsem=ldsem):
            if i == 0:
                S.wait(sy, "p1oA")
                S.wait(sy, "p1oB")
            if i >= 2:
                S.wait(sy, "xqv", S.count("xqv") - 1)
            S.inc(ldsem, sy.dma_start(out=p2ld[:, buf * W:(buf + 1) * W],
                                      in_=xn_h.ap()[dc, :, tt * W:(tt + 1) * W]), 16)

        @blk.vector
        def _(v, dc=dc, tt=tt, buf=buf, i=i, ldsem=ldsem):
            S.wait(v, ldsem)
            if i == 0:
                S.wait(v, "dsv")
            xg = p2ld[:, buf * W:(buf + 1) * W]
            v.tensor_scalar(p2i[:, :], xg, scl[:, 1:2], KQH, OP.mult, OP.add)
            v.drain()
            v.tensor_scalar(xq_sb[:, dc * T + tt * W: dc * T + (tt + 1) * W], p2i[:, :],
                            KQ, scl[:, 0:1], OP.subtract, OP.mult)
            S.inc("xqv", v.drain())

    for mi, (wdq, dsth, mbase) in enumerate([(wu_dq, up_h, MX_UP), (wg_dq, gr_h, MX_GA)]):
        @blk.sync
        def _(sy, wdq=wdq, mi=mi):
            if mi == 0:
                S.wait(sy, "dqoA")
                S.wait(sy, "dqoB")
            else:
                S.wait(sy, "p2m")
            for dc in range(DC):
                S.inc("wTA", sy.dma_start(out=wtT[:, dc * DI:(dc + 1) * DI],
                                          in_=wdq.ap()[:, dc * 128:(dc + 1) * 128], transpose=True), 16)

        NT = OCF * NCH
        m0 = S.count("p2m")
        s0_ = S.count("p2s")
        e0 = S.count("p2e")
        o0 = [S.count("p2o%d" % s4) for s4 in range(4)]
        for j in range(NT):
            oc, tt = j // NCH, j % NCH
            pslot = j % 4

            @blk.tensor
            def _(t, mi=mi, oc=oc, tt=tt, pslot=pslot, j=j, s0_=s0_):
                if j == 0:
                    S.wait(t, "wTA")
                    if mi == 0:
                        S.wait(t, "xqv")
                if mi > 0 or j >= 4:
                    t.wait_ge(S.get("p2s"), s0_ + j - 3)
                ps = p2ps[:, pslot * W:(pslot + 1) * W]
                for dc in range(DC):
                    mm = t.matmul(ps, wtT[:, dc * DI + oc * 128: dc * DI + (oc + 1) * 128],
                                  xq_sb[:, dc * T + tt * W: dc * T + (tt + 1) * W],
                                  start=(dc == 0), stop=(dc == DC - 1))
                S.inc("p2m", mm)

            if j % 8 == 7 or j == NT - 1:
                jlo = (j // 8) * 8

                @blk.scalar
                def _(sc, mi=mi, jlo=jlo, jhi=j, m0=m0, e0=e0, o0=o0):
                    for jj in range(jlo, jhi + 1):
                        ps4 = jj % 4
                        sc.wait_ge(S.get("p2m"), m0 + jj + 1)
                        if mi > 0 or jj >= 4:
                            sc.wait_ge(S.get("p2e"), e0 + jj - 3)
                            sc.wait_ge(S.get("p2o%d" % ps4), o0[ps4] + 16 * ((jj - 4) // 4 + 1))
                        S.inc("p2s", sc.copy(p2ev[:, ps4 * W:(ps4 + 1) * W],
                                             p2ps[:, ps4 * W:(ps4 + 1) * W]))

            @blk.vector
            def _(v, pslot=pslot, mbase=mbase, j=j, s0_=s0_):
                v.wait_ge(S.get("p2s"), s0_ + j + 1)
                S.inc("p2e", v.reduce_max(mxc[:, mbase + j:mbase + j + 1],
                                          p2ev[:, pslot * W:(pslot + 1) * W],
                                          axis=AX.X, apply_absolute_value=True))

            @blk.sync
            def _(sy, oc=oc, tt=tt, pslot=pslot, dsth=dsth, j=j, e0=e0):
                sy.wait_ge(S.get("p2e"), e0 + j + 1)
                S.inc("p2o%d" % pslot,
                      sy.dma_start(out=dsth.ap()[oc, :, tt * W:(tt + 1) * W],
                                   in_=p2ev[:, pslot * W:(pslot + 1) * W]), 16)

    @blk.vector
    def _(v):
        S.wait(v, "p2e")
        v.reduce_max(mxc[:, MX_TMP:MX_TMP + 1], mxc[:, MX_UP:MX_UP + OCF * NCH], axis=AX.X)
        v.reduce_max(mxc[:, MX_TMP + 1:MX_TMP + 2], mxc[:, MX_GA:MX_GA + OCF * NCH], axis=AX.X)
        S.inc("fmq", v.drain())

    finalize_scalar_max([MX_TMP, MX_TMP + 1], 128, 32, "fmq")
    collective(1, 32)
    derive_scales([(40, 32767.0, 2), (41, 32767.0, 4)], 48)

    p2.close()
    barrier(("p2o0", "p2o1", "p2o2", "p2o3", "ldA", "ldB", "wTA"))

    f2 = contextlib.ExitStack()
    fld = f2.enter_context(nc.sbuf_tensor("fld", [128, 4 * W], F32))
    fi32 = f2.enter_context(nc.sbuf_tensor("fi32", [128, W], I32))
    fgate = f2.enter_context(nc.sbuf_tensor("fgate", [128, 4 * W], F32))
    ft2 = f2.enter_context(nc.sbuf_tensor("ft2", [128, 4 * W], F32))

    def emit_f2_load(j):
        if j >= OCF * NCH:
            return
        oc, tt = j // NCH, j % NCH
        buf = j % 4

        @blk.sync
        def _(sy):
            if j == 0:
                for s4 in range(4):
                    S.wait(sy, "p2o%d" % s4)
            if j >= 4:
                S.wait(sy, "f2v2", j - 3)
            S.inc("f2l%d" % buf, sy.dma_start(out=fld[:, buf * W:(buf + 1) * W],
                                              in_=gr_h.ap()[oc, :, tt * W:(tt + 1) * W]), 16)

    def emit_f2_tile(j):
        oc, tt = j // NCH, j % NCH
        buf = j % 4

        @blk.vector
        def _(v):
            S.wait(v, "f2l%d" % buf, 16 * (j // 4 + 1))
            if j == 0:
                S.wait(v, "dsv")
            if j >= 4:
                S.wait(v, "f2sg", j - 3)
                S.wait(v, "f2s%d" % buf)
            v.tensor_scalar(fi32[:, :], fld[:, buf * W:(buf + 1) * W], scl[:, 5:6], KQH,
                            OP.mult, OP.add)
            v.drain()
            v.tensor_scalar(fgate[:, buf * W:(buf + 1) * W], fi32[:, :], KQ, scl[:, 4:5],
                            OP.subtract, OP.mult)
            S.inc("f2v1", v.drain())

        if buf == 3 or j == OCF * NCH - 1:
            @blk.scalar
            def _(sc):
                for jj in range(j - buf, j + 1):
                    bb = jj % 4
                    sc.wait_ge(S.get("f2v1"), jj + 1)
                    S.inc("f2sg", sc.activation(fld[:, bb * W:(bb + 1) * W],
                                                fgate[:, bb * W:(bb + 1) * W], AF.Tanh, scale=0.5))

        @blk.vector
        def _(v):
            S.wait(v, "f2sg", j + 1)
            v.tensor_scalar(fi32[:, :], fld[:, buf * W:(buf + 1) * W], 32768.0, 32768.0 + SGH,
                            OP.mult, OP.add)
            v.drain()
            v.tensor_scalar(fld[:, buf * W:(buf + 1) * W], fi32[:, :], 65535.0, 1.0 / 65536.0,
                            OP.min, OP.mult)
            v.drain()
            v.tensor_tensor(ft2[:, buf * W:(buf + 1) * W], fgate[:, buf * W:(buf + 1) * W],
                            fld[:, buf * W:(buf + 1) * W], OP.mult)
            v.drain()
            S.inc("f2v2", v.reduce_max(mxc[:, MX_UP + j:MX_UP + j + 1], ft2[:, buf * W:(buf + 1) * W],
                                       axis=AX.X, apply_absolute_value=True))

        @blk.sync
        def _(sy):
            S.wait(sy, "f2v2")
            S.inc("f2s%d" % buf, sy.dma_start(out=t2_h.ap()[oc, :, tt * W:(tt + 1) * W],
                                              in_=ft2[:, buf * W:(buf + 1) * W]), 16)

    def emit_f3_load(j):
        if j >= OCF * NCH:
            return
        oc, tt = j // NCH, j % NCH
        buf = j % 4

        @blk.sync
        def _(sy):
            if j == 0:
                for s4 in range(4):
                    S.wait(sy, "f2s%d" % s4)
            if j >= 4:
                S.wait(sy, "f3v", j - 3)
            S.inc("f3l%d" % buf, sy.dma_start(out=fld[:, buf * W:(buf + 1) * W],
                                              in_=t2_h.ap()[oc, :, tt * W:(tt + 1) * W]), 16)
            S.inc("f3l%d" % buf, sy.dma_start(out=fgate[:, buf * W:(buf + 1) * W],
                                              in_=up_h.ap()[oc, :, tt * W:(tt + 1) * W]), 16)

    def emit_f3_tile(j):
        oc, tt = j // NCH, j % NCH
        buf = j % 4

        @blk.vector
        def _(v):
            S.wait(v, "f3l%d" % buf, 32 * (j // 4 + 1))
            if j == 0:
                S.wait(v, "dsv")
            if j >= 4:
                S.wait(v, "f3s%d" % buf)
            v.tensor_scalar(fi32[:, :], fld[:, buf * W:(buf + 1) * W], scl[:, 7:8], KQH,
                            OP.mult, OP.add)
            v.drain()
            v.tensor_scalar(fld[:, buf * W:(buf + 1) * W], fi32[:, :], KQ, scl[:, 6:7],
                            OP.subtract, OP.mult)
            v.drain()
            v.tensor_scalar(fi32[:, :], fgate[:, buf * W:(buf + 1) * W], scl[:, 3:4], KQH,
                            OP.mult, OP.add)
            v.drain()
            v.tensor_scalar(fgate[:, buf * W:(buf + 1) * W], fi32[:, :], KQ, scl[:, 2:3],
                            OP.subtract, OP.mult)
            v.drain()
            v.tensor_tensor(ft2[:, buf * W:(buf + 1) * W], fld[:, buf * W:(buf + 1) * W],
                            fgate[:, buf * W:(buf + 1) * W], OP.mult)
            v.drain()
            S.inc("f3v", v.reduce_max(mxc[:, MX_GA + j:MX_GA + j + 1], ft2[:, buf * W:(buf + 1) * W],
                                      axis=AX.X, apply_absolute_value=True))

        @blk.sync
        def _(sy):
            S.wait(sy, "f3v")
            S.inc("f3s%d" % buf, sy.dma_start(out=or_h.ap()[oc, :, tt * W:(tt + 1) * W],
                                              in_=ft2[:, buf * W:(buf + 1) * W]), 16)

    p3 = contextlib.ExitStack()
    hstg = p3.enter_context(nc.sbuf_tensor("hstg", [128, 2 * DC * W], BF16))
    wqkvT = p3.enter_context(nc.sbuf_tensor("wqkvT", [128, DC * 3 * D], BF16))
    woT = p3.enter_context(nc.sbuf_tensor("woT", [128, DC * D], BF16))
    qf = p3.enter_context(nc.sbuf_tensor("qf", [128, DC * W], BF16))
    kf2 = p3.enter_context(nc.sbuf_tensor("kf2", [128, 2 * DC * W], BF16))
    vsb2 = p3.enter_context(nc.sbuf_tensor("vsb2", [128, 2 * 4 * H * 65], BF16))
    avn = p3.enter_context(nc.sbuf_tensor("avn", [128, DC * W], BF16))
    xnstg = p3.enter_context(nc.sbuf_tensor("xnstg", [128, 2 * W], F32))
    x2stg = p3.enter_context(nc.sbuf_tensor("x2stg", [128, 2 * W], F32))
    pTs = p3.enter_context(nc.sbuf_tensor("pTs", [128, 2 * 768], BF16))
    q2st = p3.enter_context(nc.sbuf_tensor("q2st", [128, 2 * W], F32))

    @blk.sync
    def _(sy):
        S.wait(sy, "cast1")
        for dc in range(DC):
            S.inc("wTA", sy.dma_start(out=wqkvT[:, dc * 3 * D:(dc + 1) * 3 * D],
                                      in_=wqkv_bf.ap()[:, dc * 128:(dc + 1) * 128], transpose=True), 16)
        for dc in range(DC):
            S.inc("wTA", sy.dma_start(out=woT[:, dc * D:(dc + 1) * D],
                                      in_=wo_bf.ap()[:, dc * 128:(dc + 1) * 128], transpose=True), 16)

    for c in range(NCH):
        cur, prv = c % 2, (c + 1) % 2
        VS = 4 * H * 65
        p3qm_base = S.count("p3qm")

        @blk.sync
        def _(sy, c=c, cur=cur, p3qm_base=p3qm_base):
            if c == 0:
                S.wait(sy, "p1oA")
                S.wait(sy, "p1oB")
            if c >= 2:
                S.wait(sy, "p3qm", p3qm_base - 24)
            S.inc("hlA" if cur == 0 else "hlB",
                  sy.dma_start(out=hstg[:, cur * DC * W: (cur + 1) * DC * W]
                               .rearrange("p (k w) -> p k w", w=W),
                               in_=h_hbm.ap()[:, :, c * W:(c + 1) * W].transpose([1, 0, 2])), 16)

        with nc.psum_tensor("pqk_%d" % c, [128, 2 * W], F32) as pqk:
            for wi, rowbase in ((0, 0), (1, D)):
                for oc in range(DC):
                    pslot = (wi * DC + oc) % 2

                    @blk.tensor
                    def _(t, c=c, cur=cur, wi=wi, rowbase=rowbase, oc=oc, pslot=pslot, pqk=pqk):
                        if c == 0 and wi == 0 and oc == 0:
                            S.wait(t, "wTA")
                        if wi == 0 and oc == 0:
                            S.wait(t, "hlA" if cur == 0 else "hlB")
                        if c > 0 and wi == 0 and oc == 0:
                            S.wait(t, "p3wv")
                        if S.count("p3qe") >= 2:
                            S.wait(t, "p3qe", S.count("p3qe") - 1)
                        ps = pqk[:, pslot * W:(pslot + 1) * W]
                        for dc in range(DC):
                            mm = t.matmul(ps, wqkvT[:, dc * 3 * D + rowbase + oc * 128:
                                                    dc * 3 * D + rowbase + (oc + 1) * 128],
                                          hstg[:, cur * DC * W + dc * W: cur * DC * W + (dc + 1) * W],
                                          start=(dc == 0), stop=(dc == DC - 1))
                        S.inc("p3qm", mm)

                    @blk.vector
                    def _(v, c=c, wi=wi, oc=oc, pslot=pslot, cur=cur, pqk=pqk):
                        S.wait(v, "p3qm")
                        if c > 0 and wi == 0 and oc == 0:
                            S.wait(v, "p3xm")
                        if wi == 0:
                            o = v.tensor_copy(qf[:, oc * W:(oc + 1) * W], pqk[:, pslot * W:(pslot + 1) * W])
                        else:
                            o = v.tensor_copy(kf2[:, cur * DC * W + oc * W: cur * DC * W + (oc + 1) * W],
                                              pqk[:, pslot * W:(pslot + 1) * W])
                        S.inc("p3qe", o)

        @blk.vector
        def _(v, c=c, cur=cur, VS=VS):
            S.wait(v, "p3av")
            v.memset(vsb2[:, cur * VS:(cur + 1) * VS].rearrange("p (s o) -> p s o", o=65)[:, :, 64:65], 1.0)
            S.inc("p3vo", v.drain())

        with contextlib.ExitStack() as vsc:
            pv8 = [vsc.enter_context(nc.psum_tensor("pv8_%d_%d" % (c, b), [128, 64], F32)) for b in range(8)]
            for g in range(4):
                for hg in range(2):
                    @blk.tensor
                    def _(t, c=c, g=g, hg=hg, pv8=pv8):
                        S.wait(t, "p3qe")
                        if S.count("p3ve") > 0:
                            S.wait(t, "p3ve")
                        for dc in range(DC):
                            hb = hstg[:, cur * DC * W + dc * W + g * 128: cur * DC * W + dc * W + (g + 1) * 128]
                            for hh in range(8):
                                h = hg * 8 + hh
                                mm = t.matmul(pv8[hh][:, 0:64], hb,
                                              wqkvT[:, dc * 3 * D + 2 * D + h * 64:
                                                    dc * 3 * D + 2 * D + (h + 1) * 64],
                                              start=(dc == 0), stop=(dc == DC - 1))
                        S.inc("p3qm", mm)

                    @blk.vector
                    def _(v, c=c, g=g, hg=hg, cur=cur, pv8=pv8, VS=VS):
                        S.wait(v, "p3qm")
                        S.wait(v, "p3vo")
                        for hh in range(8):
                            h = hg * 8 + hh
                            S.inc("p3ve", v.tensor_copy(
                                vsb2[:, cur * VS + (g * H + h) * 65: cur * VS + (g * H + h) * 65 + 64],
                                pv8[hh][:, 0:64]))

        with nc.psum_tensor("pnm_%d" % c, [128, W], F32) as pnm:
            its = [(wi, dc, mbase) for wi, mbase in ((0, MX_QN), (1, MX_KN + cur * 8))
                   for dc in range(DC)]
            nt_base = S.count("p3nt")
            sq_base = S.count("p3sq")

            @blk.scalar
            def _(sc, its=its, cur=cur, nt_base=nt_base):
                S.wait(sc, "p3qe")
                for i, (wi, dc, mbase) in enumerate(its):
                    if i >= 2:
                        sc.wait_ge(S.get("p3nt"), nt_base + i - 1)
                    src = (qf[:, dc * W:(dc + 1) * W] if wi == 0
                           else kf2[:, cur * DC * W + dc * W: cur * DC * W + (dc + 1) * W])
                    S.inc("p3sq", sc.activation(q2st[:, (i % 2) * W:(i % 2 + 1) * W], src, AF.Square))

            for i, (wi, dc, mbase) in enumerate(its):
                @blk.tensor
                def _(t, i=i, wi=wi, dc=dc, pnm=pnm, sq_base=sq_base):
                    t.wait_ge(S.get("p3sq"), sq_base + i + 1)
                    if wi == 0 and dc == 0:
                        S.wait(t, "p3ve")
                    S.wait(t, "p3nr")
                    S.inc("p3nt", t.matmul(pnm[0:2, 0:W], onepair[:, 0:2],
                                           q2st[:, (i % 2) * W:(i % 2 + 1) * W],
                                           start=True, stop=True))

                @blk.vector
                def _(v, i=i, mbase=mbase, dc=dc, pnm=pnm):
                    S.wait(v, "p3nt")
                    S.inc("p3nr", v.reduce_max(mxc[0:2, mbase + dc:mbase + dc + 1], pnm[0:2, 0:W],
                                               axis=AX.X))

            @blk.vector
            def _(v, c=c, cur=cur):
                S.wait(v, "p3nr")
                v.reduce_max(nrm[0:2, 0:1], mxc[0:2, MX_QN:MX_QN + 8], axis=AX.X)
                if c > 0:
                    v.reduce_max(nrm[0:2, 1:2], mxc[0:2, MX_KN:MX_KN + 16], axis=AX.X)
                else:
                    v.reduce_max(nrm[0:2, 1:2], mxc[0:2, MX_KN + cur * 8:MX_KN + cur * 8 + 8], axis=AX.X)
                S.inc("p3na", v.drain())

            @blk.tensor
            def _(t, pnm=pnm):
                S.wait(t, "p3na")
                S.inc("p3nt", t.transpose(pnm[0:2, 128:130], nrm[0:2, 0:2], idf[0:2, 0:2]))

            @blk.vector
            def _(v, pnm=pnm):
                S.wait(v, "p3nt")
                v.reduce_max(nrm[0:2, 2:3], pnm[0:2, 128:130], axis=AX.X)
                S.inc("p3na", v.drain())

            @blk.tensor
            def _(t, pnm=pnm):
                S.wait(t, "p3na")
                S.inc("p3nt", t.transpose(pnm[0:1, 130:132], nrm[0:2, 2:3], idf[0:2, 0:2]))

            @blk.vector
            def _(v, pnm=pnm):
                S.wait(v, "p3nt")
                v.reduce_sum(nrm[0:1, 3:4], pnm[0:1, 130:132], axis=AX.X)
                v.drain()
                v.tensor_single_scalar(nrm[0:1, 3:4], nrm[0:1, 3:4], -0.5 * SCALE, OP.mult)
                S.inc("p3na", v.drain())

            @blk.tensor
            def _(t, pnm=pnm):
                S.wait(t, "p3na")
                S.inc("p3nt", t.matmul(pnm[:, 132:133], ones[0:1, 0:128], nrm[0:1, 3:4],
                                       start=True, stop=True))

            @blk.vector
            def _(v, pnm=pnm):
                S.wait(v, "p3nt")
                S.wait(v, "p3xs")
                S.inc("p3nf", v.tensor_copy(mxc[:, MX_M:MX_M + 1], pnm[:, 132:133]))

        with nc.psum_tensor("pSa_%d" % c, [128, 768], F32) as pSa, \
             nc.psum_tensor("pSb_%d" % c, [128, 768], F32) as pSb, \
             nc.psum_tensor("pAVa_%d" % c, [65, W], F32) as pAVa, \
             nc.psum_tensor("pAVb_%d" % c, [65, W], F32) as pAVb, \
             nc.psum_tensor("pCB_%d" % c, [128, 2 * W], F32) as pCB:

            for dc in range(DC):
                xs_base = S.count("p3xs")
                xm_base = S.count("p3xm")
                av_base = S.count("p3av")
                eblk = []
                k = 0
                for qi2 in range(2):
                    g0 = 4 * c + 2 * qi2
                    kts = [G for G in range(g0 - 4, g0 + 2) if G >= 0]
                    nkt = len(kts)
                    j0 = 6 - nkt
                    groups = [(gi, kts[gi:gi + 3]) for gi in range(0, nkt, 3)]

                    for gi, gkts in groups:
                        last = (gi + len(gkts) == nkt)

                        @blk.tensor
                        def _(t, c=c, dc=dc, qi2=qi2, gi=gi, gkts=gkts, cur=cur, prv=prv,
                              pSa=pSa, pSb=pSb, k=k, xs_base=xs_base):
                            S.wait(t, "p3nf")
                            if k == 0:
                                S.wait(t, "p3xs")
                            else:
                                t.wait_ge(S.get("p3xs"), xs_base + 2 * k)
                            if dc == 0 and qi2 == 0 and gi == 0:
                                S.wait(t, "p3ve")
                            for aa, ps in ((0, pSa), (64, pSb)):
                                for j, G in enumerate(gkts):
                                    buf = cur if G >= 4 * c else prv
                                    kcol = buf * DC * W + dc * W + (G % 4) * 128
                                    mm = t.matmul(ps[:, j * 256:(j + 1) * 256],
                                                  kf2[aa:aa + 64, kcol:kcol + 128],
                                                  qf[aa:aa + 64, dc * W + qi2 * 256: dc * W + qi2 * 256 + 256],
                                                  start=True, stop=True)
                            S.inc("p3xm", mm)

                        @blk.vector
                        def _(v, gi=gi, gkts=gkts, j0=j0, k=k, xs_base=xs_base):
                            v.wait_ge(S.get("p3xs"), xs_base + 2 * (k + 1))
                            for aa in range(2):
                                for j in range(len(gkts)):
                                    jr = j0 + gi + j
                                    if jr in (0, 1, 4, 5):
                                        mslot = {0: 0, 1: 1, 4: 2, 5: 3}[jr]
                                        v.tensor_tensor(pTs[:, aa * 768 + j * 256: aa * 768 + (j + 1) * 256],
                                                        pTs[:, aa * 768 + j * 256: aa * 768 + (j + 1) * 256],
                                                        mq[:, mslot * 256:(mslot + 1) * 256],
                                                        OP.mult)
                            S.inc("p3xv", v.drain())

                        @blk.tensor
                        def _(t, c=c, dc=dc, qi2=qi2, gi=gi, gkts=gkts, cur=cur, prv=prv,
                              nkt=nkt, last=last, pAVa=pAVa, pAVb=pAVb, VS=VS):
                            S.wait(t, "p3xv")
                            if qi2 == 0 and gi == 0:
                                S.wait(t, "p3xe")
                            for aa, pav in ((0, pAVa), (1, pAVb)):
                                h = 2 * dc + aa
                                for j, G in enumerate(gkts):
                                    buf = cur if G >= 4 * c else prv
                                    vcol = buf * VS + ((G % 4) * H + h) * 65
                                    mm = t.matmul(pav[0:65, qi2 * 256:(qi2 + 1) * 256],
                                                  vsb2[:, vcol:vcol + 65],
                                                  pTs[:, aa * 768 + j * 256: aa * 768 + (j + 1) * 256],
                                                  start=(gi == 0 and j == 0),
                                                  stop=(last and j == len(gkts) - 1))
                            S.inc("p3av", mm)

                        eblk.append((k, gkts))
                        k += 1

                @blk.scalar
                def _(sc, eblk=eblk, xm_base=xm_base, av_base=av_base, pSa=pSa, pSb=pSb):
                    for k, gkts in eblk:
                        sc.wait_ge(S.get("p3xm"), xm_base + k + 1)
                        sc.wait_ge(S.get("p3av"), av_base + k)
                        n = len(gkts)
                        for aa, ps in ((0, pSa), (1, pSb)):
                            S.inc("p3xs", sc.activation(pTs[:, aa * 768: aa * 768 + n * 256],
                                                        ps[:, 0:n * 256], AF.Exp,
                                                        bias=mxc[:, MX_M:MX_M + 1], scale=SCALE))

                for aa, pav in ((0, pAVa), (1, pAVb)):
                    @blk.vector
                    def _(v, dc=dc, aa=aa, pav=pav):
                        S.wait(v, "p3av")
                        S.wait(v, "l3s")
                        h = 2 * dc + aa
                        po = (h % 2) * 64
                        v.tensor_copy(flat[64:65, aa * W:(aa + 1) * W], pav[64:65, 0:W])
                        v.drain()
                        S.inc("p3xe", v.tensor_copy(avn[po:po + 64, (h // 2) * W:(h // 2 + 1) * W],
                                                    pav[0:64, 0:W]))

                @blk.sync
                def _(sy, dc=dc):
                    S.wait(sy, "p3xe")
                    S.wait(sy, "dqoA")
                    S.wait(sy, "dqoB")
                    for aa in range(2):
                        S.inc("l3s", sy.dma_start(out=flat[2 * dc + aa:2 * dc + aa + 1, 1024:1024 + W],
                                                  in_=flat[64:65, aa * W:(aa + 1) * W]), 16)

            @blk.vector
            def _(v):
                S.wait(v, "l3s")
                v.reciprocal(flat[0:16, 1024:1024 + W], flat[0:16, 1024:1024 + W])
                S.inc("p3nr", v.drain())

            lb_base = S.count("p3lb")
            for dc in range(DC):
                for aa in range(2):
                    r = 2 * dc + aa
                    s2 = r % 2

                    @blk.sync
                    def _(sy, r=r, s2=s2, lb_base=lb_base):
                        S.wait(sy, "p3nr")
                        S.wait(sy, "p3lb", lb_base + r - 1 if r >= 2 else lb_base)
                        S.inc("dqoA" if s2 == 0 else "dqoB",
                              sy.dma_start(out=flat[64:65, 1024 + s2 * W:1024 + (s2 + 1) * W],
                                           in_=flat[r:r + 1, 1024:1024 + W]), 16)

                    @blk.tensor
                    def _(t, aa=aa, s2=s2, pCB=pCB):
                        S.wait(t, "dqoA" if s2 == 0 else "dqoB")
                        S.wait(t, "p3xv")
                        S.inc("p3lb", t.matmul(pCB[:, aa * W:(aa + 1) * W], ones[64:65, 0:128],
                                               flat[64:65, 1024 + s2 * W:1024 + (s2 + 1) * W],
                                               start=True, stop=True))

                    @blk.vector
                    def _(v, dc=dc, aa=aa, pCB=pCB):
                        S.wait(v, "p3lb")
                        S.wait(v, "p3xe")
                        h = 2 * dc + aa
                        po = (h % 2) * 64
                        S.inc("p3xv", v.tensor_tensor(avn[po:po + 64, (h // 2) * W:(h // 2 + 1) * W],
                                                      avn[po:po + 64, (h // 2) * W:(h // 2 + 1) * W],
                                                      pCB[po:po + 64, aa * W:(aa + 1) * W], OP.mult))

        with nc.psum_tensor("pwo_%d" % c, [128, 2 * W], F32) as pwo:
            for oc in range(DC):
                pslot = oc % 2
                ldsem = "ldA" if pslot == 0 else "ldB"

                @blk.sync
                def _(sy, c=c, oc=oc, pslot=pslot, ldsem=ldsem):
                    if S.count("p3wv") >= 2:
                        S.wait(sy, "p3wv", S.count("p3wv") - 1)
                    S.inc(ldsem, sy.dma_start(out=xnstg[:, pslot * W:(pslot + 1) * W],
                                              in_=xn_h.ap()[oc, :, c * W:(c + 1) * W]), 16)

                @blk.tensor
                def _(t, c=c, oc=oc, pslot=pslot, pwo=pwo):
                    S.wait(t, "p3xv")
                    if S.count("p3wv") >= 2:
                        S.wait(t, "p3wv", S.count("p3wv") - 1)
                    ps = pwo[:, pslot * W:(pslot + 1) * W]
                    for dc in range(DC):
                        mm = t.matmul(ps, woT[:, dc * D + oc * 128: dc * D + (oc + 1) * 128],
                                      avn[:, dc * W:(dc + 1) * W],
                                      start=(dc == 0), stop=(dc == DC - 1))
                    S.inc("p3qm", mm)

                @blk.vector
                def _(v, pslot=pslot, ldsem=ldsem, pwo=pwo):
                    S.wait(v, "p3qm")
                    S.wait(v, ldsem)
                    if S.count("p3wv") >= 2:
                        S.wait(v, "p3wA" if pslot == 0 else "p3wB")
                    S.inc("p3wv", v.tensor_tensor(x2stg[:, pslot * W:(pslot + 1) * W],
                                                  pwo[:, pslot * W:(pslot + 1) * W],
                                                  xnstg[:, pslot * W:(pslot + 1) * W], OP.add))

                @blk.sync
                def _(sy, c=c, oc=oc, pslot=pslot):
                    S.wait(sy, "p3wv")
                    S.inc("p3wA" if pslot == 0 else "p3wB",
                          sy.dma_start(out=x2_h.ap()[oc, :, c * W:(c + 1) * W],
                                       in_=x2stg[:, pslot * W:(pslot + 1) * W]), 16)

        for jf in range(c * OCF, (c + 1) * OCF):
            if c == 0 and jf == 0:
                for jl in range(4):
                    emit_f2_load(jl)
            emit_f2_load(jf + 4)
            emit_f2_tile(jf)

    p3.close()

    @blk.vector
    def _(v):
        S.wait(v, "f2v2")
        v.reduce_max(mxc[:, MX_TMP:MX_TMP + 1], mxc[:, MX_UP:MX_UP + OCF * NCH], axis=AX.X)
        S.inc("fmq", v.drain())

    finalize_scalar_max([MX_TMP], 128, 64, "fmq")
    collective(2, 64)
    derive_scales([(72, 32767.0, 6)], 80)

    barrier(("p3wA", "p3wB", "f2s0", "f2s1", "f2s2", "f2s3", "f2l0", "f2l1", "f2l2", "f2l3", "ldA", "ldB", "hlA", "hlB", "wTA"))

    p4 = contextlib.ExitStack()
    w1T = p4.enter_context(nc.sbuf_tensor("w1T", [128, DC * DFF], BF16))
    w2T = p4.enter_context(nc.sbuf_tensor("w2T", [128, 32 * D], BF16))
    x2t = p4.enter_context(nc.sbuf_tensor("x2t", [128, 2 * DC * 256], F32))
    h2t = p4.enter_context(nc.sbuf_tensor("h2t", [128, DC * 256], BF16))
    g1buf = p4.enter_context(nc.sbuf_tensor("g1buf", [128, 32 * 256], BF16))
    sqstg = p4.enter_context(nc.sbuf_tensor("sqstg", [128, 2 * 256], F32))
    pmm = p4.enter_context(nc.psum_tensor("pmm", [128, 2 * W], F32))
    pstat = p4.enter_context(nc.psum_tensor("pstat", [2, 2 * W], F32))
    pab = p4.enter_context(nc.psum_tensor("pab", [128, 2 * W], F32))

    @blk.sync
    def _(sy):
        S.wait(sy, "cast1")
        S.wait(sy, "cast2")
        for dc in range(DC):
            S.inc("wTA", sy.dma_start(out=w1T[:, dc * DFF:(dc + 1) * DFF],
                                      in_=w1_bf.ap()[:, dc * 128:(dc + 1) * 128], transpose=True), 16)
        for kc in range(32):
            S.inc("wTA", sy.dma_start(out=w2T[:, kc * D:(kc + 1) * D],
                                      in_=w2_bf.ap()[:, kc * 128:(kc + 1) * 128], transpose=True), 16)

    WH = W // 2
    for tt in range(2 * NCH):
        xob = (tt % 2) * DC * 256
        @blk.sync
        def _(sy, tt=tt, xob=xob):
            if tt == 0:
                S.wait(sy, "p3wA")
                S.wait(sy, "p3wB")
            if tt >= 2:
                S.wait(sy, "p4av", S.count("p4av") - DC)
            for dc in range(DC):
                S.inc("ldA", sy.dma_start(out=x2t[:, xob + dc * WH:xob + (dc + 1) * WH],
                                          in_=x2_h.ap()[dc, :, tt * WH:(tt + 1) * WH]), 16)

        st_base = S.count("p4st")
        sq4_base = S.count("p4sq")

        @blk.scalar
        def _(sc, tt=tt, xob=xob, st_base=st_base):
            S.wait(sc, "ldA")
            for dc in range(DC):
                if st_base + dc >= 2:
                    sc.wait_ge(S.get("p4st"), st_base + dc - 1)
                S.inc("p4sq", sc.activation(sqstg[:, (dc % 2) * WH:(dc % 2 + 1) * WH],
                                            x2t[:, xob + dc * WH:xob + (dc + 1) * WH], AF.Square))

        for dc in range(DC):
            @blk.tensor
            def _(t, tt=tt, dc=dc, xob=xob, sq4_base=sq4_base):
                t.wait_ge(S.get("p4sq"), sq4_base + dc + 1)
                if tt == 0 and dc == 0:
                    S.wait(t, "wTA")
                if dc == 0:
                    S.wait(t, "ldA")
                    if tt > 0:
                        S.wait(t, "p4va")
                t.matmul(pstat[0:1, 0:WH], ones[:, 0:1], x2t[:, xob + dc * WH:xob + (dc + 1) * WH],
                         start=(dc == 0), stop=(dc == DC - 1))
                S.inc("p4st", t.matmul(pstat[0:1, W:W + WH], ones[:, 0:1],
                                       sqstg[:, (dc % 2) * WH:(dc % 2 + 1) * WH],
                                       start=(dc == 0), stop=(dc == DC - 1)))

        FB = 128

        @blk.vector
        def _(v, tt=tt):
            S.wait(v, "p4st")
            v.tensor_single_scalar(flat[0:1, FB:FB + WH], pstat[0:1, 0:WH], 1.0 / D, OP.mult)
            v.tensor_single_scalar(flat[0:1, FB + WH:FB + 2 * WH], pstat[0:1, W:W + WH], 1.0 / D, OP.mult)
            v.drain()
            v.tensor_tensor(flat[0:1, FB + 2 * WH:FB + 3 * WH], flat[0:1, FB:FB + WH],
                            flat[0:1, FB:FB + WH], OP.mult)
            v.drain()
            v.tensor_tensor(flat[0:1, FB + 2 * WH:FB + 3 * WH], flat[0:1, FB + WH:FB + 2 * WH],
                            flat[0:1, FB + 2 * WH:FB + 3 * WH], OP.subtract)
            v.drain()
            v.tensor_single_scalar(flat[0:1, FB + 2 * WH:FB + 3 * WH], flat[0:1, FB + 2 * WH:FB + 3 * WH],
                            EPS, OP.add)
            S.inc("p4va", v.drain())

        @blk.scalar
        def _(sc):
            S.wait(sc, "p4va")
            S.inc("p4sr", sc.activation(flat[0:1, FB + 3 * WH:FB + 4 * WH],
                                        flat[0:1, FB + 2 * WH:FB + 3 * WH], AF.Sqrt))

        @blk.vector
        def _(v):
            S.wait(v, "p4sr")
            v.reciprocal(flat[0:1, FB + 2 * WH:FB + 3 * WH], flat[0:1, FB + 3 * WH:FB + 4 * WH])
            v.drain()
            v.tensor_tensor(flat[0:1, FB:FB + WH], flat[0:1, FB:FB + WH],
                            flat[0:1, FB + 2 * WH:FB + 3 * WH], OP.mult)
            v.drain()
            v.tensor_single_scalar(flat[0:1, FB:FB + WH], flat[0:1, FB:FB + WH], -1.0, OP.mult)
            S.inc("p4va", v.drain())

        @blk.tensor
        def _(t, tt=tt):
            S.wait(t, "p4va")
            if tt > 0:
                S.wait(t, "p4h2")
            t.matmul(pab[:, 0:WH], ones[0:1, 0:128], flat[0:1, FB + 2 * WH:FB + 3 * WH], start=True, stop=True)
            S.inc("p4m1", t.matmul(pab[:, W:W + WH], ones[0:1, 0:128], flat[0:1, FB:FB + WH],
                                   start=True, stop=True))

        @blk.vector
        def _(v, tt=tt):
            S.wait(v, "p4m1")
            if tt > 0:
                S.wait(v, "p4ge")
            for dc in range(DC):
                v.tensor_tensor(sqstg[:, 0:WH], x2t[:, xob + dc * WH:xob + (dc + 1) * WH], pab[:, 0:WH], OP.mult)
                v.drain()
                v.tensor_tensor(sqstg[:, 0:WH], sqstg[:, 0:WH], pab[:, W:W + WH], OP.add)
                v.drain()
                v.tensor_scalar(h2t[:, dc * WH:(dc + 1) * WH], sqstg[:, 0:WH],
                                gb[:, G2C + dc:G2C + dc + 1], gb[:, B2C + dc:B2C + dc + 1],
                                OP.mult, OP.add)
                v.drain()
            S.inc("p4h2", v.drain())

        p4m1_base = S.count("p4m1")
        p4ge_base = S.count("p4ge")
        for o1 in range(32):
            pslot = o1 % 2

            @blk.tensor
            def _(t, tt=tt, o1=o1, pslot=pslot, p4ge_base=p4ge_base):
                S.wait(t, "p4h2")
                if o1 >= 2:
                    t.wait_ge(S.get("p4ge"), p4ge_base + o1 - 1)
                for dc in range(DC):
                    mm = t.matmul(pmm[:, pslot * W:pslot * W + WH],
                                  w1T[:, dc * DFF + o1 * 128: dc * DFF + (o1 + 1) * 128],
                                  h2t[:, dc * WH:(dc + 1) * WH], start=(dc == 0), stop=(dc == DC - 1))
                S.inc("p4m1", mm)

        @blk.scalar
        def _(sc, tt=tt, p4m1_base=p4m1_base):
            if tt > 0:
                S.wait(sc, "p4m1", p4m1_base)
            for o1 in range(32):
                pslot = o1 % 2
                sc.wait_ge(S.get("p4m1"), p4m1_base + o1 + 1)
                S.inc("p4ge", sc.activation(g1buf[:, o1 * WH:(o1 + 1) * WH],
                                            pmm[:, pslot * W:pslot * W + WH],
                                            AF.Gelu, bias=b1c[:, o1:o1 + 1], scale=1.0))

        for o2 in range(DC):
            pslot = o2 % 2

            @blk.tensor
            def _(t, tt=tt, o2=o2, pslot=pslot):
                S.wait(t, "p4ge")
                if S.count("p4av") >= 2:
                    S.wait(t, "p4av", S.count("p4av") - 1)
                for kc in range(32):
                    mm = t.matmul(pmm[:, pslot * W:pslot * W + WH],
                                  w2T[:, kc * D + o2 * 128: kc * D + (o2 + 1) * 128],
                                  g1buf[:, kc * WH:(kc + 1) * WH], start=(kc == 0), stop=(kc == 31))
                S.inc("p4m1", mm)

            @blk.vector
            def _(v, o2=o2, pslot=pslot, xob=xob):
                S.wait(v, "p4m1")
                if S.count("p4av") >= 2:
                    S.wait(v, "p3wA" if pslot == 0 else "p3wB")
                S.inc("p4av", v.scalar_tensor_tensor(flat[:, 1152 + pslot * WH:1152 + (pslot + 1) * WH],
                                                     pmm[:, pslot * W:pslot * W + WH],
                                                     gb[:, MB2C + o2:MB2C + o2 + 1],
                                                     x2t[:, xob + o2 * WH:xob + (o2 + 1) * WH], OP.add, OP.add))

            @blk.sync
            def _(sy, tt=tt, o2=o2, pslot=pslot):
                S.wait(sy, "p4av")
                S.inc("p3wA" if pslot == 0 else "p3wB",
                      sy.dma_start(out=acc_h.ap()[o2, :, tt * WH:(tt + 1) * WH],
                                   in_=flat[:, 1152 + pslot * WH:1152 + (pslot + 1) * WH]), 16)

        for jf in range(tt * (OCF * NCH) // (2 * NCH), (tt + 1) * (OCF * NCH) // (2 * NCH)):
            if tt == 0 and jf == 0:
                for jl in range(4):
                    emit_f3_load(jl)
            emit_f3_load(jf + 4)
            emit_f3_tile(jf)

    p4.close()

    @blk.vector
    def _(v):
        S.wait(v, "f3v")
        v.reduce_max(mxc[:, MX_TMP:MX_TMP + 1], mxc[:, MX_GA:MX_GA + OCF * NCH], axis=AX.X)
        S.inc("fmq", v.drain())

    finalize_scalar_max([MX_TMP], 128, 96, "fmq")
    collective(3, 96)
    derive_scales([(104, 32767.0, 8)], 112)

    f2.close()
    barrier(("p3wA", "p3wB", "f3s0", "f3s1", "f3s2", "f3s3", "f3l0", "f3l1", "f3l2", "f3l3", "ldA", "wTA"))

    f4 = contextlib.ExitStack()
    wdT = f4.enter_context(nc.sbuf_tensor("wdT", [128, OCF * D], BF16))
    obuf = f4.enter_context(nc.sbuf_tensor("obuf", [128, 2 * OCF * W], BF16))
    old = f4.enter_context(nc.sbuf_tensor("old", [128, 2 * W], F32))
    oi32 = f4.enter_context(nc.sbuf_tensor("oi32", [128, W], I32))
    accld = f4.enter_context(nc.sbuf_tensor("accld", [128, 2 * W], F32))
    sumf = f4.enter_context(nc.sbuf_tensor("sumf", [128, DC * W], F32))
    xfin = f4.enter_context(nc.sbuf_tensor("xfin", [128, 2 * 1024], F32))
    outst = f4.enter_context(nc.sbuf_tensor("outst", [128, 2 * 1024], F32))
    pdn = f4.enter_context(nc.psum_tensor("pdn", [128, 2 * W], F32))
    pfin = f4.enter_context(nc.psum_tensor("pfin", [128, 2 * 1024], F32))

    @blk.sync
    def _(sy):
        S.wait(sy, "dqoA")
        S.wait(sy, "dqoB")
        for kc in range(OCF):
            S.inc("wTA", sy.dma_start(out=wdT[:, kc * D:(kc + 1) * D],
                                      in_=wd_dq.ap()[:, kc * 128:(kc + 1) * 128], transpose=True), 16)

    for tt in range(NCH):
        ob = (tt % 2) * OCF * W
        for kc in range(OCF):
            buf = kc % 2
            ldsem = "ldA" if buf == 0 else "ldB"

            @blk.sync
            def _(sy, tt=tt, kc=kc, buf=buf, ldsem=ldsem):
                if tt * OCF + kc >= 2:
                    S.wait(sy, "f4v", S.count("f4v") - 1)
                S.inc(ldsem, sy.dma_start(out=old[:, buf * W:(buf + 1) * W],
                                          in_=or_h.ap()[kc, :, tt * W:(tt + 1) * W]), 16)

            @blk.vector
            def _(v, tt=tt, kc=kc, buf=buf, ldsem=ldsem, ob=ob):
                S.wait(v, ldsem)
                if tt == 0 and kc == 0:
                    S.wait(v, "dsv")
                if tt >= 2 and kc == 0:
                    S.wait(v, "f4mm", S.count("f4mm") - 40)
                v.tensor_scalar(oi32[:, :], old[:, buf * W:(buf + 1) * W], scl[:, 9:10], KQH,
                                OP.mult, OP.add)
                v.drain()
                v.tensor_scalar(obuf[:, ob + kc * W:ob + (kc + 1) * W], oi32[:, :], KQ, scl[:, 8:9],
                                OP.subtract, OP.mult)
                S.inc("f4v", v.drain())

        for d2 in range(DC):
            pslot = d2 % 2
            lasem = "f4la" if pslot == 0 else "f4lb"

            @blk.sync
            def _(sy, tt=tt, d2=d2, pslot=pslot, lasem=lasem):
                if S.count("f4av") >= 2:
                    S.wait(sy, "f4av", S.count("f4av") - 1)
                S.inc(lasem, sy.dma_start(out=accld[:, pslot * W:(pslot + 1) * W],
                                          in_=acc_h.ap()[d2, :, tt * W:(tt + 1) * W]), 16)

            @blk.tensor
            def _(t, tt=tt, d2=d2, pslot=pslot, ob=ob):
                S.wait(t, "f4v")
                if tt == 0 and d2 == 0:
                    S.wait(t, "wTA")
                if S.count("f4av") >= 2:
                    S.wait(t, "f4av", S.count("f4av") - 1)
                for kc in range(OCF):
                    mm = t.matmul(pdn[:, pslot * W:(pslot + 1) * W],
                                  wdT[:, kc * D + d2 * 128: kc * D + (d2 + 1) * 128],
                                  obuf[:, ob + kc * W:ob + (kc + 1) * W], start=(kc == 0), stop=(kc == OCF - 1))
                S.inc("f4mm", mm)

            @blk.vector
            def _(v, d2=d2, pslot=pslot, lasem=lasem):
                S.wait(v, "f4mm")
                S.wait(v, lasem)
                S.wait(v, "f4mm")
                S.inc("f4av", v.tensor_tensor(sumf[:, d2 * W:(d2 + 1) * W],
                                              pdn[:, pslot * W:(pslot + 1) * W],
                                              accld[:, pslot * W:(pslot + 1) * W], OP.add))

        for g in range(4):
            gbuf = g % 2
            lxsem = "f4lx" if gbuf == 0 else "f4ly"
            stsem = "stA" if gbuf == 0 else "stB"

            @blk.sync
            def _(sy, tt=tt, g=g, gbuf=gbuf, lxsem=lxsem):
                if S.count("f4fv") >= 2:
                    S.wait(sy, "f4fv", S.count("f4fv") - 1)
                src = x_d[tt * W + g * 128: tt * W + (g + 1) * 128, :]
                S.inc(lxsem, sy.dma_start(out=xfin[:, gbuf * 1024:(gbuf + 1) * 1024], in_=src), 16)

            @blk.tensor
            def _(t, tt=tt, g=g, gbuf=gbuf):
                S.wait(t, "f4av")
                if S.count("f4fv") >= 2:
                    S.wait(t, "f4fv", S.count("f4fv") - 1)
                for d2 in range(DC):
                    S.inc("f4mm", t.transpose(pfin[:, gbuf * 1024 + d2 * 128: gbuf * 1024 + (d2 + 1) * 128],
                                              sumf[:, d2 * W + g * 128: d2 * W + (g + 1) * 128], idf[:]))

            @blk.vector
            def _(v, g=g, gbuf=gbuf, lxsem=lxsem, stsem=stsem):
                S.wait(v, "f4mm")
                S.wait(v, lxsem)
                if S.count("f4fv") >= 2:
                    S.wait(v, stsem)
                S.inc("f4fv", v.tensor_tensor(outst[:, gbuf * 1024:(gbuf + 1) * 1024],
                                              pfin[:, gbuf * 1024:(gbuf + 1) * 1024],
                                              xfin[:, gbuf * 1024:(gbuf + 1) * 1024], OP.add))

            @blk.sync
            def _(sy, tt=tt, g=g, gbuf=gbuf, stsem=stsem):
                S.wait(sy, "f4fv")
                S.inc(stsem, sy.dma_start(out=out_d[tt * W + g * 128: tt * W + (g + 1) * 128, :],
                                          in_=outst[:, gbuf * 1024:(gbuf + 1) * 1024]), 16)

    @blk.sync
    def _(sy):
        S.wait(sy, "stA")
        S.wait(sy, "stB")

    f4.close()
    return nc, st


_PROG_CACHE = {}


def _get_program():
    if "nc" not in _PROG_CACHE:
        nc, st = build_program()
        _PROG_CACHE["nc"] = nc
        _PROG_CACHE["st"] = st
    return _PROG_CACHE["nc"]


def make_in_maps(inputs):
    x_full = np.ascontiguousarray(np.asarray(inputs["L0_attn_x"], dtype=np.float32))
    rk = np.arange(128)
    mUh = (rk[:, None] >= rk[None, :] + 1).astype(ml_dtypes.bfloat16)
    mLh = (rk[:, None] <= rk[None, :]).astype(ml_dtypes.bfloat16)
    rq = np.arange(256)
    m0 = (rk[:, None] >= rq[None, :] + 1)
    m1 = (rk[:, None] >= rq[None, :] - 127)
    m4 = (rk[:, None] <= rq[None, :])
    m5 = (rk[:, None] <= rq[None, :] - 128)
    mqh = np.concatenate([m0, m1, m4, m5], axis=1).astype(ml_dtypes.bfloat16)
    idn = np.eye(128, dtype=np.float32)
    common = {
        "wqkv": np.ascontiguousarray(np.asarray(inputs["L0_attn_w_qkv"], dtype=np.float32)),
        "wo": np.ascontiguousarray(np.asarray(inputs["L0_attn_w_o"], dtype=np.float32)),
        "w1": np.ascontiguousarray(np.asarray(inputs["L0_attn_mlp_w1"], dtype=np.float32)),
        "b1": np.ascontiguousarray(np.asarray(inputs["L0_attn_mlp_b1"], dtype=np.float32)),
        "w2": np.ascontiguousarray(np.asarray(inputs["L0_attn_mlp_w2"], dtype=np.float32)),
        "b2": np.ascontiguousarray(np.asarray(inputs["L0_attn_mlp_b2"], dtype=np.float32)),
        "g1v": np.ascontiguousarray(np.asarray(inputs["L0_attn_ln1_g"], dtype=np.float32)),
        "b1v": np.ascontiguousarray(np.asarray(inputs["L0_attn_ln1_b"], dtype=np.float32)),
        "g2v": np.ascontiguousarray(np.asarray(inputs["L0_attn_ln2_g"], dtype=np.float32)),
        "b2v": np.ascontiguousarray(np.asarray(inputs["L0_attn_ln2_b"], dtype=np.float32)),
        "wg": np.ascontiguousarray(np.asarray(inputs["L0_ffn_w_gate"], dtype=np.float32)),
        "wu": np.ascontiguousarray(np.asarray(inputs["L0_ffn_w_up"], dtype=np.float32)),
        "wd": np.ascontiguousarray(np.asarray(inputs["L0_ffn_w_down"], dtype=np.float32)),
        "idf": idn,
        "mU": mUh,
        "mL": mLh,
        "mq": mqh,
    }
    return [dict(common, x=np.ascontiguousarray(x_full[c])) for c in range(NCORES)]


def kernel(**inputs):
    in_maps = make_in_maps(inputs)
    nc = _get_program()
    res = run_bass_kernel_spmd(nc, in_maps, core_ids=list(range(NCORES)))
    return np.stack([np.asarray(r["out"], dtype=np.float32) for r in res.results], axis=0)


# version 16
# speedup vs baseline: 1.2077x; 1.0000x over previous
import contextlib
import os
import numpy as np
import ml_dtypes

import concourse.bass as bass
import concourse.bass_utils as _bass_utils
import concourse.mybir as mybir
from concourse.bass_utils import run_bass_kernel_spmd

if not getattr(_bass_utils, "_dyn_act_table_patched", False):
    _orig_get_walrus_args = _bass_utils.get_walrus_args

    def _get_walrus_args_dyn(*a, **k):
        return _orig_get_walrus_args(*a, **k) + ["--enable-dynamic-act-table"]

    _bass_utils.get_walrus_args = _get_walrus_args_dyn
    _bass_utils._dyn_act_table_patched = True

F32 = mybir.dt.float32
BF16 = mybir.dt.bfloat16
I32 = mybir.dt.int32
AF = mybir.ActivationFunctionType
OP = mybir.AluOpType
AX = mybir.AxisListType

NCORES = 8
T = 4096
D = 1024
DC = D // 128
H = 16
HD = 64
W = 512
NCH = T // W
DFF = 4096
DI = 3072
OCF = DI // 128
_TRUNC_CAST = (os.environ.get("BASS_SIM_FP64") == "1"
               or os.environ.get("KERNEL_TRUNC_CAST") == "1")
KQ = 49152.0 if _TRUNC_CAST else 0.0
KQH = KQ + 0.5 if _TRUNC_CAST else 0.0
W8H = 8.5 if _TRUNC_CAST else 0.0
W8 = 8.0 if _TRUNC_CAST else 0.0
SGH = 0.5 if _TRUNC_CAST else 0.0
EPS = 1e-5
SCALE = 0.125

MX_XN = 0
MX_TMP = 40
MX_UP = 64
MX_GA = 256
MX_QN = 456
MX_KN = 464
MX_M = 480


class Sems:
    def __init__(self, nc, stack):
        self.nc, self.stack, self.h, self.n = nc, stack, {}, {}

    def get(self, name):
        if name not in self.h:
            self.h[name] = self.stack.enter_context(self.nc.semaphore(name))
            self.n[name] = 0
        return self.h[name]

    def inc(self, name, instr, amount=1):
        instr.then_inc(self.get(name), amount)
        self.n[name] += amount
        return instr

    def count(self, name):
        self.get(name)
        return self.n[name]

    def wait(self, eng, name, val=None):
        eng.wait_ge(self.get(name), self.n[name] if val is None else val)


def build_program():
    nc = bass.Bass("TRN2", target_bir_lowering=False)
    st = contextlib.ExitStack()
    S = Sems(nc, st)

    x_d = nc.dram_tensor("x", [T, D], F32, kind="ExternalInput").ap()
    wqkv_d = nc.dram_tensor("wqkv", [3 * D, D], F32, kind="ExternalInput").ap()
    wo_d = nc.dram_tensor("wo", [D, D], F32, kind="ExternalInput").ap()
    w1_d = nc.dram_tensor("w1", [DFF, D], F32, kind="ExternalInput").ap()
    b1_d = nc.dram_tensor("b1", [DFF], F32, kind="ExternalInput").ap()
    w2_d = nc.dram_tensor("w2", [D, DFF], F32, kind="ExternalInput").ap()
    b2_d = nc.dram_tensor("b2", [D], F32, kind="ExternalInput").ap()
    g1v_d = nc.dram_tensor("g1v", [D], F32, kind="ExternalInput").ap()
    b1v_d = nc.dram_tensor("b1v", [D], F32, kind="ExternalInput").ap()
    g2v_d = nc.dram_tensor("g2v", [D], F32, kind="ExternalInput").ap()
    b2v_d = nc.dram_tensor("b2v", [D], F32, kind="ExternalInput").ap()
    wg_d = nc.dram_tensor("wg", [DI, D], F32, kind="ExternalInput").ap()
    wu_d = nc.dram_tensor("wu", [DI, D], F32, kind="ExternalInput").ap()
    wd_d = nc.dram_tensor("wd", [D, DI], F32, kind="ExternalInput").ap()
    idf_d = nc.dram_tensor("idf", [128, 128], F32, kind="ExternalInput").ap()
    mU_d = nc.dram_tensor("mU", [128, 128], BF16, kind="ExternalInput").ap()
    mL_d = nc.dram_tensor("mL", [128, 128], BF16, kind="ExternalInput").ap()
    mq_d = nc.dram_tensor("mq", [128, 4 * 256], BF16, kind="ExternalInput").ap()
    out_d = nc.dram_tensor("out", [T, D], F32, kind="ExternalOutput").ap()

    wqkv_bf = nc.dram_tensor("wqkv_bf", [3 * D, D], BF16)
    wo_bf = nc.dram_tensor("wo_bf", [D, D], BF16)
    w1_bf = nc.dram_tensor("w1_bf", [DFF, D], BF16)
    w2_bf = nc.dram_tensor("w2_bf", [D, DFF], BF16)
    wg_dq = nc.dram_tensor("wg_dq", [DI, D], BF16)
    wu_dq = nc.dram_tensor("wu_dq", [DI, D], BF16)
    wd_dq = nc.dram_tensor("wd_dq", [D, DI], BF16)
    xn_h = nc.dram_tensor("xn_h", [DC, 128, T], F32)
    h_hbm = nc.dram_tensor("h_hbm", [DC, 128, T], BF16)
    up_h = nc.dram_tensor("up_h", [OCF, 128, T], F32)
    gr_h = nc.dram_tensor("gr_h", [OCF, 128, T], F32)
    t2_h = nc.dram_tensor("t2_h", [OCF, 128, T], F32)
    or_h = nc.dram_tensor("or_h", [OCF, 128, T], F32)
    x2_h = nc.dram_tensor("x2_h", [DC, 128, T], F32)
    acc_h = nc.dram_tensor("acc_h", [DC, 128, T], F32)
    cio = [(nc.dram_tensor("cin%d" % k, [1, 8], F32),
            nc.dram_tensor("cout%d" % k, [1, 8], F32)) for k in range(4)]

    idf = st.enter_context(nc.sbuf_tensor("idf_s", [128, 128], F32))
    mU = st.enter_context(nc.sbuf_tensor("mU_s", [128, 128], BF16))
    mL = st.enter_context(nc.sbuf_tensor("mL_s", [128, 128], BF16))
    mq = st.enter_context(nc.sbuf_tensor("mq_s", [128, 4 * 256], BF16))
    ones = st.enter_context(nc.sbuf_tensor("ones", [128, 128], F32))
    onepair = st.enter_context(nc.sbuf_tensor("onepair", [128, 2], F32))
    gb = st.enter_context(nc.sbuf_tensor("gb", [128, 48], F32))
    b1c = st.enter_context(nc.sbuf_tensor("b1c", [128, 32], F32))
    scl = st.enter_context(nc.sbuf_tensor("scl", [128, 16], F32))
    mxc = st.enter_context(nc.sbuf_tensor("mxc", [128, 512], F32))
    flat = st.enter_context(nc.sbuf_tensor("flat", [128, 2176], F32))
    nrm = st.enter_context(nc.sbuf_tensor("nrm", [128, 8], F32))

    G1C, B1C, G2C, B2C, MB2C = 0, 8, 16, 24, 32

    blk = st.enter_context(nc.Block())
    ENGS = ("sync", "gpsimd", "vector", "scalar", "tensor")

    def barrier(dma_sems=()):
        @blk.sync
        def _(sy):
            for sname in dma_sems:
                S.wait(sy, sname)
        for e in ENGS:
            @getattr(blk, e)
            def _(eng, e=e):
                S.inc("bar_" + e, eng.nop())
        for e in ENGS:
            @getattr(blk, e)
            def _(eng, e=e):
                for o in ENGS:
                    if o != e:
                        S.wait(eng, "bar_" + o)

    uniq = [0]

    def finalize_scalar_max(cols, prt, fb, wait_name):
        n = len(cols)
        c0 = cols[0]
        contig = all(cols[j] == c0 + j for j in range(n))
        assert contig
        uniq[0] += 1

        with nc.psum_tensor("pfz%d" % uniq[0], [128, 256], F32) as pfz:
            @blk.tensor
            def _(t):
                S.wait(t, wait_name)
                S.inc("fmt", t.transpose(pfz[0:n, 0:128], mxc[:, c0:c0 + n], idf[:]))

            @blk.vector
            def _(v):
                S.wait(v, "fmt")
                v.reduce_max(nrm[0:n, 7:8], pfz[0:n, 0:128], axis=AX.X)
                S.inc("fmq", v.drain())

            @blk.tensor
            def _(t):
                S.wait(t, "fmq")
                S.inc("fmt", t.transpose(pfz[0:1, 128:128 + n], nrm[0:n, 7:8], idf[0:n, 0:n]))

            @blk.vector
            def _(v):
                S.wait(v, "fmt")
                v.tensor_copy(flat[0:1, fb:fb + n], pfz[0:1, 128:128 + n])
                v.drain()
                if n < 8:
                    v.tensor_copy(flat[0:1, fb + n:fb + 8],
                                  flat[0:1, fb:fb + 1].unsqueeze(2).broadcast_to([1, 1, 8 - n]).squeeze(1))
                S.inc("fmv2", v.drain())

    def collective(k, fb):
        cin_k, cout_k = cio[k]

        @blk.gpsimd
        def _(g):
            S.wait(g, "fmv2")
            S.inc("ccw", g.dma_start(out=cin_k.ap()[0:1, 0:8], in_=flat[0:1, fb:fb + 8]), 16)
            S.wait(g, "ccw")
            S.inc("cc", g.collective_compute(
                "AllReduce", OP.max, replica_groups=[list(range(NCORES))],
                ins=[cin_k.ap().opt()], outs=[cout_k.ap().opt()]))
            S.wait(g, "cc")
            S.inc("ccr", g.dma_start(out=flat[0:1, fb + 8:fb + 16], in_=cout_k.ap()[0:1, 0:8]), 16)

    def derive_scales(pairs, fb):
        @blk.vector
        def _(v):
            S.wait(v, "ccr")
            for j, (c, q, _sc) in enumerate(pairs):
                v.tensor_scalar(flat[0:1, fb + 2 * j:fb + 2 * j + 1], flat[0:1, c:c + 1],
                                1.0 / q, 1e-12, OP.mult, OP.max)
            v.drain()
            for j in range(len(pairs)):
                v.reciprocal(flat[0:1, fb + 2 * j + 1:fb + 2 * j + 2],
                             flat[0:1, fb + 2 * j:fb + 2 * j + 1])
            S.inc("dsv", v.drain())

        uniq[0] += 1
        with nc.psum_tensor("pdsx%d" % uniq[0], [128, 16], F32) as pds:
            @blk.tensor
            def _(t):
                S.wait(t, "dsv")
                S.inc("dst", t.matmul(pds[:, 0:2 * len(pairs)], ones[0:1, 0:128],
                                      flat[0:1, fb:fb + 2 * len(pairs)], start=True, stop=True))

            @blk.vector
            def _(v):
                S.wait(v, "dst")
                for j, (_c, _q, c2) in enumerate(pairs):
                    v.tensor_copy(scl[:, c2:c2 + 2], pds[:, 2 * j:2 * j + 2])
                S.inc("dsv", v.drain())

    @blk.gpsimd
    def _(g):
        S.inc("cast1", g.dma_start(out=wqkv_bf.ap(), in_=wqkv_d), 16)
        S.inc("cast1", g.dma_start(out=wo_bf.ap(), in_=wo_d), 16)
        S.inc("cast2", g.dma_start(out=w1_bf.ap(), in_=w1_d), 16)
        S.inc("cast2", g.dma_start(out=w2_bf.ap(), in_=w2_d), 16)

    with nc.sbuf_tensor("vecst", [48, 128], F32) as vecst, \
         nc.sbuf_tensor("b1st", [32, 128], F32) as b1st, \
         nc.psum_tensor("pvec", [128, 48], F32) as pvec, \
         nc.psum_tensor("pb1", [128, 32], F32) as pb1:

        @blk.sync
        def _(sy):
            S.inc("cst", sy.dma_start(out=idf[:], in_=idf_d), 16)
            S.inc("cst", sy.dma_start(out=mU[:], in_=mU_d), 16)
            S.inc("cst", sy.dma_start(out=mL[:], in_=mL_d), 16)
            S.inc("cst", sy.dma_start(out=mq[:], in_=mq_d), 16)
            S.inc("cst", sy.dma_start(out=vecst[0:8, :], in_=g1v_d.rearrange("(a b) -> a b", b=128)), 16)
            S.inc("cst", sy.dma_start(out=vecst[8:16, :], in_=b1v_d.rearrange("(a b) -> a b", b=128)), 16)
            S.inc("cst", sy.dma_start(out=vecst[16:24, :], in_=g2v_d.rearrange("(a b) -> a b", b=128)), 16)
            S.inc("cst", sy.dma_start(out=vecst[24:32, :], in_=b2v_d.rearrange("(a b) -> a b", b=128)), 16)
            S.inc("cst", sy.dma_start(out=vecst[32:40, :], in_=b2_d.rearrange("(a b) -> a b", b=128)), 16)
            S.inc("cst", sy.dma_start(out=b1st[:, :], in_=b1_d.rearrange("(a b) -> a b", b=128)), 16)

        @blk.tensor
        def _(t):
            S.wait(t, "cst")
            t.transpose(pvec[:, 0:40], vecst[0:40, :], idf[0:40, 0:40])
            S.inc("fmt", t.transpose(pb1[:, 0:32], b1st[:, :], idf[0:32, 0:32]))

        @blk.vector
        def _(v):
            v.memset(ones[:], 1.0)
            v.memset(onepair[:, :], 0.0)
            v.drain()
            v.memset(onepair[0:64, 0:1], 1.0)
            v.memset(onepair[64:128, 1:2], 1.0)
            S.wait(v, "fmt")
            v.tensor_copy(gb[:, 0:40], pvec[:, 0:40])
            v.tensor_copy(b1c[:, 0:32], pb1[:, 0:32])
            S.inc("dsv", v.drain())

        barrier(("cst",))

    p1 = contextlib.ExitStack()
    hst = p1.enter_context(nc.sbuf_tensor("hst", [128, 2 * DC * W], BF16))
    xin = p1.enter_context(nc.sbuf_tensor("xin", [128, 2 * 4096], F32))
    p1st = p1.enter_context(nc.sbuf_tensor("p1st", [128, 64], F32))
    hpre = p1.enter_context(nc.sbuf_tensor("hpre", [128, 4096], F32))
    xnst = p1.enter_context(nc.sbuf_tensor("xnst", [128, 2 * DC * W], F32))
    ptr1 = p1.enter_context(nc.psum_tensor("ptr1", [128, 1024], F32))
    ptr2 = p1.enter_context(nc.psum_tensor("ptr2", [128, 1024], F32))

    for tt in range(NCH):
        buf = tt % 2
        xo = buf * 4096
        so = buf * DC * W
        ldsem = "ldA" if buf == 0 else "ldB"

        @blk.sync
        def _(sy, tt=tt, xo=xo, ldsem=ldsem):
            if tt >= 2:
                S.wait(sy, "p1v", S.count("p1v") - 1)
            src = x_d[tt * W:(tt + 1) * W, :].rearrange("(g p) d -> p g d", p=128)
            S.inc(ldsem, sy.dma_start(out=xin[:, xo:xo + 4096].rearrange("p (g d) -> p g d", d=1024), in_=src), 16)

        @blk.scalar
        def _(sc, tt=tt, xo=xo, ldsem=ldsem):
            S.wait(sc, ldsem)
            if tt >= 1:
                S.wait(sc, "p1t")
            for g in range(4):
                xg = xin[:, xo + g * 1024: xo + (g + 1) * 1024]
                sc.activation(hpre[:, g * 1024:(g + 1) * 1024], xg, AF.Square)
            S.inc("p1sa", sc.drain())

        @blk.vector
        def _(v, tt=tt, xo=xo, ldsem=ldsem):
            S.wait(v, ldsem)
            if tt == 0:
                S.wait(v, "dsv")
            for g in range(4):
                xg = xin[:, xo + g * 1024: xo + (g + 1) * 1024]
                v.reduce_sum(p1st[:, g:g + 1], xg, axis=AX.X)
            S.wait(v, "p1sa")
            for g in range(4):
                v.reduce_sum(p1st[:, 4 + g:5 + g], hpre[:, g * 1024:(g + 1) * 1024], axis=AX.X)
            v.drain()
            v.tensor_single_scalar(p1st[:, 8:12], p1st[:, 0:4], 1.0 / D, OP.mult)
            v.tensor_single_scalar(p1st[:, 12:16], p1st[:, 4:8], 1.0 / D, OP.mult)
            v.drain()
            v.tensor_tensor(p1st[:, 16:20], p1st[:, 8:12], p1st[:, 8:12], OP.mult)
            v.drain()
            v.tensor_tensor(p1st[:, 16:20], p1st[:, 12:16], p1st[:, 16:20], OP.subtract)
            v.drain()
            v.tensor_single_scalar(p1st[:, 16:20], p1st[:, 16:20], EPS, OP.add)
            S.inc("p1q", v.drain())

        @blk.scalar
        def _(sc):
            S.wait(sc, "p1q")
            S.inc("p1r", sc.activation(p1st[:, 20:24], p1st[:, 16:20], AF.Sqrt))

        @blk.vector
        def _(v, tt=tt, xo=xo):
            S.wait(v, "p1r")
            v.reciprocal(p1st[:, 24:28], p1st[:, 20:24])
            v.drain()
            for g in range(4):
                xg = xin[:, xo + g * 1024: xo + (g + 1) * 1024]
                v.tensor_scalar(xg, xg, p1st[:, 8 + g:9 + g], p1st[:, 24 + g:25 + g], OP.subtract, OP.mult)
            v.drain()
            for g in range(4):
                xg = xin[:, xo + g * 1024: xo + (g + 1) * 1024]
                v.reduce_max(mxc[:, MX_XN + tt * 4 + g:MX_XN + tt * 4 + g + 1], xg,
                             axis=AX.X, apply_absolute_value=True)
                v.reduce_sum(p1st[:, 32 + g:33 + g], xg, axis=AX.X)
            S.inc("p1b", v.drain())
            S.wait(v, "p1sa", S.count("p1sa") + 1)
            for g in range(4):
                v.reduce_sum(p1st[:, 36 + g:37 + g], hpre[:, g * 1024:(g + 1) * 1024], axis=AX.X)
            v.drain()
            v.tensor_single_scalar(p1st[:, 40:44], p1st[:, 32:36], 1.0 / D, OP.mult)
            v.tensor_single_scalar(p1st[:, 44:48], p1st[:, 36:40], 1.0 / D, OP.mult)
            v.drain()
            v.tensor_tensor(p1st[:, 48:52], p1st[:, 40:44], p1st[:, 40:44], OP.mult)
            v.drain()
            v.tensor_tensor(p1st[:, 48:52], p1st[:, 44:48], p1st[:, 48:52], OP.subtract)
            v.drain()
            v.tensor_single_scalar(p1st[:, 48:52], p1st[:, 48:52], EPS, OP.add)
            S.inc("p1q", v.drain())

        @blk.scalar
        def _(sc, xo=xo):
            S.wait(sc, "p1b")
            for g in range(4):
                xg = xin[:, xo + g * 1024: xo + (g + 1) * 1024]
                sc.activation(hpre[:, g * 1024:(g + 1) * 1024], xg, AF.Square)
            S.inc("p1sa", sc.drain())

        @blk.scalar
        def _(sc):
            S.wait(sc, "p1q")
            S.inc("p1r", sc.activation(p1st[:, 52:56], p1st[:, 48:52], AF.Sqrt))

        @blk.vector
        def _(v, tt=tt, xo=xo):
            S.wait(v, "p1r")
            v.reciprocal(p1st[:, 56:60], p1st[:, 52:56])
            v.drain()
            for g in range(4):
                xg = xin[:, xo + g * 1024: xo + (g + 1) * 1024]
                v.tensor_scalar(hpre[:, g * 1024:(g + 1) * 1024], xg,
                                p1st[:, 40 + g:41 + g], p1st[:, 56 + g:57 + g], OP.subtract, OP.mult)
            S.inc("p1b", v.drain())

        p1e_base = S.count("p1e")

        @blk.tensor
        def _(t, tt=tt, xo=xo, p1e_base=p1e_base):
            S.wait(t, "p1b")
            for i in range(32):
                g, dc = i // DC, i % DC
                pslot = i % 2
                if i >= 2:
                    t.wait_ge(S.get("p1e"), p1e_base + i - 1)
                t.transpose(ptr1[:, pslot * 512:pslot * 512 + 128],
                            xin[:, xo + g * 1024 + dc * 128: xo + g * 1024 + (dc + 1) * 128], idf[:])
                S.inc("p1t", t.transpose(ptr2[:, pslot * 512:pslot * 512 + 128],
                                         hpre[:, g * 1024 + dc * 128: g * 1024 + (dc + 1) * 128], idf[:]))

        p1t_base = S.count("p1t") - 32

        so2 = buf * DC * W

        @blk.vector
        def _(v, tt=tt, so=so, so2=so2, p1t_base=p1t_base):
            if tt >= 2:
                S.wait(v, "p1oA" if tt % 2 == 0 else "p1oB")
            for i in range(32):
                g, dc = i // DC, i % DC
                pslot = i % 2
                v.wait_ge(S.get("p1t"), p1t_base + i + 1)
                v.tensor_copy(xnst[:, so + dc * W + g * 128: so + dc * W + (g + 1) * 128],
                              ptr1[:, pslot * 512:pslot * 512 + 128])
                S.inc("p1e",
                      v.tensor_scalar(hst[:, so2 + dc * W + g * 128: so2 + dc * W + (g + 1) * 128],
                                      ptr2[:, pslot * 512:pslot * 512 + 128],
                                      gb[:, G1C + dc:G1C + dc + 1], gb[:, B1C + dc:B1C + dc + 1],
                                      OP.mult, OP.add))
            S.inc("p1v", v.drain())

        @blk.sync
        def _(sy, tt=tt, so=so, so2=so2):
            S.wait(sy, "p1v")
            dst = xn_h.ap()[:, :, tt * W:(tt + 1) * W].transpose([1, 0, 2])
            S.inc("p1oA" if tt % 2 == 0 else "p1oB",
                  sy.dma_start(out=dst,
                               in_=xnst[:, so:so + DC * W].rearrange("p (c w) -> p c w", w=W)), 16)
            dsth = h_hbm.ap()[:, :, tt * W:(tt + 1) * W].transpose([1, 0, 2])
            S.inc("p1oA" if tt % 2 == 0 else "p1oB",
                  sy.dma_start(out=dsth,
                               in_=hst[:, so2:so2 + DC * W].rearrange("p (c w) -> p c w", w=W)), 16)

    @blk.vector
    def _(v):
        S.wait(v, "p1e")
        v.reduce_max(mxc[:, MX_TMP:MX_TMP + 1], mxc[:, MX_XN:MX_XN + 4 * NCH], axis=AX.X)
        S.inc("fmq", v.drain())

    finalize_scalar_max([MX_TMP], 128, 0, "fmq")
    collective(0, 0)
    derive_scales([(8, 32767.0, 0)], 16)

    wst = p1.enter_context(nc.sbuf_tensor("wst", [128, 2 * DI], F32))
    wblk = p1.enter_context(nc.sbuf_tensor("wblk", [128, 3 * (DI // 32)], F32))
    wi32 = p1.enter_context(nc.sbuf_tensor("wi32", [128, DI], I32))
    wbf16 = p1.enter_context(nc.sbuf_tensor("wbf16", [128, 2 * DI], BF16))

    dqi = 0
    for mi, (src, dst, nrow, I) in enumerate(
            [(wg_d, wg_dq, DI // 128, 1024), (wu_d, wu_dq, DI // 128, 1024), (wd_d, wd_dq, D // 128, DI)]):
        nb = I // 32
        for r in range(nrow):
            buf = dqi % 2
            ldsem = "ldA" if buf == 0 else "ldB"
            stsem = "dqoA" if buf == 0 else "dqoB"

            @blk.sync
            def _(sy, src=src, r=r, buf=buf, I=I, dqi=dqi, ldsem=ldsem):
                if dqi >= 2:
                    S.wait(sy, "dqv", S.count("dqv") - 1)
                S.inc(ldsem, sy.dma_start(out=wst[:, buf * DI: buf * DI + I],
                                          in_=src[r * 128:(r + 1) * 128, :]), 16)

            @blk.vector
            def _(v, buf=buf, I=I, nb=nb, dqi=dqi, ldsem=ldsem, stsem=stsem):
                S.wait(v, ldsem)
                if dqi >= 2:
                    S.wait(v, stsem)
                w = wst[:, buf * DI: buf * DI + I]
                w3 = w.rearrange("p (b k) -> p b k", k=32)
                bm = wblk[:, 0:nb]
                sblk = wblk[:, nb:2 * nb]
                invs = wblk[:, 2 * nb:3 * nb]
                v.reduce_max(bm, w3, axis=AX.X, apply_absolute_value=True)
                v.drain()
                v.tensor_scalar(sblk, bm, 1.0 / 7.0, 1e-12, OP.mult, OP.max)
                v.drain()
                v.reciprocal(invs, sblk)
                v.drain()
                v.tensor_tensor(w3, w3, invs.unsqueeze(2).broadcast_to([128, nb, 32]), OP.mult)
                v.drain()
                v.tensor_single_scalar(wi32[:, 0:I], w, W8H, OP.add)
                v.drain()
                v.scalar_tensor_tensor(wbf16[:, buf * DI: buf * DI + I].rearrange("p (b k) -> p b k", k=32),
                                       wi32[:, 0:I].rearrange("p (b k) -> p b k", k=32), W8,
                                       sblk.unsqueeze(2).broadcast_to([128, nb, 32]),
                                       OP.subtract, OP.mult)
                S.inc("dqv", v.drain())

            @blk.sync
            def _(sy, dst=dst, r=r, buf=buf, I=I, stsem=stsem):
                S.wait(sy, "dqv")
                S.inc(stsem, sy.dma_start(out=dst.ap()[r * 128:(r + 1) * 128, :],
                                          in_=wbf16[:, buf * DI: buf * DI + I]), 16)
            dqi += 1

    p1.close()
    barrier(("p1oA", "p1oB", "dqoA", "dqoB", "ldA", "ldB", "cst"))

    p2 = contextlib.ExitStack()
    xq_sb = p2.enter_context(nc.sbuf_tensor("xq_sb", [128, DC * T], BF16))
    p2ld = p2.enter_context(nc.sbuf_tensor("p2ld", [128, 2 * W], F32))
    p2i = p2.enter_context(nc.sbuf_tensor("p2i", [128, W], I32))
    wtT = p2.enter_context(nc.sbuf_tensor("wtT", [128, DC * DI], BF16))
    p2ev = p2.enter_context(nc.sbuf_tensor("p2ev", [128, 4 * W], F32))
    p2ps = p2.enter_context(nc.psum_tensor("p2ps", [128, 4 * W], F32))

    for i in range(DC * NCH):
        dc, tt = i // NCH, i % NCH
        buf = i % 2
        ldsem = "ldA" if buf == 0 else "ldB"

        @blk.sync
        def _(sy, dc=dc, tt=tt, buf=buf, i=i, ldsem=ldsem):
            if i == 0:
                S.wait(sy, "p1oA")
                S.wait(sy, "p1oB")
            if i >= 2:
                S.wait(sy, "xqv", S.count("xqv") - 1)
            S.inc(ldsem, sy.dma_start(out=p2ld[:, buf * W:(buf + 1) * W],
                                      in_=xn_h.ap()[dc, :, tt * W:(tt + 1) * W]), 16)

        @blk.vector
        def _(v, dc=dc, tt=tt, buf=buf, i=i, ldsem=ldsem):
            S.wait(v, ldsem)
            if i == 0:
                S.wait(v, "dsv")
            xg = p2ld[:, buf * W:(buf + 1) * W]
            v.tensor_scalar(p2i[:, :], xg, scl[:, 1:2], KQH, OP.mult, OP.add)
            v.drain()
            v.tensor_scalar(xq_sb[:, dc * T + tt * W: dc * T + (tt + 1) * W], p2i[:, :],
                            KQ, scl[:, 0:1], OP.subtract, OP.mult)
            S.inc("xqv", v.drain())

    for mi, (wdq, dsth, mbase) in enumerate([(wu_dq, up_h, MX_UP), (wg_dq, gr_h, MX_GA)]):
        @blk.sync
        def _(sy, wdq=wdq, mi=mi):
            if mi == 0:
                S.wait(sy, "dqoA")
                S.wait(sy, "dqoB")
            else:
                S.wait(sy, "p2m")
            for dc in range(DC):
                S.inc("wTA", sy.dma_start(out=wtT[:, dc * DI:(dc + 1) * DI],
                                          in_=wdq.ap()[:, dc * 128:(dc + 1) * 128], transpose=True), 16)

        NT = OCF * NCH
        m0 = S.count("p2m")
        s0_ = S.count("p2s")
        e0 = S.count("p2e")
        o0 = [S.count("p2o%d" % s4) for s4 in range(4)]
        for j in range(NT):
            oc, tt = j // NCH, j % NCH
            pslot = j % 4

            @blk.tensor
            def _(t, mi=mi, oc=oc, tt=tt, pslot=pslot, j=j, s0_=s0_):
                if j == 0:
                    S.wait(t, "wTA")
                    if mi == 0:
                        S.wait(t, "xqv")
                if mi > 0 or j >= 4:
                    t.wait_ge(S.get("p2s"), s0_ + j - 3)
                ps = p2ps[:, pslot * W:(pslot + 1) * W]
                for dc in range(DC):
                    mm = t.matmul(ps, wtT[:, dc * DI + oc * 128: dc * DI + (oc + 1) * 128],
                                  xq_sb[:, dc * T + tt * W: dc * T + (tt + 1) * W],
                                  start=(dc == 0), stop=(dc == DC - 1))
                S.inc("p2m", mm)

            if j % 8 == 7 or j == NT - 1:
                jlo = (j // 8) * 8

                @blk.scalar
                def _(sc, mi=mi, jlo=jlo, jhi=j, m0=m0, e0=e0, o0=o0):
                    for jj in range(jlo, jhi + 1):
                        ps4 = jj % 4
                        sc.wait_ge(S.get("p2m"), m0 + jj + 1)
                        if mi > 0 or jj >= 4:
                            sc.wait_ge(S.get("p2e"), e0 + jj - 3)
                            sc.wait_ge(S.get("p2o%d" % ps4), o0[ps4] + 16 * ((jj - 4) // 4 + 1))
                        S.inc("p2s", sc.copy(p2ev[:, ps4 * W:(ps4 + 1) * W],
                                             p2ps[:, ps4 * W:(ps4 + 1) * W]))

            @blk.vector
            def _(v, pslot=pslot, mbase=mbase, j=j, s0_=s0_):
                v.wait_ge(S.get("p2s"), s0_ + j + 1)
                S.inc("p2e", v.reduce_max(mxc[:, mbase + j:mbase + j + 1],
                                          p2ev[:, pslot * W:(pslot + 1) * W],
                                          axis=AX.X, apply_absolute_value=True))

            @blk.sync
            def _(sy, oc=oc, tt=tt, pslot=pslot, dsth=dsth, j=j, e0=e0):
                sy.wait_ge(S.get("p2e"), e0 + j + 1)
                S.inc("p2o%d" % pslot,
                      sy.dma_start(out=dsth.ap()[oc, :, tt * W:(tt + 1) * W],
                                   in_=p2ev[:, pslot * W:(pslot + 1) * W]), 16)

    @blk.vector
    def _(v):
        S.wait(v, "p2e")
        v.reduce_max(mxc[:, MX_TMP:MX_TMP + 1], mxc[:, MX_UP:MX_UP + OCF * NCH], axis=AX.X)
        v.reduce_max(mxc[:, MX_TMP + 1:MX_TMP + 2], mxc[:, MX_GA:MX_GA + OCF * NCH], axis=AX.X)
        S.inc("fmq", v.drain())

    finalize_scalar_max([MX_TMP, MX_TMP + 1], 128, 32, "fmq")
    collective(1, 32)
    derive_scales([(40, 32767.0, 2), (41, 32767.0, 4)], 48)

    p2.close()
    barrier(("p2o0", "p2o1", "p2o2", "p2o3", "ldA", "ldB", "wTA"))

    f2 = contextlib.ExitStack()
    fld = f2.enter_context(nc.sbuf_tensor("fld", [128, 4 * W], F32))
    fi32 = f2.enter_context(nc.sbuf_tensor("fi32", [128, W], I32))
    fgate = f2.enter_context(nc.sbuf_tensor("fgate", [128, 4 * W], F32))
    ft2 = f2.enter_context(nc.sbuf_tensor("ft2", [128, 4 * W], F32))

    def emit_f2_load(j):
        if j >= OCF * NCH:
            return
        oc, tt = j // NCH, j % NCH
        buf = j % 4

        @blk.sync
        def _(sy):
            if j == 0:
                for s4 in range(4):
                    S.wait(sy, "p2o%d" % s4)
            if j >= 4:
                S.wait(sy, "f2v2", j - 3)
            S.inc("f2l%d" % buf, sy.dma_start(out=fld[:, buf * W:(buf + 1) * W],
                                              in_=gr_h.ap()[oc, :, tt * W:(tt + 1) * W]), 16)

    def emit_f2_tile(j):
        oc, tt = j // NCH, j % NCH
        buf = j % 4

        @blk.vector
        def _(v):
            S.wait(v, "f2l%d" % buf, 16 * (j // 4 + 1))
            if j == 0:
                S.wait(v, "dsv")
            if j >= 4:
                S.wait(v, "f2sg", j - 3)
                S.wait(v, "f2s%d" % buf)
            v.tensor_scalar(fi32[:, :], fld[:, buf * W:(buf + 1) * W], scl[:, 5:6], KQH,
                            OP.mult, OP.add)
            v.drain()
            v.tensor_scalar(fgate[:, buf * W:(buf + 1) * W], fi32[:, :], KQ, scl[:, 4:5],
                            OP.subtract, OP.mult)
            S.inc("f2v1", v.drain())

        if buf == 3 or j == OCF * NCH - 1:
            @blk.scalar
            def _(sc):
                for jj in range(j - buf, j + 1):
                    bb = jj % 4
                    sc.wait_ge(S.get("f2v1"), jj + 1)
                    S.inc("f2sg", sc.activation(fld[:, bb * W:(bb + 1) * W],
                                                fgate[:, bb * W:(bb + 1) * W], AF.Tanh, scale=0.5))

        @blk.vector
        def _(v):
            S.wait(v, "f2sg", j + 1)
            v.tensor_scalar(fi32[:, :], fld[:, buf * W:(buf + 1) * W], 32768.0, 32768.0 + SGH,
                            OP.mult, OP.add)
            v.drain()
            v.tensor_scalar(fld[:, buf * W:(buf + 1) * W], fi32[:, :], 65535.0, 1.0 / 65536.0,
                            OP.min, OP.mult)
            v.drain()
            v.tensor_tensor(ft2[:, buf * W:(buf + 1) * W], fgate[:, buf * W:(buf + 1) * W],
                            fld[:, buf * W:(buf + 1) * W], OP.mult)
            v.drain()
            S.inc("f2v2", v.reduce_max(mxc[:, MX_UP + j:MX_UP + j + 1], ft2[:, buf * W:(buf + 1) * W],
                                       axis=AX.X, apply_absolute_value=True))

        @blk.sync
        def _(sy):
            S.wait(sy, "f2v2")
            S.inc("f2s%d" % buf, sy.dma_start(out=t2_h.ap()[oc, :, tt * W:(tt + 1) * W],
                                              in_=ft2[:, buf * W:(buf + 1) * W]), 16)

    def emit_f3_load(j):
        if j >= OCF * NCH:
            return
        oc, tt = j // NCH, j % NCH
        buf = j % 4

        @blk.sync
        def _(sy):
            if j == 0:
                for s4 in range(4):
                    S.wait(sy, "f2s%d" % s4)
            if j >= 4:
                S.wait(sy, "f3v", j - 3)
            S.inc("f3l%d" % buf, sy.dma_start(out=fld[:, buf * W:(buf + 1) * W],
                                              in_=t2_h.ap()[oc, :, tt * W:(tt + 1) * W]), 16)
            S.inc("f3l%d" % buf, sy.dma_start(out=fgate[:, buf * W:(buf + 1) * W],
                                              in_=up_h.ap()[oc, :, tt * W:(tt + 1) * W]), 16)

    def emit_f3_tile(j):
        oc, tt = j // NCH, j % NCH
        buf = j % 4

        @blk.vector
        def _(v):
            S.wait(v, "f3l%d" % buf, 32 * (j // 4 + 1))
            if j == 0:
                S.wait(v, "dsv")
            if j >= 4:
                S.wait(v, "f3s%d" % buf)
            v.tensor_scalar(fi32[:, :], fld[:, buf * W:(buf + 1) * W], scl[:, 7:8], KQH,
                            OP.mult, OP.add)
            v.drain()
            v.tensor_scalar(fld[:, buf * W:(buf + 1) * W], fi32[:, :], KQ, scl[:, 6:7],
                            OP.subtract, OP.mult)
            v.drain()
            v.tensor_scalar(fi32[:, :], fgate[:, buf * W:(buf + 1) * W], scl[:, 3:4], KQH,
                            OP.mult, OP.add)
            v.drain()
            v.tensor_scalar(fgate[:, buf * W:(buf + 1) * W], fi32[:, :], KQ, scl[:, 2:3],
                            OP.subtract, OP.mult)
            v.drain()
            v.tensor_tensor(ft2[:, buf * W:(buf + 1) * W], fld[:, buf * W:(buf + 1) * W],
                            fgate[:, buf * W:(buf + 1) * W], OP.mult)
            v.drain()
            S.inc("f3v", v.reduce_max(mxc[:, MX_GA + j:MX_GA + j + 1], ft2[:, buf * W:(buf + 1) * W],
                                      axis=AX.X, apply_absolute_value=True))

        @blk.sync
        def _(sy):
            S.wait(sy, "f3v")
            S.inc("f3s%d" % buf, sy.dma_start(out=or_h.ap()[oc, :, tt * W:(tt + 1) * W],
                                              in_=ft2[:, buf * W:(buf + 1) * W]), 16)

    p3 = contextlib.ExitStack()
    hstg = p3.enter_context(nc.sbuf_tensor("hstg", [128, 2 * DC * W], BF16))
    wqkvT = p3.enter_context(nc.sbuf_tensor("wqkvT", [128, DC * 3 * D], BF16))
    woT = p3.enter_context(nc.sbuf_tensor("woT", [128, DC * D], BF16))
    qf = p3.enter_context(nc.sbuf_tensor("qf", [128, DC * W], BF16))
    kf2 = p3.enter_context(nc.sbuf_tensor("kf2", [128, 2 * DC * W], BF16))
    vsb2 = p3.enter_context(nc.sbuf_tensor("vsb2", [128, 2 * 4 * H * 65], BF16))
    avn = p3.enter_context(nc.sbuf_tensor("avn", [128, DC * W], BF16))
    xnstg = p3.enter_context(nc.sbuf_tensor("xnstg", [128, 2 * W], F32))
    x2stg = p3.enter_context(nc.sbuf_tensor("x2stg", [128, 2 * W], F32))
    pTs = p3.enter_context(nc.sbuf_tensor("pTs", [128, 4 * 768], BF16))
    q2st = p3.enter_context(nc.sbuf_tensor("q2st", [128, 2 * W], F32))

    @blk.sync
    def _(sy):
        S.wait(sy, "cast1")
        for dc in range(DC):
            S.inc("wTA", sy.dma_start(out=wqkvT[:, dc * 3 * D:(dc + 1) * 3 * D],
                                      in_=wqkv_bf.ap()[:, dc * 128:(dc + 1) * 128], transpose=True), 16)
        for dc in range(DC):
            S.inc("wTA", sy.dma_start(out=woT[:, dc * D:(dc + 1) * D],
                                      in_=wo_bf.ap()[:, dc * 128:(dc + 1) * 128], transpose=True), 16)

    for c in range(NCH):
        cur, prv = c % 2, (c + 1) % 2
        VS = 4 * H * 65
        p3qm_base = S.count("p3qm")

        @blk.sync
        def _(sy, c=c, cur=cur, p3qm_base=p3qm_base):
            if c == 0:
                S.wait(sy, "p1oA")
                S.wait(sy, "p1oB")
            if c >= 2:
                S.wait(sy, "p3qm", p3qm_base - 24)
            S.inc("hlA" if cur == 0 else "hlB",
                  sy.dma_start(out=hstg[:, cur * DC * W: (cur + 1) * DC * W]
                               .rearrange("p (k w) -> p k w", w=W),
                               in_=h_hbm.ap()[:, :, c * W:(c + 1) * W].transpose([1, 0, 2])), 16)

        with nc.psum_tensor("pqk_%d" % c, [128, 2 * W], F32) as pqk:
            for wi, rowbase in ((0, 0), (1, D)):
                for oc in range(DC):
                    pslot = (wi * DC + oc) % 2

                    @blk.tensor
                    def _(t, c=c, cur=cur, wi=wi, rowbase=rowbase, oc=oc, pslot=pslot, pqk=pqk):
                        if c == 0 and wi == 0 and oc == 0:
                            S.wait(t, "wTA")
                        if wi == 0 and oc == 0:
                            S.wait(t, "hlA" if cur == 0 else "hlB")
                        if c > 0 and wi == 0 and oc == 0:
                            S.wait(t, "p3wv")
                        if S.count("p3qe") >= 2:
                            S.wait(t, "p3qe", S.count("p3qe") - 1)
                        ps = pqk[:, pslot * W:(pslot + 1) * W]
                        for dc in range(DC):
                            mm = t.matmul(ps, wqkvT[:, dc * 3 * D + rowbase + oc * 128:
                                                    dc * 3 * D + rowbase + (oc + 1) * 128],
                                          hstg[:, cur * DC * W + dc * W: cur * DC * W + (dc + 1) * W],
                                          start=(dc == 0), stop=(dc == DC - 1))
                        S.inc("p3qm", mm)

                    @blk.vector
                    def _(v, c=c, wi=wi, oc=oc, pslot=pslot, cur=cur, pqk=pqk):
                        S.wait(v, "p3qm")
                        if c > 0 and wi == 0 and oc == 0:
                            S.wait(v, "p3xm")
                        if wi == 0:
                            o = v.tensor_copy(qf[:, oc * W:(oc + 1) * W], pqk[:, pslot * W:(pslot + 1) * W])
                        else:
                            o = v.tensor_copy(kf2[:, cur * DC * W + oc * W: cur * DC * W + (oc + 1) * W],
                                              pqk[:, pslot * W:(pslot + 1) * W])
                        S.inc("p3qe", o)

        @blk.vector
        def _(v, c=c, cur=cur, VS=VS):
            S.wait(v, "p3av")
            v.memset(vsb2[:, cur * VS:(cur + 1) * VS].rearrange("p (s o) -> p s o", o=65)[:, :, 64:65], 1.0)
            S.inc("p3vo", v.drain())

        with contextlib.ExitStack() as vsc:
            pv8 = [vsc.enter_context(nc.psum_tensor("pv8_%d_%d" % (c, b), [128, 64], F32)) for b in range(8)]
            for g in range(4):
                for hg in range(2):
                    @blk.tensor
                    def _(t, c=c, g=g, hg=hg, pv8=pv8):
                        S.wait(t, "p3qe")
                        if S.count("p3ve") > 0:
                            S.wait(t, "p3ve")
                        for dc in range(DC):
                            hb = hstg[:, cur * DC * W + dc * W + g * 128: cur * DC * W + dc * W + (g + 1) * 128]
                            for hh in range(8):
                                h = hg * 8 + hh
                                mm = t.matmul(pv8[hh][:, 0:64], hb,
                                              wqkvT[:, dc * 3 * D + 2 * D + h * 64:
                                                    dc * 3 * D + 2 * D + (h + 1) * 64],
                                              start=(dc == 0), stop=(dc == DC - 1))
                        S.inc("p3qm", mm)

                    @blk.vector
                    def _(v, c=c, g=g, hg=hg, cur=cur, pv8=pv8, VS=VS):
                        S.wait(v, "p3qm")
                        S.wait(v, "p3vo")
                        for hh in range(8):
                            h = hg * 8 + hh
                            S.inc("p3ve", v.tensor_copy(
                                vsb2[:, cur * VS + (g * H + h) * 65: cur * VS + (g * H + h) * 65 + 64],
                                pv8[hh][:, 0:64]))

        with nc.psum_tensor("pnm_%d" % c, [128, W], F32) as pnm:
            its = [(wi, dc, mbase) for wi, mbase in ((0, MX_QN), (1, MX_KN + cur * 8))
                   for dc in range(DC)]
            nt_base = S.count("p3nt")
            sq_base = S.count("p3sq")

            @blk.scalar
            def _(sc, its=its, cur=cur, nt_base=nt_base):
                S.wait(sc, "p3qe")
                for i, (wi, dc, mbase) in enumerate(its):
                    if i >= 2:
                        sc.wait_ge(S.get("p3nt"), nt_base + i - 1)
                    src = (qf[:, dc * W:(dc + 1) * W] if wi == 0
                           else kf2[:, cur * DC * W + dc * W: cur * DC * W + (dc + 1) * W])
                    S.inc("p3sq", sc.activation(q2st[:, (i % 2) * W:(i % 2 + 1) * W], src, AF.Square))

            for i, (wi, dc, mbase) in enumerate(its):
                @blk.tensor
                def _(t, i=i, wi=wi, dc=dc, pnm=pnm, sq_base=sq_base):
                    t.wait_ge(S.get("p3sq"), sq_base + i + 1)
                    if wi == 0 and dc == 0:
                        S.wait(t, "p3ve")
                    S.wait(t, "p3nr")
                    S.inc("p3nt", t.matmul(pnm[0:2, 0:W], onepair[:, 0:2],
                                           q2st[:, (i % 2) * W:(i % 2 + 1) * W],
                                           start=True, stop=True))

                @blk.vector
                def _(v, i=i, mbase=mbase, dc=dc, pnm=pnm):
                    S.wait(v, "p3nt")
                    S.inc("p3nr", v.reduce_max(mxc[0:2, mbase + dc:mbase + dc + 1], pnm[0:2, 0:W],
                                               axis=AX.X))

            @blk.vector
            def _(v, c=c, cur=cur):
                S.wait(v, "p3nr")
                v.reduce_max(nrm[0:2, 0:1], mxc[0:2, MX_QN:MX_QN + 8], axis=AX.X)
                if c > 0:
                    v.reduce_max(nrm[0:2, 1:2], mxc[0:2, MX_KN:MX_KN + 16], axis=AX.X)
                else:
                    v.reduce_max(nrm[0:2, 1:2], mxc[0:2, MX_KN + cur * 8:MX_KN + cur * 8 + 8], axis=AX.X)
                S.inc("p3na", v.drain())

            @blk.tensor
            def _(t, pnm=pnm):
                S.wait(t, "p3na")
                S.inc("p3nt", t.transpose(pnm[0:2, 128:130], nrm[0:2, 0:2], idf[0:2, 0:2]))

            @blk.vector
            def _(v, pnm=pnm):
                S.wait(v, "p3nt")
                v.reduce_max(nrm[0:2, 2:3], pnm[0:2, 128:130], axis=AX.X)
                S.inc("p3na", v.drain())

            @blk.tensor
            def _(t, pnm=pnm):
                S.wait(t, "p3na")
                S.inc("p3nt", t.transpose(pnm[0:1, 130:132], nrm[0:2, 2:3], idf[0:2, 0:2]))

            @blk.vector
            def _(v, pnm=pnm):
                S.wait(v, "p3nt")
                v.reduce_sum(nrm[0:1, 3:4], pnm[0:1, 130:132], axis=AX.X)
                v.drain()
                v.tensor_single_scalar(nrm[0:1, 3:4], nrm[0:1, 3:4], -0.5 * SCALE, OP.mult)
                S.inc("p3na", v.drain())

            @blk.tensor
            def _(t, pnm=pnm):
                S.wait(t, "p3na")
                S.inc("p3nt", t.matmul(pnm[:, 132:133], ones[0:1, 0:128], nrm[0:1, 3:4],
                                       start=True, stop=True))

            @blk.vector
            def _(v, pnm=pnm):
                S.wait(v, "p3nt")
                S.wait(v, "p3xs")
                S.inc("p3nf", v.tensor_copy(mxc[:, MX_M:MX_M + 1], pnm[:, 132:133]))

        with nc.psum_tensor("pSa_%d" % c, [128, 768], F32) as pSa, \
             nc.psum_tensor("pSb_%d" % c, [128, 768], F32) as pSb, \
             nc.psum_tensor("pAVa_%d" % c, [65, W], F32) as pAVa, \
             nc.psum_tensor("pAVb_%d" % c, [65, W], F32) as pAVb, \
             nc.psum_tensor("pCB_%d" % c, [128, 2 * W], F32) as pCB:

            kg0 = 0
            for dc in range(DC):
                xs_base = S.count("p3xs")
                xm_base = S.count("p3xm")
                av_base = S.count("p3av")
                eblk = []
                k = 0
                for qi2 in range(2):
                    g0 = 4 * c + 2 * qi2
                    kts = [G for G in range(g0 - 4, g0 + 2) if G >= 0]
                    nkt = len(kts)
                    j0 = 6 - nkt
                    groups = [(gi, kts[gi:gi + 3]) for gi in range(0, nkt, 3)]

                    for gi, gkts in groups:
                        last = (gi + len(gkts) == nkt)

                        @blk.tensor
                        def _(t, c=c, dc=dc, qi2=qi2, gi=gi, gkts=gkts, cur=cur, prv=prv,
                              pSa=pSa, pSb=pSb, k=k, xs_base=xs_base):
                            S.wait(t, "p3nf")
                            if k == 0:
                                S.wait(t, "p3xs")
                            else:
                                t.wait_ge(S.get("p3xs"), xs_base + 2 * k)
                            if dc == 0 and qi2 == 0 and gi == 0:
                                S.wait(t, "p3ve")
                            for aa, ps in ((0, pSa), (64, pSb)):
                                for j, G in enumerate(gkts):
                                    buf = cur if G >= 4 * c else prv
                                    kcol = buf * DC * W + dc * W + (G % 4) * 128
                                    mm = t.matmul(ps[:, j * 256:(j + 1) * 256],
                                                  kf2[aa:aa + 64, kcol:kcol + 128],
                                                  qf[aa:aa + 64, dc * W + qi2 * 256: dc * W + qi2 * 256 + 256],
                                                  start=True, stop=True)
                            S.inc("p3xm", mm)

                        @blk.vector
                        def _(v, gi=gi, gkts=gkts, j0=j0, k=k, xs_base=xs_base):
                            v.wait_ge(S.get("p3xs"), xs_base + 2 * (k + 1))
                            kb = ((kg0 + k) % 2) * 2 * 768
                            for aa in range(2):
                                for j in range(len(gkts)):
                                    jr = j0 + gi + j
                                    if jr in (0, 1, 4, 5):
                                        mslot = {0: 0, 1: 1, 4: 2, 5: 3}[jr]
                                        v.tensor_tensor(pTs[:, kb + aa * 768 + j * 256: kb + aa * 768 + (j + 1) * 256],
                                                        pTs[:, kb + aa * 768 + j * 256: kb + aa * 768 + (j + 1) * 256],
                                                        mq[:, mslot * 256:(mslot + 1) * 256],
                                                        OP.mult)
                            S.inc("p3xv", v.drain())

                        @blk.tensor
                        def _(t, c=c, dc=dc, qi2=qi2, gi=gi, gkts=gkts, cur=cur, prv=prv,
                              nkt=nkt, last=last, pAVa=pAVa, pAVb=pAVb, VS=VS):
                            S.wait(t, "p3xv")
                            if qi2 == 0 and gi == 0:
                                S.wait(t, "p3xe")
                            kb = ((kg0 + k) % 2) * 2 * 768
                            for aa, pav in ((0, pAVa), (1, pAVb)):
                                h = 2 * dc + aa
                                for j, G in enumerate(gkts):
                                    buf = cur if G >= 4 * c else prv
                                    vcol = buf * VS + ((G % 4) * H + h) * 65
                                    mm = t.matmul(pav[0:65, qi2 * 256:(qi2 + 1) * 256],
                                                  vsb2[:, vcol:vcol + 65],
                                                  pTs[:, kb + aa * 768 + j * 256: kb + aa * 768 + (j + 1) * 256],
                                                  start=(gi == 0 and j == 0),
                                                  stop=(last and j == len(gkts) - 1))
                            S.inc("p3av", mm)

                        eblk.append((k, gkts))
                        k += 1

                @blk.scalar
                def _(sc, eblk=eblk, xm_base=xm_base, av_base=av_base, pSa=pSa, pSb=pSb, kg0=kg0):
                    for k, gkts in eblk:
                        sc.wait_ge(S.get("p3xm"), xm_base + k + 1)
                        if av_base + k - 1 >= 0:
                            sc.wait_ge(S.get("p3av"), av_base + k - 1)
                        n = len(gkts)
                        kb = ((kg0 + k) % 2) * 2 * 768
                        for aa, ps in ((0, pSa), (1, pSb)):
                            S.inc("p3xs", sc.activation(pTs[:, kb + aa * 768: kb + aa * 768 + n * 256],
                                                        ps[:, 0:n * 256], AF.Exp,
                                                        bias=mxc[:, MX_M:MX_M + 1], scale=SCALE))

                kg0 += len(eblk)

                for aa, pav in ((0, pAVa), (1, pAVb)):
                    @blk.vector
                    def _(v, dc=dc, aa=aa, pav=pav):
                        S.wait(v, "p3av")
                        S.wait(v, "l3s")
                        h = 2 * dc + aa
                        po = (h % 2) * 64
                        v.tensor_copy(flat[64:65, aa * W:(aa + 1) * W], pav[64:65, 0:W])
                        v.drain()
                        S.inc("p3xe", v.tensor_copy(avn[po:po + 64, (h // 2) * W:(h // 2 + 1) * W],
                                                    pav[0:64, 0:W]))

                @blk.sync
                def _(sy, dc=dc):
                    S.wait(sy, "p3xe")
                    S.wait(sy, "dqoA")
                    S.wait(sy, "dqoB")
                    for aa in range(2):
                        S.inc("l3s", sy.dma_start(out=flat[2 * dc + aa:2 * dc + aa + 1, 1024:1024 + W],
                                                  in_=flat[64:65, aa * W:(aa + 1) * W]), 16)

            @blk.vector
            def _(v):
                S.wait(v, "l3s")
                v.reciprocal(flat[0:16, 1024:1024 + W], flat[0:16, 1024:1024 + W])
                S.inc("p3nr", v.drain())

            lb_base = S.count("p3lb")
            for dc in range(DC):
                for aa in range(2):
                    r = 2 * dc + aa
                    s2 = r % 2

                    @blk.sync
                    def _(sy, r=r, s2=s2, lb_base=lb_base):
                        S.wait(sy, "p3nr")
                        S.wait(sy, "p3lb", lb_base + r - 1 if r >= 2 else lb_base)
                        S.inc("dqoA" if s2 == 0 else "dqoB",
                              sy.dma_start(out=flat[64:65, 1024 + s2 * W:1024 + (s2 + 1) * W],
                                           in_=flat[r:r + 1, 1024:1024 + W]), 16)

                    @blk.tensor
                    def _(t, aa=aa, s2=s2, pCB=pCB):
                        S.wait(t, "dqoA" if s2 == 0 else "dqoB")
                        S.wait(t, "p3xv")
                        S.inc("p3lb", t.matmul(pCB[:, aa * W:(aa + 1) * W], ones[64:65, 0:128],
                                               flat[64:65, 1024 + s2 * W:1024 + (s2 + 1) * W],
                                               start=True, stop=True))

                    @blk.vector
                    def _(v, dc=dc, aa=aa, pCB=pCB):
                        S.wait(v, "p3lb")
                        S.wait(v, "p3xe")
                        h = 2 * dc + aa
                        po = (h % 2) * 64
                        S.inc("p3xv", v.tensor_tensor(avn[po:po + 64, (h // 2) * W:(h // 2 + 1) * W],
                                                      avn[po:po + 64, (h // 2) * W:(h // 2 + 1) * W],
                                                      pCB[po:po + 64, aa * W:(aa + 1) * W], OP.mult))

        with nc.psum_tensor("pwo_%d" % c, [128, 2 * W], F32) as pwo:
            for oc in range(DC):
                pslot = oc % 2
                ldsem = "ldA" if pslot == 0 else "ldB"

                @blk.sync
                def _(sy, c=c, oc=oc, pslot=pslot, ldsem=ldsem):
                    if S.count("p3wv") >= 2:
                        S.wait(sy, "p3wv", S.count("p3wv") - 1)
                    S.inc(ldsem, sy.dma_start(out=xnstg[:, pslot * W:(pslot + 1) * W],
                                              in_=xn_h.ap()[oc, :, c * W:(c + 1) * W]), 16)

                @blk.tensor
                def _(t, c=c, oc=oc, pslot=pslot, pwo=pwo):
                    S.wait(t, "p3xv")
                    if S.count("p3wv") >= 2:
                        S.wait(t, "p3wv", S.count("p3wv") - 1)
                    ps = pwo[:, pslot * W:(pslot + 1) * W]
                    for dc in range(DC):
                        mm = t.matmul(ps, woT[:, dc * D + oc * 128: dc * D + (oc + 1) * 128],
                                      avn[:, dc * W:(dc + 1) * W],
                                      start=(dc == 0), stop=(dc == DC - 1))
                    S.inc("p3qm", mm)

                @blk.vector
                def _(v, pslot=pslot, ldsem=ldsem, pwo=pwo):
                    S.wait(v, "p3qm")
                    S.wait(v, ldsem)
                    if S.count("p3wv") >= 2:
                        S.wait(v, "p3wA" if pslot == 0 else "p3wB")
                    S.inc("p3wv", v.tensor_tensor(x2stg[:, pslot * W:(pslot + 1) * W],
                                                  pwo[:, pslot * W:(pslot + 1) * W],
                                                  xnstg[:, pslot * W:(pslot + 1) * W], OP.add))

                @blk.sync
                def _(sy, c=c, oc=oc, pslot=pslot):
                    S.wait(sy, "p3wv")
                    S.inc("p3wA" if pslot == 0 else "p3wB",
                          sy.dma_start(out=x2_h.ap()[oc, :, c * W:(c + 1) * W],
                                       in_=x2stg[:, pslot * W:(pslot + 1) * W]), 16)

        for jf in range(c * OCF, (c + 1) * OCF):
            if c == 0 and jf == 0:
                for jl in range(4):
                    emit_f2_load(jl)
            emit_f2_load(jf + 4)
            emit_f2_tile(jf)

    p3.close()

    @blk.vector
    def _(v):
        S.wait(v, "f2v2")
        v.reduce_max(mxc[:, MX_TMP:MX_TMP + 1], mxc[:, MX_UP:MX_UP + OCF * NCH], axis=AX.X)
        S.inc("fmq", v.drain())

    finalize_scalar_max([MX_TMP], 128, 64, "fmq")
    collective(2, 64)
    derive_scales([(72, 32767.0, 6)], 80)

    barrier(("p3wA", "p3wB", "f2s0", "f2s1", "f2s2", "f2s3", "f2l0", "f2l1", "f2l2", "f2l3", "ldA", "ldB", "hlA", "hlB", "wTA"))

    p4 = contextlib.ExitStack()
    w1T = p4.enter_context(nc.sbuf_tensor("w1T", [128, DC * DFF], BF16))
    w2T = p4.enter_context(nc.sbuf_tensor("w2T", [128, 32 * D], BF16))
    x2t = p4.enter_context(nc.sbuf_tensor("x2t", [128, 2 * DC * 256], F32))
    h2t = p4.enter_context(nc.sbuf_tensor("h2t", [128, DC * 256], BF16))
    g1buf = p4.enter_context(nc.sbuf_tensor("g1buf", [128, 32 * 256], BF16))
    sqstg = p4.enter_context(nc.sbuf_tensor("sqstg", [128, 2 * 256], F32))
    pmm = p4.enter_context(nc.psum_tensor("pmm", [128, 2 * W], F32))
    pstat = p4.enter_context(nc.psum_tensor("pstat", [2, 2 * W], F32))
    pab = p4.enter_context(nc.psum_tensor("pab", [128, 2 * W], F32))

    @blk.sync
    def _(sy):
        S.wait(sy, "cast1")
        S.wait(sy, "cast2")
        for dc in range(DC):
            S.inc("wTA", sy.dma_start(out=w1T[:, dc * DFF:(dc + 1) * DFF],
                                      in_=w1_bf.ap()[:, dc * 128:(dc + 1) * 128], transpose=True), 16)
        for kc in range(32):
            S.inc("wTA", sy.dma_start(out=w2T[:, kc * D:(kc + 1) * D],
                                      in_=w2_bf.ap()[:, kc * 128:(kc + 1) * 128], transpose=True), 16)

    WH = W // 2
    for tt in range(2 * NCH):
        xob = (tt % 2) * DC * 256
        @blk.sync
        def _(sy, tt=tt, xob=xob):
            if tt == 0:
                S.wait(sy, "p3wA")
                S.wait(sy, "p3wB")
            if tt >= 2:
                S.wait(sy, "p4av", S.count("p4av") - DC)
            for dc in range(DC):
                S.inc("ldA", sy.dma_start(out=x2t[:, xob + dc * WH:xob + (dc + 1) * WH],
                                          in_=x2_h.ap()[dc, :, tt * WH:(tt + 1) * WH]), 16)

        st_base = S.count("p4st")
        sq4_base = S.count("p4sq")

        @blk.scalar
        def _(sc, tt=tt, xob=xob, st_base=st_base):
            S.wait(sc, "ldA")
            for dc in range(DC):
                if st_base + dc >= 2:
                    sc.wait_ge(S.get("p4st"), st_base + dc - 1)
                S.inc("p4sq", sc.activation(sqstg[:, (dc % 2) * WH:(dc % 2 + 1) * WH],
                                            x2t[:, xob + dc * WH:xob + (dc + 1) * WH], AF.Square))

        for dc in range(DC):
            @blk.tensor
            def _(t, tt=tt, dc=dc, xob=xob, sq4_base=sq4_base):
                t.wait_ge(S.get("p4sq"), sq4_base + dc + 1)
                if tt == 0 and dc == 0:
                    S.wait(t, "wTA")
                if dc == 0:
                    S.wait(t, "ldA")
                    if tt > 0:
                        S.wait(t, "p4va")
                t.matmul(pstat[0:1, 0:WH], ones[:, 0:1], x2t[:, xob + dc * WH:xob + (dc + 1) * WH],
                         start=(dc == 0), stop=(dc == DC - 1))
                S.inc("p4st", t.matmul(pstat[0:1, W:W + WH], ones[:, 0:1],
                                       sqstg[:, (dc % 2) * WH:(dc % 2 + 1) * WH],
                                       start=(dc == 0), stop=(dc == DC - 1)))

        FB = 128

        @blk.vector
        def _(v, tt=tt):
            S.wait(v, "p4st")
            v.tensor_single_scalar(flat[0:1, FB:FB + WH], pstat[0:1, 0:WH], 1.0 / D, OP.mult)
            v.tensor_single_scalar(flat[0:1, FB + WH:FB + 2 * WH], pstat[0:1, W:W + WH], 1.0 / D, OP.mult)
            v.drain()
            v.tensor_tensor(flat[0:1, FB + 2 * WH:FB + 3 * WH], flat[0:1, FB:FB + WH],
                            flat[0:1, FB:FB + WH], OP.mult)
            v.drain()
            v.tensor_tensor(flat[0:1, FB + 2 * WH:FB + 3 * WH], flat[0:1, FB + WH:FB + 2 * WH],
                            flat[0:1, FB + 2 * WH:FB + 3 * WH], OP.subtract)
            v.drain()
            v.tensor_single_scalar(flat[0:1, FB + 2 * WH:FB + 3 * WH], flat[0:1, FB + 2 * WH:FB + 3 * WH],
                            EPS, OP.add)
            S.inc("p4va", v.drain())

        @blk.scalar
        def _(sc):
            S.wait(sc, "p4va")
            S.inc("p4sr", sc.activation(flat[0:1, FB + 3 * WH:FB + 4 * WH],
                                        flat[0:1, FB + 2 * WH:FB + 3 * WH], AF.Sqrt))

        @blk.vector
        def _(v):
            S.wait(v, "p4sr")
            v.reciprocal(flat[0:1, FB + 2 * WH:FB + 3 * WH], flat[0:1, FB + 3 * WH:FB + 4 * WH])
            v.drain()
            v.tensor_tensor(flat[0:1, FB:FB + WH], flat[0:1, FB:FB + WH],
                            flat[0:1, FB + 2 * WH:FB + 3 * WH], OP.mult)
            v.drain()
            v.tensor_single_scalar(flat[0:1, FB:FB + WH], flat[0:1, FB:FB + WH], -1.0, OP.mult)
            S.inc("p4va", v.drain())

        @blk.tensor
        def _(t, tt=tt):
            S.wait(t, "p4va")
            if tt > 0:
                S.wait(t, "p4h2")
            t.matmul(pab[:, 0:WH], ones[0:1, 0:128], flat[0:1, FB + 2 * WH:FB + 3 * WH], start=True, stop=True)
            S.inc("p4m1", t.matmul(pab[:, W:W + WH], ones[0:1, 0:128], flat[0:1, FB:FB + WH],
                                   start=True, stop=True))

        @blk.vector
        def _(v, tt=tt):
            S.wait(v, "p4m1")
            if tt > 0:
                S.wait(v, "p4ge")
            for dc in range(DC):
                v.tensor_tensor(sqstg[:, 0:WH], x2t[:, xob + dc * WH:xob + (dc + 1) * WH], pab[:, 0:WH], OP.mult)
                v.drain()
                v.tensor_tensor(sqstg[:, 0:WH], sqstg[:, 0:WH], pab[:, W:W + WH], OP.add)
                v.drain()
                v.tensor_scalar(h2t[:, dc * WH:(dc + 1) * WH], sqstg[:, 0:WH],
                                gb[:, G2C + dc:G2C + dc + 1], gb[:, B2C + dc:B2C + dc + 1],
                                OP.mult, OP.add)
                v.drain()
            S.inc("p4h2", v.drain())

        p4m1_base = S.count("p4m1")
        p4ge_base = S.count("p4ge")
        for o1 in range(32):
            pslot = o1 % 2

            @blk.tensor
            def _(t, tt=tt, o1=o1, pslot=pslot, p4ge_base=p4ge_base):
                S.wait(t, "p4h2")
                if o1 >= 2:
                    t.wait_ge(S.get("p4ge"), p4ge_base + o1 - 1)
                for dc in range(DC):
                    mm = t.matmul(pmm[:, pslot * W:pslot * W + WH],
                                  w1T[:, dc * DFF + o1 * 128: dc * DFF + (o1 + 1) * 128],
                                  h2t[:, dc * WH:(dc + 1) * WH], start=(dc == 0), stop=(dc == DC - 1))
                S.inc("p4m1", mm)

        @blk.scalar
        def _(sc, tt=tt, p4m1_base=p4m1_base):
            if tt > 0:
                S.wait(sc, "p4m1", p4m1_base)
            for o1 in range(32):
                pslot = o1 % 2
                sc.wait_ge(S.get("p4m1"), p4m1_base + o1 + 1)
                S.inc("p4ge", sc.activation(g1buf[:, o1 * WH:(o1 + 1) * WH],
                                            pmm[:, pslot * W:pslot * W + WH],
                                            AF.Gelu, bias=b1c[:, o1:o1 + 1], scale=1.0))

        for o2 in range(DC):
            pslot = o2 % 2

            @blk.tensor
            def _(t, tt=tt, o2=o2, pslot=pslot):
                S.wait(t, "p4ge")
                if S.count("p4av") >= 2:
                    S.wait(t, "p4av", S.count("p4av") - 1)
                for kc in range(32):
                    mm = t.matmul(pmm[:, pslot * W:pslot * W + WH],
                                  w2T[:, kc * D + o2 * 128: kc * D + (o2 + 1) * 128],
                                  g1buf[:, kc * WH:(kc + 1) * WH], start=(kc == 0), stop=(kc == 31))
                S.inc("p4m1", mm)

            @blk.vector
            def _(v, o2=o2, pslot=pslot, xob=xob):
                S.wait(v, "p4m1")
                if S.count("p4av") >= 2:
                    S.wait(v, "p3wA" if pslot == 0 else "p3wB")
                S.inc("p4av", v.scalar_tensor_tensor(flat[:, 1152 + pslot * WH:1152 + (pslot + 1) * WH],
                                                     pmm[:, pslot * W:pslot * W + WH],
                                                     gb[:, MB2C + o2:MB2C + o2 + 1],
                                                     x2t[:, xob + o2 * WH:xob + (o2 + 1) * WH], OP.add, OP.add))

            @blk.sync
            def _(sy, tt=tt, o2=o2, pslot=pslot):
                S.wait(sy, "p4av")
                S.inc("p3wA" if pslot == 0 else "p3wB",
                      sy.dma_start(out=acc_h.ap()[o2, :, tt * WH:(tt + 1) * WH],
                                   in_=flat[:, 1152 + pslot * WH:1152 + (pslot + 1) * WH]), 16)

        for jf in range(tt * (OCF * NCH) // (2 * NCH), (tt + 1) * (OCF * NCH) // (2 * NCH)):
            if tt == 0 and jf == 0:
                for jl in range(4):
                    emit_f3_load(jl)
            emit_f3_load(jf + 4)
            emit_f3_tile(jf)

    p4.close()

    @blk.vector
    def _(v):
        S.wait(v, "f3v")
        v.reduce_max(mxc[:, MX_TMP:MX_TMP + 1], mxc[:, MX_GA:MX_GA + OCF * NCH], axis=AX.X)
        S.inc("fmq", v.drain())

    finalize_scalar_max([MX_TMP], 128, 96, "fmq")
    collective(3, 96)
    derive_scales([(104, 32767.0, 8)], 112)

    f2.close()
    barrier(("p3wA", "p3wB", "f3s0", "f3s1", "f3s2", "f3s3", "f3l0", "f3l1", "f3l2", "f3l3", "ldA", "wTA"))

    f4 = contextlib.ExitStack()
    wdT = f4.enter_context(nc.sbuf_tensor("wdT", [128, OCF * D], BF16))
    obuf = f4.enter_context(nc.sbuf_tensor("obuf", [128, 2 * OCF * W], BF16))
    old = f4.enter_context(nc.sbuf_tensor("old", [128, 2 * W], F32))
    oi32 = f4.enter_context(nc.sbuf_tensor("oi32", [128, W], I32))
    accld = f4.enter_context(nc.sbuf_tensor("accld", [128, 2 * W], F32))
    sumf = f4.enter_context(nc.sbuf_tensor("sumf", [128, DC * W], F32))
    xfin = f4.enter_context(nc.sbuf_tensor("xfin", [128, 2 * 1024], F32))
    outst = f4.enter_context(nc.sbuf_tensor("outst", [128, 2 * 1024], F32))
    pdn = f4.enter_context(nc.psum_tensor("pdn", [128, 2 * W], F32))
    pfin = f4.enter_context(nc.psum_tensor("pfin", [128, 2 * 1024], F32))

    @blk.sync
    def _(sy):
        S.wait(sy, "dqoA")
        S.wait(sy, "dqoB")
        for kc in range(OCF):
            S.inc("wTA", sy.dma_start(out=wdT[:, kc * D:(kc + 1) * D],
                                      in_=wd_dq.ap()[:, kc * 128:(kc + 1) * 128], transpose=True), 16)

    for tt in range(NCH):
        ob = (tt % 2) * OCF * W
        for kc in range(OCF):
            buf = kc % 2
            ldsem = "ldA" if buf == 0 else "ldB"

            @blk.sync
            def _(sy, tt=tt, kc=kc, buf=buf, ldsem=ldsem):
                if tt * OCF + kc >= 2:
                    S.wait(sy, "f4v", S.count("f4v") - 1)
                S.inc(ldsem, sy.dma_start(out=old[:, buf * W:(buf + 1) * W],
                                          in_=or_h.ap()[kc, :, tt * W:(tt + 1) * W]), 16)

            @blk.vector
            def _(v, tt=tt, kc=kc, buf=buf, ldsem=ldsem, ob=ob):
                S.wait(v, ldsem)
                if tt == 0 and kc == 0:
                    S.wait(v, "dsv")
                if tt >= 2 and kc == 0:
                    S.wait(v, "f4mm", S.count("f4mm") - 40)
                v.tensor_scalar(oi32[:, :], old[:, buf * W:(buf + 1) * W], scl[:, 9:10], KQH,
                                OP.mult, OP.add)
                v.drain()
                v.tensor_scalar(obuf[:, ob + kc * W:ob + (kc + 1) * W], oi32[:, :], KQ, scl[:, 8:9],
                                OP.subtract, OP.mult)
                S.inc("f4v", v.drain())

        for d2 in range(DC):
            pslot = d2 % 2
            lasem = "f4la" if pslot == 0 else "f4lb"

            @blk.sync
            def _(sy, tt=tt, d2=d2, pslot=pslot, lasem=lasem):
                if S.count("f4av") >= 2:
                    S.wait(sy, "f4av", S.count("f4av") - 1)
                S.inc(lasem, sy.dma_start(out=accld[:, pslot * W:(pslot + 1) * W],
                                          in_=acc_h.ap()[d2, :, tt * W:(tt + 1) * W]), 16)

            @blk.tensor
            def _(t, tt=tt, d2=d2, pslot=pslot, ob=ob):
                S.wait(t, "f4v")
                if tt == 0 and d2 == 0:
                    S.wait(t, "wTA")
                if S.count("f4av") >= 2:
                    S.wait(t, "f4av", S.count("f4av") - 1)
                for kc in range(OCF):
                    mm = t.matmul(pdn[:, pslot * W:(pslot + 1) * W],
                                  wdT[:, kc * D + d2 * 128: kc * D + (d2 + 1) * 128],
                                  obuf[:, ob + kc * W:ob + (kc + 1) * W], start=(kc == 0), stop=(kc == OCF - 1))
                S.inc("f4mm", mm)

            @blk.vector
            def _(v, d2=d2, pslot=pslot, lasem=lasem):
                S.wait(v, "f4mm")
                S.wait(v, lasem)
                S.wait(v, "f4mm")
                S.inc("f4av", v.tensor_tensor(sumf[:, d2 * W:(d2 + 1) * W],
                                              pdn[:, pslot * W:(pslot + 1) * W],
                                              accld[:, pslot * W:(pslot + 1) * W], OP.add))

        for g in range(4):
            gbuf = g % 2
            lxsem = "f4lx" if gbuf == 0 else "f4ly"
            stsem = "stA" if gbuf == 0 else "stB"

            @blk.sync
            def _(sy, tt=tt, g=g, gbuf=gbuf, lxsem=lxsem):
                if S.count("f4fv") >= 2:
                    S.wait(sy, "f4fv", S.count("f4fv") - 1)
                src = x_d[tt * W + g * 128: tt * W + (g + 1) * 128, :]
                S.inc(lxsem, sy.dma_start(out=xfin[:, gbuf * 1024:(gbuf + 1) * 1024], in_=src), 16)

            @blk.tensor
            def _(t, tt=tt, g=g, gbuf=gbuf):
                S.wait(t, "f4av")
                if S.count("f4fv") >= 2:
                    S.wait(t, "f4fv", S.count("f4fv") - 1)
                for d2 in range(DC):
                    S.inc("f4mm", t.transpose(pfin[:, gbuf * 1024 + d2 * 128: gbuf * 1024 + (d2 + 1) * 128],
                                              sumf[:, d2 * W + g * 128: d2 * W + (g + 1) * 128], idf[:]))

            @blk.vector
            def _(v, g=g, gbuf=gbuf, lxsem=lxsem, stsem=stsem):
                S.wait(v, "f4mm")
                S.wait(v, lxsem)
                if S.count("f4fv") >= 2:
                    S.wait(v, stsem)
                S.inc("f4fv", v.tensor_tensor(outst[:, gbuf * 1024:(gbuf + 1) * 1024],
                                              pfin[:, gbuf * 1024:(gbuf + 1) * 1024],
                                              xfin[:, gbuf * 1024:(gbuf + 1) * 1024], OP.add))

            @blk.sync
            def _(sy, tt=tt, g=g, gbuf=gbuf, stsem=stsem):
                S.wait(sy, "f4fv")
                S.inc(stsem, sy.dma_start(out=out_d[tt * W + g * 128: tt * W + (g + 1) * 128, :],
                                          in_=outst[:, gbuf * 1024:(gbuf + 1) * 1024]), 16)

    @blk.sync
    def _(sy):
        S.wait(sy, "stA")
        S.wait(sy, "stB")

    f4.close()
    return nc, st


_PROG_CACHE = {}


def _get_program():
    if "nc" not in _PROG_CACHE:
        nc, st = build_program()
        _PROG_CACHE["nc"] = nc
        _PROG_CACHE["st"] = st
    return _PROG_CACHE["nc"]


def make_in_maps(inputs):
    x_full = np.ascontiguousarray(np.asarray(inputs["L0_attn_x"], dtype=np.float32))
    rk = np.arange(128)
    mUh = (rk[:, None] >= rk[None, :] + 1).astype(ml_dtypes.bfloat16)
    mLh = (rk[:, None] <= rk[None, :]).astype(ml_dtypes.bfloat16)
    rq = np.arange(256)
    m0 = (rk[:, None] >= rq[None, :] + 1)
    m1 = (rk[:, None] >= rq[None, :] - 127)
    m4 = (rk[:, None] <= rq[None, :])
    m5 = (rk[:, None] <= rq[None, :] - 128)
    mqh = np.concatenate([m0, m1, m4, m5], axis=1).astype(ml_dtypes.bfloat16)
    idn = np.eye(128, dtype=np.float32)
    common = {
        "wqkv": np.ascontiguousarray(np.asarray(inputs["L0_attn_w_qkv"], dtype=np.float32)),
        "wo": np.ascontiguousarray(np.asarray(inputs["L0_attn_w_o"], dtype=np.float32)),
        "w1": np.ascontiguousarray(np.asarray(inputs["L0_attn_mlp_w1"], dtype=np.float32)),
        "b1": np.ascontiguousarray(np.asarray(inputs["L0_attn_mlp_b1"], dtype=np.float32)),
        "w2": np.ascontiguousarray(np.asarray(inputs["L0_attn_mlp_w2"], dtype=np.float32)),
        "b2": np.ascontiguousarray(np.asarray(inputs["L0_attn_mlp_b2"], dtype=np.float32)),
        "g1v": np.ascontiguousarray(np.asarray(inputs["L0_attn_ln1_g"], dtype=np.float32)),
        "b1v": np.ascontiguousarray(np.asarray(inputs["L0_attn_ln1_b"], dtype=np.float32)),
        "g2v": np.ascontiguousarray(np.asarray(inputs["L0_attn_ln2_g"], dtype=np.float32)),
        "b2v": np.ascontiguousarray(np.asarray(inputs["L0_attn_ln2_b"], dtype=np.float32)),
        "wg": np.ascontiguousarray(np.asarray(inputs["L0_ffn_w_gate"], dtype=np.float32)),
        "wu": np.ascontiguousarray(np.asarray(inputs["L0_ffn_w_up"], dtype=np.float32)),
        "wd": np.ascontiguousarray(np.asarray(inputs["L0_ffn_w_down"], dtype=np.float32)),
        "idf": idn,
        "mU": mUh,
        "mL": mLh,
        "mq": mqh,
    }
    return [dict(common, x=np.ascontiguousarray(x_full[c])) for c in range(NCORES)]


def kernel(**inputs):
    in_maps = make_in_maps(inputs)
    nc = _get_program()
    res = run_bass_kernel_spmd(nc, in_maps, core_ids=list(range(NCORES)))
    return np.stack([np.asarray(r["out"], dtype=np.float32) for r in res.results], axis=0)
